# version 02
# baseline (speedup 1.0000x reference)
import contextlib
import numpy as np
import ml_dtypes
import concourse.bass as bass
import concourse.mybir as mybir
from concourse.bass_utils import run_bass_kernel_spmd

BF = ml_dtypes.bfloat16
F32 = mybir.dt.float32
BF16 = mybir.dt.bfloat16
AF = mybir.ActivationFunctionType
ALU = mybir.AluOpType

NCORES = 8
T = 1024
D = 4096
SEQ = 4096
NCH = D // 128
H0, HD0, ROT = 16, 256, 64
G1, R1, HD1 = 8, 4, 128
HFF = 11008
FOLDH = 2752
EPS = 1e-5

_LAUNCHES = []



def to_image(xT):
    R, C = xT.shape
    ch = R // 128
    return np.ascontiguousarray(xT.reshape(ch, 128, C).transpose(1, 0, 2).reshape(128, ch * C))


def from_image(img, rows):
    P, X = img.shape
    ch = rows // 128
    C = X // ch
    return np.ascontiguousarray(img.reshape(128, ch, C).transpose(1, 0, 2).reshape(rows, C))


def w_image(wT, mb=512):
    K, M = wT.shape
    kc = K // 128
    nm = M // mb
    a = wT.reshape(kc, 128, nm, mb).transpose(2, 1, 0, 3).reshape(nm, 128, kc * mb)
    return np.ascontiguousarray(a)


def l0_head_perm():
    perm = []
    for h in range(H0):
        b = h * HD0
        perm += [b + 2 * i for i in range(32)]
        perm += [b + 2 * i + 1 for i in range(32)]
        perm += list(range(b + 64, b + 256))
    return np.array(perm)


def rope_tables(positions):
    invf = 1.0 / 10000.0 ** (np.arange(0, ROT, 2, dtype=np.float64) / ROT)
    ang = positions[None, :].astype(np.float64) * invf[:, None]
    return np.cos(ang).astype(np.float32), np.sin(ang).astype(np.float32)



class Sem:
    def __init__(self, h):
        self.h = h
        self.n = 0

    def inc(self, k=1):
        self.n += k
        return self.n


class Prog:
    def __init__(self, name):
        self.nc = bass.Bass("TRN2", target_bir_lowering=False)
        self.name = name
        self.stack = contextlib.ExitStack()
        self.sems = {}
        self.bufs = {}
        self.misc = {}

    def sem(self, name):
        if name not in self.sems:
            self.sems[name] = Sem(self.stack.enter_context(self.nc.semaphore(name)))
        return self.sems[name]

    def sbuf(self, name, shape, dtype):
        t = self.stack.enter_context(self.nc.sbuf_tensor(name, list(shape), dtype))
        self.bufs[name] = t
        return t

    def psum(self, name, shape, dtype=F32):
        t = self.stack.enter_context(self.nc.psum_tensor(name, list(shape), dtype))
        self.bufs[name] = t
        return t

    def din(self, name, shape, dtype=F32):
        return self.nc.dram_tensor(name, list(shape), dtype, kind="ExternalInput").ap()

    def dout(self, name, shape, dtype):
        return self.nc.dram_tensor(name, list(shape), dtype, kind="ExternalOutput").ap()

    def close(self):
        self.stack.close()



def build_l0_qkv():
    p = Prog("l0qkv")
    nc = p.nc
    x_h = p.din("x0i", [128, NCH * T])
    wq_h = p.din("wqi", [8, 128, NCH * 512])
    wk_h = p.din("wki", [8, 128, NCH * 512])
    wv_h = p.din("wvi", [8, 128, NCH * 512])
    cos_h = p.din("cosi", [64, T])
    sin_h = p.din("sini", [64, T])
    ones_h = p.din("onesi", [128, 128])
    qo_h = p.dout("qo", [128, NCH * T], BF16)
    ko_h = p.dout("ko", [128, NCH * T], BF16)
    vo_h = p.dout("vo", [128, (T // 128) * D], BF16)
    qn_h = p.dout("qn", [128, H0], F32)
    kn_h = p.dout("kn", [128, H0], F32)

    X = p.sbuf("X", [128, NCH * T], BF16)
    WB = [p.sbuf(f"WB{i}", [128, NCH * 512], BF16) for i in range(2)]
    QB = [p.sbuf(f"QB{i}", [128, 4 * T], BF16) for i in range(2)]
    VB = [p.sbuf(f"VB{i}", [128, 8 * 512], BF16) for i in range(2)]
    CO = p.sbuf("CO", [64, T], F32)
    SI = p.sbuf("SI", [64, T], F32)
    SC = [p.sbuf(f"SC{i}", [32, T], F32) for i in range(4)]
    ONES = p.sbuf("ONES", [128, 128], BF16)
    SQB = [p.sbuf(f"SQB{i}", [128, 512], BF16) for i in range(2)]
    RMX = p.sbuf("RMX", [128, 2], F32)
    QNB = p.sbuf("QNB", [128, H0], F32)
    KNB = p.sbuf("KNB", [128, H0], F32)
    FLU = p.sbuf("FLU", [1, 4], F32)
    PS = p.psum("PS", [128, 8 * 512], F32)

    wd = p.sem("wd")
    pe = p.sem("pe")
    ev = p.sem("ev")
    rp = p.sem("rp")
    so = p.sem("so")
    dv = p.sem("dv")
    sq = p.sem("sq")
    pn = p.sem("pn")
    dn = p.sem("dn")

    IN_DMAS = 4
    def wd_after(i):
        return 16 * (IN_DMAS + i + 1)

    def pe_qk(g, half, mc):
        return g * 8 + half * 4 + mc + 1

    def pe_v(mb, kt):
        return 16 * 8 + mb * 8 + kt + 1

    ev_qk = pe_qk
    def ev_v(mb, kt):
        return 16 * 8 + mb * 8 + kt + 1

    def sq_u(g, u):
        return g * 8 + (u + 1) * 2

    def pn_u(g, u):
        return g * 4 + u + 1

    def dn_u(g, u):
        return g * 4 + u + 1

    def rp_g(g):
        return g + 1

    def so_qk(g):
        return 16 * (g + 1)

    def so_v(mb):
        return 16 * (16 + mb + 1) + 32

    with nc.Block() as blk:
        @blk.gpsimd
        def _(g):
            g.dma_start(out=X[:, :], in_=x_h).then_inc(wd.h, 16)
            g.dma_start(out=CO[:, :], in_=cos_h).then_inc(wd.h, 16)
            g.dma_start(out=SI[:, :], in_=sin_h).then_inc(wd.h, 16)
            g.dma_start(out=ONES[:, :], in_=ones_h).then_inc(wd.h, 16)
            srcs = [wq_h, wk_h]
            for i in range(24):
                if i >= 2:
                    if i - 2 < 16:
                        g.wait_ge(pe.h, pe_qk(i - 2, 1, 3))
                    else:
                        g.wait_ge(pe.h, pe_v(i - 2 - 16, 7))
                if i < 16:
                    src = srcs[i // 8][i % 8]
                else:
                    src = wv_h[i - 16]
                g.dma_start(out=WB[i % 2][:, :], in_=src).then_inc(wd.h, 16)

        @blk.tensor
        def _(t):
            t.wait_ge(wd.h, 16 * IN_DMAS)
            for gq in range(16):
                t.wait_ge(wd.h, wd_after(gq))
                for half in range(2):
                    for mc in range(4):
                        piece = gq * 8 + half * 4 + mc
                        bank = piece % 6
                        if piece - 6 >= 0:
                            t.wait_ge(ev.h, piece - 6 + 1)
                        pv = PS[:, bank * 512:(bank + 1) * 512]
                        for kc in range(NCH):
                            ins = t.matmul(
                                pv,
                                WB[gq % 2][:, kc * 512 + mc * 128: kc * 512 + (mc + 1) * 128],
                                X[:, kc * T + half * 512: kc * T + half * 512 + 512],
                                start=(kc == 0), stop=(kc == NCH - 1))
                        ins.then_inc(pe.h, 1)
                for u in range(4):
                    par = u % 2
                    if gq * 4 + u - 2 >= 0:
                        t.wait_ge(dn.h, gq * 4 + u - 2 + 1)
                    npv = PS[:, 3072 + par * 512: 3072 + par * 512 + 512]
                    for c in range(2):
                        t.wait_ge(sq.h, sq_u(gq, u) - 1 + c)
                        ins = t.matmul(npv, ONES[:, :], SQB[c][:, :],
                                       start=(c == 0), stop=(c == 1))
                    ins.then_inc(pn.h, 1)
            for mb in range(8):
                t.wait_ge(wd.h, wd_after(16 + mb))
                if mb > 0:
                    t.wait_ge(ev.h, ev_v(mb - 1, 7))
                else:
                    t.wait_ge(ev.h, pe_qk(15, 1, 3))
                    t.wait_ge(dn.h, 64)
                for kc in range(NCH):
                    for kt in range(8):
                        pv = PS[:, kt * 512:(kt + 1) * 512]
                        ins = t.matmul(
                            pv,
                            X[:, kc * T + kt * 128: kc * T + (kt + 1) * 128],
                            WB[(16 + mb) % 2][:, kc * 512:(kc + 1) * 512],
                            start=(kc == 0), stop=(kc == NCH - 1))
                        if kc == NCH - 1:
                            ins.then_inc(pe.h, 1)

        @blk.scalar
        def _(s):
            for gq in range(16):
                for half in range(2):
                    for mc in range(4):
                        piece = gq * 8 + half * 4 + mc
                        bank = piece % 6
                        s.wait_ge(pe.h, pe_qk(gq, half, mc))
                        if gq >= 2:
                            s.wait_ge(so.h, so_qk(gq - 2))
                        pv = PS[:, bank * 512:(bank + 1) * 512]
                        s.copy(
                            out=QB[gq % 2][:, mc * T + half * 512: mc * T + half * 512 + 512],
                            in_=pv).then_inc(ev.h, 1)
                s.wait_ge(rp.h, gq + 1)
                for u in range(4):
                    hl, half = u // 2, u % 2
                    guni = gq * 4 + u
                    if guni >= 1:
                        s.wait_ge(pn.h, guni)
                    for c in range(2):
                        colc = (2 * hl + c) * T + half * 512
                        s.activation(SQB[c][:, :], QB[gq % 2][:, colc:colc + 512],
                                     AF.Square).then_inc(sq.h, 1)
            for mb in range(8):
                for kt in range(8):
                    s.wait_ge(pe.h, pe_v(mb, kt))
                    if mb >= 2:
                        s.wait_ge(so.h, so_v(mb - 2))
                    pv = PS[:, kt * 512:(kt + 1) * 512]
                    s.copy(out=VB[mb % 2][:, kt * 512:(kt + 1) * 512],
                           in_=pv).then_inc(ev.h, 1)

        @blk.vector
        def _(v):
            ndv = 0
            for gq in range(16):
                v.wait_ge(ev.h, ev_qk(gq, 1, 3))
                qb = QB[gq % 2]
                for h in range(2):
                    base = h * 2 * T
                    E = qb[0:32, base:base + T]
                    O = qb[32:64, base:base + T]
                    v.tensor_tensor(SC[0][:, :], E, CO[0:32, :], op=ALU.mult)
                    v.tensor_tensor(SC[1][:, :], E, SI[0:32, :], op=ALU.mult)
                    v.tensor_tensor(SC[2][:, :], O, CO[32:64, :], op=ALU.mult)
                    ins = v.tensor_tensor(SC[3][:, :], O, SI[32:64, :], op=ALU.mult)
                    ins.then_inc(dv.h, 1)
                    ndv += 1
                    v.wait_ge(dv.h, ndv)
                    v.tensor_tensor(E, SC[0][:, :], SC[3][:, :], op=ALU.subtract)
                    ins = v.tensor_tensor(O, SC[2][:, :], SC[1][:, :], op=ALU.add)
                    ins.then_inc(dv.h, 1)
                    ndv += 1
                    v.wait_ge(dv.h, ndv)
                v.tensor_copy(FLU[0:1, 0:2], FLU[0:1, 2:4]).then_inc(dv.h, 1)
                ndv += 1
                v.wait_ge(dv.h, ndv)
                v.nop().then_inc(rp.h, 1)
                HMX = QNB if gq < 8 else KNB
                for u in range(4):
                    hl, half = u // 2, u % 2
                    par = u % 2
                    head_col = (gq % 8) * 2 + hl
                    v.wait_ge(pn.h, gq * 4 + u + 1)
                    npv = PS[:, 3072 + par * 512: 3072 + par * 512 + 512]
                    ins = v.reduce_max(RMX[:, par:par + 1], npv, axis=mybir.AxisListType.X)
                    ins.then_inc(dv.h, 1)
                    ndv += 1
                    v.wait_ge(dv.h, ndv)
                    if half == 0:
                        ins = v.tensor_copy(HMX[:, head_col:head_col + 1],
                                            RMX[:, par:par + 1])
                    else:
                        ins = v.tensor_tensor(HMX[:, head_col:head_col + 1],
                                              RMX[:, par:par + 1],
                                              HMX[:, head_col:head_col + 1], op=ALU.max)
                    ins.then_inc(dv.h, 1)
                    ndv += 1
                    v.wait_ge(dv.h, ndv)
                    v.tensor_copy(FLU[0:1, 0:2], FLU[0:1, 2:4]).then_inc(dv.h, 1)
                    ndv += 1
                    v.wait_ge(dv.h, ndv)
                    v.nop().then_inc(dn.h, 1)

        @blk.sync
        def _(sy):
            for gq in range(16):
                sy.wait_ge(rp.h, rp_g(gq))
                dst = qo_h if gq < 8 else ko_h
                mg = gq % 8
                sy.dma_start(out=dst[:, mg * 4 * T:(mg + 1) * 4 * T],
                             in_=QB[gq % 2][:, :]).then_inc(so.h, 16)
            sy.wait_ge(dn.h, 32)
            sy.dma_start(out=qn_h, in_=QNB[:, :]).then_inc(so.h, 16)
            sy.wait_ge(dn.h, 64)
            sy.dma_start(out=kn_h, in_=KNB[:, :]).then_inc(so.h, 16)
            for mb in range(8):
                sy.wait_ge(ev.h, ev_v(mb, 7))
                src = VB[mb % 2][:, :].rearrange("p (kt m) -> p kt m", m=512)
                dst = vo_h.rearrange("p (kt c) -> p kt c", c=D)[:, :, mb * 512:(mb + 1) * 512]
                sy.dma_start(out=dst, in_=src).then_inc(so.h, 16)
            sy.wait_ge(so.h, 16 * 26)

    return p





NSLOT = 48
QB_START = (0, 16)
QB_END = (15, 47)


def build_l0_attn():
    p = Prog("l0attn")
    nc = p.nc
    x_h = p.din("x0i", [128, NCH * T])
    q_h = p.din("qi", [128, NCH * T], BF16)
    ksl_h = p.din("ksl", [128, NCH * NSLOT * 128], BF16)
    vsl_h = p.din("vsl", [128, NSLOT * D], BF16)
    mk_h = p.din("mki", [128, NSLOT * 512], mybir.dt.uint8)
    qn_h = p.din("qn4", [128, 4 * H0])
    kn_h = p.din("kn4", [128, 4 * H0])
    ones_h = p.din("onesi", [128, 128])
    wo_h = p.din("woi", [16, 128, NCH * 256])
    yo_h = p.dout("yo", [128, NCH * T], BF16)
    mu_h = p.dout("muo", [128, T], F32)
    rs_h = p.dout("rso", [128, T], F32)

    AB = p.sbuf("AB", [128, NCH * T], BF16)
    KHB = [p.sbuf(f"KHB{i}", [128, 2 * 12 * 128], BF16) for i in range(2)]
    VHB = [p.sbuf(f"VHB{i}", [128, 12 * 256], BF16) for i in range(2)]
    QHB = [p.sbuf(f"QHB{i}", [128, 2 * T], BF16) for i in range(2)]
    MK = p.sbuf("MK", [128, NSLOT * 512], mybir.dt.uint8)
    PT = [p.sbuf(f"PT{i}", [128, 512], BF16) for i in range(2)]
    QN4 = p.sbuf("QN4", [128, 4 * H0], F32)
    KN4 = p.sbuf("KN4", [128, 4 * H0], F32)
    QMX = p.sbuf("QMX", [128, H0], F32)
    KMX = p.sbuf("KMX", [128, H0], F32)
    MBALL = p.sbuf("MBALL", [128, H0], F32)
    ONES = p.sbuf("ONES", [128, 128], BF16)
    RL = p.sbuf("RL", [128, 512], F32)
    WOB = [p.sbuf(f"WOB{i}", [128, NCH * 256], BF16) for i in range(2)]
    X0C = [p.sbuf(f"X0C{i}", [128, T], BF16) for i in range(2)]
    YOB = [p.sbuf(f"YOB{i}", [128, 2 * T], BF16) for i in range(2)]
    SQY = [p.sbuf(f"SQY{i}", [128, 512], BF16) for i in range(2)]
    MU = p.sbuf("MU", [128, T], F32)
    MSQ = p.sbuf("MSQ", [128, T], F32)
    VARB = p.sbuf("VARB", [128, T], F32)
    SDB = p.sbuf("SDB", [128, T], F32)
    RSTD = p.sbuf("RSTD", [128, T], F32)
    FLU = p.sbuf("FLU", [1, 4], F32)
    PS = p.psum("PS", [128, 8 * 512], F32)

    wd = p.sem("wd")
    qd = p.sem("qd")
    st = p.sem("st")
    ae = p.sem("ae")
    pm = p.sem("pm")
    pa = p.sem("pa")
    ab = p.sem("ab")
    dv = p.sem("dv")
    mbd = p.sem("mbd")
    mba = p.sem("mba")
    wp = p.sem("wp")
    ey = p.sem("ey")
    es = p.sem("es")
    sp = p.sem("sp")
    fd = p.sem("fd")
    fa = p.sem("fa")
    so = p.sem("so")

    NPC = 4
    NMG = 16

    def qd_group(h, gr):
        return 16 * (13 * h + 1 + 3 * (gr + 1))

    def wd_w(mg):
        return 16 * (4 + 3 * mg + 1)

    def wd_x0(mg):
        return 16 * (4 + 3 * mg + 3)

    with nc.Block() as blk:
        @blk.gpsimd
        def _(g):
            g.dma_start(out=ONES[:, :], in_=ones_h).then_inc(wd.h, 16)
            g.dma_start(out=QN4[:, :], in_=qn_h).then_inc(wd.h, 16)
            g.dma_start(out=KN4[:, :], in_=kn_h).then_inc(wd.h, 16)
            g.dma_start(out=MK[:, :], in_=mk_h).then_inc(wd.h, 16)
            for h in range(H0):
                for u in range(NSLOT):
                    su = h * NSLOT + u
                    g.wait_ge(ae.h, su + 1)
                    g.tensor_tensor(PT[u % 2][:, :], PT[u % 2][:, :],
                                    MK[:, u * 512:(u + 1) * 512],
                                    op=ALU.mult).then_inc(pm.h, 1)
            for mg in range(NMG):
                if mg >= 2:
                    g.wait_ge(wp.h, (mg - 2) * NPC + NPC)
                g.dma_start(out=WOB[mg % 2][:, :], in_=wo_h[mg]).then_inc(wd.h, 16)
                for mc in range(2):
                    if mg >= 1:
                        g.wait_ge(ey.h, (mg - 1) * NPC + NPC)
                    g.dma_start(out=X0C[mc][:, :],
                                in_=x_h[:, (2 * mg + mc) * T:(2 * mg + mc + 1) * T]
                                ).then_inc(wd.h, 16)

        @blk.sync
        def _(sy):
            for h in range(H0):
                if h >= 2:
                    sy.wait_ge(pa.h, 48 * (h - 1))
                sy.dma_start(out=QHB[h % 2][:, :],
                             in_=q_h[:, 2 * h * T:(2 * h + 2) * T]).then_inc(qd.h, 16)
                for gr in range(4):
                    gidx = 4 * h + gr
                    if gidx >= 2:
                        hh, g2 = (gidx - 2) // 4, (gidx - 2) % 4
                        sy.wait_ge(pa.h, hh * 48 + g2 * 12 + 12)
                    for c in range(2):
                        sy.dma_start(
                            out=KHB[gidx % 2][:, c * 1536:(c + 1) * 1536],
                            in_=ksl_h[:, (2 * h + c) * (NSLOT * 128) + gr * 1536:
                                      (2 * h + c) * (NSLOT * 128) + (gr + 1) * 1536]
                        ).then_inc(qd.h, 16)
                    vs = vsl_h.rearrange("p (s c) -> p s c", c=D)
                    sy.dma_start(
                        out=VHB[gidx % 2][:, :].rearrange("p (s c) -> p s c", c=256),
                        in_=vs[:, gr * 12:(gr + 1) * 12, h * 256:(h + 1) * 256]
                    ).then_inc(qd.h, 16)
            for mg in range(NMG):
                sy.wait_ge(ey.h, mg * NPC + NPC)
                sy.dma_start(out=yo_h[:, 2 * mg * T:(2 * mg + 2) * T],
                             in_=YOB[mg % 2][:, :]).then_inc(so.h, 16)
            sy.wait_ge(fd.h, 3)
            sy.dma_start(out=mu_h, in_=MU[:, :]).then_inc(so.h, 16)
            sy.dma_start(out=rs_h, in_=RSTD[:, :]).then_inc(so.h, 16)
            sy.wait_ge(so.h, 16 * (NMG + 2))

        @blk.vector
        def _(v):
            ndv = 0

            def ms(semh, k=1):
                nonlocal ndv
                v.tensor_copy(FLU[0:1, 0:2], FLU[0:1, 2:4]).then_inc(dv.h, 1)
                ndv += 1
                v.wait_ge(dv.h, ndv)
                v.nop().then_inc(semh, k)

            v.wait_ge(wd.h, 16 * 3)
            for dst, srcb in ((QMX, QN4), (KMX, KN4)):
                v.tensor_copy(dst[:, :], srcb[:, 0:H0]).then_inc(dv.h, 1)
                ndv += 1
                v.wait_ge(dv.h, ndv)
                for j in range(1, 4):
                    v.tensor_tensor(dst[:, :], srcb[:, j * H0:(j + 1) * H0],
                                    dst[:, :], op=ALU.max).then_inc(dv.h, 1)
                    ndv += 1
                    v.wait_ge(dv.h, ndv)
            v.tensor_tensor(MSQ[:, 0:H0], QMX[:, :], KMX[:, :],
                            op=ALU.mult).then_inc(dv.h, 1)
            ndv += 1
            v.wait_ge(dv.h, ndv)
            ms(mbd.h, 1)
            v.wait_ge(mba.h, 1)
            v.tensor_scalar(MBALL[:, :], SDB[:, 0:H0], -1.01 / 16.0, None,
                            op0=ALU.mult).then_inc(dv.h, 1)
            ndv += 1
            v.wait_ge(dv.h, ndv)
            ms(mbd.h, 1)
            for h in range(H0):
                for qb in range(2):
                    v.wait_ge(pa.h, h * NSLOT + QB_END[qb] + 1)
                    v.reciprocal(RL[:, :], PS[:, 2048:2560]).then_inc(dv.h, 1)
                    ndv += 1
                    v.wait_ge(dv.h, ndv)
                    for c in range(2):
                        v.tensor_tensor(
                            AB[:, (2 * h + c) * T + qb * 512:(2 * h + c) * T + qb * 512 + 512],
                            PS[:, 1024 + c * 512:1024 + (c + 1) * 512],
                            RL[:, :], op=ALU.mult).then_inc(dv.h, 1)
                        ndv += 1
                        v.wait_ge(dv.h, ndv)
                    ms(ab.h, 1)
            for mg in range(NMG):
                if mg >= 2:
                    v.wait_ge(so.h, 16 * (mg - 1))
                for half in range(2):
                    for mc in range(2):
                        pc = mg * NPC + half * 2 + mc
                        v.wait_ge(wp.h, pc + 1)
                        v.wait_ge(wd.h, wd_x0(mg))
                        ins = v.tensor_tensor(
                            YOB[mg % 2][:, mc * T + half * 512: mc * T + half * 512 + 512],
                            PS[:, mc * 512:(mc + 1) * 512],
                            X0C[mc][:, half * 512:half * 512 + 512],
                            op=ALU.add)
                        ins.then_inc(dv.h, 1)
                        ndv += 1
                        v.wait_ge(dv.h, ndv)
                        ms(ey.h, 1)
            ms(ey.h, 1)
            v.wait_ge(sp.h, NMG * NPC)
            v.tensor_scalar(MU[:, :], PS[:, 1024:2048], 1.0 / D, None,
                            op0=ALU.mult).then_inc(dv.h, 1)
            ndv += 1
            v.wait_ge(dv.h, ndv)
            v.tensor_tensor(MSQ[:, :], MU[:, :], MU[:, :], op=ALU.mult).then_inc(dv.h, 1)
            ndv += 1
            v.wait_ge(dv.h, ndv)
            v.scalar_tensor_tensor(VARB[:, :], PS[:, 2048:3072], 1.0 / D,
                                   MSQ[:, :], op0=ALU.mult,
                                   op1=ALU.subtract).then_inc(dv.h, 1)
            ndv += 1
            v.wait_ge(dv.h, ndv)
            v.tensor_scalar(VARB[:, :], VARB[:, :], EPS, None,
                            op0=ALU.add).then_inc(dv.h, 1)
            ndv += 1
            v.wait_ge(dv.h, ndv)
            ms(fd.h, 1)
            v.wait_ge(fa.h, 2)
            v.reciprocal(RSTD[:, :], SDB[:, :]).then_inc(dv.h, 1)
            ndv += 1
            v.wait_ge(dv.h, ndv)
            ms(fd.h, 2)

        @blk.scalar
        def _(s):
            s.wait_ge(mbd.h, 1)
            s.activation(SDB[:, 0:H0], MSQ[:, 0:H0], AF.Sqrt).then_inc(mba.h, 1)
            for h in range(H0):
                for u in range(NSLOT):
                    su = h * NSLOT + u
                    s.wait_ge(st.h, su + 1)
                    if su >= 2:
                        s.wait_ge(pa.h, su - 1)
                    if su == 0:
                        s.wait_ge(mbd.h, 2)
                    s.activation(PT[u % 2][:, :], PS[:, (u % 2) * 512:(u % 2) * 512 + 512],
                                 AF.Exp, bias=MBALL[:, h:h + 1],
                                 scale=1.0 / 16.0).then_inc(ae.h, 1)
            for mg in range(NMG):
                for half in range(2):
                    for mc in range(2):
                        pc = mg * NPC + half * 2 + mc
                        s.wait_ge(ey.h, pc + 2)
                        if pc >= 2:
                            s.wait_ge(sp.h, pc - 1)
                        s.activation(SQY[pc % 2][:, :],
                                     YOB[mg % 2][:, mc * T + half * 512: mc * T + half * 512 + 512],
                                     AF.Square).then_inc(es.h, 1)
            s.wait_ge(fd.h, 1)
            s.activation(SDB[:, :], VARB[:, :], AF.Sqrt).then_inc(fa.h, 2)

        @blk.tensor
        def _(t):
            for h in range(H0):
                for u in range(NSLOT):
                    su = h * NSLOT + u
                    qb = 0 if u < 16 else 1
                    qoff = 0 if qb == 0 else 512
                    gr, sl = u // 12, u % 12
                    gb = (4 * h + gr) % 2
                    t.wait_ge(qd.h, qd_group(h, gr))
                    if su >= 2:
                        t.wait_ge(ae.h, su - 1)
                    stp = PS[:, (su % 2) * 512:(su % 2) * 512 + 512]
                    for c in range(2):
                        ins = t.matmul(stp,
                                       KHB[gb][:, c * 1536 + sl * 128: c * 1536 + (sl + 1) * 128],
                                       QHB[h % 2][:, c * T + qoff: c * T + qoff + 512],
                                       start=(c == 0), stop=(c == 1))
                    ins.then_inc(st.h, 1)
                    t.wait_ge(pm.h, su + 1)
                    if u == QB_START[qb]:
                        if h * 2 + qb >= 1:
                            t.wait_ge(ab.h, h * 2 + qb)
                    first = (u == QB_START[qb])
                    last = (u == QB_END[qb])
                    for c in range(2):
                        t.matmul(PS[:, 1024 + c * 512:1024 + (c + 1) * 512],
                                 VHB[gb][:, sl * 256 + c * 128: sl * 256 + (c + 1) * 128],
                                 PT[u % 2][:, :], start=first, stop=last)
                    ins = t.matmul(PS[:, 2048:2560], ONES[:, :], PT[u % 2][:, :],
                                   start=first, stop=last)
                    ins.then_inc(pa.h, 1)
            t.wait_ge(ae.h, H0 * NSLOT)
            t.wait_ge(ab.h, H0 * 2)

            def l2_stats(pc2):
                mg2, half2, mc2 = pc2 // NPC, (pc2 % NPC) // 2, pc2 % 2
                t.wait_ge(ey.h, pc2 + 1)
                t.wait_ge(es.h, pc2 + 1)
                sy_start = (mg2 == 0 and mc2 == 0)
                sy_stop = (mg2 == NMG - 1 and mc2 == 1)
                t.matmul(PS[:, 1024 + half2 * 512:1024 + (half2 + 1) * 512],
                         ONES[:, :],
                         YOB[mg2 % 2][:, mc2 * T + half2 * 512: mc2 * T + half2 * 512 + 512],
                         start=sy_start, stop=sy_stop)
                ins2 = t.matmul(PS[:, 2048 + half2 * 512:2048 + (half2 + 1) * 512],
                                ONES[:, :], SQY[pc2 % 2][:, :],
                                start=sy_start, stop=sy_stop)
                ins2.then_inc(sp.h, 1)

            for mg in range(NMG):
                t.wait_ge(wd.h, wd_w(mg))
                for half in range(2):
                    for mc in range(2):
                        pc = mg * NPC + half * 2 + mc
                        if pc >= 2:
                            t.wait_ge(ey.h, pc - 1)
                        pj = PS[:, mc * 512:(mc + 1) * 512]
                        for kc in range(NCH):
                            ins = t.matmul(
                                pj,
                                WOB[mg % 2][:, kc * 256 + mc * 128: kc * 256 + (mc + 1) * 128],
                                AB[:, kc * T + half * 512: kc * T + half * 512 + 512],
                                start=(kc == 0), stop=(kc == NCH - 1))
                        ins.then_inc(wp.h, 1)
                        if pc >= 2:
                            l2_stats(pc - 2)
            l2_stats(NMG * NPC - 2)
            l2_stats(NMG * NPC - 1)

    return p



def _prep_static(inputs):
    s = {}
    perm = l0_head_perm()
    s["wq_img"] = w_image(np.ascontiguousarray(inputs["L0_attn_w_q"][perm].T))
    s["wk_img"] = w_image(np.ascontiguousarray(inputs["L0_attn_w_k"][perm].T))
    s["wv_img"] = w_image(np.ascontiguousarray(inputs["L0_attn_w_v"].T))
    s["wo_img"] = w_image(np.ascontiguousarray(inputs["L0_attn_w_out"].T), 256)

    rows = []
    for fc in range(22):
        for st in range(4):
            for pp in range(128):
                H = st * FOLDH + fc * 128 + pp
                rows.append(H if (fc < 21 or pp < 64) else -1)
    rows = np.array(rows)
    msk = rows >= 0

    def arr_rows(w):
        out = np.zeros((NSTR * 128, w.shape[1]), np.float32)
        out[msk] = np.asarray(w, np.float32)[rows[msk]]
        return out

    w1a = arr_rows(inputs["L0_ffn_w1q"]).T
    w3a = arr_rows(inputs["L0_ffn_w3q"]).T
    s["w1_img"] = np.ascontiguousarray(np.stack(
        [w_image(np.ascontiguousarray(w1a[:, i * 128:(i + 1) * 128]), 128)[0]
         for i in range(NSTR)]))
    s["w3_img"] = np.ascontiguousarray(np.stack(
        [w_image(np.ascontiguousarray(w3a[:, i * 128:(i + 1) * 128]), 128)[0]
         for i in range(NSTR)]))
    s["b1_arr"] = np.ascontiguousarray(arr_rows(inputs["L0_ffn_b1"]).T)
    s["b3_arr"] = np.ascontiguousarray(arr_rows(inputs["L0_ffn_b3"]).T)
    a2f = np.asarray(inputs["L0_ffn_a2"], np.float32)
    a2a = np.zeros((NSTR * 128, 2), np.float32)
    a2a[msk] = a2f.T[rows[msk]]
    s["a2_img"] = np.ascontiguousarray(w_image(a2a, 2)[0])
    s["a1_img"] = np.ascontiguousarray(
        w_image(np.ascontiguousarray(np.asarray(inputs["L0_ffn_a1"], np.float32).T), 2)[0])
    s["a3_img"] = np.ascontiguousarray(
        w_image(np.ascontiguousarray(np.asarray(inputs["L0_ffn_a3"], np.float32).T), 2)[0])
    w2T = np.asarray(inputs["L0_ffn_w2q"], np.float32).T
    w2p = np.zeros((22 * 128, D), np.float32)
    w2p[:FOLDH] = w2T
    s["w2_img"] = w_image(w2p, 256)
    s["b2_arr"] = np.ascontiguousarray(np.asarray(inputs["L0_ffn_b2"], np.float32).T)

    s["wq1_img"] = w_image(np.ascontiguousarray(np.asarray(inputs["L1_attn_wq"], np.float32).T), 256)
    s["wk1_img"] = w_image(np.ascontiguousarray(np.asarray(inputs["L1_attn_wk"], np.float32).T), 256)
    s["wv1_img"] = w_image(np.ascontiguousarray(np.asarray(inputs["L1_attn_wv"], np.float32).T), 256)
    s["wo1_img"] = w_image(np.ascontiguousarray(np.asarray(inputs["L1_attn_wo"], np.float32).T), 256)

    s["worg_img"] = w_image(np.ascontiguousarray(np.asarray(inputs["L1_ffn_W_org"], np.float32).T), 256)
    dwn = np.asarray(inputs["L1_ffn_lora_down"], np.float32)
    s["down_img"] = np.ascontiguousarray(np.stack(
        [w_image(np.ascontiguousarray(dwn[e]), 64)[0] for e in range(8)]))
    up = np.asarray(inputs["L1_ffn_lora_up"], np.float32)
    s["upp_arr"] = np.ascontiguousarray(np.stack(
        [np.concatenate([up[2 * pp], up[2 * pp + 1]], axis=0) for pp in range(4)]))
    s["keys_arr"] = np.ascontiguousarray(np.asarray(inputs["L1_ffn_lora_keys"], np.float32).T)
    s["wq2_img"] = np.ascontiguousarray(
        w_image(np.ascontiguousarray(np.asarray(inputs["L1_ffn_Wq"], np.float32).T), 128)[0])
    return s


def _shard_tokens(core):
    b, g = core // 4, core % 4
    t1 = np.arange(g * 512, (g + 1) * 512)
    t2 = np.arange((7 - g) * 512, (8 - g) * 512)
    return b, np.concatenate([t1, t2])


def mk_core_data(inputs):
    x0 = np.asarray(inputs["L0_attn_hidden_states"], dtype=np.float32)
    core_data = []
    for c in range(NCORES):
        b, toks = _shard_tokens(c)
        xT = np.ascontiguousarray(x0[b][toks].T)
        core_data.append({"x0i": to_image(xT.astype(np.float32)), "toks": toks, "b": b})
    return core_data


def run_l0_qkv(inputs, static, core_data):
    p = build_l0_qkv()
    in_maps = []
    for c in range(NCORES):
        toks = core_data[c]["toks"]
        cos, sin = rope_tables(toks)
        cos2 = np.ascontiguousarray(np.vstack([cos, cos]))
        sin2 = np.ascontiguousarray(np.vstack([sin, sin]))
        in_maps.append({
            "x0i": core_data[c]["x0i"],
            "wqi": static["wq_img"], "wki": static["wk_img"], "wvi": static["wv_img"],
            "cosi": cos2, "sini": sin2,
            "onesi": np.ones((128, 128), np.float32),
        })
    res = run_bass_kernel_spmd(p.nc, in_maps, core_ids=list(range(NCORES)))
    p.close()
    return res.results





NSTR = 88
DBG_L5 = False


def build_l0_ffn(stage=4):
    p = Prog("l0ffn")
    nc = p.nc
    y_h = p.din("yi", [128, NCH * T], BF16)
    mu_h = p.din("mui", [128, T])
    rs_h = p.din("rsi", [128, T])
    w1_h = p.din("w1i", [NSTR, 128, 1024])
    w3_h = p.din("w3i", [NSTR, 128, 1024])
    w2_h = p.din("w2i", [16, 128, 22 * 256])
    a1_h = p.din("a1i", [128, 64])
    a3_h = p.din("a3i", [128, 64])
    b1_h = p.din("b1i", [2, NSTR * 128])
    b3_h = p.din("b3i", [2, NSTR * 128])
    a2_h = p.din("a2i", [128, NSTR * 2])
    b2_h = p.din("b2i", [2, D])
    wq1_h = p.din("wq1i", [16, 128, NCH * 256])
    wk1_h = p.din("wk1i", [4, 128, NCH * 256])
    wv1_h = p.din("wv1i", [4, 128, NCH * 256])
    ones_h = p.din("onesi", [128, 128])
    x2_h = p.dout("x2o", [128, NCH * T], BF16)
    if stage >= 4:
        q1_h = p.dout("q1o", [128, NCH * T], BF16)
        k1_h = p.dout("k1o", [128, 8 * T], BF16)
        v1_h = p.dout("v1o", [128, 8 * T], BF16)
        qn_h = p.dout("q1n", [128, 32], F32)
        kn_h = p.dout("k1n", [128, 8], F32)

    X = p.sbuf("X", [128, NCH * T], BF16)
    ONES = p.sbuf("ONES", [128, 128], BF16)
    FLU = p.sbuf("FLU", [1, 4], F32)
    PS = p.psum("PS", [128, 8 * 512], F32)

    wd = p.sem("wd"); qd = p.sem("qd"); nl = p.sem("nl"); dv = p.sem("dv")
    xf = p.sem("xf"); pl = p.sem("pl"); el = p.sem("el"); pg = p.sem("pg")
    sl = p.sem("sl"); dt = p.sem("dt"); hc = p.sem("hc"); t2 = p.sem("t2")
    e2 = p.sem("e2"); wp = p.sem("wp"); ey = p.sem("ey"); es = p.sem("es")
    sp = p.sem("sp"); fd = p.sem("fd"); fa = p.sem("fa"); n2 = p.sem("n2")
    pe = p.sem("pe"); ev = p.sem("ev"); sq = p.sem("sq"); pn = p.sem("pn")
    dn = p.sem("dn"); so = p.sem("so")

    def wd_stripe(i):
        fc = i // 4
        return 16 * (4 + 2 * (fc + 1) + 2 * (i + 1))

    def wd_w2(mg):
        return 16 * (4 + 2 * 22 + 2 * NSTR + 2 * (mg + 1))

    WD_B0 = 4 + 2 * 22 + 2 * NSTR + 2 * 16

    def wd_wq1(i):
        return 16 * (WD_B0 + i + 1)

    def qd_y(c):
        return 16 * (2 + c + 1)

    def pe_after(i):
        return (i + 1) * 4 if i < 20 else 80 + (i - 19) * 8

    def ev_v(mb):
        return 80 + 8 * (mb + 1)

    def so_qkv(i):
        return 16 * (NCH + i + 1)

    NPC = 4
    NDV = [0]

    import contextlib
    sA = contextlib.ExitStack()
    HF = sA.enter_context(nc.sbuf_tensor("HF", [128, 22 * T], BF16))
    XF = sA.enter_context(nc.sbuf_tensor("XF", [128, 8 * T], BF16))
    YC = sA.enter_context(nc.sbuf_tensor("YC", [128, T], BF16))
    MUB = sA.enter_context(nc.sbuf_tensor("MUB", [128, T], F32))
    RSB = sA.enter_context(nc.sbuf_tensor("RSB", [128, T], F32))
    SCR = sA.enter_context(nc.sbuf_tensor("SCR", [128, T], F32))
    WF1 = [sA.enter_context(nc.sbuf_tensor(f"WF1{i}", [128, 1024], BF16)) for i in range(2)]
    WF3 = [sA.enter_context(nc.sbuf_tensor(f"WF3{i}", [128, 1024], BF16)) for i in range(2)]
    W2B = [sA.enter_context(nc.sbuf_tensor(f"W2B{i}", [128, 22 * 256], BF16)) for i in range(2)]
    A1 = sA.enter_context(nc.sbuf_tensor("A1", [128, 64], BF16))
    A3 = sA.enter_context(nc.sbuf_tensor("A3", [128, 64], BF16))
    B1F = [sA.enter_context(nc.sbuf_tensor(f"B1F{i}", [2, 512], BF16)) for i in range(2)]
    B3F = [sA.enter_context(nc.sbuf_tensor(f"B3F{i}", [2, 512], BF16)) for i in range(2)]
    A2 = sA.enter_context(nc.sbuf_tensor("A2", [128, NSTR * 2], BF16))
    B2F = [sA.enter_context(nc.sbuf_tensor(f"B2F{i}", [2, 256], BF16)) for i in range(2)]
    T1 = sA.enter_context(nc.sbuf_tensor("T1", [2, T], BF16))
    T3 = sA.enter_context(nc.sbuf_tensor("T3", [2, T], BF16))
    T2B = sA.enter_context(nc.sbuf_tensor("T2B", [2, T], BF16))
    SG = [sA.enter_context(nc.sbuf_tensor(f"SG{i}", [128, T], BF16)) for i in range(2)]
    TS = [sA.enter_context(nc.sbuf_tensor(f"TS{i}", [128, T], BF16)) for i in range(2)]
    SQY = [sA.enter_context(nc.sbuf_tensor(f"SQY{i}", [128, 512], BF16)) for i in range(2)]
    MSQ = sA.enter_context(nc.sbuf_tensor("MSQ", [128, T], F32))
    VARB = sA.enter_context(nc.sbuf_tensor("VARB", [128, T], F32))
    SDB = sA.enter_context(nc.sbuf_tensor("SDB", [128, T], F32))

    with nc.Block() as blk:
        @blk.gpsimd
        def _(g):
            for dst, srch in ((ONES, ones_h), (A1, a1_h), (A3, a3_h), (A2, a2_h)):
                g.dma_start(out=dst[:, :], in_=srch).then_inc(wd.h, 16)
            if stage < 2:
                return
            for fc in range(22):
                if fc >= 2:
                    g.wait_ge(pg.h, (fc - 2) * 4 + 4)
                g.dma_start(out=B1F[fc % 2][:, :],
                            in_=b1_h[:, fc * 512:(fc + 1) * 512]).then_inc(wd.h, 16)
                g.dma_start(out=B3F[fc % 2][:, :],
                            in_=b3_h[:, fc * 512:(fc + 1) * 512]).then_inc(wd.h, 16)
                for s_ in range(4):
                    i = fc * 4 + s_
                    if i >= 2:
                        g.wait_ge(pg.h, i - 1)
                    g.dma_start(out=WF1[i % 2][:, :], in_=w1_h[i]).then_inc(wd.h, 16)
                    g.dma_start(out=WF3[i % 2][:, :], in_=w3_h[i]).then_inc(wd.h, 16)
            if stage < 3:
                return
            for mg in range(16):
                if mg >= 2:
                    g.wait_ge(wp.h, (mg - 2) * NPC + NPC)
                g.dma_start(out=W2B[mg % 2][:, :], in_=w2_h[mg]).then_inc(wd.h, 16)
                g.dma_start(out=B2F[mg % 2][:, :],
                            in_=b2_h[:, mg * 256:(mg + 1) * 256]).then_inc(wd.h, 16)

        @blk.sync
        def _(sy):
            sy.dma_start(out=MUB[:, :], in_=mu_h).then_inc(qd.h, 16)
            sy.dma_start(out=RSB[:, :], in_=rs_h).then_inc(qd.h, 16)
            for c in range(NCH):
                if c >= 1:
                    sy.wait_ge(nl.h, c)
                sy.dma_start(out=YC[:, :],
                             in_=y_h[:, c * T:(c + 1) * T]).then_inc(qd.h, 16)
            for c in range(NCH):
                sy.wait_ge(n2.h, c + 1)
                sy.dma_start(out=x2_h[:, c * T:(c + 1) * T],
                             in_=X[:, c * T:(c + 1) * T]).then_inc(so.h, 16)

        @blk.vector
        def _(v):
            def ssync(ins):
                ins.then_inc(dv.h, 1)
                NDV[0] += 1
                v.wait_ge(dv.h, NDV[0])

            def ms(semh, k=1):
                ssync(v.tensor_copy(FLU[0:1, 0:2], FLU[0:1, 2:4]))
                v.nop().then_inc(semh, k)

            v.wait_ge(qd.h, 32)
            for c in range(NCH):
                v.wait_ge(qd.h, qd_y(c))
                ssync(v.tensor_tensor(SCR[:, :], YC[:, :], MUB[:, :],
                                      op=ALU.subtract))
                ssync(v.tensor_tensor(X[:, c * T:(c + 1) * T], SCR[:, :], RSB[:, :],
                                      op=ALU.mult))
                ms(nl.h, 1)
            if stage < 2:
                for c in range(NCH):
                    ms(n2.h, 1)
                return
            for j in range(8):
                ssync(v.tensor_tensor(SCR[:, :], X[:, j * T:(j + 1) * T],
                                      X[:, (8 + j) * T:(9 + j) * T], op=ALU.add))
                ssync(v.tensor_tensor(MSQ[:, :], X[:, (16 + j) * T:(17 + j) * T],
                                      X[:, (24 + j) * T:(25 + j) * T], op=ALU.add))
                ssync(v.tensor_tensor(XF[:, j * T:(j + 1) * T], SCR[:, :], MSQ[:, :],
                                      op=ALU.add))
                ms(xf.h, 1)
            for i in range(NSTR):
                fc = i // 4
                v.wait_ge(sl.h, i + 1)
                if i >= 2:
                    v.wait_ge(t2.h, i - 1)
                if i % 4 == 0:
                    ssync(v.tensor_tensor(HF[:, fc * T:(fc + 1) * T], SG[i % 2][:, :],
                                          PS[:, 1024:2048], op=ALU.mult))
                    ms(dt.h, 1)
                    ms(hc.h, 1)
                else:
                    ssync(v.tensor_tensor(TS[i % 2][:, :], SG[i % 2][:, :],
                                          PS[:, 1024:2048], op=ALU.mult))
                    ms(dt.h, 1)
                    v.wait_ge(t2.h, i)
                    ssync(v.tensor_tensor(HF[:, fc * T:(fc + 1) * T], TS[i % 2][:, :],
                                          HF[:, fc * T:(fc + 1) * T], op=ALU.add))
                    ms(hc.h, 1)
            if stage < 3:
                v.wait_ge(hc.h, NSTR)
                for c in range(NCH):
                    ms(n2.h, 1)
                return
            for mg in range(16):
                for mc in range(2):
                    ch = mg * 2 + mc
                    for half in range(2):
                        pc = mg * NPC + mc * 2 + half
                        v.wait_ge(wp.h, pc + 1)
                        col = ch * T + half * 512
                        ssync(v.tensor_tensor(X[:, col:col + 512],
                                              PS[:, (pc % 4) * 512:(pc % 4) * 512 + 512],
                                              X[:, col:col + 512], op=ALU.add))
                        ms(ey.h, 1)
            v.wait_ge(sp.h, 16 * NPC)
            ssync(v.tensor_scalar(MUB[:, :], PS[:, 2048:3072], 1.0 / D, None,
                                  op0=ALU.mult))
            ssync(v.tensor_tensor(MSQ[:, :], MUB[:, :], MUB[:, :], op=ALU.mult))
            ssync(v.scalar_tensor_tensor(VARB[:, :], PS[:, 3072:4096], 1.0 / D,
                                         MSQ[:, :], op0=ALU.mult, op1=ALU.subtract))
            ssync(v.tensor_scalar(VARB[:, :], VARB[:, :], EPS, None, op0=ALU.add))
            ms(fd.h, 1)
            v.wait_ge(fa.h, 1)
            ssync(v.reciprocal(RSB[:, :], SDB[:, :]))
            for c in range(NCH):
                ssync(v.tensor_tensor(SCR[:, :], X[:, c * T:(c + 1) * T], MUB[:, :],
                                      op=ALU.subtract))
                ssync(v.tensor_tensor(X[:, c * T:(c + 1) * T], SCR[:, :], RSB[:, :],
                                      op=ALU.mult))
                ms(n2.h, 1)

        @blk.scalar
        def _(s):
            if stage < 2:
                return
            s.wait_ge(pl.h, 1)
            s.activation(T1[:, :], PS[0:2, 0:1024], AF.Copy, scale=0.5).then_inc(el.h, 1)
            s.wait_ge(pl.h, 2)
            s.activation(T3[:, :], PS[0:2, 1024:2048], AF.Copy, scale=0.5).then_inc(el.h, 1)
            for i in range(NSTR):
                s.wait_ge(pg.h, i + 1)
                if i >= 2:
                    s.wait_ge(dt.h, i - 1)
                s.activation(SG[i % 2][:, :], PS[:, 0:1024], AF.Silu).then_inc(sl.h, 1)
            s.wait_ge(t2.h, NSTR)
            s.activation(T2B[:, :], PS[0:2, 2048:3072], AF.Copy, scale=0.5).then_inc(e2.h, 1)
            if stage < 3:
                return
            for pc in range(16 * NPC):
                s.wait_ge(ey.h, pc + 1)
                if pc >= 2:
                    s.wait_ge(sp.h, pc - 1)
                mg, mc, half = pc // NPC, (pc % NPC) // 2, pc % 2
                ch = mg * 2 + mc
                col = ch * T + half * 512
                s.activation(SQY[pc % 2][:, :], X[:, col:col + 512],
                             AF.Square).then_inc(es.h, 1)
            s.wait_ge(fd.h, 1)
            s.activation(SDB[:, :], VARB[:, :], AF.Sqrt).then_inc(fa.h, 1)

        @blk.tensor
        def _(t):
            def emit_a2(i):
                t.wait_ge(dt.h, i + 1)
                fc = i // 4
                srcb = (HF[:, fc * T:(fc + 1) * T] if i % 4 == 0
                        else TS[i % 2][:, :])
                for half in range(2):
                    ins = t.matmul(PS[0:2, 2048 + half * 512:2048 + half * 512 + 512],
                                   A2[:, i * 2:(i + 1) * 2],
                                   srcb[:, half * 512:half * 512 + 512],
                                   start=(i == 0), stop=(i == NSTR - 1))
                ins.then_inc(t2.h, 1)

            if stage < 2:
                return
            t.wait_ge(nl.h, NCH)
            t.wait_ge(wd.h, 16 * 4)
            for reg, ab in ((0, A1), (1, A3)):
                for half in range(2):
                    for kc in range(NCH):
                        ins = t.matmul(PS[0:2, reg * 1024 + half * 512: reg * 1024 + half * 512 + 512],
                                       ab[:, kc * 2:(kc + 1) * 2],
                                       X[:, kc * T + half * 512: kc * T + half * 512 + 512],
                                       start=(kc == 0), stop=(kc == NCH - 1))
                ins.then_inc(pl.h, 1)
            t.wait_ge(xf.h, 8)
            t.wait_ge(el.h, 2)
            for i in range(NSTR):
                fc, st_ = i // 4, i % 4
                t.wait_ge(wd.h, wd_stripe(i))
                if i >= 1:
                    t.wait_ge(sl.h, i)
                    t.wait_ge(dt.h, i)
                for dstoff, wf, bb, tt in ((0, WF1[i % 2], B1F[fc % 2], T1),
                                           (1024, WF3[i % 2], B3F[fc % 2], T3)):
                    for half in range(2):
                        pv = PS[:, dstoff + half * 512: dstoff + half * 512 + 512]
                        for kc in range(8):
                            t.matmul(pv,
                                     wf[:, kc * 128:(kc + 1) * 128],
                                     XF[:, kc * T + half * 512: kc * T + half * 512 + 512],
                                     start=(kc == 0), stop=False)
                        ins = t.matmul(pv, bb[:, st_ * 128:(st_ + 1) * 128],
                                       tt[:, half * 512: half * 512 + 512],
                                       start=False, stop=True)
                ins.then_inc(pg.h, 1)
                if i >= 1:
                    emit_a2(i - 1)
            emit_a2(NSTR - 1)
            if stage < 3:
                return
            t.wait_ge(hc.h, NSTR)
            t.wait_ge(e2.h, 1)
            for mg in range(16):
                t.wait_ge(wd.h, wd_w2(mg))
                for mc in range(2):
                    for half in range(2):
                        pc = mg * NPC + mc * 2 + half
                        if pc >= 4:
                            t.wait_ge(ey.h, pc - 3)
                        pv = PS[:, (pc % 4) * 512:(pc % 4) * 512 + 512]
                        for kc in range(22):
                            t.matmul(pv,
                                     W2B[mg % 2][:, kc * 256 + mc * 128: kc * 256 + (mc + 1) * 128],
                                     HF[:, kc * T + half * 512: kc * T + half * 512 + 512],
                                     start=(kc == 0), stop=False)
                        ins = t.matmul(pv, B2F[mg % 2][:, mc * 128:(mc + 1) * 128],
                                       T2B[:, half * 512: half * 512 + 512],
                                       start=False, stop=True)
                        ins.then_inc(wp.h, 1)
                        t.wait_ge(ey.h, pc + 1)
                        t.wait_ge(es.h, pc + 1)
                        ch = mg * 2 + mc
                        col = ch * T + half * 512
                        sy_start = (mg == 0 and mc == 0)
                        sy_stop = (mg == 15 and mc == 1)
                        t.matmul(PS[:, 2048 + half * 512:2048 + (half + 1) * 512],
                                 ONES[:, :], X[:, col:col + 512],
                                 start=sy_start, stop=sy_stop)
                        ins = t.matmul(PS[:, 3072 + half * 512:3072 + (half + 1) * 512],
                                       ONES[:, :], SQY[pc % 2][:, :],
                                       start=sy_start, stop=sy_stop)
                        ins.then_inc(sp.h, 1)

    sA.close()
    if stage < 4:
        return p
    sB = contextlib.ExitStack()
    WQ1 = [sB.enter_context(nc.sbuf_tensor(f"WQ1{i}", [128, NCH * 256], BF16)) for i in range(2)]
    QB = [sB.enter_context(nc.sbuf_tensor(f"QB{i}", [128, 2 * T], BF16)) for i in range(2)]
    VB = [sB.enter_context(nc.sbuf_tensor(f"VB{i}", [128, 8 * 256], BF16)) for i in range(2)]
    SQB = sB.enter_context(nc.sbuf_tensor("SQB", [128, 512], BF16))
    RMX = sB.enter_context(nc.sbuf_tensor("RMX", [128, 2], F32))
    QNB = sB.enter_context(nc.sbuf_tensor("QNB", [128, 32], F32))
    KNB = sB.enter_context(nc.sbuf_tensor("KNB", [128, 8], F32))

    with nc.Block() as blk:
        @blk.gpsimd
        def _(g):
            g.wait_ge(n2.h, NCH)
            for i in range(24):
                if i >= 2:
                    g.wait_ge(pe.h, pe_after(i - 2))
                srch = (wq1_h[i] if i < 16 else
                        (wk1_h[i - 16] if i < 20 else wv1_h[i - 20]))
                g.dma_start(out=WQ1[i % 2][:, :], in_=srch).then_inc(wd.h, 16)

        @blk.sync
        def _(sy):
            for i in range(20):
                sy.wait_ge(ev.h, (i + 1) * 4)
                dst = q1_h if i < 16 else k1_h
                mg = i if i < 16 else i - 16
                sy.dma_start(out=dst[:, mg * 2 * T:(mg + 1) * 2 * T],
                             in_=QB[i % 2][:, :]).then_inc(so.h, 16)
            for mb in range(4):
                sy.wait_ge(ev.h, ev_v(mb))
                srcv = VB[mb % 2][:, :].rearrange("p (kt m) -> p kt m", m=256)
                dstv = v1_h.rearrange("p (kt c) -> p kt c", c=T)[:, :, mb * 256:(mb + 1) * 256]
                sy.dma_start(out=dstv, in_=srcv).then_inc(so.h, 16)
            sy.wait_ge(dn.h, 64)
            sy.dma_start(out=qn_h, in_=QNB[:, :]).then_inc(so.h, 16)
            sy.wait_ge(dn.h, 80)
            sy.dma_start(out=kn_h, in_=KNB[:, :]).then_inc(so.h, 16)
            sy.wait_ge(so.h, 16 * (NCH + 20 + 4 + 2))

        @blk.vector
        def _(v):
            def ssync(ins):
                ins.then_inc(dv.h, 1)
                NDV[0] += 1
                v.wait_ge(dv.h, NDV[0])

            def ms(semh, k=1):
                ssync(v.tensor_copy(FLU[0:1, 0:2], FLU[0:1, 2:4]))
                v.nop().then_inc(semh, k)

            for i in range(20):
                for u in range(4):
                    hl, half = u // 2, u % 2
                    par = u % 2
                    guni = i * 4 + u
                    v.wait_ge(ev.h, (i + 1) * 4)
                    if guni >= 1:
                        v.wait_ge(pn.h, guni)
                    colc = hl * T + half * 512
                    ssync(v.tensor_tensor(SQB[:, :], QB[i % 2][:, colc:colc + 512],
                                          QB[i % 2][:, colc:colc + 512], op=ALU.mult))
                    ms(sq.h, 1)
                    v.wait_ge(pn.h, guni + 1)
                    npv = PS[:, 3072 + par * 512: 3072 + par * 512 + 512]
                    ssync(v.reduce_max(RMX[:, par:par + 1], npv,
                                       axis=mybir.AxisListType.X))
                    if i < 16:
                        hcol, HMX = i * 2 + hl, QNB
                    else:
                        hcol, HMX = (i - 16) * 2 + hl, KNB
                    if half == 0:
                        ssync(v.tensor_copy(HMX[:, hcol:hcol + 1], RMX[:, par:par + 1]))
                    else:
                        ssync(v.tensor_tensor(HMX[:, hcol:hcol + 1], RMX[:, par:par + 1],
                                              HMX[:, hcol:hcol + 1], op=ALU.max))
                    ms(dn.h, 1)

        @blk.scalar
        def _(s):
            for i in range(20):
                for half in range(2):
                    for mc in range(2):
                        piece = i * 4 + half * 2 + mc
                        bank = piece % 6
                        s.wait_ge(pe.h, piece + 1)
                        if i >= 2:
                            s.wait_ge(so.h, so_qkv(i - 2))
                        s.copy(out=QB[i % 2][:, mc * T + half * 512: mc * T + half * 512 + 512],
                               in_=PS[:, bank * 512:(bank + 1) * 512]).then_inc(ev.h, 1)
            for mb in range(4):
                for kt in range(8):
                    s.wait_ge(pe.h, 80 + mb * 8 + kt + 1)
                    if mb >= 2:
                        s.wait_ge(so.h, 16 * (NCH + 20 + mb - 1))
                    s.copy(out=VB[mb % 2][:, kt * 256:(kt + 1) * 256],
                           in_=PS[:, kt * 512:kt * 512 + 256]).then_inc(ev.h, 1)

        @blk.tensor
        def _(t):
            t.wait_ge(n2.h, NCH)
            for i in range(20):
                t.wait_ge(wd.h, wd_wq1(i))
                for half in range(2):
                    for mc in range(2):
                        piece = i * 4 + half * 2 + mc
                        bank = piece % 6
                        if piece - 6 >= 0:
                            t.wait_ge(ev.h, piece - 6 + 1)
                        pv = PS[:, bank * 512:(bank + 1) * 512]
                        for kc in range(NCH):
                            ins = t.matmul(
                                pv,
                                WQ1[i % 2][:, kc * 256 + mc * 128: kc * 256 + (mc + 1) * 128],
                                X[:, kc * T + half * 512: kc * T + half * 512 + 512],
                                start=(kc == 0), stop=(kc == NCH - 1))
                        ins.then_inc(pe.h, 1)
                for u in range(4):
                    guni = i * 4 + u
                    par = u % 2
                    if guni - 2 >= 0:
                        t.wait_ge(dn.h, guni - 2 + 1)
                    t.wait_ge(sq.h, guni + 1)
                    npv = PS[:, 3072 + par * 512: 3072 + par * 512 + 512]
                    ins = t.matmul(npv, ONES[:, :], SQB[:, :], start=True, stop=True)
                    ins.then_inc(pn.h, 1)
            for mb in range(4):
                t.wait_ge(wd.h, wd_wq1(20 + mb))
                if mb > 0:
                    t.wait_ge(ev.h, ev_v(mb - 1))
                else:
                    t.wait_ge(ev.h, 80)
                    t.wait_ge(dn.h, 80)
                for kc in range(NCH):
                    for kt in range(8):
                        pv = PS[:, kt * 512:kt * 512 + 256]
                        ins = t.matmul(
                            pv,
                            X[:, kc * T + kt * 128: kc * T + (kt + 1) * 128],
                            WQ1[(20 + mb) % 2][:, kc * 256:(kc + 1) * 256],
                            start=(kc == 0), stop=(kc == NCH - 1))
                        if kc == NCH - 1:
                            ins.then_inc(pe.h, 1)

    sB.close()
    return p


def _l0_masks(g):
    masks = np.zeros((NSLOT, 128, 512), np.uint8)
    ktl = np.arange(128)[:, None]
    qtl = np.arange(512)[None, :]
    for qb in range(2):
        off = 0 if qb == 0 else 16
        i = g if qb == 0 else 7 - g
        nreal = 4 * (i + 1)
        nsl = 16 if qb == 0 else 32
        for s in range(nsl):
            if s < nreal - 4:
                masks[off + s] = 1
            elif s < nreal:
                d = -128 * (s - (nreal - 4))
                masks[off + s] = (ktl <= qtl + d).astype(np.uint8)
    return np.ascontiguousarray(masks.transpose(1, 0, 2).reshape(128, NSLOT * 512))


def run_l0_attn(inputs, static, core_data, r1):
    p = build_l0_attn()
    ksl, vsl, qn4, kn4 = {}, {}, {}, {}
    for b in range(2):
        Ka = np.zeros((D, SEQ), BF)
        Va = np.zeros((SEQ, D), BF)
        for c in range(4 * b, 4 * b + 4):
            _, toks = _shard_tokens(c)
            Ka[:, toks] = from_image(r1[c]["ko"], D)
            Va[toks] = from_image(r1[c]["vo"], T)
        ksl[b] = to_image(np.ascontiguousarray(np.concatenate([Ka[:, :2048], Ka], axis=1)))
        vsl[b] = to_image(np.ascontiguousarray(np.concatenate([Va[:2048], Va], axis=0)))
        qn4[b] = np.ascontiguousarray(np.concatenate(
            [r1[c]["qn"] for c in range(4 * b, 4 * b + 4)], axis=1))
        kn4[b] = np.ascontiguousarray(np.concatenate(
            [r1[c]["kn"] for c in range(4 * b, 4 * b + 4)], axis=1))
    in_maps = []
    for c in range(NCORES):
        b, g = c // 4, c % 4
        in_maps.append({
            "x0i": core_data[c]["x0i"],
            "qi": r1[c]["qo"],
            "ksl": ksl[b], "vsl": vsl[b],
            "mki": _l0_masks(g),
            "qn4": qn4[b], "kn4": kn4[b],
            "onesi": np.ones((128, 128), np.float32),
            "woi": static["wo_img"],
        })
    res = run_bass_kernel_spmd(p.nc, in_maps, core_ids=list(range(NCORES)))
    p.close()
    return res.results




def run_l0_ffn(inputs, static, r2):
    p = build_l0_ffn()
    in_maps = []
    for c in range(NCORES):
        in_maps.append({
            "yi": r2[c]["yo"], "mui": r2[c]["muo"], "rsi": r2[c]["rso"],
            "w1i": static["w1_img"], "w3i": static["w3_img"], "w2i": static["w2_img"],
            "a1i": static["a1_img"], "a3i": static["a3_img"],
            "b1i": static["b1_arr"], "b3i": static["b3_arr"],
            "a2i": static["a2_img"], "b2i": static["b2_arr"],
            "wq1i": static["wq1_img"], "wk1i": static["wk1_img"], "wv1i": static["wv1_img"],
            "onesi": np.ones((128, 128), np.float32),
        })
    res = run_bass_kernel_spmd(p.nc, in_maps, core_ids=list(range(NCORES)))
    p.close()
    return res.results





def build_l1_attn():
    p = Prog("l1attn")
    nc = p.nc
    x2i_h = p.din("x2i", [128, NCH * T], BF16)
    q1_h = p.din("q1i", [128, NCH * T], BF16)
    k1f_h = p.din("k1f", [128, 8 * SEQ], BF16)
    v1f_h = p.din("v1f", [128, (SEQ // 128) * T], BF16)
    qn_h = p.din("qn4", [128, 4 * 32])
    kn_h = p.din("kn4", [128, 4 * 8])
    ones_h = p.din("onesi", [128, 128])
    wo_h = p.din("wo1i", [16, 128, NCH * 256])
    yo_h = p.dout("y3o", [128, NCH * T], BF16)
    mu_h = p.dout("mu3o", [128, T], F32)
    rs_h = p.dout("rs3o", [128, T], F32)
    cs_h = p.dout("cso", [128, NCH], F32)

    AB = p.sbuf("AB", [128, NCH * T], BF16)
    KG = [p.sbuf(f"KG{i}", [128, SEQ], BF16) for i in range(2)]
    VG = [p.sbuf(f"VG{i}", [128, 32 * 128], BF16) for i in range(2)]
    QG = [p.sbuf(f"QG{i}", [128, 4 * T], BF16) for i in range(2)]
    PT = [p.sbuf(f"PT{i}", [128, 512], BF16) for i in range(2)]
    QN4 = p.sbuf("QN4", [128, 4 * 32], F32)
    KN4 = p.sbuf("KN4", [128, 4 * 8], F32)
    QMX = p.sbuf("QMX", [128, 32], F32)
    KMX = p.sbuf("KMX", [128, 8], F32)
    MBALL = p.sbuf("MBALL", [128, 32], F32)
    ONES = p.sbuf("ONES", [128, 128], BF16)
    RL = p.sbuf("RL", [128, 512], F32)
    WOB = [p.sbuf(f"WOB{i}", [128, NCH * 256], BF16) for i in range(2)]
    X2C = [p.sbuf(f"X2C{i}", [128, T], BF16) for i in range(2)]
    SQY = [p.sbuf(f"SQY{i}", [128, 512], BF16) for i in range(2)]
    YOB = [p.sbuf(f"YOB{i}", [128, 2 * T], BF16) for i in range(2)]
    YRC = [p.sbuf(f"YRC{i}", [128, T], BF16) for i in range(2)]
    MU = p.sbuf("MU", [128, T], F32)
    MSQ = p.sbuf("MSQ", [128, T], F32)
    VARB = p.sbuf("VARB", [128, T], F32)
    SDB = p.sbuf("SDB", [128, T], F32)
    RSTD = p.sbuf("RSTD", [128, T], F32)
    SCR = p.sbuf("SCR", [128, T], F32)
    X3C = p.sbuf("X3C", [128, T], BF16)
    CS = p.sbuf("CS", [128, NCH], F32)
    FLU = p.sbuf("FLU", [1, 4], F32)
    PS = p.psum("PS", [128, 8 * 512], F32)

    wd = p.sem("wd"); qd = p.sem("qd"); st = p.sem("st"); ae = p.sem("ae")
    pa = p.sem("pa"); ab = p.sem("ab"); dv = p.sem("dv"); mbd = p.sem("mbd")
    mba = p.sem("mba"); wp = p.sem("wp"); ey = p.sem("ey"); es = p.sem("es")
    sp = p.sem("sp"); fd = p.sem("fd"); fa = p.sem("fa"); cs = p.sem("cs")
    yr = p.sem("yr"); so = p.sem("so")

    NMG = 16
    NPC = 4
    SCALE1 = 1.0 / float(np.sqrt(128.0))

    def qd_g(g):
        return 16 * 3 * (g + 1)

    def wd_w(mg):
        return 16 * (3 + 3 * mg + 1)

    def wd_x2(mg):
        return 16 * (3 + 3 * mg + 3)

    def hq_pa(hq):
        return (hq + 1) * 32

    with nc.Block() as blk:
        @blk.gpsimd
        def _(g):
            g.dma_start(out=ONES[:, :], in_=ones_h).then_inc(wd.h, 16)
            g.dma_start(out=QN4[:, :], in_=qn_h).then_inc(wd.h, 16)
            g.dma_start(out=KN4[:, :], in_=kn_h).then_inc(wd.h, 16)
            for mg in range(NMG):
                if mg >= 2:
                    g.wait_ge(wp.h, (mg - 2) * NPC + NPC)
                g.dma_start(out=WOB[mg % 2][:, :], in_=wo_h[mg]).then_inc(wd.h, 16)
                for mc in range(2):
                    if mg >= 1:
                        g.wait_ge(ey.h, (mg - 1) * NPC + NPC)
                    g.dma_start(out=X2C[mc][:, :],
                                in_=x2i_h[:, (2 * mg + mc) * T:(2 * mg + mc + 1) * T]
                                ).then_inc(wd.h, 16)

        @blk.sync
        def _(sy):
            for g in range(8):
                if g >= 2:
                    sy.wait_ge(pa.h, hq_pa((g - 1) * 8 - 1 + 1))
                sy.dma_start(out=QG[g % 2][:, :],
                             in_=q1_h[:, 4 * g * T:(4 * g + 4) * T]).then_inc(qd.h, 16)
                sy.dma_start(out=KG[g % 2][:, :],
                             in_=k1f_h[:, g * SEQ:(g + 1) * SEQ]).then_inc(qd.h, 16)
                vsrc = v1f_h.rearrange("p (kt c) -> p kt c", c=T)[:, :, g * 128:(g + 1) * 128]
                sy.dma_start(out=VG[g % 2][:, :].rearrange("p (kt c) -> p kt c", c=128),
                             in_=vsrc).then_inc(qd.h, 16)
            for mg in range(NMG):
                sy.wait_ge(ey.h, mg * NPC + NPC)
                sy.dma_start(out=yo_h[:, 2 * mg * T:(2 * mg + 2) * T],
                             in_=YOB[mg % 2][:, :]).then_inc(so.h, 16)
            sy.wait_ge(fd.h, 3)
            sy.dma_start(out=mu_h, in_=MU[:, :]).then_inc(so.h, 16)
            sy.dma_start(out=rs_h, in_=RSTD[:, :]).then_inc(so.h, 16)
            sy.wait_ge(so.h, 16 * (NMG + 2))
            for c in range(NCH):
                if c >= 2:
                    sy.wait_ge(cs.h, c - 1)
                sy.dma_start(out=YRC[c % 2][:, :],
                             in_=yo_h[:, c * T:(c + 1) * T]).then_inc(yr.h, 16)
            sy.wait_ge(cs.h, NCH)
            sy.dma_start(out=cs_h, in_=CS[:, :]).then_inc(so.h, 16)
            sy.wait_ge(so.h, 16 * (NMG + 3))

        @blk.vector
        def _(v):
            ndv = [0]

            def ssync(ins):
                ins.then_inc(dv.h, 1)
                ndv[0] += 1
                v.wait_ge(dv.h, ndv[0])

            def ms(semh, k=1):
                ssync(v.tensor_copy(FLU[0:1, 0:2], FLU[0:1, 2:4]))
                v.nop().then_inc(semh, k)

            v.wait_ge(wd.h, 16 * 3)
            for dst, srcb, n in ((QMX, QN4, 32), (KMX, KN4, 8)):
                ssync(v.tensor_copy(dst[:, :], srcb[:, 0:n]))
                for j in range(1, 4):
                    ssync(v.tensor_tensor(dst[:, :], srcb[:, j * n:(j + 1) * n],
                                          dst[:, :], op=ALU.max))
            for h in range(32):
                ssync(v.tensor_tensor(MSQ[:, h:h + 1], QMX[:, h:h + 1],
                                      KMX[:, h // 4:h // 4 + 1], op=ALU.mult))
            ms(mbd.h, 1)
            v.wait_ge(mba.h, 1)
            ssync(v.tensor_scalar(MBALL[:, :], SDB[:, 0:32], -1.01 * SCALE1, None,
                                  op0=ALU.mult))
            ms(mbd.h, 1)
            for hq in range(64):
                v.wait_ge(pa.h, hq_pa(hq))
                ssync(v.reciprocal(RL[:, :], PS[:, 2560:3072]))
                ch = hq // 2
                qb = hq % 2
                ssync(v.tensor_tensor(AB[:, ch * T + qb * 512: ch * T + qb * 512 + 512],
                                      PS[:, 2048:2560], RL[:, :], op=ALU.mult))
                ms(ab.h, 1)
            for mg in range(NMG):
                if mg >= 2:
                    v.wait_ge(so.h, 16 * (mg - 1))
                for half in range(2):
                    for mc in range(2):
                        pc = mg * NPC + half * 2 + mc
                        v.wait_ge(wp.h, pc + 1)
                        v.wait_ge(wd.h, wd_x2(mg))
                        ssync(v.tensor_tensor(YOB[mg % 2][:, mc * T + half * 512: mc * T + half * 512 + 512],
                                              PS[:, 3072 + mc * 512:3072 + (mc + 1) * 512],
                                              X2C[mc][:, half * 512:half * 512 + 512],
                                              op=ALU.add))
                        ms(ey.h, 1)
            ms(ey.h, 1)
            v.wait_ge(sp.h, NMG * NPC)
            ssync(v.tensor_scalar(MU[:, :], PS[:, 0:1024], 1.0 / D, None, op0=ALU.mult))
            ssync(v.tensor_tensor(MSQ[:, :], MU[:, :], MU[:, :], op=ALU.mult))
            ssync(v.scalar_tensor_tensor(VARB[:, :], PS[:, 1024:2048], 1.0 / D,
                                         MSQ[:, :], op0=ALU.mult, op1=ALU.subtract))
            ssync(v.tensor_scalar(VARB[:, :], VARB[:, :], EPS, None, op0=ALU.add))
            ms(fd.h, 1)
            v.wait_ge(fa.h, 2)
            ssync(v.reciprocal(RSTD[:, :], SDB[:, :]))
            ms(fd.h, 2)
            for c in range(NCH):
                v.wait_ge(yr.h, 16 * (c + 1))
                ssync(v.tensor_tensor(SCR[:, :], YRC[c % 2][:, :], MU[:, :],
                                      op=ALU.subtract))
                ssync(v.tensor_tensor(X3C[:, :], SCR[:, :], RSTD[:, :], op=ALU.mult))
                ssync(v.reduce_sum(CS[:, c:c + 1], X3C[:, :], axis=mybir.AxisListType.X))
                ms(cs.h, 1)

        @blk.scalar
        def _(s):
            s.wait_ge(mbd.h, 1)
            s.activation(SDB[:, 0:32], MSQ[:, 0:32], AF.Sqrt).then_inc(mba.h, 1)
            for hq in range(64):
                g, r, qb = hq // 8, (hq % 8) // 2, hq % 2
                for j in range(32):
                    su = hq * 32 + j
                    s.wait_ge(st.h, su + 1)
                    if su >= 2:
                        s.wait_ge(pa.h, su - 1)
                    if su == 0:
                        s.wait_ge(mbd.h, 2)
                    s.activation(PT[j % 2][:, :], PS[:, (j % 2) * 512:(j % 2) * 512 + 512],
                                 AF.Exp, bias=MBALL[:, g * 4 + r:g * 4 + r + 1],
                                 scale=SCALE1).then_inc(ae.h, 1)
            for mg in range(NMG):
                for half in range(2):
                    for mc in range(2):
                        pc = mg * NPC + half * 2 + mc
                        s.wait_ge(ey.h, pc + 2)
                        if pc >= 2:
                            s.wait_ge(sp.h, pc - 1)
                        s.activation(SQY[pc % 2][:, :],
                                     YOB[mg % 2][:, mc * T + half * 512: mc * T + half * 512 + 512],
                                     AF.Square).then_inc(es.h, 1)
            s.wait_ge(fd.h, 1)
            s.activation(SDB[:, :], VARB[:, :], AF.Sqrt).then_inc(fa.h, 2)

        @blk.tensor
        def _(t):
            for hq in range(64):
                g, r, qb = hq // 8, (hq % 8) // 2, hq % 2
                t.wait_ge(qd.h, qd_g(g))
                first_hq = (hq == 0)
                if hq >= 1:
                    t.wait_ge(ab.h, hq)
                for j in range(32):
                    su = hq * 32 + j
                    if su >= 2:
                        t.wait_ge(ae.h, su - 1)
                    stp = PS[:, (j % 2) * 512:(j % 2) * 512 + 512]
                    ins = t.matmul(stp, KG[g % 2][:, j * 128:(j + 1) * 128],
                                   QG[g % 2][:, r * T + qb * 512: r * T + qb * 512 + 512],
                                   start=True, stop=True)
                    ins.then_inc(st.h, 1)
                    t.wait_ge(ae.h, su + 1)
                    t.matmul(PS[:, 2048:2560], VG[g % 2][:, j * 128:(j + 1) * 128],
                             PT[j % 2][:, :], start=(j == 0), stop=(j == 31))
                    ins = t.matmul(PS[:, 2560:3072], ONES[:, :], PT[j % 2][:, :],
                                   start=(j == 0), stop=(j == 31))
                    ins.then_inc(pa.h, 1)
            t.wait_ge(ab.h, 64)

            def l4_stats(pc2):
                mg2, half2, mc2 = pc2 // NPC, (pc2 % NPC) // 2, pc2 % 2
                t.wait_ge(ey.h, pc2 + 1)
                t.wait_ge(es.h, pc2 + 1)
                sy_start = (mg2 == 0 and mc2 == 0)
                sy_stop = (mg2 == NMG - 1 and mc2 == 1)
                t.matmul(PS[:, 0 + half2 * 512:(half2 + 1) * 512],
                         ONES[:, :],
                         YOB[mg2 % 2][:, mc2 * T + half2 * 512: mc2 * T + half2 * 512 + 512],
                         start=sy_start, stop=sy_stop)
                ins2 = t.matmul(PS[:, 1024 + half2 * 512:1024 + (half2 + 1) * 512],
                                ONES[:, :], SQY[pc2 % 2][:, :],
                                start=sy_start, stop=sy_stop)
                ins2.then_inc(sp.h, 1)

            for mg in range(NMG):
                t.wait_ge(wd.h, wd_w(mg))
                for half in range(2):
                    for mc in range(2):
                        pc = mg * NPC + half * 2 + mc
                        if pc >= 2:
                            t.wait_ge(ey.h, pc - 1)
                        pj = PS[:, 3072 + mc * 512:3072 + (mc + 1) * 512]
                        for kc in range(NCH):
                            ins = t.matmul(
                                pj,
                                WOB[mg % 2][:, kc * 256 + mc * 128: kc * 256 + (mc + 1) * 128],
                                AB[:, kc * T + half * 512: kc * T + half * 512 + 512],
                                start=(kc == 0), stop=(kc == NCH - 1))
                        ins.then_inc(wp.h, 1)
                        if pc >= 2:
                            l4_stats(pc - 2)
            l4_stats(NMG * NPC - 2)
            l4_stats(NMG * NPC - 1)

    return p


def run_l1_attn(inputs, static, r3, r2x=None):
    p = build_l1_attn()
    k1f, v1f, qn4, kn4 = {}, {}, {}, {}
    for b in range(2):
        Ka = np.zeros((1024, SEQ), BF)
        Va = np.zeros((SEQ, 1024), BF)
        for c in range(4 * b, 4 * b + 4):
            _, toks = _shard_tokens(c)
            Ka[:, toks] = from_image(r3[c]["k1o"], 1024)
            Va[toks] = from_image(r3[c]["v1o"], T)
        k1f[b] = to_image(np.ascontiguousarray(Ka))
        v1f[b] = to_image(np.ascontiguousarray(Va))
        qn4[b] = np.ascontiguousarray(np.concatenate(
            [r3[c]["q1n"] for c in range(4 * b, 4 * b + 4)], axis=1))
        kn4[b] = np.ascontiguousarray(np.concatenate(
            [r3[c]["k1n"] for c in range(4 * b, 4 * b + 4)], axis=1))
    in_maps = []
    for c in range(NCORES):
        b = c // 4
        in_maps.append({
            "x2i": r3[c]["x2o"], "q1i": r3[c]["q1o"],
            "k1f": k1f[b], "v1f": v1f[b],
            "qn4": qn4[b], "kn4": kn4[b],
            "onesi": np.ones((128, 128), np.float32),
            "wo1i": static["wo1_img"],
        })
    res = run_bass_kernel_spmd(p.nc, in_maps, core_ids=list(range(NCORES)))
    p.close()
    return res.results





def build_l1_ffn():
    p = Prog("l1ffn")
    nc = p.nc
    y_h = p.din("y3i", [128, NCH * T], BF16)
    mu_h = p.din("mu3i", [128, T])
    rs_h = p.din("rs3i", [128, T])
    cs_h = p.din("cs4", [128, 4 * NCH])
    wo_h = p.din("worgi", [16, 128, NCH * 256])
    dw_h = p.din("downi", [8, 128, NCH * 64])
    up_h = p.din("uppi", [4, 128, 4 * 1024])
    ke_h = p.din("keysi", [128, 8])
    wq_h = p.din("wq2i", [128, NCH * 128])
    ones_h = p.din("onesi", [128, 128])
    out_h = p.dout("outo", [128, NCH * T], F32)
    if DBG_L5:
        y4d = p.dout("y4o", [128, NCH * T], BF16)
        mu4_h = p.dout("mu4o", [128, T], F32)
        rs4_h = p.dout("rs4o", [128, T], F32)
        sq4_h = p.dout("sq4o", [128, T], F32)
    else:
        y4d = nc.dram_tensor("y4d", [128, NCH * T], BF16).ap()

    X = p.sbuf("X", [128, NCH * T], BF16)
    ONES = p.sbuf("ONES", [128, 128], BF16)
    YC = [p.sbuf(f"YC{i}", [128, T], BF16) for i in range(2)]
    MUB = p.sbuf("MUB", [128, T], F32)
    RSB = p.sbuf("RSB", [128, T], F32)
    SCR = p.sbuf("SCR", [128, T], F32)
    CSA = p.sbuf("CSA", [128, 4 * NCH], F32)
    CSUM = p.sbuf("CSUM", [128, NCH], F32)
    XM = p.sbuf("XM", [128, NCH], BF16)
    WQ2 = p.sbuf("WQ2", [128, NCH * 128], BF16)
    KEYSB = p.sbuf("KEYSB", [128, 8], BF16)
    QS = p.sbuf("QS", [128, 1], BF16)
    MX1 = p.sbuf("MX1", [1, 1], F32)
    MB1 = p.sbuf("MB1", [1, 1], F32)
    EX1 = p.sbuf("EX1", [1, 8], F32)
    SM1 = p.sbuf("SM1", [1, 1], F32)
    RS1 = p.sbuf("RS1", [1, 1], F32)
    CF = p.sbuf("CF", [1, 8], F32)
    CFB = p.sbuf("CFB", [1, 8], BF16)
    CEB = p.sbuf("CEB", [64, 8], F32)
    DWN = [p.sbuf(f"DWN{i}", [128, NCH * 64], BF16) for i in range(2)]
    UPP = p.sbuf("UPP", [128, 4 * 4096], BF16)
    HEB = p.sbuf("HEB", [128, 4 * T], BF16)
    WOB = [p.sbuf(f"WOB{i}", [128, NCH * 256], BF16) for i in range(2)]
    YOB = [p.sbuf(f"YOB{i}", [128, 2 * T], BF16) for i in range(2)]
    YRC = [p.sbuf(f"YRC{i}", [128, T], BF16) for i in range(2)]
    SQY = [p.sbuf(f"SQY{i}", [128, 512], BF16) for i in range(2)]
    MSQ = p.sbuf("MSQ", [128, T], F32)
    VARB = p.sbuf("VARB", [128, T], F32)
    SDB = p.sbuf("SDB", [128, T], F32)
    OUTC = [p.sbuf(f"OUTC{i}", [128, T], F32) for i in range(2)]
    FLU = p.sbuf("FLU", [1, 4], F32)
    PS = p.psum("PS", [128, 8 * 512], F32)

    wd = p.sem("wd"); qd = p.sem("qd"); nl = p.sem("nl"); dv = p.sem("dv")
    cf_d = p.sem("cfd"); cf_a = p.sem("cfa"); cf_p = p.sem("cfp")
    ph = p.sem("ph"); he = p.sem("he")
    wp = p.sem("wp"); ey = p.sem("ey"); es = p.sem("es"); sp = p.sem("sp")
    fd = p.sem("fd"); fa = p.sem("fa"); yr = p.sem("yr"); ox = p.sem("ox")
    so = p.sem("so")

    NMG = 16
    NPC = 4
    SCALE1 = 1.0 / float(np.sqrt(128.0))

    def wd_dw(e):
        return 16 * (8 + e + 1)

    def wd_wo(mg):
        return 16 * (8 + 8 + mg + 1)

    def qd_y(c):
        return 16 * (2 + c + 1)

    with nc.Block() as blk:
        @blk.gpsimd
        def _(g):
            g.dma_start(out=ONES[:, :], in_=ones_h).then_inc(wd.h, 16)
            g.dma_start(out=CSA[:, :], in_=cs_h).then_inc(wd.h, 16)
            g.dma_start(out=KEYSB[:, :], in_=ke_h).then_inc(wd.h, 16)
            g.dma_start(out=WQ2[:, :], in_=wq_h).then_inc(wd.h, 16)
            for pp in range(4):
                g.dma_start(out=UPP[:, pp * 4096:(pp + 1) * 4096],
                            in_=up_h[pp]).then_inc(wd.h, 16)
            for e in range(8):
                if e >= 2:
                    g.wait_ge(ph.h, e - 1)
                g.dma_start(out=DWN[e % 2][:, :], in_=dw_h[e]).then_inc(wd.h, 16)
            for mg in range(NMG):
                if mg >= 2:
                    g.wait_ge(wp.h, (mg - 2) * NPC + NPC)
                g.dma_start(out=WOB[mg % 2][:, :], in_=wo_h[mg]).then_inc(wd.h, 16)

        @blk.sync
        def _(sy):
            sy.dma_start(out=MUB[:, :], in_=mu_h).then_inc(qd.h, 16)
            sy.dma_start(out=RSB[:, :], in_=rs_h).then_inc(qd.h, 16)
            for c in range(NCH):
                if c >= 2:
                    sy.wait_ge(nl.h, c - 1)
                sy.dma_start(out=YC[c % 2][:, :],
                             in_=y_h[:, c * T:(c + 1) * T]).then_inc(qd.h, 16)
            for mg in range(NMG):
                sy.wait_ge(ey.h, mg * NPC + NPC)
                sy.dma_start(out=y4d[:, 2 * mg * T:(2 * mg + 2) * T],
                             in_=YOB[mg % 2][:, :]).then_inc(so.h, 16)
            sy.wait_ge(so.h, 16 * NMG)
            for c in range(NCH + 2):
                if c < NCH:
                    if c >= 2:
                        sy.wait_ge(ox.h, c - 1)
                    sy.dma_start(out=YRC[c % 2][:, :],
                                 in_=y4d[:, c * T:(c + 1) * T]).then_inc(yr.h, 16)
                if c >= 2:
                    sy.wait_ge(ox.h, c - 1)
                    sy.dma_start(out=out_h[:, (c - 2) * T:(c - 1) * T],
                                 in_=OUTC[(c - 2) % 2][:, :]).then_inc(so.h, 16)
            if DBG_L5:
                sy.wait_ge(fd.h, 2)
                sy.dma_start(out=mu4_h, in_=MUB[:, :]).then_inc(so.h, 16)
                sy.dma_start(out=rs4_h, in_=RSB[:, :]).then_inc(so.h, 16)
                sy.dma_start(out=sq4_h, in_=MSQ[:, :]).then_inc(so.h, 16)
                sy.wait_ge(so.h, 16 * (NMG + NCH + 2))
            else:
                sy.wait_ge(so.h, 16 * (NMG + NCH))

        @blk.vector
        def _(v):
            ndv = [0]

            def ssync(ins):
                ins.then_inc(dv.h, 1)
                ndv[0] += 1
                v.wait_ge(dv.h, ndv[0])

            def ms(semh, k=1):
                ssync(v.tensor_copy(FLU[0:1, 0:2], FLU[0:1, 2:4]))
                v.nop().then_inc(semh, k)

            v.wait_ge(qd.h, 32)
            for c in range(NCH):
                v.wait_ge(qd.h, qd_y(c))
                ssync(v.tensor_tensor(SCR[:, :], YC[c % 2][:, :], MUB[:, :],
                                      op=ALU.subtract))
                ssync(v.tensor_tensor(X[:, c * T:(c + 1) * T], SCR[:, :], RSB[:, :],
                                      op=ALU.mult))
                ms(nl.h, 1)
            v.wait_ge(wd.h, 16 * 2)
            ssync(v.tensor_copy(CSUM[:, :], CSA[:, 0:NCH]))
            for j in range(1, 4):
                ssync(v.tensor_tensor(CSUM[:, :], CSA[:, j * NCH:(j + 1) * NCH],
                                      CSUM[:, :], op=ALU.add))
            ssync(v.tensor_scalar(XM[:, :], CSUM[:, :], 1.0 / SEQ, None, op0=ALU.mult))
            ms(cf_d.h, 1)
            v.wait_ge(cf_p.h, 2)
            ssync(v.reduce_max(MX1[:, :], PS[0:1, 2560:2568], axis=mybir.AxisListType.X))
            ssync(v.tensor_scalar(MB1[:, :], MX1[:, :], -SCALE1, None, op0=ALU.mult))
            ms(cf_d.h, 1)
            v.wait_ge(cf_a.h, 2)
            ssync(v.reciprocal(RS1[:, :], SM1[:, :]))
            ssync(v.tensor_scalar(CF[:, :], EX1[:, :], RS1[:, :], None, op0=ALU.mult, op1=ALU.bypass))
            ms(cf_d.h, 1)
            for e in range(8):
                v.wait_ge(ph.h, e + 1)
                v.wait_ge(cf_p.h, 2 + e + 1)
                ssync(v.tensor_copy(CEB[:, e:e + 1], PS[0:64, 3072:3073]))
                ssync(v.tensor_scalar(HEB[(e % 2) * 64:(e % 2) * 64 + 64,
                                          (e // 2) * T:(e // 2) * T + T],
                                      PS[0:64, (e % 2) * 1024:(e % 2) * 1024 + 1024],
                                      CEB[:, e:e + 1], None, op0=ALU.mult))
                ms(he.h, 1)
            for mg in range(NMG):
                if mg >= 2:
                    v.wait_ge(so.h, 16 * (mg - 1))
                for half in range(2):
                    for mc in range(2):
                        pc = mg * NPC + half * 2 + mc
                        v.wait_ge(wp.h, pc + 1)
                        col = (2 * mg + mc) * T + half * 512
                        ssync(v.tensor_tensor(YOB[mg % 2][:, mc * T + half * 512: mc * T + half * 512 + 512],
                                              PS[:, 3072 + mc * 512:3072 + (mc + 1) * 512],
                                              X[:, col:col + 512], op=ALU.add))
                        ms(ey.h, 1)
            ms(ey.h, 1)
            v.wait_ge(sp.h, NMG * NPC)
            ssync(v.tensor_scalar(MUB[:, :], PS[:, 0:1024], 1.0 / D, None, op0=ALU.mult))
            ssync(v.tensor_tensor(MSQ[:, :], MUB[:, :], MUB[:, :], op=ALU.mult))
            ssync(v.scalar_tensor_tensor(VARB[:, :], PS[:, 1024:2048], 1.0 / D,
                                         MSQ[:, :], op0=ALU.mult, op1=ALU.subtract))
            if DBG_L5:
                ssync(v.tensor_scalar(MSQ[:, :], PS[:, 1024:2048], 1.0, None, op0=ALU.mult))
            ssync(v.tensor_scalar(VARB[:, :], VARB[:, :], EPS, None, op0=ALU.add))
            ms(fd.h, 1)
            v.wait_ge(fa.h, 1)
            ssync(v.reciprocal(RSB[:, :], SDB[:, :]))
            ms(fd.h, 1)
            for c in range(NCH):
                v.wait_ge(yr.h, 16 * (c + 1))
                if c >= 2:
                    v.wait_ge(so.h, 16 * (NMG + c - 1))
                ssync(v.tensor_tensor(SCR[:, :], YRC[c % 2][:, :], MUB[:, :],
                                      op=ALU.subtract))
                ssync(v.tensor_tensor(OUTC[c % 2][:, :], SCR[:, :], RSB[:, :],
                                      op=ALU.mult))
                ms(ox.h, 1)

        @blk.scalar
        def _(s):
            s.wait_ge(cf_p.h, 1)
            s.activation(QS[:, :], PS[:, 2048:2049], AF.Copy).then_inc(cf_a.h, 1)
            s.wait_ge(cf_d.h, 2)
            s.activation(EX1[:, :], PS[0:1, 2560:2568], AF.Exp, bias=MB1[:, :],
                         scale=SCALE1, accum_out=SM1[:, :]).then_inc(cf_a.h, 1)
            s.wait_ge(cf_d.h, 3)
            s.activation(CFB[:, :], CF[:, :], AF.Copy).then_inc(cf_a.h, 1)
            for pc in range(NMG * NPC):
                s.wait_ge(ey.h, pc + 2)
                if pc >= 2:
                    s.wait_ge(sp.h, pc - 1)
                mg, mc, half = pc // NPC, (pc % NPC) // 2, pc % 2
                s.activation(SQY[pc % 2][:, :],
                             YOB[mg % 2][:, mc * T + half * 512: mc * T + half * 512 + 512],
                             AF.Square).then_inc(es.h, 1)
            s.wait_ge(fd.h, 1)
            s.activation(SDB[:, :], VARB[:, :], AF.Sqrt).then_inc(fa.h, 1)

        @blk.tensor
        def _(t):
            t.wait_ge(cf_d.h, 1)
            t.wait_ge(wd.h, 16 * 4)
            for kc in range(NCH):
                ins = t.matmul(PS[:, 2048:2049], WQ2[:, kc * 128:(kc + 1) * 128],
                               XM[:, kc:kc + 1], start=(kc == 0), stop=(kc == NCH - 1))
            ins.then_inc(cf_p.h, 1)
            t.wait_ge(cf_a.h, 1)
            t.matmul(PS[0:1, 2560:2568], QS[:, :], KEYSB[:, :],
                     start=True, stop=True).then_inc(cf_p.h, 1)
            t.wait_ge(cf_a.h, 3)
            t.wait_ge(nl.h, NCH)
            for e in range(8):
                if e >= 1:
                    t.wait_ge(he.h, e)
                ins = t.matmul(PS[0:64, 3072:3073], ONES[0:1, 0:64],
                               CFB[:, e:e + 1], start=True, stop=True)
                ins.then_inc(cf_p.h, 1)
                t.wait_ge(wd.h, wd_dw(e))
                if e >= 2:
                    t.wait_ge(he.h, e - 1)
                for half in range(2):
                    pv = PS[0:64, (e % 2) * 1024 + half * 512:(e % 2) * 1024 + (half + 1) * 512]
                    for kc in range(NCH):
                        ins = t.matmul(pv, DWN[e % 2][:, kc * 64:(kc + 1) * 64],
                                       X[:, kc * T + half * 512: kc * T + half * 512 + 512],
                                       start=(kc == 0), stop=(kc == NCH - 1))
                ins.then_inc(ph.h, 1)
            t.wait_ge(he.h, 8)

            def l5_stats(pc2):
                mg2, half2, mc2 = pc2 // NPC, (pc2 % NPC) // 2, pc2 % 2
                t.wait_ge(ey.h, pc2 + 1)
                t.wait_ge(es.h, pc2 + 1)
                sy_start = (mg2 == 0 and mc2 == 0)
                sy_stop = (mg2 == NMG - 1 and mc2 == 1)
                t.matmul(PS[:, half2 * 512:(half2 + 1) * 512],
                         ONES[:, :],
                         YOB[mg2 % 2][:, mc2 * T + half2 * 512: mc2 * T + half2 * 512 + 512],
                         start=sy_start, stop=sy_stop)
                ins2 = t.matmul(PS[:, 1024 + half2 * 512:1024 + (half2 + 1) * 512],
                                ONES[:, :], SQY[pc2 % 2][:, :],
                                start=sy_start, stop=sy_stop)
                ins2.then_inc(sp.h, 1)

            for mg in range(NMG):
                t.wait_ge(wd.h, wd_wo(mg))
                for half in range(2):
                    for mc in range(2):
                        pc = mg * NPC + half * 2 + mc
                        if pc >= 2:
                            t.wait_ge(ey.h, pc - 1)
                        pj = PS[:, 3072 + mc * 512:3072 + (mc + 1) * 512]
                        ch = 2 * mg + mc
                        for kc in range(NCH):
                            t.matmul(pj,
                                     WOB[mg % 2][:, kc * 256 + mc * 128: kc * 256 + (mc + 1) * 128],
                                     X[:, kc * T + half * 512: kc * T + half * 512 + 512],
                                     start=(kc == 0), stop=False)
                        for pp in range(4):
                            ins = t.matmul(pj,
                                           UPP[:, pp * 4096 + ch * 128: pp * 4096 + (ch + 1) * 128],
                                           HEB[:, pp * T + half * 512: pp * T + half * 512 + 512],
                                           start=False, stop=(pp == 3))
                        ins.then_inc(wp.h, 1)
                        if pc >= 2:
                            l5_stats(pc - 2)
            l5_stats(NMG * NPC - 2)
            l5_stats(NMG * NPC - 1)

    return p


def run_l1_ffn(inputs, static, r4):
    p = build_l1_ffn()
    cs4 = {}
    for b in range(2):
        cs4[b] = np.ascontiguousarray(np.concatenate(
            [r4[c]["cso"] for c in range(4 * b, 4 * b + 4)], axis=1))
    in_maps = []
    for c in range(NCORES):
        b = c // 4
        in_maps.append({
            "y3i": r4[c]["y3o"], "mu3i": r4[c]["mu3o"], "rs3i": r4[c]["rs3o"],
            "cs4": cs4[b],
            "worgi": static["worg_img"], "downi": static["down_img"],
            "uppi": static["upp_arr"], "keysi": static["keys_arr"],
            "wq2i": static["wq2_img"],
            "onesi": np.ones((128, 128), np.float32),
        })
    res = run_bass_kernel_spmd(p.nc, in_maps, core_ids=list(range(NCORES)))
    p.close()
    return res.results





MAGIC0 = 0x51A0C300


def _fused_chunks(core):
    b = core // 4
    r = core % 2
    ch = [0, 3, 4, 7] if r == 0 else [1, 2, 5, 6]
    return b, ch


def _fused_toks(core, half):
    b, ch = _fused_chunks(core)
    cc = ch[2 * half:2 * half + 2]
    return b, np.concatenate([np.arange(c * 512, (c + 1) * 512) for c in cc]), cc


def _fused_out_half(core):
    p = (core % 4) // 2
    return 0 if p == 0 else 1


def _sw(eng_parent, body):
    for arm in eng_parent.switch_core_id(n=128):
        body(arm.logical % 8)


def emit_flag_poll(nc, regs3, FSB, FLG, nbar, r, bar):
    f, c, nr = regs3
    nc.sync.dma_start(out=FLG[0:1, nbar * 2 + r:nbar * 2 + r + 1],
                      in_=FSB[0:1, nbar:nbar + 1]).then_inc(bar.h, 16)
    nc.sync.wait_ge(bar.h, bar.n + 16)
    col = nbar * 2 + (1 - r)
    nc.sync.load(nr, FSB[0:1, nbar:nbar + 1])
    nc.sync.load(f, FLG[0:1, col:col + 1])
    nc.sync.reg_alu(c, f, nr, ALU.not_equal)
    with nc.sync.While(c):
        nc.sync.load(f, FLG[0:1, col:col + 1])
        nc.sync.reg_alu(c, f, nr, ALU.not_equal)


class ZS:
    __slots__ = ("h", "b", "sem")

    def __init__(self, sem):
        self.sem = sem
        self.h = sem.h
        self.b = sem.n


def zpool(p, names):
    return {nm: ZS(p.sem("z" + nm)) for nm in names}


def zclose(p, Z, totals):
    for nm, t in totals.items():
        p.sems["z" + nm].n = Z[nm].b + t


def emit_fence(p, sync_waits=()):
    nc = p.nc
    zf = p.sem("zf")
    zdv = p.sem("zdv")
    FLU = p.misc["FLU"]
    ndv = p.misc.setdefault("z_ndv", [0])
    k = p.misc.get("zf_k", 0) + 1
    p.misc["zf_k"] = k

    with nc.Block() as blk:
        @blk.gpsimd
        def _(g):
            g.nop().then_inc(zf.h, 1)
            g.wait_ge(zf.h, 5 * k)

        @blk.tensor
        def _(t):
            t.nop().then_inc(zf.h, 1)
            t.wait_ge(zf.h, 5 * k)

        @blk.vector
        def _(v):
            v.tensor_copy(FLU[0:1, 0:2], FLU[0:1, 2:4]).then_inc(zdv.h, 1)
            ndv[0] += 1
            v.wait_ge(zdv.h, ndv[0])
            v.nop().then_inc(zf.h, 1)
            v.wait_ge(zf.h, 5 * k)

        @blk.scalar
        def _(s):
            s.nop().then_inc(zf.h, 1)
            s.wait_ge(zf.h, 5 * k)

        @blk.sync
        def _(sy):
            for (h, vv) in sync_waits:
                sy.wait_ge(h, vv)
            sy.nop().then_inc(zf.h, 1)
            sy.wait_ge(zf.h, 5 * k)
    zf.n = 5 * k


F_SEMS = ["wd", "qd", "nl", "xf", "pl", "el", "pg", "sl", "dt", "hc", "t2",
          "e2", "wp", "ey", "es", "sp", "fd", "fa", "n2", "pe", "ev", "sq",
          "pn", "dn", "so"]
F_TOTALS = {"wd": 4480, "qd": 544, "nl": 32, "xf": 8, "pl": 2, "el": 2,
            "pg": 88, "sl": 88, "dt": 88, "hc": 88, "t2": 88, "e2": 1,
            "wp": 64, "ey": 64, "es": 64, "sp": 64, "fd": 1, "fa": 1,
            "n2": 32, "pe": 112, "ev": 112, "sq": 80, "pn": 80, "dn": 80,
            "so": 928}


def emit_ff(p, half, io):
    nc = p.nc
    y_h = io["y1sc"][half]
    mu_h = io["mu1sc"][half]
    rs_h = io["rs1sc"][half]
    w1_h, w3_h, w2_h = io["w1i"], io["w3i"], io["w2i"]
    a1_h, a3_h = io["a1i"], io["a3i"]
    b1_h, b3_h = io["b1i"], io["b3i"]
    a2_h, b2_h = io["a2i"], io["b2i"]
    wq1_h, wk1_h, wv1_h = io["wq1i"], io["wk1i"], io["wv1i"]
    ones_h = io["onesi"]
    x2_h = io["x2sc"][half]
    q1_h = io["q1sc"][half]
    k1_h = io["k1sc"][half]
    v1_h = io["v1sc"][half]
    qn_h = io["q1nsc"][half]
    kn_h = io["k1nsc"][half]
    FLU, PS = p.misc["FLU"], p.misc["PS"]

    Z = zpool(p, F_SEMS)
    wd = Z["wd"]; qd = Z["qd"]; nl = Z["nl"]; xf = Z["xf"]; pl = Z["pl"]
    el = Z["el"]; pg = Z["pg"]; sl = Z["sl"]; dt = Z["dt"]; hc = Z["hc"]
    t2 = Z["t2"]; e2 = Z["e2"]; wp = Z["wp"]; ey = Z["ey"]; es = Z["es"]
    sp = Z["sp"]; fd = Z["fd"]; fa = Z["fa"]; n2 = Z["n2"]; pe = Z["pe"]
    ev = Z["ev"]; sq = Z["sq"]; pn = Z["pn"]; dn = Z["dn"]; so = Z["so"]
    dv = p.sem("zdv")
    NDV = p.misc.setdefault("z_ndv", [0])

    def wd_stripe(i):
        fc = i // 4
        return 16 * (4 + 2 * (fc + 1) + 2 * (i + 1))

    def wd_w2(mg):
        return 16 * (4 + 2 * 22 + 2 * NSTR + 2 * (mg + 1))

    WD_B0 = 4 + 2 * 22 + 2 * NSTR + 2 * 16

    def wd_wq1(i):
        return 16 * (WD_B0 + i + 1)

    def qd_y(c):
        return 16 * (2 + c + 1)

    def pe_after(i):
        return (i + 1) * 4 if i < 20 else 80 + (i - 19) * 8

    def ev_v(mb):
        return 80 + 8 * (mb + 1)

    def so_qkv(i):
        return 16 * (NCH + i + 1)

    NPC = 4

    import contextlib
    sH = contextlib.ExitStack()
    X = sH.enter_context(nc.sbuf_tensor(f"f{half}_{half}_X", [128, NCH * T], BF16))
    ONES = sH.enter_context(nc.sbuf_tensor(f"f{half}_{half}_ONES", [128, 128], BF16))
    sA = contextlib.ExitStack()
    HF = sA.enter_context(nc.sbuf_tensor(f"f{half}_{half}_HF", [128, 22 * T], BF16))
    XF = sA.enter_context(nc.sbuf_tensor(f"f{half}_{half}_XF", [128, 8 * T], BF16))
    YC = sA.enter_context(nc.sbuf_tensor(f"f{half}_{half}_YC", [128, T], BF16))
    MUB = sA.enter_context(nc.sbuf_tensor(f"f{half}_{half}_MUB", [128, T], F32))
    RSB = sA.enter_context(nc.sbuf_tensor(f"f{half}_{half}_RSB", [128, T], F32))
    SCR = sA.enter_context(nc.sbuf_tensor(f"f{half}_{half}_SCR", [128, T], F32))
    WF1 = [sA.enter_context(nc.sbuf_tensor(f"f{half}_WF1{i}", [128, 1024], BF16)) for i in range(2)]
    WF3 = [sA.enter_context(nc.sbuf_tensor(f"f{half}_WF3{i}", [128, 1024], BF16)) for i in range(2)]
    W2B = [sA.enter_context(nc.sbuf_tensor(f"f{half}_W2B{i}", [128, 22 * 256], BF16)) for i in range(2)]
    A1 = sA.enter_context(nc.sbuf_tensor(f"f{half}_{half}_A1", [128, 64], BF16))
    A3 = sA.enter_context(nc.sbuf_tensor(f"f{half}_{half}_A3", [128, 64], BF16))
    B1F = [sA.enter_context(nc.sbuf_tensor(f"f{half}_B1F{i}", [2, 512], BF16)) for i in range(2)]
    B3F = [sA.enter_context(nc.sbuf_tensor(f"f{half}_B3F{i}", [2, 512], BF16)) for i in range(2)]
    A2 = sA.enter_context(nc.sbuf_tensor(f"f{half}_{half}_A2", [128, NSTR * 2], BF16))
    B2F = [sA.enter_context(nc.sbuf_tensor(f"f{half}_B2F{i}", [2, 256], BF16)) for i in range(2)]
    T1 = sA.enter_context(nc.sbuf_tensor(f"f{half}_{half}_T1", [2, T], BF16))
    T3 = sA.enter_context(nc.sbuf_tensor(f"f{half}_{half}_T3", [2, T], BF16))
    T2B = sA.enter_context(nc.sbuf_tensor(f"f{half}_{half}_T2B", [2, T], BF16))
    SG = [sA.enter_context(nc.sbuf_tensor(f"f{half}_SG{i}", [128, T], BF16)) for i in range(2)]
    TS = [sA.enter_context(nc.sbuf_tensor(f"f{half}_TS{i}", [128, T], BF16)) for i in range(2)]
    SQY = [sA.enter_context(nc.sbuf_tensor(f"f{half}_SQY{i}", [128, 512], BF16)) for i in range(2)]
    MSQ = sA.enter_context(nc.sbuf_tensor(f"f{half}_{half}_MSQ", [128, T], F32))
    VARB = sA.enter_context(nc.sbuf_tensor(f"f{half}_{half}_VARB", [128, T], F32))
    SDB = sA.enter_context(nc.sbuf_tensor(f"f{half}_{half}_SDB", [128, T], F32))

    with nc.Block() as blk:
        @blk.gpsimd
        def _(g):
            for dst, srch in ((ONES, ones_h), (A1, a1_h), (A3, a3_h), (A2, a2_h)):
                g.dma_start(out=dst[:, :], in_=srch).then_inc(wd.h, 16)
            for fc in range(22):
                if fc >= 2:
                    g.wait_ge(pg.h, pg.b + (fc - 2) * 4 + 4)
                g.dma_start(out=B1F[fc % 2][:, :],
                            in_=b1_h[:, fc * 512:(fc + 1) * 512]).then_inc(wd.h, 16)
                g.dma_start(out=B3F[fc % 2][:, :],
                            in_=b3_h[:, fc * 512:(fc + 1) * 512]).then_inc(wd.h, 16)
                for s_ in range(4):
                    i = fc * 4 + s_
                    if i >= 2:
                        g.wait_ge(pg.h, pg.b + i - 1)
                    g.dma_start(out=WF1[i % 2][:, :], in_=w1_h[i]).then_inc(wd.h, 16)
                    g.dma_start(out=WF3[i % 2][:, :], in_=w3_h[i]).then_inc(wd.h, 16)
            for mg in range(16):
                if mg >= 2:
                    g.wait_ge(wp.h, wp.b + (mg - 2) * NPC + NPC)
                g.dma_start(out=W2B[mg % 2][:, :], in_=w2_h[mg]).then_inc(wd.h, 16)
                g.dma_start(out=B2F[mg % 2][:, :],
                            in_=b2_h[:, mg * 256:(mg + 1) * 256]).then_inc(wd.h, 16)

        @blk.sync
        def _(sy):
            sy.dma_start(out=MUB[:, :], in_=mu_h).then_inc(qd.h, 16)
            sy.dma_start(out=RSB[:, :], in_=rs_h).then_inc(qd.h, 16)
            for c in range(NCH):
                if c >= 1:
                    sy.wait_ge(nl.h, nl.b + c)
                sy.dma_start(out=YC[:, :],
                             in_=y_h[:, c * T:(c + 1) * T]).then_inc(qd.h, 16)
            for c in range(NCH):
                sy.wait_ge(n2.h, n2.b + c + 1)
                sy.dma_start(out=x2_h[:, c * T:(c + 1) * T],
                             in_=X[:, c * T:(c + 1) * T]).then_inc(so.h, 16)

        @blk.vector
        def _(v):
            def ssync(ins):
                ins.then_inc(dv.h, 1)
                NDV[0] += 1
                v.wait_ge(dv.h, NDV[0])

            def ms(semh, k=1):
                ssync(v.tensor_copy(FLU[0:1, 0:2], FLU[0:1, 2:4]))
                v.nop().then_inc(semh, k)

            v.wait_ge(qd.h, qd.b + 32)
            for c in range(NCH):
                v.wait_ge(qd.h, qd.b + qd_y(c))
                ssync(v.tensor_tensor(SCR[:, :], YC[:, :], MUB[:, :],
                                      op=ALU.subtract))
                ssync(v.tensor_tensor(X[:, c * T:(c + 1) * T], SCR[:, :], RSB[:, :],
                                      op=ALU.mult))
                ms(nl.h, 1)
            for j in range(8):
                ssync(v.tensor_tensor(SCR[:, :], X[:, j * T:(j + 1) * T],
                                      X[:, (8 + j) * T:(9 + j) * T], op=ALU.add))
                ssync(v.tensor_tensor(MSQ[:, :], X[:, (16 + j) * T:(17 + j) * T],
                                      X[:, (24 + j) * T:(25 + j) * T], op=ALU.add))
                ssync(v.tensor_tensor(XF[:, j * T:(j + 1) * T], SCR[:, :], MSQ[:, :],
                                      op=ALU.add))
                ms(xf.h, 1)
            for i in range(NSTR):
                fc = i // 4
                v.wait_ge(sl.h, sl.b + i + 1)
                if i >= 2:
                    v.wait_ge(t2.h, t2.b + i - 1)
                if i % 4 == 0:
                    ssync(v.tensor_tensor(HF[:, fc * T:(fc + 1) * T], SG[i % 2][:, :],
                                          PS[:, 1024:2048], op=ALU.mult))
                    ms(dt.h, 1)
                    ms(hc.h, 1)
                else:
                    ssync(v.tensor_tensor(TS[i % 2][:, :], SG[i % 2][:, :],
                                          PS[:, 1024:2048], op=ALU.mult))
                    ms(dt.h, 1)
                    v.wait_ge(t2.h, t2.b + i)
                    ssync(v.tensor_tensor(HF[:, fc * T:(fc + 1) * T], TS[i % 2][:, :],
                                          HF[:, fc * T:(fc + 1) * T], op=ALU.add))
                    ms(hc.h, 1)
            for mg in range(16):
                for mc in range(2):
                    ch = mg * 2 + mc
                    for half_ in range(2):
                        pc = mg * NPC + mc * 2 + half_
                        v.wait_ge(wp.h, wp.b + pc + 1)
                        col = ch * T + half_ * 512
                        ssync(v.tensor_tensor(X[:, col:col + 512],
                                              PS[:, (pc % 4) * 512:(pc % 4) * 512 + 512],
                                              X[:, col:col + 512], op=ALU.add))
                        ms(ey.h, 1)
            v.wait_ge(sp.h, sp.b + 16 * NPC)
            ssync(v.tensor_scalar(MUB[:, :], PS[:, 2048:3072], 1.0 / D, None,
                                  op0=ALU.mult))
            ssync(v.tensor_tensor(MSQ[:, :], MUB[:, :], MUB[:, :], op=ALU.mult))
            ssync(v.scalar_tensor_tensor(VARB[:, :], PS[:, 3072:4096], 1.0 / D,
                                         MSQ[:, :], op0=ALU.mult, op1=ALU.subtract))
            ssync(v.tensor_scalar(VARB[:, :], VARB[:, :], EPS, None, op0=ALU.add))
            ms(fd.h, 1)
            v.wait_ge(fa.h, fa.b + 1)
            ssync(v.reciprocal(RSB[:, :], SDB[:, :]))
            for c in range(NCH):
                ssync(v.tensor_tensor(SCR[:, :], X[:, c * T:(c + 1) * T], MUB[:, :],
                                      op=ALU.subtract))
                ssync(v.tensor_tensor(X[:, c * T:(c + 1) * T], SCR[:, :], RSB[:, :],
                                      op=ALU.mult))
                ms(n2.h, 1)

        @blk.scalar
        def _(s):
            s.wait_ge(pl.h, pl.b + 1)
            s.activation(T1[:, :], PS[0:2, 0:1024], AF.Copy, scale=0.5).then_inc(el.h, 1)
            s.wait_ge(pl.h, pl.b + 2)
            s.activation(T3[:, :], PS[0:2, 1024:2048], AF.Copy, scale=0.5).then_inc(el.h, 1)
            for i in range(NSTR):
                s.wait_ge(pg.h, pg.b + i + 1)
                if i >= 2:
                    s.wait_ge(dt.h, dt.b + i - 1)
                s.activation(SG[i % 2][:, :], PS[:, 0:1024], AF.Silu).then_inc(sl.h, 1)
            s.wait_ge(t2.h, t2.b + NSTR)
            s.activation(T2B[:, :], PS[0:2, 2048:3072], AF.Copy, scale=0.5).then_inc(e2.h, 1)
            for pc in range(16 * NPC):
                s.wait_ge(ey.h, ey.b + pc + 1)
                if pc >= 2:
                    s.wait_ge(sp.h, sp.b + pc - 1)
                mg, mc, half_ = pc // NPC, (pc % NPC) // 2, pc % 2
                ch = mg * 2 + mc
                col = ch * T + half_ * 512
                s.activation(SQY[pc % 2][:, :], X[:, col:col + 512],
                             AF.Square).then_inc(es.h, 1)
            s.wait_ge(fd.h, fd.b + 1)
            s.activation(SDB[:, :], VARB[:, :], AF.Sqrt).then_inc(fa.h, 1)

        @blk.tensor
        def _(t):
            def emit_a2(i):
                t.wait_ge(dt.h, dt.b + i + 1)
                fc = i // 4
                srcb = (HF[:, fc * T:(fc + 1) * T] if i % 4 == 0
                        else TS[i % 2][:, :])
                for half_ in range(2):
                    ins = t.matmul(PS[0:2, 2048 + half_ * 512:2048 + half_ * 512 + 512],
                                   A2[:, i * 2:(i + 1) * 2],
                                   srcb[:, half_ * 512:half_ * 512 + 512],
                                   start=(i == 0), stop=(i == NSTR - 1))
                ins.then_inc(t2.h, 1)

            t.wait_ge(nl.h, nl.b + NCH)
            t.wait_ge(wd.h, wd.b + 16 * 4)
            for reg, ab in ((0, A1), (1, A3)):
                for half_ in range(2):
                    for kc in range(NCH):
                        ins = t.matmul(PS[0:2, reg * 1024 + half_ * 512: reg * 1024 + half_ * 512 + 512],
                                       ab[:, kc * 2:(kc + 1) * 2],
                                       X[:, kc * T + half_ * 512: kc * T + half_ * 512 + 512],
                                       start=(kc == 0), stop=(kc == NCH - 1))
                ins.then_inc(pl.h, 1)
            t.wait_ge(xf.h, xf.b + 8)
            t.wait_ge(el.h, el.b + 2)
            for i in range(NSTR):
                fc, st_ = i // 4, i % 4
                t.wait_ge(wd.h, wd.b + wd_stripe(i))
                if i >= 1:
                    t.wait_ge(sl.h, sl.b + i)
                    t.wait_ge(dt.h, dt.b + i)
                for dstoff, wf, bb, tt in ((0, WF1[i % 2], B1F[fc % 2], T1),
                                           (1024, WF3[i % 2], B3F[fc % 2], T3)):
                    for half_ in range(2):
                        pv = PS[:, dstoff + half_ * 512: dstoff + half_ * 512 + 512]
                        for kc in range(8):
                            t.matmul(pv,
                                     wf[:, kc * 128:(kc + 1) * 128],
                                     XF[:, kc * T + half_ * 512: kc * T + half_ * 512 + 512],
                                     start=(kc == 0), stop=False)
                        ins = t.matmul(pv, bb[:, st_ * 128:(st_ + 1) * 128],
                                       tt[:, half_ * 512: half_ * 512 + 512],
                                       start=False, stop=True)
                ins.then_inc(pg.h, 1)
                if i >= 1:
                    emit_a2(i - 1)
            emit_a2(NSTR - 1)
            t.wait_ge(hc.h, hc.b + NSTR)
            t.wait_ge(e2.h, e2.b + 1)
            for mg in range(16):
                t.wait_ge(wd.h, wd.b + wd_w2(mg))
                for mc in range(2):
                    for half_ in range(2):
                        pc = mg * NPC + mc * 2 + half_
                        if pc >= 4:
                            t.wait_ge(ey.h, ey.b + pc - 3)
                        pv = PS[:, (pc % 4) * 512:(pc % 4) * 512 + 512]
                        for kc in range(22):
                            t.matmul(pv,
                                     W2B[mg % 2][:, kc * 256 + mc * 128: kc * 256 + (mc + 1) * 128],
                                     HF[:, kc * T + half_ * 512: kc * T + half_ * 512 + 512],
                                     start=(kc == 0), stop=False)
                        ins = t.matmul(pv, B2F[mg % 2][:, mc * 128:(mc + 1) * 128],
                                       T2B[:, half_ * 512: half_ * 512 + 512],
                                       start=False, stop=True)
                        ins.then_inc(wp.h, 1)
                        t.wait_ge(ey.h, ey.b + pc + 1)
                        t.wait_ge(es.h, es.b + pc + 1)
                        ch = mg * 2 + mc
                        col = ch * T + half_ * 512
                        sy_start = (mg == 0 and mc == 0)
                        sy_stop = (mg == 15 and mc == 1)
                        t.matmul(PS[:, 2048 + half_ * 512:2048 + (half_ + 1) * 512],
                                 ONES[:, :], X[:, col:col + 512],
                                 start=sy_start, stop=sy_stop)
                        ins = t.matmul(PS[:, 3072 + half_ * 512:3072 + (half_ + 1) * 512],
                                       ONES[:, :], SQY[pc % 2][:, :],
                                       start=sy_start, stop=sy_stop)
                        ins.then_inc(sp.h, 1)

    sA.close()
    sB = contextlib.ExitStack()
    WQ1 = [sB.enter_context(nc.sbuf_tensor(f"f{half}_WQ1{i}", [128, NCH * 256], BF16)) for i in range(2)]
    QB = [sB.enter_context(nc.sbuf_tensor(f"f{half}_QB{i}", [128, 2 * T], BF16)) for i in range(2)]
    VB = [sB.enter_context(nc.sbuf_tensor(f"f{half}_VB{i}", [128, 8 * 256], BF16)) for i in range(2)]
    SQB = sB.enter_context(nc.sbuf_tensor(f"f{half}_{half}_SQB", [128, 512], BF16))
    RMX = sB.enter_context(nc.sbuf_tensor(f"f{half}_{half}_RMX", [128, 2], F32))
    QNB = sB.enter_context(nc.sbuf_tensor(f"f{half}_{half}_QNB", [128, 32], F32))
    KNB = sB.enter_context(nc.sbuf_tensor(f"f{half}_{half}_KNB", [128, 8], F32))

    with nc.Block() as blk:
        @blk.gpsimd
        def _(g):
            g.wait_ge(n2.h, n2.b + NCH)
            for i in range(24):
                if i >= 2:
                    g.wait_ge(pe.h, pe.b + pe_after(i - 2))
                srch = (wq1_h[i] if i < 16 else
                        (wk1_h[i - 16] if i < 20 else wv1_h[i - 20]))
                g.dma_start(out=WQ1[i % 2][:, :], in_=srch).then_inc(wd.h, 16)

        @blk.sync
        def _(sy):
            for i in range(20):
                sy.wait_ge(ev.h, ev.b + (i + 1) * 4)
                dst = q1_h if i < 16 else k1_h
                mg = i if i < 16 else i - 16
                sy.dma_start(out=dst[:, mg * 2 * T:(mg + 1) * 2 * T],
                             in_=QB[i % 2][:, :]).then_inc(so.h, 16)
            for mb in range(4):
                sy.wait_ge(ev.h, ev.b + ev_v(mb))
                srcv = VB[mb % 2][:, :].rearrange("p (kt m) -> p kt m", m=256)
                dstv = v1_h.rearrange("p (kt c) -> p kt c", c=T)[:, :, mb * 256:(mb + 1) * 256]
                sy.dma_start(out=dstv, in_=srcv).then_inc(so.h, 16)
            sy.wait_ge(dn.h, dn.b + 64)
            sy.dma_start(out=qn_h, in_=QNB[:, :]).then_inc(so.h, 16)
            sy.wait_ge(dn.h, dn.b + 80)
            sy.dma_start(out=kn_h, in_=KNB[:, :]).then_inc(so.h, 16)
            sy.wait_ge(so.h, so.b + 16 * (NCH + 20 + 4 + 2))

        @blk.vector
        def _(v):
            def ssync(ins):
                ins.then_inc(dv.h, 1)
                NDV[0] += 1
                v.wait_ge(dv.h, NDV[0])

            def ms(semh, k=1):
                ssync(v.tensor_copy(FLU[0:1, 0:2], FLU[0:1, 2:4]))
                v.nop().then_inc(semh, k)

            for i in range(20):
                for u in range(4):
                    hl, half_ = u // 2, u % 2
                    par = u % 2
                    guni = i * 4 + u
                    v.wait_ge(ev.h, ev.b + (i + 1) * 4)
                    if guni >= 1:
                        v.wait_ge(pn.h, pn.b + guni)
                    colc = hl * T + half_ * 512
                    ssync(v.tensor_tensor(SQB[:, :], QB[i % 2][:, colc:colc + 512],
                                          QB[i % 2][:, colc:colc + 512], op=ALU.mult))
                    ms(sq.h, 1)
                    v.wait_ge(pn.h, pn.b + guni + 1)
                    npv = PS[:, 3072 + par * 512: 3072 + par * 512 + 512]
                    ssync(v.reduce_max(RMX[:, par:par + 1], npv,
                                       axis=mybir.AxisListType.X))
                    if i < 16:
                        hcol, HMX = i * 2 + hl, QNB
                    else:
                        hcol, HMX = (i - 16) * 2 + hl, KNB
                    if half_ == 0:
                        ssync(v.tensor_copy(HMX[:, hcol:hcol + 1], RMX[:, par:par + 1]))
                    else:
                        ssync(v.tensor_tensor(HMX[:, hcol:hcol + 1], RMX[:, par:par + 1],
                                              HMX[:, hcol:hcol + 1], op=ALU.max))
                    ms(dn.h, 1)

        @blk.scalar
        def _(s):
            for i in range(20):
                for half_ in range(2):
                    for mc in range(2):
                        piece = i * 4 + half_ * 2 + mc
                        bank = piece % 6
                        s.wait_ge(pe.h, pe.b + piece + 1)
                        if i >= 2:
                            s.wait_ge(so.h, so.b + so_qkv(i - 2))
                        s.copy(out=QB[i % 2][:, mc * T + half_ * 512: mc * T + half_ * 512 + 512],
                               in_=PS[:, bank * 512:(bank + 1) * 512]).then_inc(ev.h, 1)
            for mb in range(4):
                for kt in range(8):
                    s.wait_ge(pe.h, pe.b + 80 + mb * 8 + kt + 1)
                    if mb >= 2:
                        s.wait_ge(so.h, so.b + 16 * (NCH + 20 + mb - 1))
                    s.copy(out=VB[mb % 2][:, kt * 256:(kt + 1) * 256],
                           in_=PS[:, kt * 512:kt * 512 + 256]).then_inc(ev.h, 1)

        @blk.tensor
        def _(t):
            t.wait_ge(n2.h, n2.b + NCH)
            for i in range(20):
                t.wait_ge(wd.h, wd.b + wd_wq1(i))
                for half_ in range(2):
                    for mc in range(2):
                        piece = i * 4 + half_ * 2 + mc
                        bank = piece % 6
                        if piece - 6 >= 0:
                            t.wait_ge(ev.h, ev.b + piece - 6 + 1)
                        pv = PS[:, bank * 512:(bank + 1) * 512]
                        for kc in range(NCH):
                            ins = t.matmul(
                                pv,
                                WQ1[i % 2][:, kc * 256 + mc * 128: kc * 256 + (mc + 1) * 128],
                                X[:, kc * T + half_ * 512: kc * T + half_ * 512 + 512],
                                start=(kc == 0), stop=(kc == NCH - 1))
                        ins.then_inc(pe.h, 1)
                for u in range(4):
                    guni = i * 4 + u
                    par = u % 2
                    if guni - 2 >= 0:
                        t.wait_ge(dn.h, dn.b + guni - 2 + 1)
                    t.wait_ge(sq.h, sq.b + guni + 1)
                    npv = PS[:, 3072 + par * 512: 3072 + par * 512 + 512]
                    ins = t.matmul(npv, ONES[:, :], SQB[:, :], start=True, stop=True)
                    ins.then_inc(pn.h, 1)
            for mb in range(4):
                t.wait_ge(wd.h, wd.b + wd_wq1(20 + mb))
                if mb > 0:
                    t.wait_ge(ev.h, ev.b + ev_v(mb - 1))
                else:
                    t.wait_ge(ev.h, ev.b + 80)
                    t.wait_ge(dn.h, dn.b + 80)
                for kc in range(NCH):
                    for kt in range(8):
                        pv = PS[:, kt * 512:kt * 512 + 256]
                        ins = t.matmul(
                            pv,
                            X[:, kc * T + kt * 128: kc * T + (kt + 1) * 128],
                            WQ1[(20 + mb) % 2][:, kc * 256:(kc + 1) * 256],
                            start=(kc == 0), stop=(kc == NCH - 1))
                        if kc == NCH - 1:
                            ins.then_inc(pe.h, 1)

    sB.close()
    sH.close()
    zclose(p, Z, F_TOTALS)


def emit_scatter2(p, io, FSB, FLG):
    nc = p.nc
    bar = p.sem("barso")
    bard = p.sem("bardone")
    K1SH, V1SH, N1SH = io["k1sh"], io["v1sh"], io["n1sh"]

    import contextlib
    with nc.Block() as blk, contextlib.ExitStack() as regs:
        f = regs.enter_context(nc.sync.register("b1f"))
        c = regs.enter_context(nc.sync.register("b1c"))
        nr = regs.enter_context(nc.sync.register("b1n"))

        @blk.sync
        def _(sy):
            for arm in nc.sync.switch_core_id(n=128):
                rank = arm.logical % 8
                r = rank % 2
                n = 0
                for half in range(2):
                    ch = ([0, 3, 4, 7] if r == 0 else [1, 2, 5, 6])[2 * half:2 * half + 2]
                    for j, ac in enumerate(ch):
                        srck = io["k1sc"][half].rearrange("p (hd t) -> p hd t", t=T)[
                            :, :, j * 512:(j + 1) * 512]
                        dstk = K1SH.rearrange("p (hd s) -> p hd s", s=SEQ)[
                            :, :, ac * 512:(ac + 1) * 512]
                        nc.sync.dma_start(out=dstk, in_=srck).then_inc(bar.h, 16)
                        n += 16
                        srcv = io["v1sc"][half].rearrange("p (kt c) -> p kt c", c=T)[
                            :, j * 4:(j + 1) * 4, :]
                        dstv = V1SH.rearrange("p (kt c) -> p kt c", c=T)[
                            :, ac * 4:(ac + 1) * 4, :]
                        nc.sync.dma_start(out=dstv, in_=srcv).then_inc(bar.h, 16)
                        n += 16
                    slot = r * 2 + half
                    nc.sync.dma_start(out=N1SH[:, slot * 40:slot * 40 + 32],
                                      in_=io["q1nsc"][half]).then_inc(bar.h, 16)
                    n += 16
                    nc.sync.dma_start(out=N1SH[:, slot * 40 + 32:slot * 40 + 40],
                                      in_=io["k1nsc"][half]).then_inc(bar.h, 16)
                    n += 16
                nc.sync.wait_ge(bar.h, bar.n + n)
                nc.sync.dma_start(out=FLG[0:1, 2 + r:3 + r],
                                  in_=FSB[0:1, 1:2]).then_inc(bar.h, 16)
                nc.sync.wait_ge(bar.h, bar.n + n + 16)
                col = 2 + (1 - r)
                nc.sync.load(nr, FSB[0:1, 1:2])
                nc.sync.load(f, FLG[0:1, col:col + 1])
                nc.sync.reg_alu(c, f, nr, ALU.not_equal)
                with nc.sync.While(c):
                    nc.sync.load(f, FLG[0:1, col:col + 1])
                    nc.sync.reg_alu(c, f, nr, ALU.not_equal)
                nc.sync.nop().then_inc(bard.h, 1)
                nd = n
            bar.n += nd + 16
            bard.n += 1


B_SEMS = ["wd", "qd", "st", "ae", "pa", "ab", "mbd", "mba", "wp", "ey", "es",
          "sp", "fd", "fa", "cs", "yr", "so"]
B_TOTALS = {"wd": 816, "qd": 384, "st": 2048, "ae": 2048, "pa": 2048,
            "ab": 64, "mbd": 2, "mba": 1, "wp": 64, "ey": 65, "es": 64,
            "sp": 64, "fd": 3, "fa": 2, "cs": 32, "yr": 512, "so": 304}


def emit_fb(p, half, io):
    nc = p.nc
    x2i_h = io["x2sc"][half]
    q1_h = io["q1sc"][half]
    k1f_h = io["k1sh"]
    v1f_h = io["v1sh"]
    n1sh = io["n1sh"]
    ones_h = io["onesi"]
    wo_h = io["wo1i"]
    yo_h = io["y3sc"][half]
    mu_h = io["mu3sc"][half]
    rs_h = io["rs3sc"][half]
    cs_h = io["cssc"][half]
    FLU, PS = p.misc["FLU"], p.misc["PS"]

    Z = zpool(p, B_SEMS)
    wd = Z["wd"]; qd = Z["qd"]; st = Z["st"]; ae = Z["ae"]; pa = Z["pa"]
    ab = Z["ab"]; mbd = Z["mbd"]; mba = Z["mba"]; wp = Z["wp"]; ey = Z["ey"]
    es = Z["es"]; sp = Z["sp"]; fd = Z["fd"]; fa = Z["fa"]; cs = Z["cs"]
    yr = Z["yr"]; so = Z["so"]
    dv = p.sem("zdv")
    NDV = p.misc.setdefault("z_ndv", [0])
    bard = p.sem("bardone")

    NMG = 16
    NPC = 4
    SCALE1 = 1.0 / float(np.sqrt(128.0))

    def qd_g(g):
        return 16 * 3 * (g + 1)

    def wd_w(mg):
        return 16 * (3 + 3 * mg + 1)

    def wd_x2(mg):
        return 16 * (3 + 3 * mg + 3)

    def hq_pa(hq):
        return (hq + 1) * 32

    import contextlib
    sB = contextlib.ExitStack()

    def sb(name, shape, dtype=BF16):
        return sB.enter_context(nc.sbuf_tensor(f"b{half}_{name}", list(shape), dtype))

    AB = sb("AB", [128, NCH * T])
    KG = [sb(f"KG{i}", [128, SEQ]) for i in range(2)]
    VG = [sb(f"VG{i}", [128, 32 * 128]) for i in range(2)]
    QG = [sb(f"QG{i}", [128, 4 * T]) for i in range(2)]
    PT = [sb(f"PT{i}", [128, 512]) for i in range(2)]
    QN4 = sb("QN4", [128, 4 * 32], F32)
    KN4 = sb("KN4", [128, 4 * 8], F32)
    QMX = sb("QMX", [128, 32], F32)
    KMX = sb("KMX", [128, 8], F32)
    MBALL = sb("MBALL", [128, 32], F32)
    ONES = sb("ONES", [128, 128])
    RL = sb("RL", [128, 512], F32)
    WOB = [sb(f"WOB{i}", [128, NCH * 256]) for i in range(2)]
    X2C = [sb(f"X2C{i}", [128, T]) for i in range(2)]
    SQY = [sb(f"SQY{i}", [128, 512]) for i in range(2)]
    YOB = [sb(f"YOB{i}", [128, 2 * T]) for i in range(2)]
    YRC = [sb(f"YRC{i}", [128, T]) for i in range(2)]
    MU = sb("MU", [128, T], F32)
    MSQ = sb("MSQ", [128, T], F32)
    VARB = sb("VARB", [128, T], F32)
    SDB = sb("SDB", [128, T], F32)
    RSTD = sb("RSTD", [128, T], F32)
    SCR = sb("SCR", [128, T], F32)
    X3C = sb("X3C", [128, T])
    CS = sb("CS", [128, NCH], F32)

    with nc.Block() as blk:
        @blk.gpsimd
        def _(g):
            g.wait_ge(bard.h, 2)
            g.dma_start(out=ONES[:, :], in_=ones_h).then_inc(wd.h, 16)
            nsrc = n1sh.rearrange("p (s c) -> p s c", c=40)
            g.dma_start(out=QN4[:, :].rearrange("p (s c) -> p s c", c=32),
                        in_=nsrc[:, :, 0:32]).then_inc(wd.h, 16)
            g.dma_start(out=KN4[:, :].rearrange("p (s c) -> p s c", c=8),
                        in_=nsrc[:, :, 32:40]).then_inc(wd.h, 16)
            for mg in range(NMG):
                if mg >= 2:
                    g.wait_ge(wp.h, wp.b + (mg - 2) * NPC + NPC)
                g.dma_start(out=WOB[mg % 2][:, :], in_=wo_h[mg]).then_inc(wd.h, 16)
                for mc in range(2):
                    if mg >= 1:
                        g.wait_ge(ey.h, ey.b + (mg - 1) * NPC + NPC)
                    g.dma_start(out=X2C[mc][:, :],
                                in_=x2i_h[:, (2 * mg + mc) * T:(2 * mg + mc + 1) * T]
                                ).then_inc(wd.h, 16)

        @blk.sync
        def _(sy):
            for g in range(8):
                if g >= 2:
                    sy.wait_ge(pa.h, pa.b + hq_pa((g - 1) * 8 - 1 + 1))
                sy.dma_start(out=QG[g % 2][:, :],
                             in_=q1_h[:, 4 * g * T:(4 * g + 4) * T]).then_inc(qd.h, 16)
                sy.dma_start(out=KG[g % 2][:, :],
                             in_=k1f_h[:, g * SEQ:(g + 1) * SEQ]).then_inc(qd.h, 16)
                vsrc = v1f_h.rearrange("p (kt c) -> p kt c", c=T)[:, :, g * 128:(g + 1) * 128]
                sy.dma_start(out=VG[g % 2][:, :].rearrange("p (kt c) -> p kt c", c=128),
                             in_=vsrc).then_inc(qd.h, 16)
            for mg in range(NMG):
                sy.wait_ge(ey.h, ey.b + mg * NPC + NPC)
                sy.dma_start(out=yo_h[:, 2 * mg * T:(2 * mg + 2) * T],
                             in_=YOB[mg % 2][:, :]).then_inc(so.h, 16)
            sy.wait_ge(fd.h, fd.b + 3)
            sy.dma_start(out=mu_h, in_=MU[:, :]).then_inc(so.h, 16)
            sy.dma_start(out=rs_h, in_=RSTD[:, :]).then_inc(so.h, 16)
            sy.wait_ge(so.h, so.b + 16 * (NMG + 2))
            for c in range(NCH):
                if c >= 2:
                    sy.wait_ge(cs.h, cs.b + c - 1)
                sy.dma_start(out=YRC[c % 2][:, :],
                             in_=yo_h[:, c * T:(c + 1) * T]).then_inc(yr.h, 16)
            sy.wait_ge(cs.h, cs.b + NCH)
            sy.dma_start(out=cs_h, in_=CS[:, :]).then_inc(so.h, 16)
            sy.wait_ge(so.h, so.b + 16 * (NMG + 3))

        @blk.vector
        def _(v):
            def ssync(ins):
                ins.then_inc(dv.h, 1)
                NDV[0] += 1
                v.wait_ge(dv.h, NDV[0])

            def ms(semh, k=1):
                ssync(v.tensor_copy(FLU[0:1, 0:2], FLU[0:1, 2:4]))
                v.nop().then_inc(semh, k)

            v.wait_ge(wd.h, wd.b + 16 * 3)
            for dst, srcb, n in ((QMX, QN4, 32), (KMX, KN4, 8)):
                ssync(v.tensor_copy(dst[:, :], srcb[:, 0:n]))
                for j in range(1, 4):
                    ssync(v.tensor_tensor(dst[:, :], srcb[:, j * n:(j + 1) * n],
                                          dst[:, :], op=ALU.max))
            for h in range(32):
                ssync(v.tensor_tensor(MSQ[:, h:h + 1], QMX[:, h:h + 1],
                                      KMX[:, h // 4:h // 4 + 1], op=ALU.mult))
            ms(mbd.h, 1)
            v.wait_ge(mba.h, mba.b + 1)
            ssync(v.tensor_scalar(MBALL[:, :], SDB[:, 0:32], -1.01 * SCALE1, None,
                                  op0=ALU.mult))
            ms(mbd.h, 1)
            for hq in range(64):
                v.wait_ge(pa.h, pa.b + hq_pa(hq))
                ssync(v.reciprocal(RL[:, :], PS[:, 2560:3072]))
                ch = hq // 2
                qb = hq % 2
                ssync(v.tensor_tensor(AB[:, ch * T + qb * 512: ch * T + qb * 512 + 512],
                                      PS[:, 2048:2560], RL[:, :], op=ALU.mult))
                ms(ab.h, 1)
            for mg in range(NMG):
                if mg >= 2:
                    v.wait_ge(so.h, so.b + 16 * (mg - 1))
                for half_ in range(2):
                    for mc in range(2):
                        pc = mg * NPC + half_ * 2 + mc
                        v.wait_ge(wp.h, wp.b + pc + 1)
                        v.wait_ge(wd.h, wd.b + wd_x2(mg))
                        ssync(v.tensor_tensor(YOB[mg % 2][:, mc * T + half_ * 512: mc * T + half_ * 512 + 512],
                                              PS[:, 3072 + mc * 512:3072 + (mc + 1) * 512],
                                              X2C[mc][:, half_ * 512:half_ * 512 + 512],
                                              op=ALU.add))
                        ms(ey.h, 1)
            ms(ey.h, 1)
            v.wait_ge(sp.h, sp.b + NMG * NPC)
            ssync(v.tensor_scalar(MU[:, :], PS[:, 0:1024], 1.0 / D, None, op0=ALU.mult))
            ssync(v.tensor_tensor(MSQ[:, :], MU[:, :], MU[:, :], op=ALU.mult))
            ssync(v.scalar_tensor_tensor(VARB[:, :], PS[:, 1024:2048], 1.0 / D,
                                         MSQ[:, :], op0=ALU.mult, op1=ALU.subtract))
            ssync(v.tensor_scalar(VARB[:, :], VARB[:, :], EPS, None, op0=ALU.add))
            ms(fd.h, 1)
            v.wait_ge(fa.h, fa.b + 2)
            ssync(v.reciprocal(RSTD[:, :], SDB[:, :]))
            ms(fd.h, 2)
            for c in range(NCH):
                v.wait_ge(yr.h, yr.b + 16 * (c + 1))
                ssync(v.tensor_tensor(SCR[:, :], YRC[c % 2][:, :], MU[:, :],
                                      op=ALU.subtract))
                ssync(v.tensor_tensor(X3C[:, :], SCR[:, :], RSTD[:, :], op=ALU.mult))
                ssync(v.reduce_sum(CS[:, c:c + 1], X3C[:, :], axis=mybir.AxisListType.X))
                ms(cs.h, 1)

        @blk.scalar
        def _(s):
            s.wait_ge(mbd.h, mbd.b + 1)
            s.activation(SDB[:, 0:32], MSQ[:, 0:32], AF.Sqrt).then_inc(mba.h, 1)
            for hq in range(64):
                g, r, qb = hq // 8, (hq % 8) // 2, hq % 2
                for j in range(32):
                    su = hq * 32 + j
                    s.wait_ge(st.h, st.b + su + 1)
                    if su >= 2:
                        s.wait_ge(pa.h, pa.b + su - 1)
                    if su == 0:
                        s.wait_ge(mbd.h, mbd.b + 2)
                    s.activation(PT[j % 2][:, :], PS[:, (j % 2) * 512:(j % 2) * 512 + 512],
                                 AF.Exp, bias=MBALL[:, g * 4 + r:g * 4 + r + 1],
                                 scale=SCALE1).then_inc(ae.h, 1)
            for mg in range(NMG):
                for half_ in range(2):
                    for mc in range(2):
                        pc = mg * NPC + half_ * 2 + mc
                        s.wait_ge(ey.h, ey.b + pc + 2)
                        if pc >= 2:
                            s.wait_ge(sp.h, sp.b + pc - 1)
                        s.activation(SQY[pc % 2][:, :],
                                     YOB[mg % 2][:, mc * T + half_ * 512: mc * T + half_ * 512 + 512],
                                     AF.Square).then_inc(es.h, 1)
            s.wait_ge(fd.h, fd.b + 1)
            s.activation(SDB[:, :], VARB[:, :], AF.Sqrt).then_inc(fa.h, 2)

        @blk.tensor
        def _(t):
            for hq in range(64):
                g, r, qb = hq // 8, (hq % 8) // 2, hq % 2
                t.wait_ge(qd.h, qd.b + qd_g(g))
                if hq >= 1:
                    t.wait_ge(ab.h, ab.b + hq)
                for j in range(32):
                    su = hq * 32 + j
                    if su >= 2:
                        t.wait_ge(ae.h, ae.b + su - 1)
                    stp = PS[:, (j % 2) * 512:(j % 2) * 512 + 512]
                    ins = t.matmul(stp, KG[g % 2][:, j * 128:(j + 1) * 128],
                                   QG[g % 2][:, r * T + qb * 512: r * T + qb * 512 + 512],
                                   start=True, stop=True)
                    ins.then_inc(st.h, 1)
                    t.wait_ge(ae.h, ae.b + su + 1)
                    t.matmul(PS[:, 2048:2560], VG[g % 2][:, j * 128:(j + 1) * 128],
                             PT[j % 2][:, :], start=(j == 0), stop=(j == 31))
                    ins = t.matmul(PS[:, 2560:3072], ONES[:, :], PT[j % 2][:, :],
                                   start=(j == 0), stop=(j == 31))
                    ins.then_inc(pa.h, 1)
            t.wait_ge(ab.h, ab.b + 64)

            def l4_stats(pc2):
                mg2, half2, mc2 = pc2 // NPC, (pc2 % NPC) // 2, pc2 % 2
                t.wait_ge(ey.h, ey.b + pc2 + 1)
                t.wait_ge(es.h, es.b + pc2 + 1)
                sy_start = (mg2 == 0 and mc2 == 0)
                sy_stop = (mg2 == NMG - 1 and mc2 == 1)
                t.matmul(PS[:, 0 + half2 * 512:(half2 + 1) * 512],
                         ONES[:, :],
                         YOB[mg2 % 2][:, mc2 * T + half2 * 512: mc2 * T + half2 * 512 + 512],
                         start=sy_start, stop=sy_stop)
                ins2 = t.matmul(PS[:, 1024 + half2 * 512:1024 + (half2 + 1) * 512],
                                ONES[:, :], SQY[pc2 % 2][:, :],
                                start=sy_start, stop=sy_stop)
                ins2.then_inc(sp.h, 1)

            for mg in range(NMG):
                t.wait_ge(wd.h, wd.b + wd_w(mg))
                for half_ in range(2):
                    for mc in range(2):
                        pc = mg * NPC + half_ * 2 + mc
                        if pc >= 2:
                            t.wait_ge(ey.h, ey.b + pc - 1)
                        pj = PS[:, 3072 + mc * 512:3072 + (mc + 1) * 512]
                        for kc in range(NCH):
                            ins = t.matmul(
                                pj,
                                WOB[mg % 2][:, kc * 256 + mc * 128: kc * 256 + (mc + 1) * 128],
                                AB[:, kc * T + half_ * 512: kc * T + half_ * 512 + 512],
                                start=(kc == 0), stop=(kc == NCH - 1))
                        ins.then_inc(wp.h, 1)
                        if pc >= 2:
                            l4_stats(pc - 2)
            l4_stats(NMG * NPC - 2)
            l4_stats(NMG * NPC - 1)

    sB.close()
    zclose(p, Z, B_TOTALS)


def emit_scatter3(p, io, FSB, FLG):
    nc = p.nc
    bar = p.sem("barso")
    bard = p.sem("bardone")
    CSSH = io["cssh"]

    import contextlib
    with nc.Block() as blk, contextlib.ExitStack() as regs:
        f = regs.enter_context(nc.sync.register("b2f"))
        c = regs.enter_context(nc.sync.register("b2c"))
        nr = regs.enter_context(nc.sync.register("b2n"))

        @blk.sync
        def _(sy):
            for arm in nc.sync.switch_core_id(n=128):
                rank = arm.logical % 8
                r = rank % 2
                oh = 0 if (rank % 4) // 2 == 0 else 1
                for half in range(2):
                    slot = r * 2 + half
                    nc.sync.dma_start(out=CSSH[:, slot * NCH:(slot + 1) * NCH],
                                      in_=io["cssc"][half]).then_inc(bar.h, 16)
                nc.sync.dma_start(out=io["y3out"], in_=io["y3sc"][oh]).then_inc(bar.h, 16)
                nc.sync.dma_start(out=io["mu3out"], in_=io["mu3sc"][oh]).then_inc(bar.h, 16)
                nc.sync.dma_start(out=io["rs3out"], in_=io["rs3sc"][oh]).then_inc(bar.h, 16)
                nc.sync.wait_ge(bar.h, bar.n + 80)
                nc.sync.dma_start(out=FLG[0:1, 4 + r:5 + r],
                                  in_=FSB[0:1, 2:3]).then_inc(bar.h, 16)
                nc.sync.wait_ge(bar.h, bar.n + 96)
                col = 4 + (1 - r)
                nc.sync.load(nr, FSB[0:1, 2:3])
                nc.sync.load(f, FLG[0:1, col:col + 1])
                nc.sync.reg_alu(c, f, nr, ALU.not_equal)
                with nc.sync.While(c):
                    nc.sync.load(f, FLG[0:1, col:col + 1])
                    nc.sync.reg_alu(c, f, nr, ALU.not_equal)
                nc.sync.nop().then_inc(bard.h, 1)
            bar.n += 96
            bard.n += 1


C_SEMS = ["wd", "qd", "nl", "cfd", "cfa", "cfp", "ph", "he", "wp", "ey", "es",
          "sp", "fd", "fa", "yr", "ox", "so"]
C_TOTALS = {"wd": 512, "qd": 544, "nl": 32, "cfd": 3, "cfa": 3, "cfp": 10,
            "ph": 8, "he": 8, "wp": 64, "ey": 65, "es": 64, "sp": 64,
            "fd": 2, "fa": 1, "yr": 512, "ox": 32, "so": 768}


def emit_fc(p, io):
    nc = p.nc
    y_h = io["y3out"]
    mu_h = io["mu3out"]
    rs_h = io["rs3out"]
    cs_h = io["cssh"]
    wo_h = io["worgi"]
    dw_h = io["downi"]
    up_h = io["uppi"]
    ke_h = io["keysi"]
    wq_h = io["wq2i"]
    ones_h = io["onesi"]
    out_h = io["outo"]
    y4d = io["y4d"]
    FLU, PS = p.misc["FLU"], p.misc["PS"]

    Z = zpool(p, C_SEMS)
    wd = Z["wd"]; qd = Z["qd"]; nl = Z["nl"]; cf_d = Z["cfd"]; cf_a = Z["cfa"]
    cf_p = Z["cfp"]; ph = Z["ph"]; he = Z["he"]; wp = Z["wp"]; ey = Z["ey"]
    es = Z["es"]; sp = Z["sp"]; fd = Z["fd"]; fa = Z["fa"]; yr = Z["yr"]
    ox = Z["ox"]; so = Z["so"]
    dv = p.sem("zdv")
    NDV = p.misc.setdefault("z_ndv", [0])
    bard = p.sem("bardone")

    NMG = 16
    NPC = 4
    SCALE1 = 1.0 / float(np.sqrt(128.0))

    def wd_dw(e):
        return 16 * (8 + e + 1)

    def wd_wo(mg):
        return 16 * (8 + 8 + mg + 1)

    def qd_y(c):
        return 16 * (2 + c + 1)

    import contextlib
    sC = contextlib.ExitStack()

    def sb(name, shape, dtype=BF16):
        return sC.enter_context(nc.sbuf_tensor(f"c_{name}", list(shape), dtype))

    X = sb("X", [128, NCH * T])
    ONES = sb("ONES", [128, 128])
    YC = [sb(f"YC{i}", [128, T]) for i in range(2)]
    MUB = sb("MUB", [128, T], F32)
    RSB = sb("RSB", [128, T], F32)
    SCR = sb("SCR", [128, T], F32)
    CSA = sb("CSA", [128, 4 * NCH], F32)
    CSUM = sb("CSUM", [128, NCH], F32)
    XM = sb("XM", [128, NCH])
    WQ2 = sb("WQ2", [128, NCH * 128])
    KEYSB = sb("KEYSB", [128, 8])
    QS = sb("QS", [128, 1])
    MX1 = sb("MX1", [1, 1], F32)
    MB1 = sb("MB1", [1, 1], F32)
    EX1 = sb("EX1", [1, 8], F32)
    SM1 = sb("SM1", [1, 1], F32)
    RS1 = sb("RS1", [1, 1], F32)
    CF = sb("CF", [1, 8], F32)
    CFB = sb("CFB", [1, 8])
    CEB = sb("CEB", [64, 8], F32)
    DWN = [sb(f"DWN{i}", [128, NCH * 64]) for i in range(2)]
    UPP = sb("UPP", [128, 4 * 4096])
    HEB = sb("HEB", [128, 4 * T])
    WOB = [sb(f"WOB{i}", [128, NCH * 256]) for i in range(2)]
    YOB = [sb(f"YOB{i}", [128, 2 * T]) for i in range(2)]
    YRC = [sb(f"YRC{i}", [128, T]) for i in range(2)]
    SQY = [sb(f"SQY{i}", [128, 512]) for i in range(2)]
    MSQ = sb("MSQ", [128, T], F32)
    VARB = sb("VARB", [128, T], F32)
    SDB = sb("SDB", [128, T], F32)
    OUTC = [sb(f"OUTC{i}", [128, T], F32) for i in range(2)]

    with nc.Block() as blk:
        @blk.gpsimd
        def _(g):
            g.wait_ge(bard.h, 3)
            g.dma_start(out=ONES[:, :], in_=ones_h).then_inc(wd.h, 16)
            g.dma_start(out=CSA[:, :], in_=cs_h).then_inc(wd.h, 16)
            g.dma_start(out=KEYSB[:, :], in_=ke_h).then_inc(wd.h, 16)
            g.dma_start(out=WQ2[:, :], in_=wq_h).then_inc(wd.h, 16)
            for pp in range(4):
                g.dma_start(out=UPP[:, pp * 4096:(pp + 1) * 4096],
                            in_=up_h[pp]).then_inc(wd.h, 16)
            for e in range(8):
                if e >= 2:
                    g.wait_ge(ph.h, ph.b + e - 1)
                g.dma_start(out=DWN[e % 2][:, :], in_=dw_h[e]).then_inc(wd.h, 16)
            for mg in range(NMG):
                if mg >= 2:
                    g.wait_ge(wp.h, wp.b + (mg - 2) * NPC + NPC)
                g.dma_start(out=WOB[mg % 2][:, :], in_=wo_h[mg]).then_inc(wd.h, 16)

        @blk.sync
        def _(sy):
            sy.dma_start(out=MUB[:, :], in_=mu_h).then_inc(qd.h, 16)
            sy.dma_start(out=RSB[:, :], in_=rs_h).then_inc(qd.h, 16)
            for c in range(NCH):
                if c >= 2:
                    sy.wait_ge(nl.h, nl.b + c - 1)
                sy.dma_start(out=YC[c % 2][:, :],
                             in_=y_h[:, c * T:(c + 1) * T]).then_inc(qd.h, 16)
            for mg in range(NMG):
                sy.wait_ge(ey.h, ey.b + mg * NPC + NPC)
                sy.dma_start(out=y4d[:, 2 * mg * T:(2 * mg + 2) * T],
                             in_=YOB[mg % 2][:, :]).then_inc(so.h, 16)
            sy.wait_ge(so.h, so.b + 16 * NMG)
            for c in range(NCH + 2):
                if c < NCH:
                    if c >= 2:
                        sy.wait_ge(ox.h, ox.b + c - 1)
                    sy.dma_start(out=YRC[c % 2][:, :],
                                 in_=y4d[:, c * T:(c + 1) * T]).then_inc(yr.h, 16)
                if c >= 2:
                    sy.wait_ge(ox.h, ox.b + c - 1)
                    sy.dma_start(out=out_h[:, (c - 2) * T:(c - 1) * T],
                                 in_=OUTC[(c - 2) % 2][:, :]).then_inc(so.h, 16)
            sy.wait_ge(so.h, so.b + 16 * (NMG + NCH))

        @blk.vector
        def _(v):
            def ssync(ins):
                ins.then_inc(dv.h, 1)
                NDV[0] += 1
                v.wait_ge(dv.h, NDV[0])

            def ms(semh, k=1):
                ssync(v.tensor_copy(FLU[0:1, 0:2], FLU[0:1, 2:4]))
                v.nop().then_inc(semh, k)

            v.wait_ge(qd.h, qd.b + 32)
            for c in range(NCH):
                v.wait_ge(qd.h, qd.b + qd_y(c))
                ssync(v.tensor_tensor(SCR[:, :], YC[c % 2][:, :], MUB[:, :],
                                      op=ALU.subtract))
                ssync(v.tensor_tensor(X[:, c * T:(c + 1) * T], SCR[:, :], RSB[:, :],
                                      op=ALU.mult))
                ms(nl.h, 1)
            v.wait_ge(wd.h, wd.b + 16 * 2)
            ssync(v.tensor_copy(CSUM[:, :], CSA[:, 0:NCH]))
            for j in range(1, 4):
                ssync(v.tensor_tensor(CSUM[:, :], CSA[:, j * NCH:(j + 1) * NCH],
                                      CSUM[:, :], op=ALU.add))
            ssync(v.tensor_scalar(XM[:, :], CSUM[:, :], 1.0 / SEQ, None, op0=ALU.mult))
            ms(cf_d.h, 1)
            v.wait_ge(cf_p.h, cf_p.b + 2)
            ssync(v.reduce_max(MX1[:, :], PS[0:1, 2560:2568], axis=mybir.AxisListType.X))
            ssync(v.tensor_scalar(MB1[:, :], MX1[:, :], -SCALE1, None, op0=ALU.mult))
            ms(cf_d.h, 1)
            v.wait_ge(cf_a.h, cf_a.b + 2)
            ssync(v.reciprocal(RS1[:, :], SM1[:, :]))
            ssync(v.tensor_scalar(CF[:, :], EX1[:, :], RS1[:, :], None, op0=ALU.mult, op1=ALU.bypass))
            ms(cf_d.h, 1)
            for e in range(8):
                v.wait_ge(ph.h, ph.b + e + 1)
                v.wait_ge(cf_p.h, cf_p.b + 2 + e + 1)
                ssync(v.tensor_copy(CEB[:, e:e + 1], PS[0:64, 3072:3073]))
                ssync(v.tensor_scalar(HEB[(e % 2) * 64:(e % 2) * 64 + 64,
                                          (e // 2) * T:(e // 2) * T + T],
                                      PS[0:64, (e % 2) * 1024:(e % 2) * 1024 + 1024],
                                      CEB[:, e:e + 1], None, op0=ALU.mult))
                ms(he.h, 1)
            for mg in range(NMG):
                if mg >= 2:
                    v.wait_ge(so.h, so.b + 16 * (mg - 1))
                for half_ in range(2):
                    for mc in range(2):
                        pc = mg * NPC + half_ * 2 + mc
                        v.wait_ge(wp.h, wp.b + pc + 1)
                        col = (2 * mg + mc) * T + half_ * 512
                        ssync(v.tensor_tensor(YOB[mg % 2][:, mc * T + half_ * 512: mc * T + half_ * 512 + 512],
                                              PS[:, 3072 + mc * 512:3072 + (mc + 1) * 512],
                                              X[:, col:col + 512], op=ALU.add))
                        ms(ey.h, 1)
            ms(ey.h, 1)
            v.wait_ge(sp.h, sp.b + NMG * NPC)
            ssync(v.tensor_scalar(MUB[:, :], PS[:, 0:1024], 1.0 / D, None, op0=ALU.mult))
            ssync(v.tensor_tensor(MSQ[:, :], MUB[:, :], MUB[:, :], op=ALU.mult))
            ssync(v.scalar_tensor_tensor(VARB[:, :], PS[:, 1024:2048], 1.0 / D,
                                         MSQ[:, :], op0=ALU.mult, op1=ALU.subtract))
            ssync(v.tensor_scalar(VARB[:, :], VARB[:, :], EPS, None, op0=ALU.add))
            ms(fd.h, 1)
            v.wait_ge(fa.h, fa.b + 1)
            ssync(v.reciprocal(RSB[:, :], SDB[:, :]))
            ms(fd.h, 1)
            for c in range(NCH):
                v.wait_ge(yr.h, yr.b + 16 * (c + 1))
                if c >= 2:
                    v.wait_ge(so.h, so.b + 16 * (NMG + c - 1))
                ssync(v.tensor_tensor(SCR[:, :], YRC[c % 2][:, :], MUB[:, :],
                                      op=ALU.subtract))
                ssync(v.tensor_tensor(OUTC[c % 2][:, :], SCR[:, :], RSB[:, :],
                                      op=ALU.mult))
                ms(ox.h, 1)

        @blk.scalar
        def _(s):
            s.wait_ge(cf_p.h, cf_p.b + 1)
            s.activation(QS[:, :], PS[:, 2048:2049], AF.Copy).then_inc(cf_a.h, 1)
            s.wait_ge(cf_d.h, cf_d.b + 2)
            s.activation(EX1[:, :], PS[0:1, 2560:2568], AF.Exp, bias=MB1[:, :],
                         scale=SCALE1, accum_out=SM1[:, :]).then_inc(cf_a.h, 1)
            s.wait_ge(cf_d.h, cf_d.b + 3)
            s.activation(CFB[:, :], CF[:, :], AF.Copy).then_inc(cf_a.h, 1)
            for pc in range(NMG * NPC):
                s.wait_ge(ey.h, ey.b + pc + 2)
                if pc >= 2:
                    s.wait_ge(sp.h, sp.b + pc - 1)
                mg, mc, half_ = pc // NPC, (pc % NPC) // 2, pc % 2
                s.activation(SQY[pc % 2][:, :],
                             YOB[mg % 2][:, mc * T + half_ * 512: mc * T + half_ * 512 + 512],
                             AF.Square).then_inc(es.h, 1)
            s.wait_ge(fd.h, fd.b + 1)
            s.activation(SDB[:, :], VARB[:, :], AF.Sqrt).then_inc(fa.h, 1)

        @blk.tensor
        def _(t):
            t.wait_ge(cf_d.h, cf_d.b + 1)
            t.wait_ge(wd.h, wd.b + 16 * 4)
            for kc in range(NCH):
                ins = t.matmul(PS[:, 2048:2049], WQ2[:, kc * 128:(kc + 1) * 128],
                               XM[:, kc:kc + 1], start=(kc == 0), stop=(kc == NCH - 1))
            ins.then_inc(cf_p.h, 1)
            t.wait_ge(cf_a.h, cf_a.b + 1)
            t.matmul(PS[0:1, 2560:2568], QS[:, :], KEYSB[:, :],
                     start=True, stop=True).then_inc(cf_p.h, 1)
            t.wait_ge(cf_a.h, cf_a.b + 3)
            t.wait_ge(nl.h, nl.b + NCH)
            for e in range(8):
                if e >= 1:
                    t.wait_ge(he.h, he.b + e)
                ins = t.matmul(PS[0:64, 3072:3073], ONES[0:1, 0:64],
                               CFB[:, e:e + 1], start=True, stop=True)
                ins.then_inc(cf_p.h, 1)
                t.wait_ge(wd.h, wd.b + wd_dw(e))
                if e >= 2:
                    t.wait_ge(he.h, he.b + e - 1)
                for half_ in range(2):
                    pv = PS[0:64, (e % 2) * 1024 + half_ * 512:(e % 2) * 1024 + (half_ + 1) * 512]
                    for kc in range(NCH):
                        ins = t.matmul(pv, DWN[e % 2][:, kc * 64:(kc + 1) * 64],
                                       X[:, kc * T + half_ * 512: kc * T + half_ * 512 + 512],
                                       start=(kc == 0), stop=(kc == NCH - 1))
                ins.then_inc(ph.h, 1)
            t.wait_ge(he.h, he.b + 8)

            def l5_stats(pc2):
                mg2, half2, mc2 = pc2 // NPC, (pc2 % NPC) // 2, pc2 % 2
                t.wait_ge(ey.h, ey.b + pc2 + 1)
                t.wait_ge(es.h, es.b + pc2 + 1)
                sy_start = (mg2 == 0 and mc2 == 0)
                sy_stop = (mg2 == NMG - 1 and mc2 == 1)
                t.matmul(PS[:, half2 * 512:(half2 + 1) * 512],
                         ONES[:, :],
                         YOB[mg2 % 2][:, mc2 * T + half2 * 512: mc2 * T + half2 * 512 + 512],
                         start=sy_start, stop=sy_stop)
                ins2 = t.matmul(PS[:, 1024 + half2 * 512:1024 + (half2 + 1) * 512],
                                ONES[:, :], SQY[pc2 % 2][:, :],
                                start=sy_start, stop=sy_stop)
                ins2.then_inc(sp.h, 1)

            for mg in range(NMG):
                t.wait_ge(wd.h, wd.b + wd_wo(mg))
                for half_ in range(2):
                    for mc in range(2):
                        pc = mg * NPC + half_ * 2 + mc
                        if pc >= 2:
                            t.wait_ge(ey.h, ey.b + pc - 1)
                        pj = PS[:, 3072 + mc * 512:3072 + (mc + 1) * 512]
                        ch = 2 * mg + mc
                        for kc in range(NCH):
                            t.matmul(pj,
                                     WOB[mg % 2][:, kc * 256 + mc * 128: kc * 256 + (mc + 1) * 128],
                                     X[:, kc * T + half_ * 512: kc * T + half_ * 512 + 512],
                                     start=(kc == 0), stop=False)
                        for pp in range(4):
                            ins = t.matmul(pj,
                                           UPP[:, pp * 4096 + ch * 128: pp * 4096 + (ch + 1) * 128],
                                           HEB[:, pp * T + half_ * 512: pp * T + half_ * 512 + 512],
                                           start=False, stop=(pp == 3))
                        ins.then_inc(wp.h, 1)
                        if pc >= 2:
                            l5_stats(pc - 2)
            l5_stats(NMG * NPC - 2)
            l5_stats(NMG * NPC - 1)

    sC.close()
    zclose(p, Z, C_TOTALS)


def emit_fq(p, half, io, bufs):
    nc = p.nc
    x_h = io["x0i"][half]
    wq_h, wk_h, wv_h = io["wqi"], io["wki"], io["wvi"]
    cos_h, sin_h, ones_h = io["cosi"][half], io["sini"][half], io["onesi"]
    qo_h, ko_h, vo_h = io["qsc"][half], io["ksc"][half], io["vsc"][half]
    qn_h, kn_h = io["qnsc"][half], io["knsc"][half]

    (X, WB, QB, VB, CO, SI, SC, ONES, SQB, RMX, QNB, KNB, FLU, PS) = bufs

    wd = p.sem("q_wd"); pe = p.sem("q_pe"); ev = p.sem("q_ev"); rp = p.sem("q_rp")
    so = p.sem("q_so"); dv = p.sem("q_dv"); sq = p.sem("q_sq"); pn = p.sem("q_pn")
    dn = p.sem("q_dn")

    IN_DMAS = 4
    OWD = half * 16 * (IN_DMAS + 24)
    OPE = half * 192
    OEV = OPE
    ORP = half * 16
    OSQ = half * 128
    OPN = half * 64
    ODN = half * 64
    OSO = half * 16 * 26

    def wd_after(i):
        return OWD + 16 * (IN_DMAS + i + 1)

    def pe_qk(g, hf, mc):
        return OPE + g * 8 + hf * 4 + mc + 1

    def pe_v(mb, kt):
        return OPE + 16 * 8 + mb * 8 + kt + 1

    ev_qk = pe_qk

    def ev_v(mb, kt):
        return OEV + 16 * 8 + mb * 8 + kt + 1

    def so_qk(g):
        return OSO + 16 * (g + 1)

    def so_v(mb):
        return OSO + 16 * (16 + mb + 1) + 32

    with nc.Block() as blk:
        @blk.gpsimd
        def _(g):
            g.dma_start(out=X[:, :], in_=x_h).then_inc(wd.h, 16)
            g.dma_start(out=CO[:, :], in_=cos_h).then_inc(wd.h, 16)
            g.dma_start(out=SI[:, :], in_=sin_h).then_inc(wd.h, 16)
            g.dma_start(out=ONES[:, :], in_=ones_h).then_inc(wd.h, 16)
            srcs = [wq_h, wk_h]
            for i in range(24):
                if i >= 2:
                    if i - 2 < 16:
                        g.wait_ge(pe.h, pe_qk(i - 2, 1, 3))
                    else:
                        g.wait_ge(pe.h, pe_v(i - 2 - 16, 7))
                if i < 16:
                    srch = srcs[i // 8][i % 8]
                else:
                    srch = wv_h[i - 16]
                g.dma_start(out=WB[i % 2][:, :], in_=srch).then_inc(wd.h, 16)

        @blk.tensor
        def _(t):
            t.wait_ge(wd.h, OWD + 16 * IN_DMAS)
            for gq in range(16):
                t.wait_ge(wd.h, wd_after(gq))
                for hf in range(2):
                    for mc in range(4):
                        piece = gq * 8 + hf * 4 + mc
                        bank = piece % 6
                        if piece - 6 >= 0:
                            t.wait_ge(ev.h, OEV + piece - 6 + 1)
                        pv = PS[:, bank * 512:(bank + 1) * 512]
                        for kc in range(NCH):
                            ins = t.matmul(
                                pv,
                                WB[gq % 2][:, kc * 512 + mc * 128: kc * 512 + (mc + 1) * 128],
                                X[:, kc * T + hf * 512: kc * T + hf * 512 + 512],
                                start=(kc == 0), stop=(kc == NCH - 1))
                        ins.then_inc(pe.h, 1)
                for u in range(4):
                    par = u % 2
                    if gq * 4 + u - 2 >= 0:
                        t.wait_ge(dn.h, ODN + gq * 4 + u - 2 + 1)
                    npv = PS[:, 3072 + par * 512: 3072 + par * 512 + 512]
                    for c in range(2):
                        t.wait_ge(sq.h, OSQ + gq * 8 + u * 2 + c + 1)
                        ins = t.matmul(npv, ONES[:, :], SQB[c][:, :],
                                       start=(c == 0), stop=(c == 1))
                    ins.then_inc(pn.h, 1)
            for mb in range(8):
                t.wait_ge(wd.h, wd_after(16 + mb))
                if mb > 0:
                    t.wait_ge(ev.h, ev_v(mb - 1, 7))
                else:
                    t.wait_ge(ev.h, OEV + 128)
                    t.wait_ge(dn.h, ODN + 64)
                for kc in range(NCH):
                    for kt in range(8):
                        pv = PS[:, kt * 512:(kt + 1) * 512]
                        ins = t.matmul(
                            pv,
                            X[:, kc * T + kt * 128: kc * T + (kt + 1) * 128],
                            WB[(16 + mb) % 2][:, kc * 512:(kc + 1) * 512],
                            start=(kc == 0), stop=(kc == NCH - 1))
                        if kc == NCH - 1:
                            ins.then_inc(pe.h, 1)

        @blk.scalar
        def _(s):
            for gq in range(16):
                for hf in range(2):
                    for mc in range(4):
                        piece = gq * 8 + hf * 4 + mc
                        bank = piece % 6
                        s.wait_ge(pe.h, pe_qk(gq, hf, mc))
                        if gq >= 2:
                            s.wait_ge(so.h, so_qk(gq - 2))
                        pv = PS[:, bank * 512:(bank + 1) * 512]
                        s.copy(
                            out=QB[gq % 2][:, mc * T + hf * 512: mc * T + hf * 512 + 512],
                            in_=pv).then_inc(ev.h, 1)
                s.wait_ge(rp.h, ORP + gq + 1)
                for u in range(4):
                    hl, hf = u // 2, u % 2
                    guni = gq * 4 + u
                    if guni >= 1:
                        s.wait_ge(pn.h, OPN + guni)
                    for c in range(2):
                        colc = (2 * hl + c) * T + hf * 512
                        s.activation(SQB[c][:, :], QB[gq % 2][:, colc:colc + 512],
                                     AF.Square).then_inc(sq.h, 1)
            for mb in range(8):
                for kt in range(8):
                    s.wait_ge(pe.h, pe_v(mb, kt))
                    if mb >= 2:
                        s.wait_ge(so.h, so_v(mb - 2))
                    pv = PS[:, kt * 512:(kt + 1) * 512]
                    s.copy(out=VB[mb % 2][:, kt * 512:(kt + 1) * 512],
                           in_=pv).then_inc(ev.h, 1)

        @blk.vector
        def _(v):
            ndv = p.misc.setdefault("q_ndv", [0])

            def ssync(ins):
                ins.then_inc(dv.h, 1)
                ndv[0] += 1
                v.wait_ge(dv.h, ndv[0])

            for gq in range(16):
                v.wait_ge(ev.h, ev_qk(gq, 1, 3))
                qb = QB[gq % 2]
                for h2 in range(2):
                    base = h2 * 2 * T
                    E = qb[0:32, base:base + T]
                    O = qb[32:64, base:base + T]
                    ssync(v.tensor_tensor(SC[0][:, :], E, CO[0:32, :], op=ALU.mult))
                    ssync(v.tensor_tensor(SC[1][:, :], E, SI[0:32, :], op=ALU.mult))
                    ssync(v.tensor_tensor(SC[2][:, :], O, CO[32:64, :], op=ALU.mult))
                    ssync(v.tensor_tensor(SC[3][:, :], O, SI[32:64, :], op=ALU.mult))
                    ssync(v.tensor_tensor(E, SC[0][:, :], SC[3][:, :], op=ALU.subtract))
                    ssync(v.tensor_tensor(O, SC[2][:, :], SC[1][:, :], op=ALU.add))
                ssync(v.tensor_copy(FLU[0:1, 0:2], FLU[0:1, 2:4]))
                v.nop().then_inc(rp.h, 1)
                HMX = QNB if gq < 8 else KNB
                for u in range(4):
                    hl, hf = u // 2, u % 2
                    par = u % 2
                    head_col = (gq % 8) * 2 + hl
                    v.wait_ge(pn.h, OPN + gq * 4 + u + 1)
                    npv = PS[:, 3072 + par * 512: 3072 + par * 512 + 512]
                    ssync(v.reduce_max(RMX[:, par:par + 1], npv,
                                       axis=mybir.AxisListType.X))
                    if hf == 0:
                        ssync(v.tensor_copy(HMX[:, head_col:head_col + 1],
                                            RMX[:, par:par + 1]))
                    else:
                        ssync(v.tensor_tensor(HMX[:, head_col:head_col + 1],
                                              RMX[:, par:par + 1],
                                              HMX[:, head_col:head_col + 1], op=ALU.max))
                    ssync(v.tensor_copy(FLU[0:1, 0:2], FLU[0:1, 2:4]))
                    v.nop().then_inc(dn.h, 1)

        @blk.sync
        def _(sy):
            for gq in range(16):
                sy.wait_ge(rp.h, ORP + gq + 1)
                dst = qo_h if gq < 8 else ko_h
                mg = gq % 8
                sy.dma_start(out=dst[:, mg * 4 * T:(mg + 1) * 4 * T],
                             in_=QB[gq % 2][:, :]).then_inc(so.h, 16)
            sy.wait_ge(dn.h, ODN + 32)
            sy.dma_start(out=qn_h, in_=QNB[:, :]).then_inc(so.h, 16)
            sy.wait_ge(dn.h, ODN + 64)
            sy.dma_start(out=kn_h, in_=KNB[:, :]).then_inc(so.h, 16)
            for mb in range(8):
                sy.wait_ge(ev.h, ev_v(mb, 7))
                srcv = VB[mb % 2][:, :].rearrange("p (kt m) -> p kt m", m=512)
                dstv = vo_h.rearrange("p (kt c) -> p kt c", c=D)[:, :, mb * 512:(mb + 1) * 512]
                sy.dma_start(out=dstv, in_=srcv).then_inc(so.h, 16)
            sy.wait_ge(so.h, OSO + 16 * 26)


def emit_scatter_q(p, io, FSB, FLG, mode=3):
    nc = p.nc
    bar = p.sem("barso")
    bard = p.sem("bardone")
    KABS, VABS, NSH = io["kabs"], io["vabs"], io["nsh"]

    import contextlib
    with nc.Block() as blk, contextlib.ExitStack() as regs:
        f = regs.enter_context(nc.sync.register("b0f"))
        c = regs.enter_context(nc.sync.register("b0c"))
        nr = regs.enter_context(nc.sync.register("b0n"))

        @blk.sync
        def _(sy):
            nd = 0
            for arm in nc.sync.switch_core_id(n=128):
                rank = arm.logical % 8
                r = rank % 2
                n = 0
                for half in range(2):
                    ch = ([0, 3, 4, 7] if r == 0 else [1, 2, 5, 6])[2 * half:2 * half + 2]
                    for j, ac in enumerate(ch):
                        srck = io["ksc"][half].rearrange("p (c t) -> p c t", t=T)[
                            :, :, j * 512:(j + 1) * 512]
                        dstk = KABS.rearrange("p (c t) -> p c t", t=SEQ)[
                            :, :, ac * 512:(ac + 1) * 512]
                        nc.sync.dma_start(out=dstk, in_=srck).then_inc(bar.h, 16)
                        n += 16
                        srcv = io["vsc"][half].rearrange("p (kt c) -> p kt c", c=D)[
                            :, j * 4:(j + 1) * 4, :]
                        dstv = VABS.rearrange("p (kt c) -> p kt c", c=D)[
                            :, ac * 4:(ac + 1) * 4, :]
                        nc.sync.dma_start(out=dstv, in_=srcv).then_inc(bar.h, 16)
                        n += 16
                    nc.sync.dma_start(
                        out=NSH[:, (r * 4 + half * 2) * H0:(r * 4 + half * 2 + 1) * H0],
                        in_=io["qnsc"][half]).then_inc(bar.h, 16)
                    n += 16
                    nc.sync.dma_start(
                        out=NSH[:, (r * 4 + half * 2 + 1) * H0:(r * 4 + half * 2 + 2) * H0],
                        in_=io["knsc"][half]).then_inc(bar.h, 16)
                    n += 16
                nc.sync.wait_ge(bar.h, bar.n + n)
                if mode >= 2:
                    nc.sync.dma_start(out=FLG[0:1, r:r + 1],
                                      in_=FSB[0:1, 0:1]).then_inc(bar.h, 16)
                    nc.sync.wait_ge(bar.h, bar.n + n + 16)
                    n += 16
                if mode >= 3:
                    col = 1 - r
                    nc.sync.load(nr, FSB[0:1, 0:1])
                    nc.sync.load(f, FLG[0:1, col:col + 1])
                    nc.sync.reg_alu(c, f, nr, ALU.not_equal)
                    with nc.sync.While(c):
                        nc.sync.load(f, FLG[0:1, col:col + 1])
                        nc.sync.reg_alu(c, f, nr, ALU.not_equal)
                nc.sync.nop().then_inc(bard.h, 1)
                nd = n
            bar.n += nd
            bard.n += 1




NSQB_F = [(8, 16), (24, 32)]


def emit_fa_prep(p, io, bufs):
    nc = p.nc
    (MK, QMX, KMX, MBALL, NS4, SDB, MSQ, ONES, FLU) = bufs
    wd = p.sem("a_wd"); dv = p.sem("a_dv"); mbd = p.sem("a_mbd"); mba = p.sem("a_mba")

    with nc.Block() as blk:
        @blk.gpsimd
        def _(g):
            g.dma_start(out=MK[:, :], in_=io["mki"]).then_inc(wd.h, 16)
            g.dma_start(out=ONES[:, :], in_=io["onesi"]).then_inc(wd.h, 16)
            g.wait_ge(p.sem("bardone").h, p.sem("bardone").n)
            g.dma_start(out=NS4[:, :], in_=io["nsh"]).then_inc(wd.h, 16)
            wd.n += 48

        @blk.vector
        def _(v):
            ndv = p.misc.setdefault("a_ndv", [0])

            def ssync(ins):
                ins.then_inc(dv.h, 1)
                ndv[0] += 1
                v.wait_ge(dv.h, ndv[0])

            def ms(semh, k=1):
                ssync(v.tensor_copy(FLU[0:1, 0:2], FLU[0:1, 2:4]))
                v.nop().then_inc(semh, k)

            v.wait_ge(wd.h, 48)
            for dst, off in ((QMX, 0), (KMX, 1)):
                ssync(v.tensor_copy(dst[:, :], NS4[:, off * H0:(off + 1) * H0]))
                for j in range(1, 4):
                    sl = j * 2 + off
                    ssync(v.tensor_tensor(dst[:, :], NS4[:, sl * H0:(sl + 1) * H0],
                                          dst[:, :], op=ALU.max))
            ssync(v.tensor_tensor(MSQ[:, 0:H0], QMX[:, :], KMX[:, :], op=ALU.mult))
            ms(mbd.h, 1)
            v.wait_ge(mba.h, 1)
            ssync(v.tensor_scalar(MBALL[:, :], SDB[:, 0:H0], -1.01 / 16.0, None,
                                  op0=ALU.mult))
            ms(mbd.h, 1)

        @blk.scalar
        def _(s):
            s.wait_ge(mbd.h, 1)
            s.activation(SDB[:, 0:H0], MSQ[:, 0:H0], AF.Sqrt).then_inc(mba.h, 1)


def emit_fa(p, half, io, bufs):
    nc = p.nc
    (AB, KHB, VHB, QHB, MK, PT, MBALL, ONES, RL, WOB, X0C, YOB, SQY,
     MU, MSQ, VARB, SDB, RSTD, FLU, PS) = bufs
    KABS, VABS = io["kabs"], io["vabs"]
    q_h = io["qsc"][half]
    x_h = io["x0i"][half]
    yo_h, mu_h, rs_h = io["y1sc"][half], io["mu1sc"][half], io["rs1sc"][half]

    wd = p.sem("a_wd"); qd = p.sem("a_qd"); st = p.sem("a_st"); ae = p.sem("a_ae")
    pm = p.sem("a_pm"); pa = p.sem("a_pa"); ab = p.sem("a_ab"); dv = p.sem("a_dv")
    wp = p.sem("a_wp"); ey = p.sem("a_ey"); es = p.sem("a_es"); sp = p.sem("a_sp")
    fd = p.sem("a_fd"); fa = p.sem("a_fa"); so = p.sem("a_so"); mbd = p.sem("a_mbd")

    NS = NSQB_F[half]
    NSL = NS[0] + NS[1]
    NG = NSL // 8
    QBS = (0, NS[0])
    QBE = (NS[0] - 1, NSL - 1)
    MOFF = 0 if half == 0 else (NSQB_F[0][0] + NSQB_F[0][1])

    NS_PREV = 0 if half == 0 else (NSQB_F[0][0] + NSQB_F[0][1])
    OSU = half * 16 * NS_PREV
    OHQ = half * 32
    NG0 = (NSQB_F[0][0] + NSQB_F[0][1]) // 8
    OQD = half * 16 * 16 * (1 + 3 * NG0)
    NMG = 16
    NPC = 4
    OWP = half * 64
    OEY = half * 65
    OSO = half * 16 * 18
    OWD_BASE = 48
    OWD = OWD_BASE + half * 16 * 48

    def qd_group(head, g):
        return OQD + 16 * (head * (1 + 3 * NG) + 1 + 3 * (g + 1))

    def wd_w(mg):
        return OWD + 16 * (3 * mg + 1)

    def wd_x0(mg):
        return OWD + 16 * (3 * mg + 3)

    with nc.Block() as blk:
        @blk.gpsimd
        def _(g):
            for h in range(H0):
                for u in range(NSL):
                    su = OSU + h * NSL + u
                    g.wait_ge(ae.h, su + 1)
                    g.tensor_tensor(PT[u % 2][:, :], PT[u % 2][:, :],
                                    MK[:, (MOFF + u) * 512:(MOFF + u + 1) * 512],
                                    op=ALU.mult).then_inc(pm.h, 1)
            for mg in range(NMG):
                if mg >= 2:
                    g.wait_ge(wp.h, OWP + (mg - 2) * NPC + NPC)
                g.dma_start(out=WOB[mg % 2][:, :], in_=io["woi"][mg]).then_inc(wd.h, 16)
                for mc in range(2):
                    if mg >= 1:
                        g.wait_ge(ey.h, OEY + (mg - 1) * NPC + NPC)
                    g.dma_start(out=X0C[mc][:, :],
                                in_=x_h[:, (2 * mg + mc) * T:(2 * mg + mc + 1) * T]
                                ).then_inc(wd.h, 16)

        @blk.sync
        def _(sy):
            for h in range(H0):
                if h >= 2:
                    sy.wait_ge(pa.h, OSU + NSL * (h - 1))
                sy.dma_start(out=QHB[h % 2][:, :],
                             in_=q_h[:, 2 * h * T:(2 * h + 2) * T]).then_inc(qd.h, 16)
                for g in range(NG):
                    gidx = h * NG + g
                    if gidx >= 2:
                        sy.wait_ge(pa.h, OSU + (gidx - 1) * 8)
                    qb = 0 if 8 * g < NS[0] else 1
                    t0 = 8 * g - (0 if qb == 0 else NS[0])
                    for c in range(2):
                        src = KABS.rearrange("p (ch t) -> p ch t", t=SEQ)[
                            :, 2 * h + c, t0 * 128:(t0 + 8) * 128]
                        sy.dma_start(out=KHB[gidx % 2][:, c * 1024:(c + 1) * 1024],
                                     in_=src).then_inc(qd.h, 16)
                    vs = VABS.rearrange("p (kt c) -> p kt c", c=D)
                    sy.dma_start(
                        out=VHB[gidx % 2][:, :].rearrange("p (s c) -> p s c", c=256),
                        in_=vs[:, t0:t0 + 8, h * 256:(h + 1) * 256]
                    ).then_inc(qd.h, 16)
            for mg in range(NMG):
                sy.wait_ge(ey.h, OEY + mg * NPC + NPC + 1)
                sy.dma_start(out=yo_h[:, 2 * mg * T:(2 * mg + 2) * T],
                             in_=YOB[mg % 2][:, :]).then_inc(so.h, 16)
            sy.wait_ge(fd.h, half * 3 + 3)
            sy.dma_start(out=mu_h, in_=MU[:, :]).then_inc(so.h, 16)
            sy.dma_start(out=rs_h, in_=RSTD[:, :]).then_inc(so.h, 16)
            sy.wait_ge(so.h, OSO + 16 * 18)

        @blk.vector
        def _(v):
            ndv = p.misc["a_ndv"]

            def ssync(ins):
                ins.then_inc(dv.h, 1)
                ndv[0] += 1
                v.wait_ge(dv.h, ndv[0])

            def ms(semh, k=1):
                ssync(v.tensor_copy(FLU[0:1, 0:2], FLU[0:1, 2:4]))
                v.nop().then_inc(semh, k)

            for h in range(H0):
                for qb in range(2):
                    v.wait_ge(pa.h, OSU + h * NSL + QBE[qb] + 1)
                    ssync(v.reciprocal(RL[:, :], PS[:, 2048:2560]))
                    for c in range(2):
                        ssync(v.tensor_tensor(
                            AB[:, (2 * h + c) * T + qb * 512:(2 * h + c) * T + qb * 512 + 512],
                            PS[:, 1024 + c * 512:1024 + (c + 1) * 512],
                            RL[:, :], op=ALU.mult))
                    ms(ab.h, 1)
            for mg in range(NMG):
                if mg >= 2:
                    v.wait_ge(so.h, OSO + 16 * (mg - 1))
                for hf in range(2):
                    for mc in range(2):
                        pc = mg * NPC + hf * 2 + mc
                        v.wait_ge(wp.h, OWP + pc + 1)
                        v.wait_ge(wd.h, wd_x0(mg))
                        ssync(v.tensor_tensor(
                            YOB[mg % 2][:, mc * T + hf * 512: mc * T + hf * 512 + 512],
                            PS[:, mc * 512:(mc + 1) * 512],
                            X0C[mc][:, hf * 512:hf * 512 + 512],
                            op=ALU.add))
                        ms(ey.h, 1)
            ms(ey.h, 1)
            v.wait_ge(sp.h, OWP + NMG * NPC)
            ssync(v.tensor_scalar(MU[:, :], PS[:, 1024:2048], 1.0 / D, None,
                                  op0=ALU.mult))
            ssync(v.tensor_tensor(MSQ[:, :], MU[:, :], MU[:, :], op=ALU.mult))
            ssync(v.scalar_tensor_tensor(VARB[:, :], PS[:, 2048:3072], 1.0 / D,
                                         MSQ[:, :], op0=ALU.mult, op1=ALU.subtract))
            ssync(v.tensor_scalar(VARB[:, :], VARB[:, :], EPS, None, op0=ALU.add))
            ms(fd.h, 1)
            v.wait_ge(fa.h, half + 1)
            ssync(v.reciprocal(RSTD[:, :], SDB[:, :]))
            ms(fd.h, 2)

        @blk.scalar
        def _(s):
            for h in range(H0):
                for u in range(NSL):
                    su = OSU + h * NSL + u
                    s.wait_ge(st.h, su + 1)
                    if su >= 2:
                        s.wait_ge(pa.h, su - 1)
                    if su == 0:
                        s.wait_ge(mbd.h, 2)
                    s.activation(PT[u % 2][:, :], PS[:, (u % 2) * 512:(u % 2) * 512 + 512],
                                 AF.Exp, bias=MBALL[:, h:h + 1],
                                 scale=1.0 / 16.0).then_inc(ae.h, 1)
            for mg in range(NMG):
                for hf in range(2):
                    for mc in range(2):
                        pc = mg * NPC + hf * 2 + mc
                        s.wait_ge(ey.h, OEY + pc + 2)
                        if pc >= 2:
                            s.wait_ge(sp.h, OWP + pc - 1)
                        s.activation(SQY[pc % 2][:, :],
                                     YOB[mg % 2][:, mc * T + hf * 512: mc * T + hf * 512 + 512],
                                     AF.Square).then_inc(es.h, 1)
            s.wait_ge(fd.h, half * 3 + 1)
            s.activation(SDB[:, :], VARB[:, :], AF.Sqrt).then_inc(fa.h, 1)

        @blk.tensor
        def _(t):
            for h in range(H0):
                for u in range(NSL):
                    su = OSU + h * NSL + u
                    qb = 0 if u < NS[0] else 1
                    qoff = qb * 512
                    j = u - QBS[qb]
                    g = u // 8
                    gb = (h * NG + g) % 2
                    sl = u % 8
                    t.wait_ge(qd.h, qd_group(h, g))
                    if su >= 2:
                        t.wait_ge(ae.h, su - 1)
                    stp = PS[:, (u % 2) * 512:(u % 2) * 512 + 512]
                    for c in range(2):
                        ins = t.matmul(stp,
                                       KHB[gb][:, c * 1024 + sl * 128: c * 1024 + (sl + 1) * 128],
                                       QHB[h % 2][:, c * T + qoff: c * T + qoff + 512],
                                       start=(c == 0), stop=(c == 1))
                    ins.then_inc(st.h, 1)
                    t.wait_ge(pm.h, su + 1)
                    first = (u == QBS[qb])
                    last = (u == QBE[qb])
                    if first:
                        ghq = OHQ + h * 2 + qb
                        if ghq >= 1:
                            t.wait_ge(ab.h, ghq)
                    for c in range(2):
                        t.matmul(PS[:, 1024 + c * 512:1024 + (c + 1) * 512],
                                 VHB[gb][:, sl * 256 + c * 128: sl * 256 + (c + 1) * 128],
                                 PT[u % 2][:, :], start=first, stop=last)
                    ins = t.matmul(PS[:, 2048:2560], ONES[:, :], PT[u % 2][:, :],
                                   start=first, stop=last)
                    ins.then_inc(pa.h, 1)
            t.wait_ge(ae.h, OSU + H0 * NSL)
            t.wait_ge(ab.h, OHQ + H0 * 2)

            def fa_stats(pc2l):
                mg2, hf2, mc2 = pc2l // NPC, (pc2l % NPC) // 2, pc2l % 2
                t.wait_ge(ey.h, OEY + pc2l + 1)
                t.wait_ge(es.h, OWP + pc2l + 1)
                sy_start = (mg2 == 0 and mc2 == 0)
                sy_stop = (mg2 == NMG - 1 and mc2 == 1)
                t.matmul(PS[:, 1024 + hf2 * 512:1024 + (hf2 + 1) * 512],
                         ONES[:, :],
                         YOB[mg2 % 2][:, mc2 * T + hf2 * 512: mc2 * T + hf2 * 512 + 512],
                         start=sy_start, stop=sy_stop)
                ins2 = t.matmul(PS[:, 2048 + hf2 * 512:2048 + (hf2 + 1) * 512],
                                ONES[:, :], SQY[pc2l % 2][:, :],
                                start=sy_start, stop=sy_stop)
                ins2.then_inc(sp.h, 1)

            for mg in range(NMG):
                t.wait_ge(wd.h, wd_w(mg))
                for hf in range(2):
                    for mc in range(2):
                        pcl = mg * NPC + hf * 2 + mc
                        if pcl >= 2:
                            t.wait_ge(ey.h, OEY + pcl - 1)
                        pj = PS[:, mc * 512:(mc + 1) * 512]
                        for kc in range(NCH):
                            ins = t.matmul(
                                pj,
                                WOB[mg % 2][:, kc * 256 + mc * 128: kc * 256 + (mc + 1) * 128],
                                AB[:, kc * T + hf * 512: kc * T + hf * 512 + 512],
                                start=(kc == 0), stop=(kc == NCH - 1))
                        ins.then_inc(wp.h, 1)
                        if pcl >= 2:
                            fa_stats(pcl - 2)
            fa_stats(NMG * NPC - 2)
            fa_stats(NMG * NPC - 1)


def build_fused(upto=99, barmode=3):
    p = Prog("fused")
    p.misc["barmode"] = barmode
    nc = p.nc
    io = {}
    io["x0i"] = [p.din(f"x0i{h}", [128, NCH * T]) for h in range(2)]
    io["wqi"] = p.din("wqi", [8, 128, NCH * 512])
    io["wki"] = p.din("wki", [8, 128, NCH * 512])
    io["wvi"] = p.din("wvi", [8, 128, NCH * 512])
    io["cosi"] = [p.din(f"cosi{h}", [64, T]) for h in range(2)]
    io["sini"] = [p.din(f"sini{h}", [64, T]) for h in range(2)]
    io["onesi"] = p.din("onesi", [128, 128])
    io["bnonce"] = p.din("bnonce", [1, 16], mybir.dt.int32)

    def priv(name, shape, dtype=BF16):
        return nc.dram_tensor(name, list(shape), dtype).ap()

    io["qsc"] = [priv(f"qsc{h}", [128, NCH * T]) for h in range(2)]
    io["ksc"] = [priv(f"ksc{h}", [128, NCH * T]) for h in range(2)]
    io["vsc"] = [priv(f"vsc{h}", [128, 8 * D]) for h in range(2)]
    io["qnsc"] = [priv(f"qnsc{h}", [128, H0], F32) for h in range(2)]
    io["knsc"] = [priv(f"knsc{h}", [128, H0], F32) for h in range(2)]
    if upto <= 1:
        kdbg = p.dout("kdbg", [128, NCH * SEQ], BF16)
        vdbg = p.dout("vdbg", [128, (SEQ // 128) * D], BF16)
        ndbg = p.dout("ndbg", [128, 8 * H0], F32)
    io["kabs"] = nc.dram_tensor("kabs", [128, NCH * SEQ], BF16, addr_space="Shared").ap()
    io["vabs"] = nc.dram_tensor("vabs", [128, (SEQ // 128) * D], BF16, addr_space="Shared").ap()
    io["nsh"] = nc.dram_tensor("nsh", [128, 8 * H0], F32, addr_space="Shared").ap()
    FLG = nc.dram_tensor("flg", [1, 16], mybir.dt.int32, addr_space="Shared").ap()

    FSB = p.sbuf("FSB", [1, 16], mybir.dt.int32)
    FLU = p.sbuf("FLUf", [1, 4], F32)
    PS = p.psum("PSf", [128, 8 * 512], F32)
    p.misc["FLU"] = FLU
    p.misc["PS"] = PS
    fprep = p.sem("fprep")

    with nc.Block() as blk:
        @blk.sync
        def _(sy):
            sy.dma_start(out=FSB[0:1, :], in_=io["bnonce"]).then_inc(fprep.h, 16)
            sy.wait_ge(fprep.h, 16)

    import contextlib
    sQ = contextlib.ExitStack()
    X = sQ.enter_context(nc.sbuf_tensor("qX", [128, NCH * T], BF16))
    WB = [sQ.enter_context(nc.sbuf_tensor(f"qWB{i}", [128, NCH * 512], BF16)) for i in range(2)]
    QB = [sQ.enter_context(nc.sbuf_tensor(f"qQB{i}", [128, 4 * T], BF16)) for i in range(2)]
    VB = [sQ.enter_context(nc.sbuf_tensor(f"qVB{i}", [128, 8 * 512], BF16)) for i in range(2)]
    CO = sQ.enter_context(nc.sbuf_tensor("qCO", [64, T], F32))
    SI = sQ.enter_context(nc.sbuf_tensor("qSI", [64, T], F32))
    SC = [sQ.enter_context(nc.sbuf_tensor(f"qSC{i}", [32, T], F32)) for i in range(4)]
    ONESB = sQ.enter_context(nc.sbuf_tensor("qONES", [128, 128], BF16))
    SQB = [sQ.enter_context(nc.sbuf_tensor(f"qSQB{i}", [128, 512], BF16)) for i in range(2)]
    RMX = sQ.enter_context(nc.sbuf_tensor("qRMX", [128, 2], F32))
    QNB = sQ.enter_context(nc.sbuf_tensor("qQNB", [128, H0], F32))
    KNB = sQ.enter_context(nc.sbuf_tensor("qKNB", [128, H0], F32))
    bufsQ = (X, WB, QB, VB, CO, SI, SC, ONESB, SQB, RMX, QNB, KNB, FLU, PS)
    for half in range(2):
        emit_fq(p, half, io, bufsQ)
    sQ.close()

    if upto >= 1:
        emit_scatter_q(p, io, FSB, FLG, mode=p.misc.get("barmode", 3))

    if upto <= 1:
        fso = p.sem("fso")
        with nc.Block() as blk:
            @blk.sync
            def _(sy):
                sy.dma_start(out=kdbg, in_=io["kabs"]).then_inc(fso.h, 16)
                sy.dma_start(out=vdbg, in_=io["vabs"]).then_inc(fso.h, 16)
                sy.dma_start(out=ndbg, in_=io["nsh"]).then_inc(fso.h, 16)
                sy.wait_ge(fso.h, 48)
        return p

    if upto >= 2:
        io["mki"] = p.din("mki", [128, 80 * 512], mybir.dt.uint8)
        io["woi"] = p.din("woi", [16, 128, NCH * 256])
        if upto <= 2:
            io["y1sc"] = [p.dout(f"y1sc{h}", [128, NCH * T], BF16) for h in range(2)]
            io["mu1sc"] = [p.dout(f"mu1sc{h}", [128, T], F32) for h in range(2)]
            io["rs1sc"] = [p.dout(f"rs1sc{h}", [128, T], F32) for h in range(2)]
        else:
            io["y1sc"] = [priv(f"y1sc{h}", [128, NCH * T]) for h in range(2)]
            io["mu1sc"] = [priv(f"mu1sc{h}", [128, T], F32) for h in range(2)]
            io["rs1sc"] = [priv(f"rs1sc{h}", [128, T], F32) for h in range(2)]
        sA = contextlib.ExitStack()
        AB = sA.enter_context(nc.sbuf_tensor("aAB", [128, NCH * T], BF16))
        MK = sA.enter_context(nc.sbuf_tensor("aMK", [128, 80 * 512], mybir.dt.uint8))
        KHB = [sA.enter_context(nc.sbuf_tensor(f"aKHB{i}", [128, 2048], BF16)) for i in range(2)]
        VHB = [sA.enter_context(nc.sbuf_tensor(f"aVHB{i}", [128, 2048], BF16)) for i in range(2)]
        QHB = [sA.enter_context(nc.sbuf_tensor(f"aQHB{i}", [128, 2 * T], BF16)) for i in range(2)]
        PT = [sA.enter_context(nc.sbuf_tensor(f"aPT{i}", [128, 512], BF16)) for i in range(2)]
        QMX = sA.enter_context(nc.sbuf_tensor("aQMX", [128, H0], F32))
        KMX = sA.enter_context(nc.sbuf_tensor("aKMX", [128, H0], F32))
        MBALL = sA.enter_context(nc.sbuf_tensor("aMBALL", [128, H0], F32))
        NS4 = sA.enter_context(nc.sbuf_tensor("aNS4", [128, 8 * H0], F32))
        ONESA = sA.enter_context(nc.sbuf_tensor("aONES", [128, 128], BF16))
        RL = sA.enter_context(nc.sbuf_tensor("aRL", [128, 512], F32))
        WOB = [sA.enter_context(nc.sbuf_tensor(f"aWOB{i}", [128, NCH * 256], BF16)) for i in range(2)]
        X0C = [sA.enter_context(nc.sbuf_tensor(f"aX0C{i}", [128, T], BF16)) for i in range(2)]
        YOB = [sA.enter_context(nc.sbuf_tensor(f"aYOB{i}", [128, 2 * T], BF16)) for i in range(2)]
        SQY = [sA.enter_context(nc.sbuf_tensor(f"aSQY{i}", [128, 512], BF16)) for i in range(2)]
        MUA = sA.enter_context(nc.sbuf_tensor("aMU", [128, T], F32))
        MSQA = sA.enter_context(nc.sbuf_tensor("aMSQ", [128, T], F32))
        VARA = sA.enter_context(nc.sbuf_tensor("aVAR", [128, T], F32))
        SDA = sA.enter_context(nc.sbuf_tensor("aSD", [128, T], F32))
        RSTDA = sA.enter_context(nc.sbuf_tensor("aRSTD", [128, T], F32))
        emit_fa_prep(p, io, (MK, QMX, KMX, MBALL, NS4, SDA, MSQA, ONESA, FLU))
        bufsA = (AB, KHB, VHB, QHB, MK, PT, MBALL, ONESA, RL, WOB, X0C, YOB, SQY,
                 MUA, MSQA, VARA, SDA, RSTDA, FLU, PS)
        for half in range(2):
            emit_fa(p, half, io, bufsA)
        sA.close()

    if upto <= 2:
        return p

    io["w1i"] = p.din("w1i", [NSTR, 128, 1024])
    io["w3i"] = p.din("w3i", [NSTR, 128, 1024])
    io["w2i"] = p.din("w2i", [16, 128, 22 * 256])
    io["a1i"] = p.din("a1i", [128, 64])
    io["a3i"] = p.din("a3i", [128, 64])
    io["b1i"] = p.din("b1i", [2, NSTR * 128])
    io["b3i"] = p.din("b3i", [2, NSTR * 128])
    io["a2i"] = p.din("a2i", [128, NSTR * 2])
    io["b2i"] = p.din("b2i", [2, D])
    io["wq1i"] = p.din("wq1i", [16, 128, NCH * 256])
    io["wk1i"] = p.din("wk1i", [4, 128, NCH * 256])
    io["wv1i"] = p.din("wv1i", [4, 128, NCH * 256])
    if upto <= 3:
        io["x2sc"] = [p.dout(f"x2sc{h}", [128, NCH * T], BF16) for h in range(2)]
        io["q1sc"] = [p.dout(f"q1sc{h}", [128, NCH * T], BF16) for h in range(2)]
        io["k1sc"] = [p.dout(f"k1sc{h}", [128, 8 * T], BF16) for h in range(2)]
        io["v1sc"] = [p.dout(f"v1sc{h}", [128, 8 * T], BF16) for h in range(2)]
        io["q1nsc"] = [p.dout(f"q1nsc{h}", [128, 32], F32) for h in range(2)]
        io["k1nsc"] = [p.dout(f"k1nsc{h}", [128, 8], F32) for h in range(2)]
    else:
        io["x2sc"] = [priv(f"x2sc{h}", [128, NCH * T]) for h in range(2)]
        io["q1sc"] = [priv(f"q1sc{h}", [128, NCH * T]) for h in range(2)]
        io["k1sc"] = [priv(f"k1sc{h}", [128, 8 * T]) for h in range(2)]
        io["v1sc"] = [priv(f"v1sc{h}", [128, 8 * T]) for h in range(2)]
        io["q1nsc"] = [priv(f"q1nsc{h}", [128, 32], F32) for h in range(2)]
        io["k1nsc"] = [priv(f"k1nsc{h}", [128, 8], F32) for h in range(2)]

    emit_fence(p, sync_waits=[(p.sem("a_so").h, 16 * 18 * 2)])
    emit_ff(p, 0, io)
    emit_fence(p)
    emit_ff(p, 1, io)

    if upto <= 3:
        return p

    io["wo1i"] = p.din("wo1i", [16, 128, NCH * 256])
    io["k1sh"] = nc.dram_tensor("k1sh", [128, 8 * SEQ], BF16, addr_space="Shared").ap()
    io["v1sh"] = nc.dram_tensor("v1sh", [128, 32 * T], BF16, addr_space="Shared").ap()
    io["n1sh"] = nc.dram_tensor("n1sh", [128, 4 * 40], F32, addr_space="Shared").ap()
    if upto <= 4:
        io["y3sc"] = [p.dout(f"y3sc{h}", [128, NCH * T], BF16) for h in range(2)]
        io["mu3sc"] = [p.dout(f"mu3sc{h}", [128, T], F32) for h in range(2)]
        io["rs3sc"] = [p.dout(f"rs3sc{h}", [128, T], F32) for h in range(2)]
        io["cssc"] = [p.dout(f"cssc{h}", [128, NCH], F32) for h in range(2)]
    else:
        io["y3sc"] = [priv(f"y3sc{h}", [128, NCH * T]) for h in range(2)]
        io["mu3sc"] = [priv(f"mu3sc{h}", [128, T], F32) for h in range(2)]
        io["rs3sc"] = [priv(f"rs3sc{h}", [128, T], F32) for h in range(2)]
        io["cssc"] = [priv(f"cssc{h}", [128, NCH], F32) for h in range(2)]

    emit_fence(p)
    emit_scatter2(p, io, FSB, FLG)
    emit_fb(p, 0, io)
    emit_fence(p)
    emit_fb(p, 1, io)

    if upto <= 4:
        return p

    io["worgi"] = p.din("worgi", [16, 128, NCH * 256])
    io["downi"] = p.din("downi", [8, 128, NCH * 64])
    io["uppi"] = p.din("uppi", [4, 128, 4 * 1024])
    io["keysi"] = p.din("keysi", [128, 8])
    io["wq2i"] = p.din("wq2i", [128, NCH * 128])
    io["cssh"] = nc.dram_tensor("cssh", [128, 4 * NCH], F32, addr_space="Shared").ap()
    io["y3out"] = priv("y3out", [128, NCH * T])
    io["mu3out"] = priv("mu3out", [128, T], F32)
    io["rs3out"] = priv("rs3out", [128, T], F32)
    io["y4d"] = priv("y4d", [128, NCH * T])
    io["outo"] = p.dout("outo", [128, NCH * T], F32)

    emit_fence(p)
    emit_scatter3(p, io, FSB, FLG)
    emit_fc(p, io)
    return p


def _fused_masks(core):
    _, ch = _fused_chunks(core)
    masks = np.zeros((80, 128, 512), np.uint8)
    ktl = np.arange(128)[:, None]
    qtl = np.arange(512)[None, :]
    off = 0
    for h in range(2):
        for qb in range(2):
            i = ch[2 * h + qb]
            nsl = NSQB_F[h][qb]
            nreal = 4 * (i + 1)
            for s_ in range(nsl):
                if s_ < nreal - 4:
                    masks[off + s_] = 1
                elif s_ < nreal:
                    d = -128 * (s_ - (nreal - 4))
                    masks[off + s_] = (ktl <= qtl + d).astype(np.uint8)
            off += nsl
    return np.ascontiguousarray(masks.transpose(1, 0, 2).reshape(128, 80 * 512))


def run_fused(inputs, static, upto=99, barmode=3):
    p = build_fused(upto, barmode)
    x0 = np.asarray(inputs["L0_attn_hidden_states"], dtype=np.float32)
    in_maps = []
    import time as _time
    nonce = (int(_time.time() * 37) & 0x3FFFFFFF) + 7
    for c in range(NCORES):
        m = {
            "wqi": static["wq_img"], "wki": static["wk_img"], "wvi": static["wv_img"],
            "onesi": np.ones((128, 128), np.float32),
            "bnonce": np.arange(nonce, nonce + 16, dtype=np.int32).reshape(1, 16),
        }
        if upto >= 2:
            m["mki"] = _fused_masks(c)
            m["woi"] = static["wo_img"]
        if upto >= 3:
            m.update({
                "w1i": static["w1_img"], "w3i": static["w3_img"], "w2i": static["w2_img"],
                "a1i": static["a1_img"], "a3i": static["a3_img"],
                "b1i": static["b1_arr"], "b3i": static["b3_arr"],
                "a2i": static["a2_img"], "b2i": static["b2_arr"],
                "wq1i": static["wq1_img"], "wk1i": static["wk1_img"],
                "wv1i": static["wv1_img"],
            })
        if upto >= 4:
            m["wo1i"] = static["wo1_img"]
        if upto >= 5:
            m.update({
                "worgi": static["worg_img"], "downi": static["down_img"],
                "uppi": static["upp_arr"], "keysi": static["keys_arr"],
                "wq2i": static["wq2_img"],
            })
        for h in range(2):
            b, toks, _ = _fused_toks(c, h)
            xT = np.ascontiguousarray(x0[b][toks].T)
            m[f"x0i{h}"] = to_image(xT.astype(np.float32))
            cos, sin = rope_tables(toks)
            m[f"cosi{h}"] = np.ascontiguousarray(np.vstack([cos, cos]))
            m[f"sini{h}"] = np.ascontiguousarray(np.vstack([sin, sin]))
        in_maps.append(m)
    res = run_bass_kernel_spmd(p.nc, in_maps, core_ids=list(range(NCORES)))
    p.close()
    return res.results


def kernel(**inputs):
    inputs = {k: np.asarray(v) for k, v in inputs.items()}
    static = _prep_static(inputs)
    res = run_fused(inputs, static, upto=5)
    out = np.zeros((2, SEQ, D), np.float32)
    for c in range(NCORES):
        oh = _fused_out_half(c)
        b, toks, _ = _fused_toks(c, oh)
        out[b][toks] = from_image(res[c]["outo"], D).T
    return out


# version 03
# speedup vs baseline: 1.5110x; 1.5110x over previous
import contextlib
import numpy as np
import ml_dtypes
import concourse.bass as bass
import concourse.mybir as mybir
from concourse.bass_utils import run_bass_kernel_spmd

BF = ml_dtypes.bfloat16
F32 = mybir.dt.float32
BF16 = mybir.dt.bfloat16
AF = mybir.ActivationFunctionType
ALU = mybir.AluOpType

NCORES = 8
T = 1024
D = 4096
SEQ = 4096
NCH = D // 128
H0, HD0, ROT = 16, 256, 64
G1, R1, HD1 = 8, 4, 128
HFF = 11008
FOLDH = 2752
EPS = 1e-5

_LAUNCHES = []



def to_image(xT):
    R, C = xT.shape
    ch = R // 128
    return np.ascontiguousarray(xT.reshape(ch, 128, C).transpose(1, 0, 2).reshape(128, ch * C))


def from_image(img, rows):
    P, X = img.shape
    ch = rows // 128
    C = X // ch
    return np.ascontiguousarray(img.reshape(128, ch, C).transpose(1, 0, 2).reshape(rows, C))


def w_image(wT, mb=512):
    K, M = wT.shape
    kc = K // 128
    nm = M // mb
    a = wT.reshape(kc, 128, nm, mb).transpose(2, 1, 0, 3).reshape(nm, 128, kc * mb)
    return np.ascontiguousarray(a)


def l0_head_perm():
    perm = []
    for h in range(H0):
        b = h * HD0
        perm += [b + 2 * i for i in range(32)]
        perm += [b + 2 * i + 1 for i in range(32)]
        perm += list(range(b + 64, b + 256))
    return np.array(perm)


def rope_tables(positions):
    invf = 1.0 / 10000.0 ** (np.arange(0, ROT, 2, dtype=np.float64) / ROT)
    ang = positions[None, :].astype(np.float64) * invf[:, None]
    return np.cos(ang).astype(np.float32), np.sin(ang).astype(np.float32)



class Sem:
    def __init__(self, h):
        self.h = h
        self.n = 0

    def inc(self, k=1):
        self.n += k
        return self.n


class Prog:
    def __init__(self, name):
        self.nc = bass.Bass("TRN2", target_bir_lowering=False)
        self.name = name
        self.stack = contextlib.ExitStack()
        self.sems = {}
        self.bufs = {}
        self.misc = {}

    def sem(self, name):
        if name not in self.sems:
            self.sems[name] = Sem(self.stack.enter_context(self.nc.semaphore(name)))
        return self.sems[name]

    def sbuf(self, name, shape, dtype):
        t = self.stack.enter_context(self.nc.sbuf_tensor(name, list(shape), dtype))
        self.bufs[name] = t
        return t

    def psum(self, name, shape, dtype=F32):
        t = self.stack.enter_context(self.nc.psum_tensor(name, list(shape), dtype))
        self.bufs[name] = t
        return t

    def din(self, name, shape, dtype=F32):
        return self.nc.dram_tensor(name, list(shape), dtype, kind="ExternalInput").ap()

    def dout(self, name, shape, dtype):
        return self.nc.dram_tensor(name, list(shape), dtype, kind="ExternalOutput").ap()

    def close(self):
        self.stack.close()



def build_l0_qkv():
    p = Prog("l0qkv")
    nc = p.nc
    x_h = p.din("x0i", [128, NCH * T])
    wq_h = p.din("wqi", [8, 128, NCH * 512])
    wk_h = p.din("wki", [8, 128, NCH * 512])
    wv_h = p.din("wvi", [8, 128, NCH * 512])
    cos_h = p.din("cosi", [64, T])
    sin_h = p.din("sini", [64, T])
    ones_h = p.din("onesi", [128, 128])
    qo_h = p.dout("qo", [128, NCH * T], BF16)
    ko_h = p.dout("ko", [128, NCH * T], BF16)
    vo_h = p.dout("vo", [128, (T // 128) * D], BF16)
    qn_h = p.dout("qn", [128, H0], F32)
    kn_h = p.dout("kn", [128, H0], F32)

    X = p.sbuf("X", [128, NCH * T], BF16)
    WB = [p.sbuf(f"WB{i}", [128, NCH * 512], BF16) for i in range(2)]
    QB = [p.sbuf(f"QB{i}", [128, 4 * T], BF16) for i in range(2)]
    VB = [p.sbuf(f"VB{i}", [128, 8 * 512], BF16) for i in range(2)]
    CO = p.sbuf("CO", [64, T], F32)
    SI = p.sbuf("SI", [64, T], F32)
    SC = [p.sbuf(f"SC{i}", [32, T], F32) for i in range(4)]
    ONES = p.sbuf("ONES", [128, 128], BF16)
    SQB = [p.sbuf(f"SQB{i}", [128, 512], BF16) for i in range(2)]
    RMX = p.sbuf("RMX", [128, 2], F32)
    QNB = p.sbuf("QNB", [128, H0], F32)
    KNB = p.sbuf("KNB", [128, H0], F32)
    FLU = p.sbuf("FLU", [1, 4], F32)
    PS = p.psum("PS", [128, 8 * 512], F32)

    wd = p.sem("wd")
    pe = p.sem("pe")
    ev = p.sem("ev")
    rp = p.sem("rp")
    so = p.sem("so")
    dv = p.sem("dv")
    sq = p.sem("sq")
    pn = p.sem("pn")
    dn = p.sem("dn")

    IN_DMAS = 4
    def wd_after(i):
        return 16 * (IN_DMAS + i + 1)

    def pe_qk(g, half, mc):
        return g * 8 + half * 4 + mc + 1

    def pe_v(mb, kt):
        return 16 * 8 + mb * 8 + kt + 1

    ev_qk = pe_qk
    def ev_v(mb, kt):
        return 16 * 8 + mb * 8 + kt + 1

    def sq_u(g, u):
        return g * 8 + (u + 1) * 2

    def pn_u(g, u):
        return g * 4 + u + 1

    def dn_u(g, u):
        return g * 4 + u + 1

    def rp_g(g):
        return g + 1

    def so_qk(g):
        return 16 * (g + 1)

    def so_v(mb):
        return 16 * (16 + mb + 1) + 32

    with nc.Block() as blk:
        @blk.gpsimd
        def _(g):
            g.dma_start(out=X[:, :], in_=x_h).then_inc(wd.h, 16)
            g.dma_start(out=CO[:, :], in_=cos_h).then_inc(wd.h, 16)
            g.dma_start(out=SI[:, :], in_=sin_h).then_inc(wd.h, 16)
            g.dma_start(out=ONES[:, :], in_=ones_h).then_inc(wd.h, 16)
            srcs = [wq_h, wk_h]
            for i in range(24):
                if i >= 2:
                    if i - 2 < 16:
                        g.wait_ge(pe.h, pe_qk(i - 2, 1, 3))
                    else:
                        g.wait_ge(pe.h, pe_v(i - 2 - 16, 7))
                if i < 16:
                    src = srcs[i // 8][i % 8]
                else:
                    src = wv_h[i - 16]
                g.dma_start(out=WB[i % 2][:, :], in_=src).then_inc(wd.h, 16)

        @blk.tensor
        def _(t):
            t.wait_ge(wd.h, 16 * IN_DMAS)
            for gq in range(16):
                t.wait_ge(wd.h, wd_after(gq))
                for half in range(2):
                    for mc in range(4):
                        piece = gq * 8 + half * 4 + mc
                        bank = piece % 6
                        if piece - 6 >= 0:
                            t.wait_ge(ev.h, piece - 6 + 1)
                        pv = PS[:, bank * 512:(bank + 1) * 512]
                        for kc in range(NCH):
                            ins = t.matmul(
                                pv,
                                WB[gq % 2][:, kc * 512 + mc * 128: kc * 512 + (mc + 1) * 128],
                                X[:, kc * T + half * 512: kc * T + half * 512 + 512],
                                start=(kc == 0), stop=(kc == NCH - 1))
                        ins.then_inc(pe.h, 1)
                for u in range(4):
                    par = u % 2
                    if gq * 4 + u - 2 >= 0:
                        t.wait_ge(dn.h, gq * 4 + u - 2 + 1)
                    npv = PS[:, 3072 + par * 512: 3072 + par * 512 + 512]
                    for c in range(2):
                        t.wait_ge(sq.h, sq_u(gq, u) - 1 + c)
                        ins = t.matmul(npv, ONES[:, :], SQB[c][:, :],
                                       start=(c == 0), stop=(c == 1))
                    ins.then_inc(pn.h, 1)
            for mb in range(8):
                t.wait_ge(wd.h, wd_after(16 + mb))
                if mb > 0:
                    t.wait_ge(ev.h, ev_v(mb - 1, 7))
                else:
                    t.wait_ge(ev.h, pe_qk(15, 1, 3))
                    t.wait_ge(dn.h, 64)
                for kc in range(NCH):
                    for kt in range(8):
                        pv = PS[:, kt * 512:(kt + 1) * 512]
                        ins = t.matmul(
                            pv,
                            X[:, kc * T + kt * 128: kc * T + (kt + 1) * 128],
                            WB[(16 + mb) % 2][:, kc * 512:(kc + 1) * 512],
                            start=(kc == 0), stop=(kc == NCH - 1))
                        if kc == NCH - 1:
                            ins.then_inc(pe.h, 1)

        @blk.scalar
        def _(s):
            for gq in range(16):
                for half in range(2):
                    for mc in range(4):
                        piece = gq * 8 + half * 4 + mc
                        bank = piece % 6
                        s.wait_ge(pe.h, pe_qk(gq, half, mc))
                        if gq >= 2:
                            s.wait_ge(so.h, so_qk(gq - 2))
                        pv = PS[:, bank * 512:(bank + 1) * 512]
                        s.copy(
                            out=QB[gq % 2][:, mc * T + half * 512: mc * T + half * 512 + 512],
                            in_=pv).then_inc(ev.h, 1)
                s.wait_ge(rp.h, gq + 1)
                for u in range(4):
                    hl, half = u // 2, u % 2
                    guni = gq * 4 + u
                    if guni >= 1:
                        s.wait_ge(pn.h, guni)
                    for c in range(2):
                        colc = (2 * hl + c) * T + half * 512
                        s.activation(SQB[c][:, :], QB[gq % 2][:, colc:colc + 512],
                                     AF.Square).then_inc(sq.h, 1)
            for mb in range(8):
                for kt in range(8):
                    s.wait_ge(pe.h, pe_v(mb, kt))
                    if mb >= 2:
                        s.wait_ge(so.h, so_v(mb - 2))
                    pv = PS[:, kt * 512:(kt + 1) * 512]
                    s.copy(out=VB[mb % 2][:, kt * 512:(kt + 1) * 512],
                           in_=pv).then_inc(ev.h, 1)

        @blk.vector
        def _(v):
            ndv = 0
            for gq in range(16):
                v.wait_ge(ev.h, ev_qk(gq, 1, 3))
                qb = QB[gq % 2]
                for h in range(2):
                    base = h * 2 * T
                    E = qb[0:32, base:base + T]
                    O = qb[32:64, base:base + T]
                    v.tensor_tensor(SC[0][:, :], E, CO[0:32, :], op=ALU.mult)
                    v.tensor_tensor(SC[1][:, :], E, SI[0:32, :], op=ALU.mult)
                    v.tensor_tensor(SC[2][:, :], O, CO[32:64, :], op=ALU.mult)
                    ins = v.tensor_tensor(SC[3][:, :], O, SI[32:64, :], op=ALU.mult)
                    ins.then_inc(dv.h, 1)
                    ndv += 1
                    v.wait_ge(dv.h, ndv)
                    v.tensor_tensor(E, SC[0][:, :], SC[3][:, :], op=ALU.subtract)
                    ins = v.tensor_tensor(O, SC[2][:, :], SC[1][:, :], op=ALU.add)
                    ins.then_inc(dv.h, 1)
                    ndv += 1
                    v.wait_ge(dv.h, ndv)
                v.tensor_copy(FLU[0:1, 0:2], FLU[0:1, 2:4]).then_inc(dv.h, 1)
                ndv += 1
                v.wait_ge(dv.h, ndv)
                v.nop().then_inc(rp.h, 1)
                HMX = QNB if gq < 8 else KNB
                for u in range(4):
                    hl, half = u // 2, u % 2
                    par = u % 2
                    head_col = (gq % 8) * 2 + hl
                    v.wait_ge(pn.h, gq * 4 + u + 1)
                    npv = PS[:, 3072 + par * 512: 3072 + par * 512 + 512]
                    ins = v.reduce_max(RMX[:, par:par + 1], npv, axis=mybir.AxisListType.X)
                    ins.then_inc(dv.h, 1)
                    ndv += 1
                    v.wait_ge(dv.h, ndv)
                    if half == 0:
                        ins = v.tensor_copy(HMX[:, head_col:head_col + 1],
                                            RMX[:, par:par + 1])
                    else:
                        ins = v.tensor_tensor(HMX[:, head_col:head_col + 1],
                                              RMX[:, par:par + 1],
                                              HMX[:, head_col:head_col + 1], op=ALU.max)
                    ins.then_inc(dv.h, 1)
                    ndv += 1
                    v.wait_ge(dv.h, ndv)
                    v.tensor_copy(FLU[0:1, 0:2], FLU[0:1, 2:4]).then_inc(dv.h, 1)
                    ndv += 1
                    v.wait_ge(dv.h, ndv)
                    v.nop().then_inc(dn.h, 1)

        @blk.sync
        def _(sy):
            for gq in range(16):
                sy.wait_ge(rp.h, rp_g(gq))
                dst = qo_h if gq < 8 else ko_h
                mg = gq % 8
                sy.dma_start(out=dst[:, mg * 4 * T:(mg + 1) * 4 * T],
                             in_=QB[gq % 2][:, :]).then_inc(so.h, 16)
            sy.wait_ge(dn.h, 32)
            sy.dma_start(out=qn_h, in_=QNB[:, :]).then_inc(so.h, 16)
            sy.wait_ge(dn.h, 64)
            sy.dma_start(out=kn_h, in_=KNB[:, :]).then_inc(so.h, 16)
            for mb in range(8):
                sy.wait_ge(ev.h, ev_v(mb, 7))
                src = VB[mb % 2][:, :].rearrange("p (kt m) -> p kt m", m=512)
                dst = vo_h.rearrange("p (kt c) -> p kt c", c=D)[:, :, mb * 512:(mb + 1) * 512]
                sy.dma_start(out=dst, in_=src).then_inc(so.h, 16)
            sy.wait_ge(so.h, 16 * 26)

    return p





NSLOT = 48
QB_START = (0, 16)
QB_END = (15, 47)


def build_l0_attn():
    p = Prog("l0attn")
    nc = p.nc
    x_h = p.din("x0i", [128, NCH * T])
    q_h = p.din("qi", [128, NCH * T], BF16)
    ksl_h = p.din("ksl", [128, NCH * NSLOT * 128], BF16)
    vsl_h = p.din("vsl", [128, NSLOT * D], BF16)
    mk_h = p.din("mki", [128, NSLOT * 512], mybir.dt.uint8)
    qn_h = p.din("qn4", [128, 4 * H0])
    kn_h = p.din("kn4", [128, 4 * H0])
    ones_h = p.din("onesi", [128, 128])
    wo_h = p.din("woi", [16, 128, NCH * 256])
    yo_h = p.dout("yo", [128, NCH * T], BF16)
    mu_h = p.dout("muo", [128, T], F32)
    rs_h = p.dout("rso", [128, T], F32)

    AB = p.sbuf("AB", [128, NCH * T], BF16)
    KHB = [p.sbuf(f"KHB{i}", [128, 2 * 12 * 128], BF16) for i in range(2)]
    VHB = [p.sbuf(f"VHB{i}", [128, 12 * 256], BF16) for i in range(2)]
    QHB = [p.sbuf(f"QHB{i}", [128, 2 * T], BF16) for i in range(2)]
    MK = p.sbuf("MK", [128, NSLOT * 512], mybir.dt.uint8)
    PT = [p.sbuf(f"PT{i}", [128, 512], BF16) for i in range(2)]
    QN4 = p.sbuf("QN4", [128, 4 * H0], F32)
    KN4 = p.sbuf("KN4", [128, 4 * H0], F32)
    QMX = p.sbuf("QMX", [128, H0], F32)
    KMX = p.sbuf("KMX", [128, H0], F32)
    MBALL = p.sbuf("MBALL", [128, H0], F32)
    ONES = p.sbuf("ONES", [128, 128], BF16)
    RL = p.sbuf("RL", [128, 512], F32)
    WOB = [p.sbuf(f"WOB{i}", [128, NCH * 256], BF16) for i in range(2)]
    X0C = [p.sbuf(f"X0C{i}", [128, T], BF16) for i in range(2)]
    YOB = [p.sbuf(f"YOB{i}", [128, 2 * T], BF16) for i in range(2)]
    SQY = [p.sbuf(f"SQY{i}", [128, 512], BF16) for i in range(2)]
    MU = p.sbuf("MU", [128, T], F32)
    MSQ = p.sbuf("MSQ", [128, T], F32)
    VARB = p.sbuf("VARB", [128, T], F32)
    SDB = p.sbuf("SDB", [128, T], F32)
    RSTD = p.sbuf("RSTD", [128, T], F32)
    FLU = p.sbuf("FLU", [1, 4], F32)
    PS = p.psum("PS", [128, 8 * 512], F32)

    wd = p.sem("wd")
    qd = p.sem("qd")
    st = p.sem("st")
    ae = p.sem("ae")
    pm = p.sem("pm")
    pa = p.sem("pa")
    ab = p.sem("ab")
    dv = p.sem("dv")
    mbd = p.sem("mbd")
    mba = p.sem("mba")
    wp = p.sem("wp")
    ey = p.sem("ey")
    es = p.sem("es")
    sp = p.sem("sp")
    fd = p.sem("fd")
    fa = p.sem("fa")
    so = p.sem("so")

    NPC = 4
    NMG = 16

    def qd_group(h, gr):
        return 16 * (13 * h + 1 + 3 * (gr + 1))

    def wd_w(mg):
        return 16 * (4 + 3 * mg + 1)

    def wd_x0(mg):
        return 16 * (4 + 3 * mg + 3)

    with nc.Block() as blk:
        @blk.gpsimd
        def _(g):
            g.dma_start(out=ONES[:, :], in_=ones_h).then_inc(wd.h, 16)
            g.dma_start(out=QN4[:, :], in_=qn_h).then_inc(wd.h, 16)
            g.dma_start(out=KN4[:, :], in_=kn_h).then_inc(wd.h, 16)
            g.dma_start(out=MK[:, :], in_=mk_h).then_inc(wd.h, 16)
            for h in range(H0):
                for u in range(NSLOT):
                    su = h * NSLOT + u
                    g.wait_ge(ae.h, su + 1)
                    g.tensor_tensor(PT[u % 2][:, :], PT[u % 2][:, :],
                                    MK[:, u * 512:(u + 1) * 512],
                                    op=ALU.mult).then_inc(pm.h, 1)
            for mg in range(NMG):
                if mg >= 2:
                    g.wait_ge(wp.h, (mg - 2) * NPC + NPC)
                g.dma_start(out=WOB[mg % 2][:, :], in_=wo_h[mg]).then_inc(wd.h, 16)
                for mc in range(2):
                    if mg >= 1:
                        g.wait_ge(ey.h, (mg - 1) * NPC + NPC)
                    g.dma_start(out=X0C[mc][:, :],
                                in_=x_h[:, (2 * mg + mc) * T:(2 * mg + mc + 1) * T]
                                ).then_inc(wd.h, 16)

        @blk.sync
        def _(sy):
            for h in range(H0):
                if h >= 2:
                    sy.wait_ge(pa.h, 48 * (h - 1))
                sy.dma_start(out=QHB[h % 2][:, :],
                             in_=q_h[:, 2 * h * T:(2 * h + 2) * T]).then_inc(qd.h, 16)
                for gr in range(4):
                    gidx = 4 * h + gr
                    if gidx >= 2:
                        hh, g2 = (gidx - 2) // 4, (gidx - 2) % 4
                        sy.wait_ge(pa.h, hh * 48 + g2 * 12 + 12)
                    for c in range(2):
                        sy.dma_start(
                            out=KHB[gidx % 2][:, c * 1536:(c + 1) * 1536],
                            in_=ksl_h[:, (2 * h + c) * (NSLOT * 128) + gr * 1536:
                                      (2 * h + c) * (NSLOT * 128) + (gr + 1) * 1536]
                        ).then_inc(qd.h, 16)
                    vs = vsl_h.rearrange("p (s c) -> p s c", c=D)
                    sy.dma_start(
                        out=VHB[gidx % 2][:, :].rearrange("p (s c) -> p s c", c=256),
                        in_=vs[:, gr * 12:(gr + 1) * 12, h * 256:(h + 1) * 256]
                    ).then_inc(qd.h, 16)
            for mg in range(NMG):
                sy.wait_ge(ey.h, mg * NPC + NPC)
                sy.dma_start(out=yo_h[:, 2 * mg * T:(2 * mg + 2) * T],
                             in_=YOB[mg % 2][:, :]).then_inc(so.h, 16)
            sy.wait_ge(fd.h, 3)
            sy.dma_start(out=mu_h, in_=MU[:, :]).then_inc(so.h, 16)
            sy.dma_start(out=rs_h, in_=RSTD[:, :]).then_inc(so.h, 16)
            sy.wait_ge(so.h, 16 * (NMG + 2))

        @blk.vector
        def _(v):
            ndv = 0

            def ms(semh, k=1):
                nonlocal ndv
                v.tensor_copy(FLU[0:1, 0:2], FLU[0:1, 2:4]).then_inc(dv.h, 1)
                ndv += 1
                v.wait_ge(dv.h, ndv)
                v.nop().then_inc(semh, k)

            v.wait_ge(wd.h, 16 * 3)
            for dst, srcb in ((QMX, QN4), (KMX, KN4)):
                v.tensor_copy(dst[:, :], srcb[:, 0:H0]).then_inc(dv.h, 1)
                ndv += 1
                v.wait_ge(dv.h, ndv)
                for j in range(1, 4):
                    v.tensor_tensor(dst[:, :], srcb[:, j * H0:(j + 1) * H0],
                                    dst[:, :], op=ALU.max).then_inc(dv.h, 1)
                    ndv += 1
                    v.wait_ge(dv.h, ndv)
            v.tensor_tensor(MSQ[:, 0:H0], QMX[:, :], KMX[:, :],
                            op=ALU.mult).then_inc(dv.h, 1)
            ndv += 1
            v.wait_ge(dv.h, ndv)
            ms(mbd.h, 1)
            v.wait_ge(mba.h, 1)
            v.tensor_scalar(MBALL[:, :], SDB[:, 0:H0], -1.01 / 16.0, None,
                            op0=ALU.mult).then_inc(dv.h, 1)
            ndv += 1
            v.wait_ge(dv.h, ndv)
            ms(mbd.h, 1)
            for h in range(H0):
                for qb in range(2):
                    v.wait_ge(pa.h, h * NSLOT + QB_END[qb] + 1)
                    v.reciprocal(RL[:, :], PS[:, 2048:2560]).then_inc(dv.h, 1)
                    ndv += 1
                    v.wait_ge(dv.h, ndv)
                    for c in range(2):
                        v.tensor_tensor(
                            AB[:, (2 * h + c) * T + qb * 512:(2 * h + c) * T + qb * 512 + 512],
                            PS[:, 1024 + c * 512:1024 + (c + 1) * 512],
                            RL[:, :], op=ALU.mult).then_inc(dv.h, 1)
                        ndv += 1
                        v.wait_ge(dv.h, ndv)
                    ms(ab.h, 1)
            for mg in range(NMG):
                if mg >= 2:
                    v.wait_ge(so.h, 16 * (mg - 1))
                for half in range(2):
                    for mc in range(2):
                        pc = mg * NPC + half * 2 + mc
                        v.wait_ge(wp.h, pc + 1)
                        v.wait_ge(wd.h, wd_x0(mg))
                        ins = v.tensor_tensor(
                            YOB[mg % 2][:, mc * T + half * 512: mc * T + half * 512 + 512],
                            PS[:, mc * 512:(mc + 1) * 512],
                            X0C[mc][:, half * 512:half * 512 + 512],
                            op=ALU.add)
                        ins.then_inc(dv.h, 1)
                        ndv += 1
                        v.wait_ge(dv.h, ndv)
                        ms(ey.h, 1)
            ms(ey.h, 1)
            v.wait_ge(sp.h, NMG * NPC)
            v.tensor_scalar(MU[:, :], PS[:, 1024:2048], 1.0 / D, None,
                            op0=ALU.mult).then_inc(dv.h, 1)
            ndv += 1
            v.wait_ge(dv.h, ndv)
            v.tensor_tensor(MSQ[:, :], MU[:, :], MU[:, :], op=ALU.mult).then_inc(dv.h, 1)
            ndv += 1
            v.wait_ge(dv.h, ndv)
            v.scalar_tensor_tensor(VARB[:, :], PS[:, 2048:3072], 1.0 / D,
                                   MSQ[:, :], op0=ALU.mult,
                                   op1=ALU.subtract).then_inc(dv.h, 1)
            ndv += 1
            v.wait_ge(dv.h, ndv)
            v.tensor_scalar(VARB[:, :], VARB[:, :], EPS, None,
                            op0=ALU.add).then_inc(dv.h, 1)
            ndv += 1
            v.wait_ge(dv.h, ndv)
            ms(fd.h, 1)
            v.wait_ge(fa.h, 2)
            v.reciprocal(RSTD[:, :], SDB[:, :]).then_inc(dv.h, 1)
            ndv += 1
            v.wait_ge(dv.h, ndv)
            ms(fd.h, 2)

        @blk.scalar
        def _(s):
            s.wait_ge(mbd.h, 1)
            s.activation(SDB[:, 0:H0], MSQ[:, 0:H0], AF.Sqrt).then_inc(mba.h, 1)
            for h in range(H0):
                for u in range(NSLOT):
                    su = h * NSLOT + u
                    s.wait_ge(st.h, su + 1)
                    if su >= 2:
                        s.wait_ge(pa.h, su - 1)
                    if su == 0:
                        s.wait_ge(mbd.h, 2)
                    s.activation(PT[u % 2][:, :], PS[:, (u % 2) * 512:(u % 2) * 512 + 512],
                                 AF.Exp, bias=MBALL[:, h:h + 1],
                                 scale=1.0 / 16.0).then_inc(ae.h, 1)
            for mg in range(NMG):
                for half in range(2):
                    for mc in range(2):
                        pc = mg * NPC + half * 2 + mc
                        s.wait_ge(ey.h, pc + 2)
                        if pc >= 2:
                            s.wait_ge(sp.h, pc - 1)
                        s.activation(SQY[pc % 2][:, :],
                                     YOB[mg % 2][:, mc * T + half * 512: mc * T + half * 512 + 512],
                                     AF.Square).then_inc(es.h, 1)
            s.wait_ge(fd.h, 1)
            s.activation(SDB[:, :], VARB[:, :], AF.Sqrt).then_inc(fa.h, 2)

        @blk.tensor
        def _(t):
            for h in range(H0):
                for u in range(NSLOT):
                    su = h * NSLOT + u
                    qb = 0 if u < 16 else 1
                    qoff = 0 if qb == 0 else 512
                    gr, sl = u // 12, u % 12
                    gb = (4 * h + gr) % 2
                    t.wait_ge(qd.h, qd_group(h, gr))
                    if su >= 2:
                        t.wait_ge(ae.h, su - 1)
                    stp = PS[:, (su % 2) * 512:(su % 2) * 512 + 512]
                    for c in range(2):
                        ins = t.matmul(stp,
                                       KHB[gb][:, c * 1536 + sl * 128: c * 1536 + (sl + 1) * 128],
                                       QHB[h % 2][:, c * T + qoff: c * T + qoff + 512],
                                       start=(c == 0), stop=(c == 1))
                    ins.then_inc(st.h, 1)
                    t.wait_ge(pm.h, su + 1)
                    if u == QB_START[qb]:
                        if h * 2 + qb >= 1:
                            t.wait_ge(ab.h, h * 2 + qb)
                    first = (u == QB_START[qb])
                    last = (u == QB_END[qb])
                    for c in range(2):
                        t.matmul(PS[:, 1024 + c * 512:1024 + (c + 1) * 512],
                                 VHB[gb][:, sl * 256 + c * 128: sl * 256 + (c + 1) * 128],
                                 PT[u % 2][:, :], start=first, stop=last)
                    ins = t.matmul(PS[:, 2048:2560], ONES[:, :], PT[u % 2][:, :],
                                   start=first, stop=last)
                    ins.then_inc(pa.h, 1)
            t.wait_ge(ae.h, H0 * NSLOT)
            t.wait_ge(ab.h, H0 * 2)

            def l2_stats(pc2):
                mg2, half2, mc2 = pc2 // NPC, (pc2 % NPC) // 2, pc2 % 2
                t.wait_ge(ey.h, pc2 + 1)
                t.wait_ge(es.h, pc2 + 1)
                sy_start = (mg2 == 0 and mc2 == 0)
                sy_stop = (mg2 == NMG - 1 and mc2 == 1)
                t.matmul(PS[:, 1024 + half2 * 512:1024 + (half2 + 1) * 512],
                         ONES[:, :],
                         YOB[mg2 % 2][:, mc2 * T + half2 * 512: mc2 * T + half2 * 512 + 512],
                         start=sy_start, stop=sy_stop)
                ins2 = t.matmul(PS[:, 2048 + half2 * 512:2048 + (half2 + 1) * 512],
                                ONES[:, :], SQY[pc2 % 2][:, :],
                                start=sy_start, stop=sy_stop)
                ins2.then_inc(sp.h, 1)

            for mg in range(NMG):
                t.wait_ge(wd.h, wd_w(mg))
                for half in range(2):
                    for mc in range(2):
                        pc = mg * NPC + half * 2 + mc
                        if pc >= 2:
                            t.wait_ge(ey.h, pc - 1)
                        pj = PS[:, mc * 512:(mc + 1) * 512]
                        for kc in range(NCH):
                            ins = t.matmul(
                                pj,
                                WOB[mg % 2][:, kc * 256 + mc * 128: kc * 256 + (mc + 1) * 128],
                                AB[:, kc * T + half * 512: kc * T + half * 512 + 512],
                                start=(kc == 0), stop=(kc == NCH - 1))
                        ins.then_inc(wp.h, 1)
                        if pc >= 2:
                            l2_stats(pc - 2)
            l2_stats(NMG * NPC - 2)
            l2_stats(NMG * NPC - 1)

    return p



def _prep_static(inputs):
    s = {}
    perm = l0_head_perm()
    s["wq_img"] = w_image(np.ascontiguousarray(inputs["L0_attn_w_q"][perm].T))
    s["wk_img"] = w_image(np.ascontiguousarray(inputs["L0_attn_w_k"][perm].T))
    s["wv_img"] = w_image(np.ascontiguousarray(inputs["L0_attn_w_v"].T))
    s["wo_img"] = w_image(np.ascontiguousarray(inputs["L0_attn_w_out"].T), 256)

    rows = []
    for fc in range(22):
        for st in range(4):
            for pp in range(128):
                H = st * FOLDH + fc * 128 + pp
                rows.append(H if (fc < 21 or pp < 64) else -1)
    rows = np.array(rows)
    msk = rows >= 0

    def arr_rows(w):
        out = np.zeros((NSTR * 128, w.shape[1]), np.float32)
        out[msk] = np.asarray(w, np.float32)[rows[msk]]
        return out

    w1a = arr_rows(inputs["L0_ffn_w1q"]).T
    w3a = arr_rows(inputs["L0_ffn_w3q"]).T
    s["w1_img"] = np.ascontiguousarray(np.stack(
        [w_image(np.ascontiguousarray(w1a[:, i * 128:(i + 1) * 128]), 128)[0]
         for i in range(NSTR)]))
    s["w3_img"] = np.ascontiguousarray(np.stack(
        [w_image(np.ascontiguousarray(w3a[:, i * 128:(i + 1) * 128]), 128)[0]
         for i in range(NSTR)]))
    s["b1_arr"] = np.ascontiguousarray(arr_rows(inputs["L0_ffn_b1"]).T)
    s["b3_arr"] = np.ascontiguousarray(arr_rows(inputs["L0_ffn_b3"]).T)
    a2f = np.asarray(inputs["L0_ffn_a2"], np.float32)
    a2a = np.zeros((NSTR * 128, 2), np.float32)
    a2a[msk] = a2f.T[rows[msk]]
    s["a2_img"] = np.ascontiguousarray(w_image(a2a, 2)[0])
    s["a1_img"] = np.ascontiguousarray(
        w_image(np.ascontiguousarray(np.asarray(inputs["L0_ffn_a1"], np.float32).T), 2)[0])
    s["a3_img"] = np.ascontiguousarray(
        w_image(np.ascontiguousarray(np.asarray(inputs["L0_ffn_a3"], np.float32).T), 2)[0])
    w2T = np.asarray(inputs["L0_ffn_w2q"], np.float32).T
    w2p = np.zeros((22 * 128, D), np.float32)
    w2p[:FOLDH] = w2T
    s["w2_img"] = w_image(w2p, 256)
    s["b2_arr"] = np.ascontiguousarray(np.asarray(inputs["L0_ffn_b2"], np.float32).T)

    s["wq1_img"] = w_image(np.ascontiguousarray(np.asarray(inputs["L1_attn_wq"], np.float32).T), 256)
    s["wk1_img"] = w_image(np.ascontiguousarray(np.asarray(inputs["L1_attn_wk"], np.float32).T), 256)
    s["wv1_img"] = w_image(np.ascontiguousarray(np.asarray(inputs["L1_attn_wv"], np.float32).T), 256)
    s["wo1_img"] = w_image(np.ascontiguousarray(np.asarray(inputs["L1_attn_wo"], np.float32).T), 256)

    s["worg_img"] = w_image(np.ascontiguousarray(np.asarray(inputs["L1_ffn_W_org"], np.float32).T), 256)
    dwn = np.asarray(inputs["L1_ffn_lora_down"], np.float32)
    s["down_img"] = np.ascontiguousarray(np.stack(
        [w_image(np.ascontiguousarray(dwn[e]), 64)[0] for e in range(8)]))
    up = np.asarray(inputs["L1_ffn_lora_up"], np.float32)
    s["upp_arr"] = np.ascontiguousarray(np.stack(
        [np.concatenate([up[2 * pp], up[2 * pp + 1]], axis=0) for pp in range(4)]))
    s["keys_arr"] = np.ascontiguousarray(np.asarray(inputs["L1_ffn_lora_keys"], np.float32).T)
    s["wq2_img"] = np.ascontiguousarray(
        w_image(np.ascontiguousarray(np.asarray(inputs["L1_ffn_Wq"], np.float32).T), 128)[0])
    return s


def _shard_tokens(core):
    b, g = core // 4, core % 4
    t1 = np.arange(g * 512, (g + 1) * 512)
    t2 = np.arange((7 - g) * 512, (8 - g) * 512)
    return b, np.concatenate([t1, t2])


def mk_core_data(inputs):
    x0 = np.asarray(inputs["L0_attn_hidden_states"], dtype=np.float32)
    core_data = []
    for c in range(NCORES):
        b, toks = _shard_tokens(c)
        xT = np.ascontiguousarray(x0[b][toks].T)
        core_data.append({"x0i": to_image(xT.astype(np.float32)), "toks": toks, "b": b})
    return core_data


def run_l0_qkv(inputs, static, core_data):
    p = build_l0_qkv()
    in_maps = []
    for c in range(NCORES):
        toks = core_data[c]["toks"]
        cos, sin = rope_tables(toks)
        cos2 = np.ascontiguousarray(np.vstack([cos, cos]))
        sin2 = np.ascontiguousarray(np.vstack([sin, sin]))
        in_maps.append({
            "x0i": core_data[c]["x0i"],
            "wqi": static["wq_img"], "wki": static["wk_img"], "wvi": static["wv_img"],
            "cosi": cos2, "sini": sin2,
            "onesi": np.ones((128, 128), np.float32),
        })
    res = run_bass_kernel_spmd(p.nc, in_maps, core_ids=list(range(NCORES)))
    p.close()
    return res.results





NSTR = 88
DBG_L5 = False


def build_l0_ffn(stage=4):
    p = Prog("l0ffn")
    nc = p.nc
    y_h = p.din("yi", [128, NCH * T], BF16)
    mu_h = p.din("mui", [128, T])
    rs_h = p.din("rsi", [128, T])
    w1_h = p.din("w1i", [NSTR, 128, 1024])
    w3_h = p.din("w3i", [NSTR, 128, 1024])
    w2_h = p.din("w2i", [16, 128, 22 * 256])
    a1_h = p.din("a1i", [128, 64])
    a3_h = p.din("a3i", [128, 64])
    b1_h = p.din("b1i", [2, NSTR * 128])
    b3_h = p.din("b3i", [2, NSTR * 128])
    a2_h = p.din("a2i", [128, NSTR * 2])
    b2_h = p.din("b2i", [2, D])
    wq1_h = p.din("wq1i", [16, 128, NCH * 256])
    wk1_h = p.din("wk1i", [4, 128, NCH * 256])
    wv1_h = p.din("wv1i", [4, 128, NCH * 256])
    ones_h = p.din("onesi", [128, 128])
    x2_h = p.dout("x2o", [128, NCH * T], BF16)
    if stage >= 4:
        q1_h = p.dout("q1o", [128, NCH * T], BF16)
        k1_h = p.dout("k1o", [128, 8 * T], BF16)
        v1_h = p.dout("v1o", [128, 8 * T], BF16)
        qn_h = p.dout("q1n", [128, 32], F32)
        kn_h = p.dout("k1n", [128, 8], F32)

    X = p.sbuf("X", [128, NCH * T], BF16)
    ONES = p.sbuf("ONES", [128, 128], BF16)
    FLU = p.sbuf("FLU", [1, 4], F32)
    PS = p.psum("PS", [128, 8 * 512], F32)

    wd = p.sem("wd"); qd = p.sem("qd"); nl = p.sem("nl"); dv = p.sem("dv")
    xf = p.sem("xf"); pl = p.sem("pl"); el = p.sem("el"); pg = p.sem("pg")
    sl = p.sem("sl"); dt = p.sem("dt"); hc = p.sem("hc"); t2 = p.sem("t2")
    e2 = p.sem("e2"); wp = p.sem("wp"); ey = p.sem("ey"); es = p.sem("es")
    sp = p.sem("sp"); fd = p.sem("fd"); fa = p.sem("fa"); n2 = p.sem("n2")
    pe = p.sem("pe"); ev = p.sem("ev"); sq = p.sem("sq"); pn = p.sem("pn")
    dn = p.sem("dn"); so = p.sem("so")

    def wd_stripe(i):
        fc = i // 4
        return 16 * (4 + 2 * (fc + 1) + 2 * (i + 1))

    def wd_w2(mg):
        return 16 * (4 + 2 * 22 + 2 * NSTR + 2 * (mg + 1))

    WD_B0 = 4 + 2 * 22 + 2 * NSTR + 2 * 16

    def wd_wq1(i):
        return 16 * (WD_B0 + i + 1)

    def qd_y(c):
        return 16 * (2 + c + 1)

    def pe_after(i):
        return (i + 1) * 4 if i < 20 else 80 + (i - 19) * 8

    def ev_v(mb):
        return 80 + 8 * (mb + 1)

    def so_qkv(i):
        return 16 * (NCH + i + 1)

    NPC = 4
    NDV = [0]

    import contextlib
    sA = contextlib.ExitStack()
    HF = sA.enter_context(nc.sbuf_tensor("HF", [128, 22 * T], BF16))
    XF = sA.enter_context(nc.sbuf_tensor("XF", [128, 8 * T], BF16))
    YC = sA.enter_context(nc.sbuf_tensor("YC", [128, T], BF16))
    MUB = sA.enter_context(nc.sbuf_tensor("MUB", [128, T], F32))
    RSB = sA.enter_context(nc.sbuf_tensor("RSB", [128, T], F32))
    SCR = sA.enter_context(nc.sbuf_tensor("SCR", [128, T], F32))
    WF1 = [sA.enter_context(nc.sbuf_tensor(f"WF1{i}", [128, 1024], BF16)) for i in range(2)]
    WF3 = [sA.enter_context(nc.sbuf_tensor(f"WF3{i}", [128, 1024], BF16)) for i in range(2)]
    W2B = [sA.enter_context(nc.sbuf_tensor(f"W2B{i}", [128, 22 * 256], BF16)) for i in range(2)]
    A1 = sA.enter_context(nc.sbuf_tensor("A1", [128, 64], BF16))
    A3 = sA.enter_context(nc.sbuf_tensor("A3", [128, 64], BF16))
    B1F = [sA.enter_context(nc.sbuf_tensor(f"B1F{i}", [2, 512], BF16)) for i in range(2)]
    B3F = [sA.enter_context(nc.sbuf_tensor(f"B3F{i}", [2, 512], BF16)) for i in range(2)]
    A2 = sA.enter_context(nc.sbuf_tensor("A2", [128, NSTR * 2], BF16))
    B2F = [sA.enter_context(nc.sbuf_tensor(f"B2F{i}", [2, 256], BF16)) for i in range(2)]
    T1 = sA.enter_context(nc.sbuf_tensor("T1", [2, T], BF16))
    T3 = sA.enter_context(nc.sbuf_tensor("T3", [2, T], BF16))
    T2B = sA.enter_context(nc.sbuf_tensor("T2B", [2, T], BF16))
    SG = [sA.enter_context(nc.sbuf_tensor(f"SG{i}", [128, T], BF16)) for i in range(2)]
    TS = [sA.enter_context(nc.sbuf_tensor(f"TS{i}", [128, T], BF16)) for i in range(2)]
    SQY = [sA.enter_context(nc.sbuf_tensor(f"SQY{i}", [128, 512], BF16)) for i in range(2)]
    MSQ = sA.enter_context(nc.sbuf_tensor("MSQ", [128, T], F32))
    VARB = sA.enter_context(nc.sbuf_tensor("VARB", [128, T], F32))
    SDB = sA.enter_context(nc.sbuf_tensor("SDB", [128, T], F32))

    with nc.Block() as blk:
        @blk.gpsimd
        def _(g):
            for dst, srch in ((ONES, ones_h), (A1, a1_h), (A3, a3_h), (A2, a2_h)):
                g.dma_start(out=dst[:, :], in_=srch).then_inc(wd.h, 16)
            if stage < 2:
                return
            for fc in range(22):
                if fc >= 2:
                    g.wait_ge(pg.h, (fc - 2) * 4 + 4)
                g.dma_start(out=B1F[fc % 2][:, :],
                            in_=b1_h[:, fc * 512:(fc + 1) * 512]).then_inc(wd.h, 16)
                g.dma_start(out=B3F[fc % 2][:, :],
                            in_=b3_h[:, fc * 512:(fc + 1) * 512]).then_inc(wd.h, 16)
                for s_ in range(4):
                    i = fc * 4 + s_
                    if i >= 2:
                        g.wait_ge(pg.h, i - 1)
                    g.dma_start(out=WF1[i % 2][:, :], in_=w1_h[i]).then_inc(wd.h, 16)
                    g.dma_start(out=WF3[i % 2][:, :], in_=w3_h[i]).then_inc(wd.h, 16)
            if stage < 3:
                return
            for mg in range(16):
                if mg >= 2:
                    g.wait_ge(wp.h, (mg - 2) * NPC + NPC)
                g.dma_start(out=W2B[mg % 2][:, :], in_=w2_h[mg]).then_inc(wd.h, 16)
                g.dma_start(out=B2F[mg % 2][:, :],
                            in_=b2_h[:, mg * 256:(mg + 1) * 256]).then_inc(wd.h, 16)

        @blk.sync
        def _(sy):
            sy.dma_start(out=MUB[:, :], in_=mu_h).then_inc(qd.h, 16)
            sy.dma_start(out=RSB[:, :], in_=rs_h).then_inc(qd.h, 16)
            for c in range(NCH):
                if c >= 1:
                    sy.wait_ge(nl.h, c)
                sy.dma_start(out=YC[:, :],
                             in_=y_h[:, c * T:(c + 1) * T]).then_inc(qd.h, 16)
            for c in range(NCH):
                sy.wait_ge(n2.h, c + 1)
                sy.dma_start(out=x2_h[:, c * T:(c + 1) * T],
                             in_=X[:, c * T:(c + 1) * T]).then_inc(so.h, 16)

        @blk.vector
        def _(v):
            def ssync(ins):
                ins.then_inc(dv.h, 1)
                NDV[0] += 1
                v.wait_ge(dv.h, NDV[0])

            def ms(semh, k=1):
                ssync(v.tensor_copy(FLU[0:1, 0:2], FLU[0:1, 2:4]))
                v.nop().then_inc(semh, k)

            v.wait_ge(qd.h, 32)
            for c in range(NCH):
                v.wait_ge(qd.h, qd_y(c))
                ssync(v.tensor_tensor(SCR[:, :], YC[:, :], MUB[:, :],
                                      op=ALU.subtract))
                ssync(v.tensor_tensor(X[:, c * T:(c + 1) * T], SCR[:, :], RSB[:, :],
                                      op=ALU.mult))
                ms(nl.h, 1)
            if stage < 2:
                for c in range(NCH):
                    ms(n2.h, 1)
                return
            for j in range(8):
                ssync(v.tensor_tensor(SCR[:, :], X[:, j * T:(j + 1) * T],
                                      X[:, (8 + j) * T:(9 + j) * T], op=ALU.add))
                ssync(v.tensor_tensor(MSQ[:, :], X[:, (16 + j) * T:(17 + j) * T],
                                      X[:, (24 + j) * T:(25 + j) * T], op=ALU.add))
                ssync(v.tensor_tensor(XF[:, j * T:(j + 1) * T], SCR[:, :], MSQ[:, :],
                                      op=ALU.add))
                ms(xf.h, 1)
            for i in range(NSTR):
                fc = i // 4
                v.wait_ge(sl.h, i + 1)
                if i >= 2:
                    v.wait_ge(t2.h, i - 1)
                if i % 4 == 0:
                    ssync(v.tensor_tensor(HF[:, fc * T:(fc + 1) * T], SG[i % 2][:, :],
                                          PS[:, 1024:2048], op=ALU.mult))
                    ms(dt.h, 1)
                    ms(hc.h, 1)
                else:
                    ssync(v.tensor_tensor(TS[i % 2][:, :], SG[i % 2][:, :],
                                          PS[:, 1024:2048], op=ALU.mult))
                    ms(dt.h, 1)
                    v.wait_ge(t2.h, i)
                    ssync(v.tensor_tensor(HF[:, fc * T:(fc + 1) * T], TS[i % 2][:, :],
                                          HF[:, fc * T:(fc + 1) * T], op=ALU.add))
                    ms(hc.h, 1)
            if stage < 3:
                v.wait_ge(hc.h, NSTR)
                for c in range(NCH):
                    ms(n2.h, 1)
                return
            for mg in range(16):
                for mc in range(2):
                    ch = mg * 2 + mc
                    for half in range(2):
                        pc = mg * NPC + mc * 2 + half
                        v.wait_ge(wp.h, pc + 1)
                        col = ch * T + half * 512
                        ssync(v.tensor_tensor(X[:, col:col + 512],
                                              PS[:, (pc % 4) * 512:(pc % 4) * 512 + 512],
                                              X[:, col:col + 512], op=ALU.add))
                        ms(ey.h, 1)
            v.wait_ge(sp.h, 16 * NPC)
            ssync(v.tensor_scalar(MUB[:, :], PS[:, 2048:3072], 1.0 / D, None,
                                  op0=ALU.mult))
            ssync(v.tensor_tensor(MSQ[:, :], MUB[:, :], MUB[:, :], op=ALU.mult))
            ssync(v.scalar_tensor_tensor(VARB[:, :], PS[:, 3072:4096], 1.0 / D,
                                         MSQ[:, :], op0=ALU.mult, op1=ALU.subtract))
            ssync(v.tensor_scalar(VARB[:, :], VARB[:, :], EPS, None, op0=ALU.add))
            ms(fd.h, 1)
            v.wait_ge(fa.h, 1)
            ssync(v.reciprocal(RSB[:, :], SDB[:, :]))
            for c in range(NCH):
                ssync(v.tensor_tensor(SCR[:, :], X[:, c * T:(c + 1) * T], MUB[:, :],
                                      op=ALU.subtract))
                ssync(v.tensor_tensor(X[:, c * T:(c + 1) * T], SCR[:, :], RSB[:, :],
                                      op=ALU.mult))
                ms(n2.h, 1)

        @blk.scalar
        def _(s):
            if stage < 2:
                return
            s.wait_ge(pl.h, 1)
            s.activation(T1[:, :], PS[0:2, 0:1024], AF.Copy, scale=0.5).then_inc(el.h, 1)
            s.wait_ge(pl.h, 2)
            s.activation(T3[:, :], PS[0:2, 1024:2048], AF.Copy, scale=0.5).then_inc(el.h, 1)
            for i in range(NSTR):
                s.wait_ge(pg.h, i + 1)
                if i >= 2:
                    s.wait_ge(dt.h, i - 1)
                s.activation(SG[i % 2][:, :], PS[:, 0:1024], AF.Silu).then_inc(sl.h, 1)
            s.wait_ge(t2.h, NSTR)
            s.activation(T2B[:, :], PS[0:2, 2048:3072], AF.Copy, scale=0.5).then_inc(e2.h, 1)
            if stage < 3:
                return
            for pc in range(16 * NPC):
                s.wait_ge(ey.h, pc + 1)
                if pc >= 2:
                    s.wait_ge(sp.h, pc - 1)
                mg, mc, half = pc // NPC, (pc % NPC) // 2, pc % 2
                ch = mg * 2 + mc
                col = ch * T + half * 512
                s.activation(SQY[pc % 2][:, :], X[:, col:col + 512],
                             AF.Square).then_inc(es.h, 1)
            s.wait_ge(fd.h, 1)
            s.activation(SDB[:, :], VARB[:, :], AF.Sqrt).then_inc(fa.h, 1)

        @blk.tensor
        def _(t):
            def emit_a2(i):
                t.wait_ge(dt.h, i + 1)
                fc = i // 4
                srcb = (HF[:, fc * T:(fc + 1) * T] if i % 4 == 0
                        else TS[i % 2][:, :])
                for half in range(2):
                    ins = t.matmul(PS[0:2, 2048 + half * 512:2048 + half * 512 + 512],
                                   A2[:, i * 2:(i + 1) * 2],
                                   srcb[:, half * 512:half * 512 + 512],
                                   start=(i == 0), stop=(i == NSTR - 1))
                ins.then_inc(t2.h, 1)

            if stage < 2:
                return
            t.wait_ge(nl.h, NCH)
            t.wait_ge(wd.h, 16 * 4)
            for reg, ab in ((0, A1), (1, A3)):
                for half in range(2):
                    for kc in range(NCH):
                        ins = t.matmul(PS[0:2, reg * 1024 + half * 512: reg * 1024 + half * 512 + 512],
                                       ab[:, kc * 2:(kc + 1) * 2],
                                       X[:, kc * T + half * 512: kc * T + half * 512 + 512],
                                       start=(kc == 0), stop=(kc == NCH - 1))
                ins.then_inc(pl.h, 1)
            t.wait_ge(xf.h, 8)
            t.wait_ge(el.h, 2)
            for i in range(NSTR):
                fc, st_ = i // 4, i % 4
                t.wait_ge(wd.h, wd_stripe(i))
                if i >= 1:
                    t.wait_ge(sl.h, i)
                    t.wait_ge(dt.h, i)
                for dstoff, wf, bb, tt in ((0, WF1[i % 2], B1F[fc % 2], T1),
                                           (1024, WF3[i % 2], B3F[fc % 2], T3)):
                    for half in range(2):
                        pv = PS[:, dstoff + half * 512: dstoff + half * 512 + 512]
                        for kc in range(8):
                            t.matmul(pv,
                                     wf[:, kc * 128:(kc + 1) * 128],
                                     XF[:, kc * T + half * 512: kc * T + half * 512 + 512],
                                     start=(kc == 0), stop=False)
                        ins = t.matmul(pv, bb[:, st_ * 128:(st_ + 1) * 128],
                                       tt[:, half * 512: half * 512 + 512],
                                       start=False, stop=True)
                ins.then_inc(pg.h, 1)
                if i >= 1:
                    emit_a2(i - 1)
            emit_a2(NSTR - 1)
            if stage < 3:
                return
            t.wait_ge(hc.h, NSTR)
            t.wait_ge(e2.h, 1)
            for mg in range(16):
                t.wait_ge(wd.h, wd_w2(mg))
                for mc in range(2):
                    for half in range(2):
                        pc = mg * NPC + mc * 2 + half
                        if pc >= 4:
                            t.wait_ge(ey.h, pc - 3)
                        pv = PS[:, (pc % 4) * 512:(pc % 4) * 512 + 512]
                        for kc in range(22):
                            t.matmul(pv,
                                     W2B[mg % 2][:, kc * 256 + mc * 128: kc * 256 + (mc + 1) * 128],
                                     HF[:, kc * T + half * 512: kc * T + half * 512 + 512],
                                     start=(kc == 0), stop=False)
                        ins = t.matmul(pv, B2F[mg % 2][:, mc * 128:(mc + 1) * 128],
                                       T2B[:, half * 512: half * 512 + 512],
                                       start=False, stop=True)
                        ins.then_inc(wp.h, 1)
                        t.wait_ge(ey.h, pc + 1)
                        t.wait_ge(es.h, pc + 1)
                        ch = mg * 2 + mc
                        col = ch * T + half * 512
                        sy_start = (mg == 0 and mc == 0)
                        sy_stop = (mg == 15 and mc == 1)
                        t.matmul(PS[:, 2048 + half * 512:2048 + (half + 1) * 512],
                                 ONES[:, :], X[:, col:col + 512],
                                 start=sy_start, stop=sy_stop)
                        ins = t.matmul(PS[:, 3072 + half * 512:3072 + (half + 1) * 512],
                                       ONES[:, :], SQY[pc % 2][:, :],
                                       start=sy_start, stop=sy_stop)
                        ins.then_inc(sp.h, 1)

    sA.close()
    if stage < 4:
        return p
    sB = contextlib.ExitStack()
    WQ1 = [sB.enter_context(nc.sbuf_tensor(f"WQ1{i}", [128, NCH * 256], BF16)) for i in range(2)]
    QB = [sB.enter_context(nc.sbuf_tensor(f"QB{i}", [128, 2 * T], BF16)) for i in range(2)]
    VB = [sB.enter_context(nc.sbuf_tensor(f"VB{i}", [128, 8 * 256], BF16)) for i in range(2)]
    SQB = sB.enter_context(nc.sbuf_tensor("SQB", [128, 512], BF16))
    RMX = sB.enter_context(nc.sbuf_tensor("RMX", [128, 2], F32))
    QNB = sB.enter_context(nc.sbuf_tensor("QNB", [128, 32], F32))
    KNB = sB.enter_context(nc.sbuf_tensor("KNB", [128, 8], F32))

    with nc.Block() as blk:
        @blk.gpsimd
        def _(g):
            g.wait_ge(n2.h, NCH)
            for i in range(24):
                if i >= 2:
                    g.wait_ge(pe.h, pe_after(i - 2))
                srch = (wq1_h[i] if i < 16 else
                        (wk1_h[i - 16] if i < 20 else wv1_h[i - 20]))
                g.dma_start(out=WQ1[i % 2][:, :], in_=srch).then_inc(wd.h, 16)

        @blk.sync
        def _(sy):
            for i in range(20):
                sy.wait_ge(ev.h, (i + 1) * 4)
                dst = q1_h if i < 16 else k1_h
                mg = i if i < 16 else i - 16
                sy.dma_start(out=dst[:, mg * 2 * T:(mg + 1) * 2 * T],
                             in_=QB[i % 2][:, :]).then_inc(so.h, 16)
            for mb in range(4):
                sy.wait_ge(ev.h, ev_v(mb))
                srcv = VB[mb % 2][:, :].rearrange("p (kt m) -> p kt m", m=256)
                dstv = v1_h.rearrange("p (kt c) -> p kt c", c=T)[:, :, mb * 256:(mb + 1) * 256]
                sy.dma_start(out=dstv, in_=srcv).then_inc(so.h, 16)
            sy.wait_ge(dn.h, 64)
            sy.dma_start(out=qn_h, in_=QNB[:, :]).then_inc(so.h, 16)
            sy.wait_ge(dn.h, 80)
            sy.dma_start(out=kn_h, in_=KNB[:, :]).then_inc(so.h, 16)
            sy.wait_ge(so.h, 16 * (NCH + 20 + 4 + 2))

        @blk.vector
        def _(v):
            def ssync(ins):
                ins.then_inc(dv.h, 1)
                NDV[0] += 1
                v.wait_ge(dv.h, NDV[0])

            def ms(semh, k=1):
                ssync(v.tensor_copy(FLU[0:1, 0:2], FLU[0:1, 2:4]))
                v.nop().then_inc(semh, k)

            for i in range(20):
                for u in range(4):
                    hl, half = u // 2, u % 2
                    par = u % 2
                    guni = i * 4 + u
                    v.wait_ge(ev.h, (i + 1) * 4)
                    if guni >= 1:
                        v.wait_ge(pn.h, guni)
                    colc = hl * T + half * 512
                    ssync(v.tensor_tensor(SQB[:, :], QB[i % 2][:, colc:colc + 512],
                                          QB[i % 2][:, colc:colc + 512], op=ALU.mult))
                    ms(sq.h, 1)
                    v.wait_ge(pn.h, guni + 1)
                    npv = PS[:, 3072 + par * 512: 3072 + par * 512 + 512]
                    ssync(v.reduce_max(RMX[:, par:par + 1], npv,
                                       axis=mybir.AxisListType.X))
                    if i < 16:
                        hcol, HMX = i * 2 + hl, QNB
                    else:
                        hcol, HMX = (i - 16) * 2 + hl, KNB
                    if half == 0:
                        ssync(v.tensor_copy(HMX[:, hcol:hcol + 1], RMX[:, par:par + 1]))
                    else:
                        ssync(v.tensor_tensor(HMX[:, hcol:hcol + 1], RMX[:, par:par + 1],
                                              HMX[:, hcol:hcol + 1], op=ALU.max))
                    ms(dn.h, 1)

        @blk.scalar
        def _(s):
            for i in range(20):
                for half in range(2):
                    for mc in range(2):
                        piece = i * 4 + half * 2 + mc
                        bank = piece % 6
                        s.wait_ge(pe.h, piece + 1)
                        if i >= 2:
                            s.wait_ge(so.h, so_qkv(i - 2))
                        s.copy(out=QB[i % 2][:, mc * T + half * 512: mc * T + half * 512 + 512],
                               in_=PS[:, bank * 512:(bank + 1) * 512]).then_inc(ev.h, 1)
            for mb in range(4):
                for kt in range(8):
                    s.wait_ge(pe.h, 80 + mb * 8 + kt + 1)
                    if mb >= 2:
                        s.wait_ge(so.h, 16 * (NCH + 20 + mb - 1))
                    s.copy(out=VB[mb % 2][:, kt * 256:(kt + 1) * 256],
                           in_=PS[:, kt * 512:kt * 512 + 256]).then_inc(ev.h, 1)

        @blk.tensor
        def _(t):
            t.wait_ge(n2.h, NCH)
            for i in range(20):
                t.wait_ge(wd.h, wd_wq1(i))
                for half in range(2):
                    for mc in range(2):
                        piece = i * 4 + half * 2 + mc
                        bank = piece % 6
                        if piece - 6 >= 0:
                            t.wait_ge(ev.h, piece - 6 + 1)
                        pv = PS[:, bank * 512:(bank + 1) * 512]
                        for kc in range(NCH):
                            ins = t.matmul(
                                pv,
                                WQ1[i % 2][:, kc * 256 + mc * 128: kc * 256 + (mc + 1) * 128],
                                X[:, kc * T + half * 512: kc * T + half * 512 + 512],
                                start=(kc == 0), stop=(kc == NCH - 1))
                        ins.then_inc(pe.h, 1)
                for u in range(4):
                    guni = i * 4 + u
                    par = u % 2
                    if guni - 2 >= 0:
                        t.wait_ge(dn.h, guni - 2 + 1)
                    t.wait_ge(sq.h, guni + 1)
                    npv = PS[:, 3072 + par * 512: 3072 + par * 512 + 512]
                    ins = t.matmul(npv, ONES[:, :], SQB[:, :], start=True, stop=True)
                    ins.then_inc(pn.h, 1)
            for mb in range(4):
                t.wait_ge(wd.h, wd_wq1(20 + mb))
                if mb > 0:
                    t.wait_ge(ev.h, ev_v(mb - 1))
                else:
                    t.wait_ge(ev.h, 80)
                    t.wait_ge(dn.h, 80)
                for kc in range(NCH):
                    for kt in range(8):
                        pv = PS[:, kt * 512:kt * 512 + 256]
                        ins = t.matmul(
                            pv,
                            X[:, kc * T + kt * 128: kc * T + (kt + 1) * 128],
                            WQ1[(20 + mb) % 2][:, kc * 256:(kc + 1) * 256],
                            start=(kc == 0), stop=(kc == NCH - 1))
                        if kc == NCH - 1:
                            ins.then_inc(pe.h, 1)

    sB.close()
    return p


def _l0_masks(g):
    masks = np.zeros((NSLOT, 128, 512), np.uint8)
    ktl = np.arange(128)[:, None]
    qtl = np.arange(512)[None, :]
    for qb in range(2):
        off = 0 if qb == 0 else 16
        i = g if qb == 0 else 7 - g
        nreal = 4 * (i + 1)
        nsl = 16 if qb == 0 else 32
        for s in range(nsl):
            if s < nreal - 4:
                masks[off + s] = 1
            elif s < nreal:
                d = -128 * (s - (nreal - 4))
                masks[off + s] = (ktl <= qtl + d).astype(np.uint8)
    return np.ascontiguousarray(masks.transpose(1, 0, 2).reshape(128, NSLOT * 512))


def run_l0_attn(inputs, static, core_data, r1):
    p = build_l0_attn()
    ksl, vsl, qn4, kn4 = {}, {}, {}, {}
    for b in range(2):
        Ka = np.zeros((D, SEQ), BF)
        Va = np.zeros((SEQ, D), BF)
        for c in range(4 * b, 4 * b + 4):
            _, toks = _shard_tokens(c)
            Ka[:, toks] = from_image(r1[c]["ko"], D)
            Va[toks] = from_image(r1[c]["vo"], T)
        ksl[b] = to_image(np.ascontiguousarray(np.concatenate([Ka[:, :2048], Ka], axis=1)))
        vsl[b] = to_image(np.ascontiguousarray(np.concatenate([Va[:2048], Va], axis=0)))
        qn4[b] = np.ascontiguousarray(np.concatenate(
            [r1[c]["qn"] for c in range(4 * b, 4 * b + 4)], axis=1))
        kn4[b] = np.ascontiguousarray(np.concatenate(
            [r1[c]["kn"] for c in range(4 * b, 4 * b + 4)], axis=1))
    in_maps = []
    for c in range(NCORES):
        b, g = c // 4, c % 4
        in_maps.append({
            "x0i": core_data[c]["x0i"],
            "qi": r1[c]["qo"],
            "ksl": ksl[b], "vsl": vsl[b],
            "mki": _l0_masks(g),
            "qn4": qn4[b], "kn4": kn4[b],
            "onesi": np.ones((128, 128), np.float32),
            "woi": static["wo_img"],
        })
    res = run_bass_kernel_spmd(p.nc, in_maps, core_ids=list(range(NCORES)))
    p.close()
    return res.results




def run_l0_ffn(inputs, static, r2):
    p = build_l0_ffn()
    in_maps = []
    for c in range(NCORES):
        in_maps.append({
            "yi": r2[c]["yo"], "mui": r2[c]["muo"], "rsi": r2[c]["rso"],
            "w1i": static["w1_img"], "w3i": static["w3_img"], "w2i": static["w2_img"],
            "a1i": static["a1_img"], "a3i": static["a3_img"],
            "b1i": static["b1_arr"], "b3i": static["b3_arr"],
            "a2i": static["a2_img"], "b2i": static["b2_arr"],
            "wq1i": static["wq1_img"], "wk1i": static["wk1_img"], "wv1i": static["wv1_img"],
            "onesi": np.ones((128, 128), np.float32),
        })
    res = run_bass_kernel_spmd(p.nc, in_maps, core_ids=list(range(NCORES)))
    p.close()
    return res.results





def build_l1_attn():
    p = Prog("l1attn")
    nc = p.nc
    x2i_h = p.din("x2i", [128, NCH * T], BF16)
    q1_h = p.din("q1i", [128, NCH * T], BF16)
    k1f_h = p.din("k1f", [128, 8 * SEQ], BF16)
    v1f_h = p.din("v1f", [128, (SEQ // 128) * T], BF16)
    qn_h = p.din("qn4", [128, 4 * 32])
    kn_h = p.din("kn4", [128, 4 * 8])
    ones_h = p.din("onesi", [128, 128])
    wo_h = p.din("wo1i", [16, 128, NCH * 256])
    yo_h = p.dout("y3o", [128, NCH * T], BF16)
    mu_h = p.dout("mu3o", [128, T], F32)
    rs_h = p.dout("rs3o", [128, T], F32)
    cs_h = p.dout("cso", [128, NCH], F32)

    AB = p.sbuf("AB", [128, NCH * T], BF16)
    KG = [p.sbuf(f"KG{i}", [128, SEQ], BF16) for i in range(2)]
    VG = [p.sbuf(f"VG{i}", [128, 32 * 128], BF16) for i in range(2)]
    QG = [p.sbuf(f"QG{i}", [128, 4 * T], BF16) for i in range(2)]
    PT = [p.sbuf(f"PT{i}", [128, 512], BF16) for i in range(2)]
    QN4 = p.sbuf("QN4", [128, 4 * 32], F32)
    KN4 = p.sbuf("KN4", [128, 4 * 8], F32)
    QMX = p.sbuf("QMX", [128, 32], F32)
    KMX = p.sbuf("KMX", [128, 8], F32)
    MBALL = p.sbuf("MBALL", [128, 32], F32)
    ONES = p.sbuf("ONES", [128, 128], BF16)
    RL = p.sbuf("RL", [128, 512], F32)
    WOB = [p.sbuf(f"WOB{i}", [128, NCH * 256], BF16) for i in range(2)]
    X2C = [p.sbuf(f"X2C{i}", [128, T], BF16) for i in range(2)]
    SQY = [p.sbuf(f"SQY{i}", [128, 512], BF16) for i in range(2)]
    YOB = [p.sbuf(f"YOB{i}", [128, 2 * T], BF16) for i in range(2)]
    YRC = [p.sbuf(f"YRC{i}", [128, T], BF16) for i in range(2)]
    MU = p.sbuf("MU", [128, T], F32)
    MSQ = p.sbuf("MSQ", [128, T], F32)
    VARB = p.sbuf("VARB", [128, T], F32)
    SDB = p.sbuf("SDB", [128, T], F32)
    RSTD = p.sbuf("RSTD", [128, T], F32)
    SCR = p.sbuf("SCR", [128, T], F32)
    X3C = p.sbuf("X3C", [128, T], BF16)
    CS = p.sbuf("CS", [128, NCH], F32)
    FLU = p.sbuf("FLU", [1, 4], F32)
    PS = p.psum("PS", [128, 8 * 512], F32)

    wd = p.sem("wd"); qd = p.sem("qd"); st = p.sem("st"); ae = p.sem("ae")
    pa = p.sem("pa"); ab = p.sem("ab"); dv = p.sem("dv"); mbd = p.sem("mbd")
    mba = p.sem("mba"); wp = p.sem("wp"); ey = p.sem("ey"); es = p.sem("es")
    sp = p.sem("sp"); fd = p.sem("fd"); fa = p.sem("fa"); cs = p.sem("cs")
    yr = p.sem("yr"); so = p.sem("so")

    NMG = 16
    NPC = 4
    SCALE1 = 1.0 / float(np.sqrt(128.0))

    def qd_g(g):
        return 16 * 3 * (g + 1)

    def wd_w(mg):
        return 16 * (3 + 3 * mg + 1)

    def wd_x2(mg):
        return 16 * (3 + 3 * mg + 3)

    def hq_pa(hq):
        return (hq + 1) * 32

    with nc.Block() as blk:
        @blk.gpsimd
        def _(g):
            g.dma_start(out=ONES[:, :], in_=ones_h).then_inc(wd.h, 16)
            g.dma_start(out=QN4[:, :], in_=qn_h).then_inc(wd.h, 16)
            g.dma_start(out=KN4[:, :], in_=kn_h).then_inc(wd.h, 16)
            for mg in range(NMG):
                if mg >= 2:
                    g.wait_ge(wp.h, (mg - 2) * NPC + NPC)
                g.dma_start(out=WOB[mg % 2][:, :], in_=wo_h[mg]).then_inc(wd.h, 16)
                for mc in range(2):
                    if mg >= 1:
                        g.wait_ge(ey.h, (mg - 1) * NPC + NPC)
                    g.dma_start(out=X2C[mc][:, :],
                                in_=x2i_h[:, (2 * mg + mc) * T:(2 * mg + mc + 1) * T]
                                ).then_inc(wd.h, 16)

        @blk.sync
        def _(sy):
            for g in range(8):
                if g >= 2:
                    sy.wait_ge(pa.h, hq_pa((g - 1) * 8 - 1 + 1))
                sy.dma_start(out=QG[g % 2][:, :],
                             in_=q1_h[:, 4 * g * T:(4 * g + 4) * T]).then_inc(qd.h, 16)
                sy.dma_start(out=KG[g % 2][:, :],
                             in_=k1f_h[:, g * SEQ:(g + 1) * SEQ]).then_inc(qd.h, 16)
                vsrc = v1f_h.rearrange("p (kt c) -> p kt c", c=T)[:, :, g * 128:(g + 1) * 128]
                sy.dma_start(out=VG[g % 2][:, :].rearrange("p (kt c) -> p kt c", c=128),
                             in_=vsrc).then_inc(qd.h, 16)
            for mg in range(NMG):
                sy.wait_ge(ey.h, mg * NPC + NPC)
                sy.dma_start(out=yo_h[:, 2 * mg * T:(2 * mg + 2) * T],
                             in_=YOB[mg % 2][:, :]).then_inc(so.h, 16)
            sy.wait_ge(fd.h, 3)
            sy.dma_start(out=mu_h, in_=MU[:, :]).then_inc(so.h, 16)
            sy.dma_start(out=rs_h, in_=RSTD[:, :]).then_inc(so.h, 16)
            sy.wait_ge(so.h, 16 * (NMG + 2))
            for c in range(NCH):
                if c >= 2:
                    sy.wait_ge(cs.h, c - 1)
                sy.dma_start(out=YRC[c % 2][:, :],
                             in_=yo_h[:, c * T:(c + 1) * T]).then_inc(yr.h, 16)
            sy.wait_ge(cs.h, NCH)
            sy.dma_start(out=cs_h, in_=CS[:, :]).then_inc(so.h, 16)
            sy.wait_ge(so.h, 16 * (NMG + 3))

        @blk.vector
        def _(v):
            ndv = [0]

            def ssync(ins):
                ins.then_inc(dv.h, 1)
                ndv[0] += 1
                v.wait_ge(dv.h, ndv[0])

            def ms(semh, k=1):
                ssync(v.tensor_copy(FLU[0:1, 0:2], FLU[0:1, 2:4]))
                v.nop().then_inc(semh, k)

            v.wait_ge(wd.h, 16 * 3)
            for dst, srcb, n in ((QMX, QN4, 32), (KMX, KN4, 8)):
                ssync(v.tensor_copy(dst[:, :], srcb[:, 0:n]))
                for j in range(1, 4):
                    ssync(v.tensor_tensor(dst[:, :], srcb[:, j * n:(j + 1) * n],
                                          dst[:, :], op=ALU.max))
            for h in range(32):
                ssync(v.tensor_tensor(MSQ[:, h:h + 1], QMX[:, h:h + 1],
                                      KMX[:, h // 4:h // 4 + 1], op=ALU.mult))
            ms(mbd.h, 1)
            v.wait_ge(mba.h, 1)
            ssync(v.tensor_scalar(MBALL[:, :], SDB[:, 0:32], -1.01 * SCALE1, None,
                                  op0=ALU.mult))
            ms(mbd.h, 1)
            for hq in range(64):
                v.wait_ge(pa.h, hq_pa(hq))
                ssync(v.reciprocal(RL[:, :], PS[:, 2560:3072]))
                ch = hq // 2
                qb = hq % 2
                ssync(v.tensor_tensor(AB[:, ch * T + qb * 512: ch * T + qb * 512 + 512],
                                      PS[:, 2048:2560], RL[:, :], op=ALU.mult))
                ms(ab.h, 1)
            for mg in range(NMG):
                if mg >= 2:
                    v.wait_ge(so.h, 16 * (mg - 1))
                for half in range(2):
                    for mc in range(2):
                        pc = mg * NPC + half * 2 + mc
                        v.wait_ge(wp.h, pc + 1)
                        v.wait_ge(wd.h, wd_x2(mg))
                        ssync(v.tensor_tensor(YOB[mg % 2][:, mc * T + half * 512: mc * T + half * 512 + 512],
                                              PS[:, 3072 + mc * 512:3072 + (mc + 1) * 512],
                                              X2C[mc][:, half * 512:half * 512 + 512],
                                              op=ALU.add))
                        ms(ey.h, 1)
            ms(ey.h, 1)
            v.wait_ge(sp.h, NMG * NPC)
            ssync(v.tensor_scalar(MU[:, :], PS[:, 0:1024], 1.0 / D, None, op0=ALU.mult))
            ssync(v.tensor_tensor(MSQ[:, :], MU[:, :], MU[:, :], op=ALU.mult))
            ssync(v.scalar_tensor_tensor(VARB[:, :], PS[:, 1024:2048], 1.0 / D,
                                         MSQ[:, :], op0=ALU.mult, op1=ALU.subtract))
            ssync(v.tensor_scalar(VARB[:, :], VARB[:, :], EPS, None, op0=ALU.add))
            ms(fd.h, 1)
            v.wait_ge(fa.h, 2)
            ssync(v.reciprocal(RSTD[:, :], SDB[:, :]))
            ms(fd.h, 2)
            for c in range(NCH):
                v.wait_ge(yr.h, 16 * (c + 1))
                ssync(v.tensor_tensor(SCR[:, :], YRC[c % 2][:, :], MU[:, :],
                                      op=ALU.subtract))
                ssync(v.tensor_tensor(X3C[:, :], SCR[:, :], RSTD[:, :], op=ALU.mult))
                ssync(v.reduce_sum(CS[:, c:c + 1], X3C[:, :], axis=mybir.AxisListType.X))
                ms(cs.h, 1)

        @blk.scalar
        def _(s):
            s.wait_ge(mbd.h, 1)
            s.activation(SDB[:, 0:32], MSQ[:, 0:32], AF.Sqrt).then_inc(mba.h, 1)
            for hq in range(64):
                g, r, qb = hq // 8, (hq % 8) // 2, hq % 2
                for j in range(32):
                    su = hq * 32 + j
                    s.wait_ge(st.h, su + 1)
                    if su >= 2:
                        s.wait_ge(pa.h, su - 1)
                    if su == 0:
                        s.wait_ge(mbd.h, 2)
                    s.activation(PT[j % 2][:, :], PS[:, (j % 2) * 512:(j % 2) * 512 + 512],
                                 AF.Exp, bias=MBALL[:, g * 4 + r:g * 4 + r + 1],
                                 scale=SCALE1).then_inc(ae.h, 1)
            for mg in range(NMG):
                for half in range(2):
                    for mc in range(2):
                        pc = mg * NPC + half * 2 + mc
                        s.wait_ge(ey.h, pc + 2)
                        if pc >= 2:
                            s.wait_ge(sp.h, pc - 1)
                        s.activation(SQY[pc % 2][:, :],
                                     YOB[mg % 2][:, mc * T + half * 512: mc * T + half * 512 + 512],
                                     AF.Square).then_inc(es.h, 1)
            s.wait_ge(fd.h, 1)
            s.activation(SDB[:, :], VARB[:, :], AF.Sqrt).then_inc(fa.h, 2)

        @blk.tensor
        def _(t):
            for hq in range(64):
                g, r, qb = hq // 8, (hq % 8) // 2, hq % 2
                t.wait_ge(qd.h, qd_g(g))
                first_hq = (hq == 0)
                if hq >= 1:
                    t.wait_ge(ab.h, hq)
                for j in range(32):
                    su = hq * 32 + j
                    if su >= 2:
                        t.wait_ge(ae.h, su - 1)
                    stp = PS[:, (j % 2) * 512:(j % 2) * 512 + 512]
                    ins = t.matmul(stp, KG[g % 2][:, j * 128:(j + 1) * 128],
                                   QG[g % 2][:, r * T + qb * 512: r * T + qb * 512 + 512],
                                   start=True, stop=True)
                    ins.then_inc(st.h, 1)
                    t.wait_ge(ae.h, su + 1)
                    t.matmul(PS[:, 2048:2560], VG[g % 2][:, j * 128:(j + 1) * 128],
                             PT[j % 2][:, :], start=(j == 0), stop=(j == 31))
                    ins = t.matmul(PS[:, 2560:3072], ONES[:, :], PT[j % 2][:, :],
                                   start=(j == 0), stop=(j == 31))
                    ins.then_inc(pa.h, 1)
            t.wait_ge(ab.h, 64)

            def l4_stats(pc2):
                mg2, half2, mc2 = pc2 // NPC, (pc2 % NPC) // 2, pc2 % 2
                t.wait_ge(ey.h, pc2 + 1)
                t.wait_ge(es.h, pc2 + 1)
                sy_start = (mg2 == 0 and mc2 == 0)
                sy_stop = (mg2 == NMG - 1 and mc2 == 1)
                t.matmul(PS[:, 0 + half2 * 512:(half2 + 1) * 512],
                         ONES[:, :],
                         YOB[mg2 % 2][:, mc2 * T + half2 * 512: mc2 * T + half2 * 512 + 512],
                         start=sy_start, stop=sy_stop)
                ins2 = t.matmul(PS[:, 1024 + half2 * 512:1024 + (half2 + 1) * 512],
                                ONES[:, :], SQY[pc2 % 2][:, :],
                                start=sy_start, stop=sy_stop)
                ins2.then_inc(sp.h, 1)

            for mg in range(NMG):
                t.wait_ge(wd.h, wd_w(mg))
                for half in range(2):
                    for mc in range(2):
                        pc = mg * NPC + half * 2 + mc
                        if pc >= 2:
                            t.wait_ge(ey.h, pc - 1)
                        pj = PS[:, 3072 + mc * 512:3072 + (mc + 1) * 512]
                        for kc in range(NCH):
                            ins = t.matmul(
                                pj,
                                WOB[mg % 2][:, kc * 256 + mc * 128: kc * 256 + (mc + 1) * 128],
                                AB[:, kc * T + half * 512: kc * T + half * 512 + 512],
                                start=(kc == 0), stop=(kc == NCH - 1))
                        ins.then_inc(wp.h, 1)
                        if pc >= 2:
                            l4_stats(pc - 2)
            l4_stats(NMG * NPC - 2)
            l4_stats(NMG * NPC - 1)

    return p


def run_l1_attn(inputs, static, r3, r2x=None):
    p = build_l1_attn()
    k1f, v1f, qn4, kn4 = {}, {}, {}, {}
    for b in range(2):
        Ka = np.zeros((1024, SEQ), BF)
        Va = np.zeros((SEQ, 1024), BF)
        for c in range(4 * b, 4 * b + 4):
            _, toks = _shard_tokens(c)
            Ka[:, toks] = from_image(r3[c]["k1o"], 1024)
            Va[toks] = from_image(r3[c]["v1o"], T)
        k1f[b] = to_image(np.ascontiguousarray(Ka))
        v1f[b] = to_image(np.ascontiguousarray(Va))
        qn4[b] = np.ascontiguousarray(np.concatenate(
            [r3[c]["q1n"] for c in range(4 * b, 4 * b + 4)], axis=1))
        kn4[b] = np.ascontiguousarray(np.concatenate(
            [r3[c]["k1n"] for c in range(4 * b, 4 * b + 4)], axis=1))
    in_maps = []
    for c in range(NCORES):
        b = c // 4
        in_maps.append({
            "x2i": r3[c]["x2o"], "q1i": r3[c]["q1o"],
            "k1f": k1f[b], "v1f": v1f[b],
            "qn4": qn4[b], "kn4": kn4[b],
            "onesi": np.ones((128, 128), np.float32),
            "wo1i": static["wo1_img"],
        })
    res = run_bass_kernel_spmd(p.nc, in_maps, core_ids=list(range(NCORES)))
    p.close()
    return res.results





def build_l1_ffn():
    p = Prog("l1ffn")
    nc = p.nc
    y_h = p.din("y3i", [128, NCH * T], BF16)
    mu_h = p.din("mu3i", [128, T])
    rs_h = p.din("rs3i", [128, T])
    cs_h = p.din("cs4", [128, 4 * NCH])
    wo_h = p.din("worgi", [16, 128, NCH * 256])
    dw_h = p.din("downi", [8, 128, NCH * 64])
    up_h = p.din("uppi", [4, 128, 4 * 1024])
    ke_h = p.din("keysi", [128, 8])
    wq_h = p.din("wq2i", [128, NCH * 128])
    ones_h = p.din("onesi", [128, 128])
    out_h = p.dout("outo", [128, NCH * T], F32)
    if DBG_L5:
        y4d = p.dout("y4o", [128, NCH * T], BF16)
        mu4_h = p.dout("mu4o", [128, T], F32)
        rs4_h = p.dout("rs4o", [128, T], F32)
        sq4_h = p.dout("sq4o", [128, T], F32)
    else:
        y4d = nc.dram_tensor("y4d", [128, NCH * T], BF16).ap()

    X = p.sbuf("X", [128, NCH * T], BF16)
    ONES = p.sbuf("ONES", [128, 128], BF16)
    YC = [p.sbuf(f"YC{i}", [128, T], BF16) for i in range(2)]
    MUB = p.sbuf("MUB", [128, T], F32)
    RSB = p.sbuf("RSB", [128, T], F32)
    SCR = p.sbuf("SCR", [128, T], F32)
    CSA = p.sbuf("CSA", [128, 4 * NCH], F32)
    CSUM = p.sbuf("CSUM", [128, NCH], F32)
    XM = p.sbuf("XM", [128, NCH], BF16)
    WQ2 = p.sbuf("WQ2", [128, NCH * 128], BF16)
    KEYSB = p.sbuf("KEYSB", [128, 8], BF16)
    QS = p.sbuf("QS", [128, 1], BF16)
    MX1 = p.sbuf("MX1", [1, 1], F32)
    MB1 = p.sbuf("MB1", [1, 1], F32)
    EX1 = p.sbuf("EX1", [1, 8], F32)
    SM1 = p.sbuf("SM1", [1, 1], F32)
    RS1 = p.sbuf("RS1", [1, 1], F32)
    CF = p.sbuf("CF", [1, 8], F32)
    CFB = p.sbuf("CFB", [1, 8], BF16)
    CEB = p.sbuf("CEB", [64, 8], F32)
    DWN = [p.sbuf(f"DWN{i}", [128, NCH * 64], BF16) for i in range(2)]
    UPP = p.sbuf("UPP", [128, 4 * 4096], BF16)
    HEB = p.sbuf("HEB", [128, 4 * T], BF16)
    WOB = [p.sbuf(f"WOB{i}", [128, NCH * 256], BF16) for i in range(2)]
    YOB = [p.sbuf(f"YOB{i}", [128, 2 * T], BF16) for i in range(2)]
    YRC = [p.sbuf(f"YRC{i}", [128, T], BF16) for i in range(2)]
    SQY = [p.sbuf(f"SQY{i}", [128, 512], BF16) for i in range(2)]
    MSQ = p.sbuf("MSQ", [128, T], F32)
    VARB = p.sbuf("VARB", [128, T], F32)
    SDB = p.sbuf("SDB", [128, T], F32)
    OUTC = [p.sbuf(f"OUTC{i}", [128, T], F32) for i in range(2)]
    FLU = p.sbuf("FLU", [1, 4], F32)
    PS = p.psum("PS", [128, 8 * 512], F32)

    wd = p.sem("wd"); qd = p.sem("qd"); nl = p.sem("nl"); dv = p.sem("dv")
    cf_d = p.sem("cfd"); cf_a = p.sem("cfa"); cf_p = p.sem("cfp")
    ph = p.sem("ph"); he = p.sem("he")
    wp = p.sem("wp"); ey = p.sem("ey"); es = p.sem("es"); sp = p.sem("sp")
    fd = p.sem("fd"); fa = p.sem("fa"); yr = p.sem("yr"); ox = p.sem("ox")
    so = p.sem("so")

    NMG = 16
    NPC = 4
    SCALE1 = 1.0 / float(np.sqrt(128.0))

    def wd_dw(e):
        return 16 * (8 + e + 1)

    def wd_wo(mg):
        return 16 * (8 + 8 + mg + 1)

    def qd_y(c):
        return 16 * (2 + c + 1)

    with nc.Block() as blk:
        @blk.gpsimd
        def _(g):
            g.dma_start(out=ONES[:, :], in_=ones_h).then_inc(wd.h, 16)
            g.dma_start(out=CSA[:, :], in_=cs_h).then_inc(wd.h, 16)
            g.dma_start(out=KEYSB[:, :], in_=ke_h).then_inc(wd.h, 16)
            g.dma_start(out=WQ2[:, :], in_=wq_h).then_inc(wd.h, 16)
            for pp in range(4):
                g.dma_start(out=UPP[:, pp * 4096:(pp + 1) * 4096],
                            in_=up_h[pp]).then_inc(wd.h, 16)
            for e in range(8):
                if e >= 2:
                    g.wait_ge(ph.h, e - 1)
                g.dma_start(out=DWN[e % 2][:, :], in_=dw_h[e]).then_inc(wd.h, 16)
            for mg in range(NMG):
                if mg >= 2:
                    g.wait_ge(wp.h, (mg - 2) * NPC + NPC)
                g.dma_start(out=WOB[mg % 2][:, :], in_=wo_h[mg]).then_inc(wd.h, 16)

        @blk.sync
        def _(sy):
            sy.dma_start(out=MUB[:, :], in_=mu_h).then_inc(qd.h, 16)
            sy.dma_start(out=RSB[:, :], in_=rs_h).then_inc(qd.h, 16)
            for c in range(NCH):
                if c >= 2:
                    sy.wait_ge(nl.h, c - 1)
                sy.dma_start(out=YC[c % 2][:, :],
                             in_=y_h[:, c * T:(c + 1) * T]).then_inc(qd.h, 16)
            for mg in range(NMG):
                sy.wait_ge(ey.h, mg * NPC + NPC)
                sy.dma_start(out=y4d[:, 2 * mg * T:(2 * mg + 2) * T],
                             in_=YOB[mg % 2][:, :]).then_inc(so.h, 16)
            sy.wait_ge(so.h, 16 * NMG)
            for c in range(NCH + 2):
                if c < NCH:
                    if c >= 2:
                        sy.wait_ge(ox.h, c - 1)
                    sy.dma_start(out=YRC[c % 2][:, :],
                                 in_=y4d[:, c * T:(c + 1) * T]).then_inc(yr.h, 16)
                if c >= 2:
                    sy.wait_ge(ox.h, c - 1)
                    sy.dma_start(out=out_h[:, (c - 2) * T:(c - 1) * T],
                                 in_=OUTC[(c - 2) % 2][:, :]).then_inc(so.h, 16)
            if DBG_L5:
                sy.wait_ge(fd.h, 2)
                sy.dma_start(out=mu4_h, in_=MUB[:, :]).then_inc(so.h, 16)
                sy.dma_start(out=rs4_h, in_=RSB[:, :]).then_inc(so.h, 16)
                sy.dma_start(out=sq4_h, in_=MSQ[:, :]).then_inc(so.h, 16)
                sy.wait_ge(so.h, 16 * (NMG + NCH + 2))
            else:
                sy.wait_ge(so.h, 16 * (NMG + NCH))

        @blk.vector
        def _(v):
            ndv = [0]

            def ssync(ins):
                ins.then_inc(dv.h, 1)
                ndv[0] += 1
                v.wait_ge(dv.h, ndv[0])

            def ms(semh, k=1):
                ssync(v.tensor_copy(FLU[0:1, 0:2], FLU[0:1, 2:4]))
                v.nop().then_inc(semh, k)

            v.wait_ge(qd.h, 32)
            for c in range(NCH):
                v.wait_ge(qd.h, qd_y(c))
                ssync(v.tensor_tensor(SCR[:, :], YC[c % 2][:, :], MUB[:, :],
                                      op=ALU.subtract))
                ssync(v.tensor_tensor(X[:, c * T:(c + 1) * T], SCR[:, :], RSB[:, :],
                                      op=ALU.mult))
                ms(nl.h, 1)
            v.wait_ge(wd.h, 16 * 2)
            ssync(v.tensor_copy(CSUM[:, :], CSA[:, 0:NCH]))
            for j in range(1, 4):
                ssync(v.tensor_tensor(CSUM[:, :], CSA[:, j * NCH:(j + 1) * NCH],
                                      CSUM[:, :], op=ALU.add))
            ssync(v.tensor_scalar(XM[:, :], CSUM[:, :], 1.0 / SEQ, None, op0=ALU.mult))
            ms(cf_d.h, 1)
            v.wait_ge(cf_p.h, 2)
            ssync(v.reduce_max(MX1[:, :], PS[0:1, 2560:2568], axis=mybir.AxisListType.X))
            ssync(v.tensor_scalar(MB1[:, :], MX1[:, :], -SCALE1, None, op0=ALU.mult))
            ms(cf_d.h, 1)
            v.wait_ge(cf_a.h, 2)
            ssync(v.reciprocal(RS1[:, :], SM1[:, :]))
            ssync(v.tensor_scalar(CF[:, :], EX1[:, :], RS1[:, :], None, op0=ALU.mult, op1=ALU.bypass))
            ms(cf_d.h, 1)
            for e in range(8):
                v.wait_ge(ph.h, e + 1)
                v.wait_ge(cf_p.h, 2 + e + 1)
                ssync(v.tensor_copy(CEB[:, e:e + 1], PS[0:64, 3072:3073]))
                ssync(v.tensor_scalar(HEB[(e % 2) * 64:(e % 2) * 64 + 64,
                                          (e // 2) * T:(e // 2) * T + T],
                                      PS[0:64, (e % 2) * 1024:(e % 2) * 1024 + 1024],
                                      CEB[:, e:e + 1], None, op0=ALU.mult))
                ms(he.h, 1)
            for mg in range(NMG):
                if mg >= 2:
                    v.wait_ge(so.h, 16 * (mg - 1))
                for half in range(2):
                    for mc in range(2):
                        pc = mg * NPC + half * 2 + mc
                        v.wait_ge(wp.h, pc + 1)
                        col = (2 * mg + mc) * T + half * 512
                        ssync(v.tensor_tensor(YOB[mg % 2][:, mc * T + half * 512: mc * T + half * 512 + 512],
                                              PS[:, 3072 + mc * 512:3072 + (mc + 1) * 512],
                                              X[:, col:col + 512], op=ALU.add))
                        ms(ey.h, 1)
            ms(ey.h, 1)
            v.wait_ge(sp.h, NMG * NPC)
            ssync(v.tensor_scalar(MUB[:, :], PS[:, 0:1024], 1.0 / D, None, op0=ALU.mult))
            ssync(v.tensor_tensor(MSQ[:, :], MUB[:, :], MUB[:, :], op=ALU.mult))
            ssync(v.scalar_tensor_tensor(VARB[:, :], PS[:, 1024:2048], 1.0 / D,
                                         MSQ[:, :], op0=ALU.mult, op1=ALU.subtract))
            if DBG_L5:
                ssync(v.tensor_scalar(MSQ[:, :], PS[:, 1024:2048], 1.0, None, op0=ALU.mult))
            ssync(v.tensor_scalar(VARB[:, :], VARB[:, :], EPS, None, op0=ALU.add))
            ms(fd.h, 1)
            v.wait_ge(fa.h, 1)
            ssync(v.reciprocal(RSB[:, :], SDB[:, :]))
            ms(fd.h, 1)
            for c in range(NCH):
                v.wait_ge(yr.h, 16 * (c + 1))
                if c >= 2:
                    v.wait_ge(so.h, 16 * (NMG + c - 1))
                ssync(v.tensor_tensor(SCR[:, :], YRC[c % 2][:, :], MUB[:, :],
                                      op=ALU.subtract))
                ssync(v.tensor_tensor(OUTC[c % 2][:, :], SCR[:, :], RSB[:, :],
                                      op=ALU.mult))
                ms(ox.h, 1)

        @blk.scalar
        def _(s):
            s.wait_ge(cf_p.h, 1)
            s.activation(QS[:, :], PS[:, 2048:2049], AF.Copy).then_inc(cf_a.h, 1)
            s.wait_ge(cf_d.h, 2)
            s.activation(EX1[:, :], PS[0:1, 2560:2568], AF.Exp, bias=MB1[:, :],
                         scale=SCALE1, accum_out=SM1[:, :]).then_inc(cf_a.h, 1)
            s.wait_ge(cf_d.h, 3)
            s.activation(CFB[:, :], CF[:, :], AF.Copy).then_inc(cf_a.h, 1)
            for pc in range(NMG * NPC):
                s.wait_ge(ey.h, pc + 2)
                if pc >= 2:
                    s.wait_ge(sp.h, pc - 1)
                mg, mc, half = pc // NPC, (pc % NPC) // 2, pc % 2
                s.activation(SQY[pc % 2][:, :],
                             YOB[mg % 2][:, mc * T + half * 512: mc * T + half * 512 + 512],
                             AF.Square).then_inc(es.h, 1)
            s.wait_ge(fd.h, 1)
            s.activation(SDB[:, :], VARB[:, :], AF.Sqrt).then_inc(fa.h, 1)

        @blk.tensor
        def _(t):
            t.wait_ge(cf_d.h, 1)
            t.wait_ge(wd.h, 16 * 4)
            for kc in range(NCH):
                ins = t.matmul(PS[:, 2048:2049], WQ2[:, kc * 128:(kc + 1) * 128],
                               XM[:, kc:kc + 1], start=(kc == 0), stop=(kc == NCH - 1))
            ins.then_inc(cf_p.h, 1)
            t.wait_ge(cf_a.h, 1)
            t.matmul(PS[0:1, 2560:2568], QS[:, :], KEYSB[:, :],
                     start=True, stop=True).then_inc(cf_p.h, 1)
            t.wait_ge(cf_a.h, 3)
            t.wait_ge(nl.h, NCH)
            for e in range(8):
                if e >= 1:
                    t.wait_ge(he.h, e)
                ins = t.matmul(PS[0:64, 3072:3073], ONES[0:1, 0:64],
                               CFB[:, e:e + 1], start=True, stop=True)
                ins.then_inc(cf_p.h, 1)
                t.wait_ge(wd.h, wd_dw(e))
                if e >= 2:
                    t.wait_ge(he.h, e - 1)
                for half in range(2):
                    pv = PS[0:64, (e % 2) * 1024 + half * 512:(e % 2) * 1024 + (half + 1) * 512]
                    for kc in range(NCH):
                        ins = t.matmul(pv, DWN[e % 2][:, kc * 64:(kc + 1) * 64],
                                       X[:, kc * T + half * 512: kc * T + half * 512 + 512],
                                       start=(kc == 0), stop=(kc == NCH - 1))
                ins.then_inc(ph.h, 1)
            t.wait_ge(he.h, 8)

            def l5_stats(pc2):
                mg2, half2, mc2 = pc2 // NPC, (pc2 % NPC) // 2, pc2 % 2
                t.wait_ge(ey.h, pc2 + 1)
                t.wait_ge(es.h, pc2 + 1)
                sy_start = (mg2 == 0 and mc2 == 0)
                sy_stop = (mg2 == NMG - 1 and mc2 == 1)
                t.matmul(PS[:, half2 * 512:(half2 + 1) * 512],
                         ONES[:, :],
                         YOB[mg2 % 2][:, mc2 * T + half2 * 512: mc2 * T + half2 * 512 + 512],
                         start=sy_start, stop=sy_stop)
                ins2 = t.matmul(PS[:, 1024 + half2 * 512:1024 + (half2 + 1) * 512],
                                ONES[:, :], SQY[pc2 % 2][:, :],
                                start=sy_start, stop=sy_stop)
                ins2.then_inc(sp.h, 1)

            for mg in range(NMG):
                t.wait_ge(wd.h, wd_wo(mg))
                for half in range(2):
                    for mc in range(2):
                        pc = mg * NPC + half * 2 + mc
                        if pc >= 2:
                            t.wait_ge(ey.h, pc - 1)
                        pj = PS[:, 3072 + mc * 512:3072 + (mc + 1) * 512]
                        ch = 2 * mg + mc
                        for kc in range(NCH):
                            t.matmul(pj,
                                     WOB[mg % 2][:, kc * 256 + mc * 128: kc * 256 + (mc + 1) * 128],
                                     X[:, kc * T + half * 512: kc * T + half * 512 + 512],
                                     start=(kc == 0), stop=False)
                        for pp in range(4):
                            ins = t.matmul(pj,
                                           UPP[:, pp * 4096 + ch * 128: pp * 4096 + (ch + 1) * 128],
                                           HEB[:, pp * T + half * 512: pp * T + half * 512 + 512],
                                           start=False, stop=(pp == 3))
                        ins.then_inc(wp.h, 1)
                        if pc >= 2:
                            l5_stats(pc - 2)
            l5_stats(NMG * NPC - 2)
            l5_stats(NMG * NPC - 1)

    return p


def run_l1_ffn(inputs, static, r4):
    p = build_l1_ffn()
    cs4 = {}
    for b in range(2):
        cs4[b] = np.ascontiguousarray(np.concatenate(
            [r4[c]["cso"] for c in range(4 * b, 4 * b + 4)], axis=1))
    in_maps = []
    for c in range(NCORES):
        b = c // 4
        in_maps.append({
            "y3i": r4[c]["y3o"], "mu3i": r4[c]["mu3o"], "rs3i": r4[c]["rs3o"],
            "cs4": cs4[b],
            "worgi": static["worg_img"], "downi": static["down_img"],
            "uppi": static["upp_arr"], "keysi": static["keys_arr"],
            "wq2i": static["wq2_img"],
            "onesi": np.ones((128, 128), np.float32),
        })
    res = run_bass_kernel_spmd(p.nc, in_maps, core_ids=list(range(NCORES)))
    p.close()
    return res.results





MAGIC0 = 0x51A0C300


def _fused_chunks(core):
    b = core // 4
    r = core % 2
    ch = [0, 3, 4, 7] if r == 0 else [1, 2, 5, 6]
    return b, ch


def _fused_toks(core, half):
    b, ch = _fused_chunks(core)
    cc = ch[2 * half:2 * half + 2]
    return b, np.concatenate([np.arange(c * 512, (c + 1) * 512) for c in cc]), cc


def _fused_out_half(core):
    p = (core % 4) // 2
    return 0 if p == 0 else 1


def _sw(eng_parent, body):
    for arm in eng_parent.switch_core_id(n=128):
        body(arm.logical % 8)


def emit_flag_poll(nc, regs3, FSB, FLG, nbar, r, bar):
    f, c, nr = regs3
    nc.sync.dma_start(out=FLG[0:1, nbar * 2 + r:nbar * 2 + r + 1],
                      in_=FSB[0:1, nbar:nbar + 1]).then_inc(bar.h, 16)
    nc.sync.wait_ge(bar.h, bar.n + 16)
    col = nbar * 2 + (1 - r)
    nc.sync.load(nr, FSB[0:1, nbar:nbar + 1])
    nc.sync.load(f, FLG[0:1, col:col + 1])
    nc.sync.reg_alu(c, f, nr, ALU.not_equal)
    with nc.sync.While(c):
        nc.sync.load(f, FLG[0:1, col:col + 1])
        nc.sync.reg_alu(c, f, nr, ALU.not_equal)


class ZS:
    __slots__ = ("h", "b", "sem")

    def __init__(self, sem):
        self.sem = sem
        self.h = sem.h
        self.b = sem.n


def zpool(p, names):
    return {nm: ZS(p.sem("z" + nm)) for nm in names}


def zclose(p, Z, totals):
    for nm, t in totals.items():
        p.sems["z" + nm].n = Z[nm].b + t


def emit_fence(p, sync_waits=()):
    nc = p.nc
    zf = p.sem("zf")
    zdv = p.sem("zdv")
    FLU = p.misc["FLU"]
    ndv = p.misc.setdefault("z_ndv", [0])
    k = p.misc.get("zf_k", 0) + 1
    p.misc["zf_k"] = k

    with nc.Block() as blk:
        @blk.gpsimd
        def _(g):
            g.nop().then_inc(zf.h, 1)
            g.wait_ge(zf.h, 5 * k)

        @blk.tensor
        def _(t):
            t.nop().then_inc(zf.h, 1)
            t.wait_ge(zf.h, 5 * k)

        @blk.vector
        def _(v):
            v.tensor_copy(FLU[0:1, 0:2], FLU[0:1, 2:4]).then_inc(zdv.h, 1)
            ndv[0] += 1
            v.wait_ge(zdv.h, ndv[0])
            v.nop().then_inc(zf.h, 1)
            v.wait_ge(zf.h, 5 * k)

        @blk.scalar
        def _(s):
            s.nop().then_inc(zf.h, 1)
            s.wait_ge(zf.h, 5 * k)

        @blk.sync
        def _(sy):
            for (h, vv) in sync_waits:
                sy.wait_ge(h, vv)
            sy.nop().then_inc(zf.h, 1)
            sy.wait_ge(zf.h, 5 * k)
    zf.n = 5 * k


F_SEMS = ["wd", "qd", "nl", "xf", "pl", "el", "pg", "sl", "dt", "hc", "t2",
          "e2", "wp", "ey", "es", "sp", "fd", "fa", "n2", "pe", "ev", "sq",
          "pn", "dn", "so"]
F_TOTALS = {"wd": 4480, "qd": 544, "nl": 32, "xf": 8, "pl": 2, "el": 2,
            "pg": 88, "sl": 88, "dt": 88, "hc": 88, "t2": 88, "e2": 1,
            "wp": 64, "ey": 64, "es": 64, "sp": 64, "fd": 1, "fa": 1,
            "n2": 32, "pe": 112, "ev": 112, "sq": 80, "pn": 80, "dn": 80,
            "so": 928}


def emit_ff(p, half, io):
    nc = p.nc
    y_h = io["y1sc"][half]
    mu_h = io["mu1sc"][half]
    rs_h = io["rs1sc"][half]
    w1_h, w3_h, w2_h = io["w1i"], io["w3i"], io["w2i"]
    a1_h, a3_h = io["a1i"], io["a3i"]
    b1_h, b3_h = io["b1i"], io["b3i"]
    a2_h, b2_h = io["a2i"], io["b2i"]
    wq1_h, wk1_h, wv1_h = io["wq1i"], io["wk1i"], io["wv1i"]
    ones_h = io["onesi"]
    x2_h = io["x2sc"][half]
    q1_h = io["q1sc"][half]
    k1_h = io["k1sc"][half]
    v1_h = io["v1sc"][half]
    qn_h = io["q1nsc"][half]
    kn_h = io["k1nsc"][half]
    FLU, PS = p.misc["FLU"], p.misc["PS"]

    Z = zpool(p, F_SEMS)
    wd = Z["wd"]; qd = Z["qd"]; nl = Z["nl"]; xf = Z["xf"]; pl = Z["pl"]
    el = Z["el"]; pg = Z["pg"]; sl = Z["sl"]; dt = Z["dt"]; hc = Z["hc"]
    t2 = Z["t2"]; e2 = Z["e2"]; wp = Z["wp"]; ey = Z["ey"]; es = Z["es"]
    sp = Z["sp"]; fd = Z["fd"]; fa = Z["fa"]; n2 = Z["n2"]; pe = Z["pe"]
    ev = Z["ev"]; sq = Z["sq"]; pn = Z["pn"]; dn = Z["dn"]; so = Z["so"]
    dv = p.sem("zdv")
    NDV = p.misc.setdefault("z_ndv", [0])

    def wd_stripe(i):
        fc = i // 4
        return 16 * (4 + 2 * (fc + 1) + 2 * (i + 1))

    def wd_w2(mg):
        return 16 * (4 + 2 * 22 + 2 * NSTR + 2 * (mg + 1))

    WD_B0 = 4 + 2 * 22 + 2 * NSTR + 2 * 16

    def wd_wq1(i):
        return 16 * (WD_B0 + i + 1)

    def qd_y(c):
        return 16 * (2 + c + 1)

    def pe_after(i):
        return (i + 1) * 4 if i < 20 else 80 + (i - 19) * 8

    def ev_v(mb):
        return 80 + 8 * (mb + 1)

    def so_qkv(i):
        return 16 * (NCH + i + 1)

    NPC = 4

    import contextlib
    sH = contextlib.ExitStack()
    X = sH.enter_context(nc.sbuf_tensor(f"f{half}_{half}_X", [128, NCH * T], BF16))
    ONES = sH.enter_context(nc.sbuf_tensor(f"f{half}_{half}_ONES", [128, 128], BF16))
    sA = contextlib.ExitStack()
    HF = sA.enter_context(nc.sbuf_tensor(f"f{half}_{half}_HF", [128, 22 * T], BF16))
    XF = sA.enter_context(nc.sbuf_tensor(f"f{half}_{half}_XF", [128, 8 * T], BF16))
    YC = sA.enter_context(nc.sbuf_tensor(f"f{half}_{half}_YC", [128, T], BF16))
    MUB = sA.enter_context(nc.sbuf_tensor(f"f{half}_{half}_MUB", [128, T], F32))
    RSB = sA.enter_context(nc.sbuf_tensor(f"f{half}_{half}_RSB", [128, T], F32))
    SCR = sA.enter_context(nc.sbuf_tensor(f"f{half}_{half}_SCR", [128, T], F32))
    WF1 = [sA.enter_context(nc.sbuf_tensor(f"f{half}_WF1{i}", [128, 1024], BF16)) for i in range(2)]
    WF3 = [sA.enter_context(nc.sbuf_tensor(f"f{half}_WF3{i}", [128, 1024], BF16)) for i in range(2)]
    W2B = [sA.enter_context(nc.sbuf_tensor(f"f{half}_W2B{i}", [128, 22 * 256], BF16)) for i in range(2)]
    A1 = sA.enter_context(nc.sbuf_tensor(f"f{half}_{half}_A1", [128, 64], BF16))
    A3 = sA.enter_context(nc.sbuf_tensor(f"f{half}_{half}_A3", [128, 64], BF16))
    B1F = [sA.enter_context(nc.sbuf_tensor(f"f{half}_B1F{i}", [2, 512], BF16)) for i in range(2)]
    B3F = [sA.enter_context(nc.sbuf_tensor(f"f{half}_B3F{i}", [2, 512], BF16)) for i in range(2)]
    A2 = sA.enter_context(nc.sbuf_tensor(f"f{half}_{half}_A2", [128, NSTR * 2], BF16))
    B2F = [sA.enter_context(nc.sbuf_tensor(f"f{half}_B2F{i}", [2, 256], BF16)) for i in range(2)]
    T1 = sA.enter_context(nc.sbuf_tensor(f"f{half}_{half}_T1", [2, T], BF16))
    T3 = sA.enter_context(nc.sbuf_tensor(f"f{half}_{half}_T3", [2, T], BF16))
    T2B = sA.enter_context(nc.sbuf_tensor(f"f{half}_{half}_T2B", [2, T], BF16))
    SG = [sA.enter_context(nc.sbuf_tensor(f"f{half}_SG{i}", [128, T], BF16)) for i in range(2)]
    TS = [sA.enter_context(nc.sbuf_tensor(f"f{half}_TS{i}", [128, T], BF16)) for i in range(2)]
    SQY = [sA.enter_context(nc.sbuf_tensor(f"f{half}_SQY{i}", [128, 512], BF16)) for i in range(2)]
    MSQ = sA.enter_context(nc.sbuf_tensor(f"f{half}_{half}_MSQ", [128, T], F32))
    VARB = sA.enter_context(nc.sbuf_tensor(f"f{half}_{half}_VARB", [128, T], F32))
    SDB = sA.enter_context(nc.sbuf_tensor(f"f{half}_{half}_SDB", [128, T], F32))

    with nc.Block() as blk:
        @blk.gpsimd
        def _(g):
            for dst, srch in ((ONES, ones_h), (A1, a1_h), (A3, a3_h), (A2, a2_h)):
                g.dma_start(out=dst[:, :], in_=srch).then_inc(wd.h, 16)
            for fc in range(22):
                if fc >= 2:
                    g.wait_ge(pg.h, pg.b + (fc - 2) * 4 + 4)
                g.dma_start(out=B1F[fc % 2][:, :],
                            in_=b1_h[:, fc * 512:(fc + 1) * 512]).then_inc(wd.h, 16)
                g.dma_start(out=B3F[fc % 2][:, :],
                            in_=b3_h[:, fc * 512:(fc + 1) * 512]).then_inc(wd.h, 16)
                for s_ in range(4):
                    i = fc * 4 + s_
                    if i >= 2:
                        g.wait_ge(pg.h, pg.b + i - 1)
                    g.dma_start(out=WF1[i % 2][:, :], in_=w1_h[i]).then_inc(wd.h, 16)
                    g.dma_start(out=WF3[i % 2][:, :], in_=w3_h[i]).then_inc(wd.h, 16)
            for mg in range(16):
                if mg >= 2:
                    g.wait_ge(wp.h, wp.b + (mg - 2) * NPC + NPC)
                g.dma_start(out=W2B[mg % 2][:, :], in_=w2_h[mg]).then_inc(wd.h, 16)
                g.dma_start(out=B2F[mg % 2][:, :],
                            in_=b2_h[:, mg * 256:(mg + 1) * 256]).then_inc(wd.h, 16)

        @blk.sync
        def _(sy):
            sy.dma_start(out=MUB[:, :], in_=mu_h).then_inc(qd.h, 16)
            sy.dma_start(out=RSB[:, :], in_=rs_h).then_inc(qd.h, 16)
            for c in range(NCH):
                if c >= 1:
                    sy.wait_ge(nl.h, nl.b + c)
                sy.dma_start(out=YC[:, :],
                             in_=y_h[:, c * T:(c + 1) * T]).then_inc(qd.h, 16)
            for c in range(NCH):
                sy.wait_ge(n2.h, n2.b + c + 1)
                sy.dma_start(out=x2_h[:, c * T:(c + 1) * T],
                             in_=X[:, c * T:(c + 1) * T]).then_inc(so.h, 16)

        @blk.vector
        def _(v):
            def ssync(ins):
                ins.then_inc(dv.h, 1)
                NDV[0] += 1
                v.wait_ge(dv.h, NDV[0])

            def ms(semh, k=1):
                ssync(v.tensor_copy(FLU[0:1, 0:2], FLU[0:1, 2:4]))
                v.nop().then_inc(semh, k)

            v.wait_ge(qd.h, qd.b + 32)
            for c in range(NCH):
                v.wait_ge(qd.h, qd.b + qd_y(c))
                ssync(v.tensor_tensor(SCR[:, :], YC[:, :], MUB[:, :],
                                      op=ALU.subtract))
                ssync(v.tensor_tensor(X[:, c * T:(c + 1) * T], SCR[:, :], RSB[:, :],
                                      op=ALU.mult))
                ms(nl.h, 1)
            for j in range(8):
                ssync(v.tensor_tensor(SCR[:, :], X[:, j * T:(j + 1) * T],
                                      X[:, (8 + j) * T:(9 + j) * T], op=ALU.add))
                ssync(v.tensor_tensor(MSQ[:, :], X[:, (16 + j) * T:(17 + j) * T],
                                      X[:, (24 + j) * T:(25 + j) * T], op=ALU.add))
                ssync(v.tensor_tensor(XF[:, j * T:(j + 1) * T], SCR[:, :], MSQ[:, :],
                                      op=ALU.add))
                ms(xf.h, 1)
            for i in range(NSTR):
                fc = i // 4
                v.wait_ge(sl.h, sl.b + i + 1)
                if i >= 2:
                    v.wait_ge(t2.h, t2.b + i - 1)
                if i % 4 == 0:
                    ssync(v.tensor_tensor(HF[:, fc * T:(fc + 1) * T], SG[i % 2][:, :],
                                          PS[:, 1024:2048], op=ALU.mult))
                    ms(dt.h, 1)
                    ms(hc.h, 1)
                else:
                    ssync(v.tensor_tensor(TS[i % 2][:, :], SG[i % 2][:, :],
                                          PS[:, 1024:2048], op=ALU.mult))
                    ms(dt.h, 1)
                    v.wait_ge(t2.h, t2.b + i)
                    ssync(v.tensor_tensor(HF[:, fc * T:(fc + 1) * T], TS[i % 2][:, :],
                                          HF[:, fc * T:(fc + 1) * T], op=ALU.add))
                    ms(hc.h, 1)
            for mg in range(16):
                for mc in range(2):
                    ch = mg * 2 + mc
                    for half_ in range(2):
                        pc = mg * NPC + mc * 2 + half_
                        v.wait_ge(wp.h, wp.b + pc + 1)
                        col = ch * T + half_ * 512
                        ssync(v.tensor_tensor(X[:, col:col + 512],
                                              PS[:, (pc % 4) * 512:(pc % 4) * 512 + 512],
                                              X[:, col:col + 512], op=ALU.add))
                        ms(ey.h, 1)
            v.wait_ge(sp.h, sp.b + 16 * NPC)
            ssync(v.tensor_scalar(MUB[:, :], PS[:, 2048:3072], 1.0 / D, None,
                                  op0=ALU.mult))
            ssync(v.tensor_tensor(MSQ[:, :], MUB[:, :], MUB[:, :], op=ALU.mult))
            ssync(v.scalar_tensor_tensor(VARB[:, :], PS[:, 3072:4096], 1.0 / D,
                                         MSQ[:, :], op0=ALU.mult, op1=ALU.subtract))
            ssync(v.tensor_scalar(VARB[:, :], VARB[:, :], EPS, None, op0=ALU.add))
            ms(fd.h, 1)
            v.wait_ge(fa.h, fa.b + 1)
            ssync(v.reciprocal(RSB[:, :], SDB[:, :]))
            for c in range(NCH):
                ssync(v.tensor_tensor(SCR[:, :], X[:, c * T:(c + 1) * T], MUB[:, :],
                                      op=ALU.subtract))
                ssync(v.tensor_tensor(X[:, c * T:(c + 1) * T], SCR[:, :], RSB[:, :],
                                      op=ALU.mult))
                ms(n2.h, 1)

        @blk.scalar
        def _(s):
            s.wait_ge(pl.h, pl.b + 1)
            s.activation(T1[:, :], PS[0:2, 0:1024], AF.Copy, scale=0.5).then_inc(el.h, 1)
            s.wait_ge(pl.h, pl.b + 2)
            s.activation(T3[:, :], PS[0:2, 1024:2048], AF.Copy, scale=0.5).then_inc(el.h, 1)
            for i in range(NSTR):
                s.wait_ge(pg.h, pg.b + i + 1)
                if i >= 2:
                    s.wait_ge(dt.h, dt.b + i - 1)
                s.activation(SG[i % 2][:, :], PS[:, 0:1024], AF.Silu).then_inc(sl.h, 1)
            s.wait_ge(t2.h, t2.b + NSTR)
            s.activation(T2B[:, :], PS[0:2, 2048:3072], AF.Copy, scale=0.5).then_inc(e2.h, 1)
            for pc in range(16 * NPC):
                s.wait_ge(ey.h, ey.b + pc + 1)
                if pc >= 2:
                    s.wait_ge(sp.h, sp.b + pc - 1)
                mg, mc, half_ = pc // NPC, (pc % NPC) // 2, pc % 2
                ch = mg * 2 + mc
                col = ch * T + half_ * 512
                s.activation(SQY[pc % 2][:, :], X[:, col:col + 512],
                             AF.Square).then_inc(es.h, 1)
            s.wait_ge(fd.h, fd.b + 1)
            s.activation(SDB[:, :], VARB[:, :], AF.Sqrt).then_inc(fa.h, 1)

        @blk.tensor
        def _(t):
            def emit_a2(i):
                t.wait_ge(dt.h, dt.b + i + 1)
                fc = i // 4
                srcb = (HF[:, fc * T:(fc + 1) * T] if i % 4 == 0
                        else TS[i % 2][:, :])
                for half_ in range(2):
                    ins = t.matmul(PS[0:2, 2048 + half_ * 512:2048 + half_ * 512 + 512],
                                   A2[:, i * 2:(i + 1) * 2],
                                   srcb[:, half_ * 512:half_ * 512 + 512],
                                   start=(i == 0), stop=(i == NSTR - 1))
                ins.then_inc(t2.h, 1)

            t.wait_ge(nl.h, nl.b + NCH)
            t.wait_ge(wd.h, wd.b + 16 * 4)
            for reg, ab in ((0, A1), (1, A3)):
                for half_ in range(2):
                    for kc in range(NCH):
                        ins = t.matmul(PS[0:2, reg * 1024 + half_ * 512: reg * 1024 + half_ * 512 + 512],
                                       ab[:, kc * 2:(kc + 1) * 2],
                                       X[:, kc * T + half_ * 512: kc * T + half_ * 512 + 512],
                                       start=(kc == 0), stop=(kc == NCH - 1))
                ins.then_inc(pl.h, 1)
            t.wait_ge(xf.h, xf.b + 8)
            t.wait_ge(el.h, el.b + 2)
            for i in range(NSTR):
                fc, st_ = i // 4, i % 4
                t.wait_ge(wd.h, wd.b + wd_stripe(i))
                if i >= 1:
                    t.wait_ge(sl.h, sl.b + i)
                    t.wait_ge(dt.h, dt.b + i)
                for dstoff, wf, bb, tt in ((0, WF1[i % 2], B1F[fc % 2], T1),
                                           (1024, WF3[i % 2], B3F[fc % 2], T3)):
                    for half_ in range(2):
                        pv = PS[:, dstoff + half_ * 512: dstoff + half_ * 512 + 512]
                        for kc in range(8):
                            t.matmul(pv,
                                     wf[:, kc * 128:(kc + 1) * 128],
                                     XF[:, kc * T + half_ * 512: kc * T + half_ * 512 + 512],
                                     start=(kc == 0), stop=False)
                        ins = t.matmul(pv, bb[:, st_ * 128:(st_ + 1) * 128],
                                       tt[:, half_ * 512: half_ * 512 + 512],
                                       start=False, stop=True)
                ins.then_inc(pg.h, 1)
                if i >= 1:
                    emit_a2(i - 1)
            emit_a2(NSTR - 1)
            t.wait_ge(hc.h, hc.b + NSTR)
            t.wait_ge(e2.h, e2.b + 1)
            for mg in range(16):
                t.wait_ge(wd.h, wd.b + wd_w2(mg))
                for mc in range(2):
                    for half_ in range(2):
                        pc = mg * NPC + mc * 2 + half_
                        if pc >= 4:
                            t.wait_ge(ey.h, ey.b + pc - 3)
                        pv = PS[:, (pc % 4) * 512:(pc % 4) * 512 + 512]
                        for kc in range(22):
                            t.matmul(pv,
                                     W2B[mg % 2][:, kc * 256 + mc * 128: kc * 256 + (mc + 1) * 128],
                                     HF[:, kc * T + half_ * 512: kc * T + half_ * 512 + 512],
                                     start=(kc == 0), stop=False)
                        ins = t.matmul(pv, B2F[mg % 2][:, mc * 128:(mc + 1) * 128],
                                       T2B[:, half_ * 512: half_ * 512 + 512],
                                       start=False, stop=True)
                        ins.then_inc(wp.h, 1)
                        t.wait_ge(ey.h, ey.b + pc + 1)
                        t.wait_ge(es.h, es.b + pc + 1)
                        ch = mg * 2 + mc
                        col = ch * T + half_ * 512
                        sy_start = (mg == 0 and mc == 0)
                        sy_stop = (mg == 15 and mc == 1)
                        t.matmul(PS[:, 2048 + half_ * 512:2048 + (half_ + 1) * 512],
                                 ONES[:, :], X[:, col:col + 512],
                                 start=sy_start, stop=sy_stop)
                        ins = t.matmul(PS[:, 3072 + half_ * 512:3072 + (half_ + 1) * 512],
                                       ONES[:, :], SQY[pc % 2][:, :],
                                       start=sy_start, stop=sy_stop)
                        ins.then_inc(sp.h, 1)

    sA.close()
    sB = contextlib.ExitStack()
    WQ1 = [sB.enter_context(nc.sbuf_tensor(f"f{half}_WQ1{i}", [128, NCH * 256], BF16)) for i in range(2)]
    QB = [sB.enter_context(nc.sbuf_tensor(f"f{half}_QB{i}", [128, 2 * T], BF16)) for i in range(2)]
    VB = [sB.enter_context(nc.sbuf_tensor(f"f{half}_VB{i}", [128, 8 * 256], BF16)) for i in range(2)]
    SQB = sB.enter_context(nc.sbuf_tensor(f"f{half}_{half}_SQB", [128, 512], BF16))
    RMX = sB.enter_context(nc.sbuf_tensor(f"f{half}_{half}_RMX", [128, 2], F32))
    QNB = sB.enter_context(nc.sbuf_tensor(f"f{half}_{half}_QNB", [128, 32], F32))
    KNB = sB.enter_context(nc.sbuf_tensor(f"f{half}_{half}_KNB", [128, 8], F32))

    with nc.Block() as blk:
        @blk.gpsimd
        def _(g):
            g.wait_ge(n2.h, n2.b + NCH)
            for i in range(24):
                if i >= 2:
                    g.wait_ge(pe.h, pe.b + pe_after(i - 2))
                srch = (wq1_h[i] if i < 16 else
                        (wk1_h[i - 16] if i < 20 else wv1_h[i - 20]))
                g.dma_start(out=WQ1[i % 2][:, :], in_=srch).then_inc(wd.h, 16)

        @blk.sync
        def _(sy):
            for i in range(20):
                sy.wait_ge(ev.h, ev.b + (i + 1) * 4)
                dst = q1_h if i < 16 else k1_h
                mg = i if i < 16 else i - 16
                sy.dma_start(out=dst[:, mg * 2 * T:(mg + 1) * 2 * T],
                             in_=QB[i % 2][:, :]).then_inc(so.h, 16)
            for mb in range(4):
                sy.wait_ge(ev.h, ev.b + ev_v(mb))
                srcv = VB[mb % 2][:, :].rearrange("p (kt m) -> p kt m", m=256)
                dstv = v1_h.rearrange("p (kt c) -> p kt c", c=T)[:, :, mb * 256:(mb + 1) * 256]
                sy.dma_start(out=dstv, in_=srcv).then_inc(so.h, 16)
            sy.wait_ge(dn.h, dn.b + 64)
            sy.dma_start(out=qn_h, in_=QNB[:, :]).then_inc(so.h, 16)
            sy.wait_ge(dn.h, dn.b + 80)
            sy.dma_start(out=kn_h, in_=KNB[:, :]).then_inc(so.h, 16)
            sy.wait_ge(so.h, so.b + 16 * (NCH + 20 + 4 + 2))

        @blk.vector
        def _(v):
            def ssync(ins):
                ins.then_inc(dv.h, 1)
                NDV[0] += 1
                v.wait_ge(dv.h, NDV[0])

            def ms(semh, k=1):
                ssync(v.tensor_copy(FLU[0:1, 0:2], FLU[0:1, 2:4]))
                v.nop().then_inc(semh, k)

            for i in range(20):
                for u in range(4):
                    hl, half_ = u // 2, u % 2
                    par = u % 2
                    guni = i * 4 + u
                    v.wait_ge(ev.h, ev.b + (i + 1) * 4)
                    if guni >= 1:
                        v.wait_ge(pn.h, pn.b + guni)
                    colc = hl * T + half_ * 512
                    ssync(v.tensor_tensor(SQB[:, :], QB[i % 2][:, colc:colc + 512],
                                          QB[i % 2][:, colc:colc + 512], op=ALU.mult))
                    ms(sq.h, 1)
                    v.wait_ge(pn.h, pn.b + guni + 1)
                    npv = PS[:, 3072 + par * 512: 3072 + par * 512 + 512]
                    ssync(v.reduce_max(RMX[:, par:par + 1], npv,
                                       axis=mybir.AxisListType.X))
                    if i < 16:
                        hcol, HMX = i * 2 + hl, QNB
                    else:
                        hcol, HMX = (i - 16) * 2 + hl, KNB
                    if half_ == 0:
                        ssync(v.tensor_copy(HMX[:, hcol:hcol + 1], RMX[:, par:par + 1]))
                    else:
                        ssync(v.tensor_tensor(HMX[:, hcol:hcol + 1], RMX[:, par:par + 1],
                                              HMX[:, hcol:hcol + 1], op=ALU.max))
                    ms(dn.h, 1)

        @blk.scalar
        def _(s):
            for i in range(20):
                for half_ in range(2):
                    for mc in range(2):
                        piece = i * 4 + half_ * 2 + mc
                        bank = piece % 6
                        s.wait_ge(pe.h, pe.b + piece + 1)
                        if i >= 2:
                            s.wait_ge(so.h, so.b + so_qkv(i - 2))
                        s.copy(out=QB[i % 2][:, mc * T + half_ * 512: mc * T + half_ * 512 + 512],
                               in_=PS[:, bank * 512:(bank + 1) * 512]).then_inc(ev.h, 1)
            for mb in range(4):
                for kt in range(8):
                    s.wait_ge(pe.h, pe.b + 80 + mb * 8 + kt + 1)
                    if mb >= 2:
                        s.wait_ge(so.h, so.b + 16 * (NCH + 20 + mb - 1))
                    s.copy(out=VB[mb % 2][:, kt * 256:(kt + 1) * 256],
                           in_=PS[:, kt * 512:kt * 512 + 256]).then_inc(ev.h, 1)

        @blk.tensor
        def _(t):
            t.wait_ge(n2.h, n2.b + NCH)
            for i in range(20):
                t.wait_ge(wd.h, wd.b + wd_wq1(i))
                for half_ in range(2):
                    for mc in range(2):
                        piece = i * 4 + half_ * 2 + mc
                        bank = piece % 6
                        if piece - 6 >= 0:
                            t.wait_ge(ev.h, ev.b + piece - 6 + 1)
                        pv = PS[:, bank * 512:(bank + 1) * 512]
                        for kc in range(NCH):
                            ins = t.matmul(
                                pv,
                                WQ1[i % 2][:, kc * 256 + mc * 128: kc * 256 + (mc + 1) * 128],
                                X[:, kc * T + half_ * 512: kc * T + half_ * 512 + 512],
                                start=(kc == 0), stop=(kc == NCH - 1))
                        ins.then_inc(pe.h, 1)
                for u in range(4):
                    guni = i * 4 + u
                    par = u % 2
                    if guni - 2 >= 0:
                        t.wait_ge(dn.h, dn.b + guni - 2 + 1)
                    t.wait_ge(sq.h, sq.b + guni + 1)
                    npv = PS[:, 3072 + par * 512: 3072 + par * 512 + 512]
                    ins = t.matmul(npv, ONES[:, :], SQB[:, :], start=True, stop=True)
                    ins.then_inc(pn.h, 1)
            for mb in range(4):
                t.wait_ge(wd.h, wd.b + wd_wq1(20 + mb))
                if mb > 0:
                    t.wait_ge(ev.h, ev.b + ev_v(mb - 1))
                else:
                    t.wait_ge(ev.h, ev.b + 80)
                    t.wait_ge(dn.h, dn.b + 80)
                for kc in range(NCH):
                    for kt in range(8):
                        pv = PS[:, kt * 512:kt * 512 + 256]
                        ins = t.matmul(
                            pv,
                            X[:, kc * T + kt * 128: kc * T + (kt + 1) * 128],
                            WQ1[(20 + mb) % 2][:, kc * 256:(kc + 1) * 256],
                            start=(kc == 0), stop=(kc == NCH - 1))
                        if kc == NCH - 1:
                            ins.then_inc(pe.h, 1)

    sB.close()
    sH.close()
    zclose(p, Z, F_TOTALS)


def emit_scatter2(p, io, FSB, FLG):
    nc = p.nc
    bar = p.sem("barso")
    bard = p.sem("bardone")
    K1SH, V1SH, N1SH = io["k1sh"], io["v1sh"], io["n1sh"]

    import contextlib
    with nc.Block() as blk, contextlib.ExitStack() as regs:
        f = regs.enter_context(nc.sync.register("b1f"))
        c = regs.enter_context(nc.sync.register("b1c"))
        nr = regs.enter_context(nc.sync.register("b1n"))

        @blk.sync
        def _(sy):
            for arm in nc.sync.switch_core_id(n=128):
                rank = arm.logical % 8
                r = rank % 2
                n = 0
                for half in range(2):
                    ch = ([0, 3, 4, 7] if r == 0 else [1, 2, 5, 6])[2 * half:2 * half + 2]
                    for j, ac in enumerate(ch):
                        srck = io["k1sc"][half].rearrange("p (hd t) -> p hd t", t=T)[
                            :, :, j * 512:(j + 1) * 512]
                        dstk = K1SH.rearrange("p (hd s) -> p hd s", s=SEQ)[
                            :, :, ac * 512:(ac + 1) * 512]
                        nc.sync.dma_start(out=dstk, in_=srck).then_inc(bar.h, 16)
                        n += 16
                        srcv = io["v1sc"][half].rearrange("p (kt c) -> p kt c", c=T)[
                            :, j * 4:(j + 1) * 4, :]
                        dstv = V1SH.rearrange("p (kt c) -> p kt c", c=T)[
                            :, ac * 4:(ac + 1) * 4, :]
                        nc.sync.dma_start(out=dstv, in_=srcv).then_inc(bar.h, 16)
                        n += 16
                    slot = r * 2 + half
                    nc.sync.dma_start(out=N1SH[:, slot * 40:slot * 40 + 32],
                                      in_=io["q1nsc"][half]).then_inc(bar.h, 16)
                    n += 16
                    nc.sync.dma_start(out=N1SH[:, slot * 40 + 32:slot * 40 + 40],
                                      in_=io["k1nsc"][half]).then_inc(bar.h, 16)
                    n += 16
                nc.sync.wait_ge(bar.h, bar.n + n)
                nc.sync.dma_start(out=FLG[0:1, 2 + r:3 + r],
                                  in_=FSB[0:1, 1:2]).then_inc(bar.h, 16)
                nc.sync.wait_ge(bar.h, bar.n + n + 16)
                col = 2 + (1 - r)
                nc.sync.load(nr, FSB[0:1, 1:2])
                nc.sync.load(f, FLG[0:1, col:col + 1])
                nc.sync.reg_alu(c, f, nr, ALU.not_equal)
                with nc.sync.While(c):
                    nc.sync.load(f, FLG[0:1, col:col + 1])
                    nc.sync.reg_alu(c, f, nr, ALU.not_equal)
                nc.sync.nop().then_inc(bard.h, 1)
                nd = n
            bar.n += nd + 16
            bard.n += 1


B_SEMS = ["wd", "qd", "st", "ae", "pa", "ab", "mbd", "mba", "wp", "ey", "es",
          "sp", "fd", "fa", "cs", "yr", "so"]
B_TOTALS = {"wd": 816, "qd": 384, "st": 2048, "ae": 2048, "pa": 2048,
            "ab": 64, "mbd": 2, "mba": 1, "wp": 64, "ey": 65, "es": 64,
            "sp": 64, "fd": 3, "fa": 2, "cs": 32, "yr": 512, "so": 304}


def emit_fb(p, half, io):
    nc = p.nc
    x2i_h = io["x2sc"][half]
    q1_h = io["q1sc"][half]
    k1f_h = io["k1sh"]
    v1f_h = io["v1sh"]
    n1sh = io["n1sh"]
    ones_h = io["onesi"]
    wo_h = io["wo1i"]
    yo_h = io["y3sc"][half]
    mu_h = io["mu3sc"][half]
    rs_h = io["rs3sc"][half]
    cs_h = io["cssc"][half]
    FLU, PS = p.misc["FLU"], p.misc["PS"]

    Z = zpool(p, B_SEMS)
    wd = Z["wd"]; qd = Z["qd"]; st = Z["st"]; ae = Z["ae"]; pa = Z["pa"]
    ab = Z["ab"]; mbd = Z["mbd"]; mba = Z["mba"]; wp = Z["wp"]; ey = Z["ey"]
    es = Z["es"]; sp = Z["sp"]; fd = Z["fd"]; fa = Z["fa"]; cs = Z["cs"]
    yr = Z["yr"]; so = Z["so"]
    dv = p.sem("zdv")
    NDV = p.misc.setdefault("z_ndv", [0])
    bard = p.sem("bardone")

    NMG = 16
    NPC = 4
    SCALE1 = 1.0 / float(np.sqrt(128.0))

    def qd_g(g):
        return 16 * 3 * (g + 1)

    def wd_w(mg):
        return 16 * (3 + 3 * mg + 1)

    def wd_x2(mg):
        return 16 * (3 + 3 * mg + 3)

    def hq_pa(hq):
        return (hq + 1) * 32

    import contextlib
    sB = contextlib.ExitStack()

    def sb(name, shape, dtype=BF16):
        return sB.enter_context(nc.sbuf_tensor(f"b{half}_{name}", list(shape), dtype))

    AB = sb("AB", [128, NCH * T])
    KG = [sb(f"KG{i}", [128, SEQ]) for i in range(2)]
    VG = [sb(f"VG{i}", [128, 32 * 128]) for i in range(2)]
    QG = [sb(f"QG{i}", [128, 4 * T]) for i in range(2)]
    PT = [sb(f"PT{i}", [128, 512]) for i in range(4)]
    QN4 = sb("QN4", [128, 4 * 32], F32)
    KN4 = sb("KN4", [128, 4 * 8], F32)
    QMX = sb("QMX", [128, 32], F32)
    KMX = sb("KMX", [128, 8], F32)
    MBALL = sb("MBALL", [128, 32], F32)
    ONES = sb("ONES", [128, 128])
    RL = sb("RL", [128, 512], F32)
    WOB = [sb(f"WOB{i}", [128, NCH * 256]) for i in range(2)]
    X2C = [sb(f"X2C{i}", [128, T]) for i in range(2)]
    SQY = [sb(f"SQY{i}", [128, 512]) for i in range(2)]
    YOB = [sb(f"YOB{i}", [128, 2 * T]) for i in range(2)]
    YRC = [sb(f"YRC{i}", [128, T]) for i in range(2)]
    MU = sb("MU", [128, T], F32)
    MSQ = sb("MSQ", [128, T], F32)
    VARB = sb("VARB", [128, T], F32)
    SDB = sb("SDB", [128, T], F32)
    RSTD = sb("RSTD", [128, T], F32)
    SCR = sb("SCR", [128, T], F32)
    X3C = sb("X3C", [128, T])
    CS = sb("CS", [128, NCH], F32)

    with nc.Block() as blk:
        @blk.gpsimd
        def _(g):
            g.wait_ge(bard.h, 2)
            g.dma_start(out=ONES[:, :], in_=ones_h).then_inc(wd.h, 16)
            nsrc = n1sh.rearrange("p (s c) -> p s c", c=40)
            g.dma_start(out=QN4[:, :].rearrange("p (s c) -> p s c", c=32),
                        in_=nsrc[:, :, 0:32]).then_inc(wd.h, 16)
            g.dma_start(out=KN4[:, :].rearrange("p (s c) -> p s c", c=8),
                        in_=nsrc[:, :, 32:40]).then_inc(wd.h, 16)
            for mg in range(NMG):
                if mg >= 2:
                    g.wait_ge(wp.h, wp.b + (mg - 2) * NPC + NPC)
                g.dma_start(out=WOB[mg % 2][:, :], in_=wo_h[mg]).then_inc(wd.h, 16)
                for mc in range(2):
                    if mg >= 1:
                        g.wait_ge(ey.h, ey.b + (mg - 1) * NPC + NPC)
                    g.dma_start(out=X2C[mc][:, :],
                                in_=x2i_h[:, (2 * mg + mc) * T:(2 * mg + mc + 1) * T]
                                ).then_inc(wd.h, 16)

        @blk.sync
        def _(sy):
            for g in range(8):
                if g >= 2:
                    sy.wait_ge(pa.h, pa.b + hq_pa((g - 1) * 8 - 1 + 1))
                sy.dma_start(out=QG[g % 2][:, :],
                             in_=q1_h[:, 4 * g * T:(4 * g + 4) * T]).then_inc(qd.h, 16)
                sy.dma_start(out=KG[g % 2][:, :],
                             in_=k1f_h[:, g * SEQ:(g + 1) * SEQ]).then_inc(qd.h, 16)
                vsrc = v1f_h.rearrange("p (kt c) -> p kt c", c=T)[:, :, g * 128:(g + 1) * 128]
                sy.dma_start(out=VG[g % 2][:, :].rearrange("p (kt c) -> p kt c", c=128),
                             in_=vsrc).then_inc(qd.h, 16)
            for mg in range(NMG):
                sy.wait_ge(ey.h, ey.b + mg * NPC + NPC)
                sy.dma_start(out=yo_h[:, 2 * mg * T:(2 * mg + 2) * T],
                             in_=YOB[mg % 2][:, :]).then_inc(so.h, 16)
            sy.wait_ge(fd.h, fd.b + 3)
            sy.dma_start(out=mu_h, in_=MU[:, :]).then_inc(so.h, 16)
            sy.dma_start(out=rs_h, in_=RSTD[:, :]).then_inc(so.h, 16)
            sy.wait_ge(so.h, so.b + 16 * (NMG + 2))
            for c in range(NCH):
                if c >= 2:
                    sy.wait_ge(cs.h, cs.b + c - 1)
                sy.dma_start(out=YRC[c % 2][:, :],
                             in_=yo_h[:, c * T:(c + 1) * T]).then_inc(yr.h, 16)
            sy.wait_ge(cs.h, cs.b + NCH)
            sy.dma_start(out=cs_h, in_=CS[:, :]).then_inc(so.h, 16)
            sy.wait_ge(so.h, so.b + 16 * (NMG + 3))

        @blk.vector
        def _(v):
            def ssync(ins):
                ins.then_inc(dv.h, 1)
                NDV[0] += 1
                v.wait_ge(dv.h, NDV[0])

            def ms(semh, k=1):
                ssync(v.tensor_copy(FLU[0:1, 0:2], FLU[0:1, 2:4]))
                v.nop().then_inc(semh, k)

            v.wait_ge(wd.h, wd.b + 16 * 3)
            for dst, srcb, n in ((QMX, QN4, 32), (KMX, KN4, 8)):
                ssync(v.tensor_copy(dst[:, :], srcb[:, 0:n]))
                for j in range(1, 4):
                    ssync(v.tensor_tensor(dst[:, :], srcb[:, j * n:(j + 1) * n],
                                          dst[:, :], op=ALU.max))
            for h in range(32):
                ssync(v.tensor_tensor(MSQ[:, h:h + 1], QMX[:, h:h + 1],
                                      KMX[:, h // 4:h // 4 + 1], op=ALU.mult))
            ms(mbd.h, 1)
            v.wait_ge(mba.h, mba.b + 1)
            ssync(v.tensor_scalar(MBALL[:, :], SDB[:, 0:32], -1.01 * SCALE1, None,
                                  op0=ALU.mult))
            ms(mbd.h, 1)
            for hq in range(64):
                v.wait_ge(pa.h, pa.b + hq_pa(hq))
                ssync(v.reciprocal(RL[:, :], PS[:, 2560:3072]))
                ch = hq // 2
                qb = hq % 2
                ssync(v.tensor_tensor(AB[:, ch * T + qb * 512: ch * T + qb * 512 + 512],
                                      PS[:, 2048:2560], RL[:, :], op=ALU.mult))
                ms(ab.h, 1)
            for mg in range(NMG):
                if mg >= 2:
                    v.wait_ge(so.h, so.b + 16 * (mg - 1))
                for half_ in range(2):
                    for mc in range(2):
                        pc = mg * NPC + half_ * 2 + mc
                        v.wait_ge(wp.h, wp.b + pc + 1)
                        v.wait_ge(wd.h, wd.b + wd_x2(mg))
                        ssync(v.tensor_tensor(YOB[mg % 2][:, mc * T + half_ * 512: mc * T + half_ * 512 + 512],
                                              PS[:, 3072 + mc * 512:3072 + (mc + 1) * 512],
                                              X2C[mc][:, half_ * 512:half_ * 512 + 512],
                                              op=ALU.add))
                        ms(ey.h, 1)
            ms(ey.h, 1)
            v.wait_ge(sp.h, sp.b + NMG * NPC)
            ssync(v.tensor_scalar(MU[:, :], PS[:, 0:1024], 1.0 / D, None, op0=ALU.mult))
            ssync(v.tensor_tensor(MSQ[:, :], MU[:, :], MU[:, :], op=ALU.mult))
            ssync(v.scalar_tensor_tensor(VARB[:, :], PS[:, 1024:2048], 1.0 / D,
                                         MSQ[:, :], op0=ALU.mult, op1=ALU.subtract))
            ssync(v.tensor_scalar(VARB[:, :], VARB[:, :], EPS, None, op0=ALU.add))
            ms(fd.h, 1)
            v.wait_ge(fa.h, fa.b + 2)
            ssync(v.reciprocal(RSTD[:, :], SDB[:, :]))
            ms(fd.h, 2)
            for c in range(NCH):
                v.wait_ge(yr.h, yr.b + 16 * (c + 1))
                ssync(v.tensor_tensor(SCR[:, :], YRC[c % 2][:, :], MU[:, :],
                                      op=ALU.subtract))
                ssync(v.tensor_tensor(X3C[:, :], SCR[:, :], RSTD[:, :], op=ALU.mult))
                ssync(v.reduce_sum(CS[:, c:c + 1], X3C[:, :], axis=mybir.AxisListType.X))
                ms(cs.h, 1)

        @blk.scalar
        def _(s):
            s.wait_ge(mbd.h, mbd.b + 1)
            s.activation(SDB[:, 0:32], MSQ[:, 0:32], AF.Sqrt).then_inc(mba.h, 1)
            for hq in range(64):
                g, r, qb = hq // 8, (hq % 8) // 2, hq % 2
                for j in range(32):
                    su = hq * 32 + j
                    s.wait_ge(st.h, st.b + su + 1)
                    if su >= 4:
                        s.wait_ge(pa.h, pa.b + su - 3)
                    if su == 0:
                        s.wait_ge(mbd.h, mbd.b + 2)
                    s.activation(PT[j % 4][:, :], PS[:, (j % 4) * 512:(j % 4) * 512 + 512],
                                 AF.Exp, bias=MBALL[:, g * 4 + r:g * 4 + r + 1],
                                 scale=SCALE1).then_inc(ae.h, 1)
            for mg in range(NMG):
                for half_ in range(2):
                    for mc in range(2):
                        pc = mg * NPC + half_ * 2 + mc
                        s.wait_ge(ey.h, ey.b + pc + 2)
                        if pc >= 2:
                            s.wait_ge(sp.h, sp.b + pc - 1)
                        s.activation(SQY[pc % 2][:, :],
                                     YOB[mg % 2][:, mc * T + half_ * 512: mc * T + half_ * 512 + 512],
                                     AF.Square).then_inc(es.h, 1)
            s.wait_ge(fd.h, fd.b + 1)
            s.activation(SDB[:, :], VARB[:, :], AF.Sqrt).then_inc(fa.h, 2)

        @blk.tensor
        def _(t):
            for hq in range(64):
                g, r, qb = hq // 8, (hq % 8) // 2, hq % 2
                t.wait_ge(qd.h, qd.b + qd_g(g))
                if hq >= 1:
                    t.wait_ge(ab.h, ab.b + hq)

                def b_s(j):
                    su = hq * 32 + j
                    if su >= 4:
                        t.wait_ge(ae.h, ae.b + su - 3)
                    stp = PS[:, (j % 4) * 512:(j % 4) * 512 + 512]
                    ins = t.matmul(stp, KG[g % 2][:, j * 128:(j + 1) * 128],
                                   QG[g % 2][:, r * T + qb * 512: r * T + qb * 512 + 512],
                                   start=True, stop=True)
                    ins.then_inc(st.h, 1)

                def b_pv(j):
                    su = hq * 32 + j
                    t.wait_ge(ae.h, ae.b + su + 1)
                    t.matmul(PS[:, 2048:2560], VG[g % 2][:, j * 128:(j + 1) * 128],
                             PT[j % 4][:, :], start=(j == 0), stop=(j == 31))
                    ins = t.matmul(PS[:, 2560:3072], ONES[:, :], PT[j % 4][:, :],
                                   start=(j == 0), stop=(j == 31))
                    ins.then_inc(pa.h, 1)

                for j in range(32):
                    b_s(j)
                    if j >= 2:
                        b_pv(j - 2)
                b_pv(30)
                b_pv(31)
            t.wait_ge(ab.h, ab.b + 64)

            def l4_stats(pc2):
                mg2, half2, mc2 = pc2 // NPC, (pc2 % NPC) // 2, pc2 % 2
                t.wait_ge(ey.h, ey.b + pc2 + 1)
                t.wait_ge(es.h, es.b + pc2 + 1)
                sy_start = (mg2 == 0 and mc2 == 0)
                sy_stop = (mg2 == NMG - 1 and mc2 == 1)
                t.matmul(PS[:, 0 + half2 * 512:(half2 + 1) * 512],
                         ONES[:, :],
                         YOB[mg2 % 2][:, mc2 * T + half2 * 512: mc2 * T + half2 * 512 + 512],
                         start=sy_start, stop=sy_stop)
                ins2 = t.matmul(PS[:, 1024 + half2 * 512:1024 + (half2 + 1) * 512],
                                ONES[:, :], SQY[pc2 % 2][:, :],
                                start=sy_start, stop=sy_stop)
                ins2.then_inc(sp.h, 1)

            for mg in range(NMG):
                t.wait_ge(wd.h, wd.b + wd_w(mg))
                for half_ in range(2):
                    for mc in range(2):
                        pc = mg * NPC + half_ * 2 + mc
                        if pc >= 2:
                            t.wait_ge(ey.h, ey.b + pc - 1)
                        pj = PS[:, 3072 + mc * 512:3072 + (mc + 1) * 512]
                        for kc in range(NCH):
                            ins = t.matmul(
                                pj,
                                WOB[mg % 2][:, kc * 256 + mc * 128: kc * 256 + (mc + 1) * 128],
                                AB[:, kc * T + half_ * 512: kc * T + half_ * 512 + 512],
                                start=(kc == 0), stop=(kc == NCH - 1))
                        ins.then_inc(wp.h, 1)
                        if pc >= 2:
                            l4_stats(pc - 2)
            l4_stats(NMG * NPC - 2)
            l4_stats(NMG * NPC - 1)

    sB.close()
    zclose(p, Z, B_TOTALS)


def emit_scatter3(p, io, FSB, FLG):
    nc = p.nc
    bar = p.sem("barso")
    bard = p.sem("bardone")
    CSSH = io["cssh"]

    import contextlib
    with nc.Block() as blk, contextlib.ExitStack() as regs:
        f = regs.enter_context(nc.sync.register("b2f"))
        c = regs.enter_context(nc.sync.register("b2c"))
        nr = regs.enter_context(nc.sync.register("b2n"))

        @blk.sync
        def _(sy):
            for arm in nc.sync.switch_core_id(n=128):
                rank = arm.logical % 8
                r = rank % 2
                oh = 0 if (rank % 4) // 2 == 0 else 1
                for half in range(2):
                    slot = r * 2 + half
                    nc.sync.dma_start(out=CSSH[:, slot * NCH:(slot + 1) * NCH],
                                      in_=io["cssc"][half]).then_inc(bar.h, 16)
                nc.sync.dma_start(out=io["y3out"], in_=io["y3sc"][oh]).then_inc(bar.h, 16)
                nc.sync.dma_start(out=io["mu3out"], in_=io["mu3sc"][oh]).then_inc(bar.h, 16)
                nc.sync.dma_start(out=io["rs3out"], in_=io["rs3sc"][oh]).then_inc(bar.h, 16)
                nc.sync.wait_ge(bar.h, bar.n + 80)
                nc.sync.dma_start(out=FLG[0:1, 4 + r:5 + r],
                                  in_=FSB[0:1, 2:3]).then_inc(bar.h, 16)
                nc.sync.wait_ge(bar.h, bar.n + 96)
                col = 4 + (1 - r)
                nc.sync.load(nr, FSB[0:1, 2:3])
                nc.sync.load(f, FLG[0:1, col:col + 1])
                nc.sync.reg_alu(c, f, nr, ALU.not_equal)
                with nc.sync.While(c):
                    nc.sync.load(f, FLG[0:1, col:col + 1])
                    nc.sync.reg_alu(c, f, nr, ALU.not_equal)
                nc.sync.nop().then_inc(bard.h, 1)
            bar.n += 96
            bard.n += 1


C_SEMS = ["wd", "qd", "nl", "cfd", "cfa", "cfp", "ph", "he", "wp", "ey", "es",
          "sp", "fd", "fa", "yr", "ox", "so"]
C_TOTALS = {"wd": 512, "qd": 544, "nl": 32, "cfd": 3, "cfa": 3, "cfp": 10,
            "ph": 8, "he": 8, "wp": 64, "ey": 65, "es": 64, "sp": 64,
            "fd": 2, "fa": 1, "yr": 512, "ox": 32, "so": 768}


def emit_fc(p, io):
    nc = p.nc
    y_h = io["y3out"]
    mu_h = io["mu3out"]
    rs_h = io["rs3out"]
    cs_h = io["cssh"]
    wo_h = io["worgi"]
    dw_h = io["downi"]
    up_h = io["uppi"]
    ke_h = io["keysi"]
    wq_h = io["wq2i"]
    ones_h = io["onesi"]
    out_h = io["outo"]
    y4d = io["y4d"]
    FLU, PS = p.misc["FLU"], p.misc["PS"]

    Z = zpool(p, C_SEMS)
    wd = Z["wd"]; qd = Z["qd"]; nl = Z["nl"]; cf_d = Z["cfd"]; cf_a = Z["cfa"]
    cf_p = Z["cfp"]; ph = Z["ph"]; he = Z["he"]; wp = Z["wp"]; ey = Z["ey"]
    es = Z["es"]; sp = Z["sp"]; fd = Z["fd"]; fa = Z["fa"]; yr = Z["yr"]
    ox = Z["ox"]; so = Z["so"]
    dv = p.sem("zdv")
    NDV = p.misc.setdefault("z_ndv", [0])
    bard = p.sem("bardone")

    NMG = 16
    NPC = 4
    SCALE1 = 1.0 / float(np.sqrt(128.0))

    def wd_dw(e):
        return 16 * (8 + e + 1)

    def wd_wo(mg):
        return 16 * (8 + 8 + mg + 1)

    def qd_y(c):
        return 16 * (2 + c + 1)

    import contextlib
    sC = contextlib.ExitStack()

    def sb(name, shape, dtype=BF16):
        return sC.enter_context(nc.sbuf_tensor(f"c_{name}", list(shape), dtype))

    X = sb("X", [128, NCH * T])
    ONES = sb("ONES", [128, 128])
    YC = [sb(f"YC{i}", [128, T]) for i in range(2)]
    MUB = sb("MUB", [128, T], F32)
    RSB = sb("RSB", [128, T], F32)
    SCR = sb("SCR", [128, T], F32)
    CSA = sb("CSA", [128, 4 * NCH], F32)
    CSUM = sb("CSUM", [128, NCH], F32)
    XM = sb("XM", [128, NCH])
    WQ2 = sb("WQ2", [128, NCH * 128])
    KEYSB = sb("KEYSB", [128, 8])
    QS = sb("QS", [128, 1])
    MX1 = sb("MX1", [1, 1], F32)
    MB1 = sb("MB1", [1, 1], F32)
    EX1 = sb("EX1", [1, 8], F32)
    SM1 = sb("SM1", [1, 1], F32)
    RS1 = sb("RS1", [1, 1], F32)
    CF = sb("CF", [1, 8], F32)
    CFB = sb("CFB", [1, 8])
    CEB = sb("CEB", [64, 8], F32)
    DWN = [sb(f"DWN{i}", [128, NCH * 64]) for i in range(2)]
    UPP = sb("UPP", [128, 4 * 4096])
    HEB = sb("HEB", [128, 4 * T])
    WOB = [sb(f"WOB{i}", [128, NCH * 256]) for i in range(2)]
    YOB = [sb(f"YOB{i}", [128, 2 * T]) for i in range(2)]
    YRC = [sb(f"YRC{i}", [128, T]) for i in range(2)]
    SQY = [sb(f"SQY{i}", [128, 512]) for i in range(2)]
    MSQ = sb("MSQ", [128, T], F32)
    VARB = sb("VARB", [128, T], F32)
    SDB = sb("SDB", [128, T], F32)
    OUTC = [sb(f"OUTC{i}", [128, T], F32) for i in range(2)]

    with nc.Block() as blk:
        @blk.gpsimd
        def _(g):
            g.wait_ge(bard.h, 3)
            g.dma_start(out=ONES[:, :], in_=ones_h).then_inc(wd.h, 16)
            g.dma_start(out=CSA[:, :], in_=cs_h).then_inc(wd.h, 16)
            g.dma_start(out=KEYSB[:, :], in_=ke_h).then_inc(wd.h, 16)
            g.dma_start(out=WQ2[:, :], in_=wq_h).then_inc(wd.h, 16)
            for pp in range(4):
                g.dma_start(out=UPP[:, pp * 4096:(pp + 1) * 4096],
                            in_=up_h[pp]).then_inc(wd.h, 16)
            for e in range(8):
                if e >= 2:
                    g.wait_ge(ph.h, ph.b + e - 1)
                g.dma_start(out=DWN[e % 2][:, :], in_=dw_h[e]).then_inc(wd.h, 16)
            for mg in range(NMG):
                if mg >= 2:
                    g.wait_ge(wp.h, wp.b + (mg - 2) * NPC + NPC)
                g.dma_start(out=WOB[mg % 2][:, :], in_=wo_h[mg]).then_inc(wd.h, 16)

        @blk.sync
        def _(sy):
            sy.dma_start(out=MUB[:, :], in_=mu_h).then_inc(qd.h, 16)
            sy.dma_start(out=RSB[:, :], in_=rs_h).then_inc(qd.h, 16)
            for c in range(NCH):
                if c >= 2:
                    sy.wait_ge(nl.h, nl.b + c - 1)
                sy.dma_start(out=YC[c % 2][:, :],
                             in_=y_h[:, c * T:(c + 1) * T]).then_inc(qd.h, 16)
            for mg in range(NMG):
                sy.wait_ge(ey.h, ey.b + mg * NPC + NPC)
                sy.dma_start(out=y4d[:, 2 * mg * T:(2 * mg + 2) * T],
                             in_=YOB[mg % 2][:, :]).then_inc(so.h, 16)
            sy.wait_ge(so.h, so.b + 16 * NMG)
            for c in range(NCH + 2):
                if c < NCH:
                    if c >= 2:
                        sy.wait_ge(ox.h, ox.b + c - 1)
                    sy.dma_start(out=YRC[c % 2][:, :],
                                 in_=y4d[:, c * T:(c + 1) * T]).then_inc(yr.h, 16)
                if c >= 2:
                    sy.wait_ge(ox.h, ox.b + c - 1)
                    sy.dma_start(out=out_h[:, (c - 2) * T:(c - 1) * T],
                                 in_=OUTC[(c - 2) % 2][:, :]).then_inc(so.h, 16)
            sy.wait_ge(so.h, so.b + 16 * (NMG + NCH))

        @blk.vector
        def _(v):
            def ssync(ins):
                ins.then_inc(dv.h, 1)
                NDV[0] += 1
                v.wait_ge(dv.h, NDV[0])

            def ms(semh, k=1):
                ssync(v.tensor_copy(FLU[0:1, 0:2], FLU[0:1, 2:4]))
                v.nop().then_inc(semh, k)

            v.wait_ge(qd.h, qd.b + 32)
            for c in range(NCH):
                v.wait_ge(qd.h, qd.b + qd_y(c))
                ssync(v.tensor_tensor(SCR[:, :], YC[c % 2][:, :], MUB[:, :],
                                      op=ALU.subtract))
                ssync(v.tensor_tensor(X[:, c * T:(c + 1) * T], SCR[:, :], RSB[:, :],
                                      op=ALU.mult))
                ms(nl.h, 1)
            v.wait_ge(wd.h, wd.b + 16 * 2)
            ssync(v.tensor_copy(CSUM[:, :], CSA[:, 0:NCH]))
            for j in range(1, 4):
                ssync(v.tensor_tensor(CSUM[:, :], CSA[:, j * NCH:(j + 1) * NCH],
                                      CSUM[:, :], op=ALU.add))
            ssync(v.tensor_scalar(XM[:, :], CSUM[:, :], 1.0 / SEQ, None, op0=ALU.mult))
            ms(cf_d.h, 1)
            v.wait_ge(cf_p.h, cf_p.b + 2)
            ssync(v.reduce_max(MX1[:, :], PS[0:1, 2560:2568], axis=mybir.AxisListType.X))
            ssync(v.tensor_scalar(MB1[:, :], MX1[:, :], -SCALE1, None, op0=ALU.mult))
            ms(cf_d.h, 1)
            v.wait_ge(cf_a.h, cf_a.b + 2)
            ssync(v.reciprocal(RS1[:, :], SM1[:, :]))
            ssync(v.tensor_scalar(CF[:, :], EX1[:, :], RS1[:, :], None, op0=ALU.mult, op1=ALU.bypass))
            ms(cf_d.h, 1)
            for e in range(8):
                v.wait_ge(ph.h, ph.b + e + 1)
                v.wait_ge(cf_p.h, cf_p.b + 2 + e + 1)
                ssync(v.tensor_copy(CEB[:, e:e + 1], PS[0:64, 3072:3073]))
                ssync(v.tensor_scalar(HEB[(e % 2) * 64:(e % 2) * 64 + 64,
                                          (e // 2) * T:(e // 2) * T + T],
                                      PS[0:64, (e % 2) * 1024:(e % 2) * 1024 + 1024],
                                      CEB[:, e:e + 1], None, op0=ALU.mult))
                ms(he.h, 1)
            for mg in range(NMG):
                if mg >= 2:
                    v.wait_ge(so.h, so.b + 16 * (mg - 1))
                for half_ in range(2):
                    for mc in range(2):
                        pc = mg * NPC + half_ * 2 + mc
                        v.wait_ge(wp.h, wp.b + pc + 1)
                        col = (2 * mg + mc) * T + half_ * 512
                        ssync(v.tensor_tensor(YOB[mg % 2][:, mc * T + half_ * 512: mc * T + half_ * 512 + 512],
                                              PS[:, 3072 + mc * 512:3072 + (mc + 1) * 512],
                                              X[:, col:col + 512], op=ALU.add))
                        ms(ey.h, 1)
            ms(ey.h, 1)
            v.wait_ge(sp.h, sp.b + NMG * NPC)
            ssync(v.tensor_scalar(MUB[:, :], PS[:, 0:1024], 1.0 / D, None, op0=ALU.mult))
            ssync(v.tensor_tensor(MSQ[:, :], MUB[:, :], MUB[:, :], op=ALU.mult))
            ssync(v.scalar_tensor_tensor(VARB[:, :], PS[:, 1024:2048], 1.0 / D,
                                         MSQ[:, :], op0=ALU.mult, op1=ALU.subtract))
            ssync(v.tensor_scalar(VARB[:, :], VARB[:, :], EPS, None, op0=ALU.add))
            ms(fd.h, 1)
            v.wait_ge(fa.h, fa.b + 1)
            ssync(v.reciprocal(RSB[:, :], SDB[:, :]))
            ms(fd.h, 1)
            for c in range(NCH):
                v.wait_ge(yr.h, yr.b + 16 * (c + 1))
                if c >= 2:
                    v.wait_ge(so.h, so.b + 16 * (NMG + c - 1))
                ssync(v.tensor_tensor(SCR[:, :], YRC[c % 2][:, :], MUB[:, :],
                                      op=ALU.subtract))
                ssync(v.tensor_tensor(OUTC[c % 2][:, :], SCR[:, :], RSB[:, :],
                                      op=ALU.mult))
                ms(ox.h, 1)

        @blk.scalar
        def _(s):
            s.wait_ge(cf_p.h, cf_p.b + 1)
            s.activation(QS[:, :], PS[:, 2048:2049], AF.Copy).then_inc(cf_a.h, 1)
            s.wait_ge(cf_d.h, cf_d.b + 2)
            s.activation(EX1[:, :], PS[0:1, 2560:2568], AF.Exp, bias=MB1[:, :],
                         scale=SCALE1, accum_out=SM1[:, :]).then_inc(cf_a.h, 1)
            s.wait_ge(cf_d.h, cf_d.b + 3)
            s.activation(CFB[:, :], CF[:, :], AF.Copy).then_inc(cf_a.h, 1)
            for pc in range(NMG * NPC):
                s.wait_ge(ey.h, ey.b + pc + 2)
                if pc >= 2:
                    s.wait_ge(sp.h, sp.b + pc - 1)
                mg, mc, half_ = pc // NPC, (pc % NPC) // 2, pc % 2
                s.activation(SQY[pc % 2][:, :],
                             YOB[mg % 2][:, mc * T + half_ * 512: mc * T + half_ * 512 + 512],
                             AF.Square).then_inc(es.h, 1)
            s.wait_ge(fd.h, fd.b + 1)
            s.activation(SDB[:, :], VARB[:, :], AF.Sqrt).then_inc(fa.h, 1)

        @blk.tensor
        def _(t):
            t.wait_ge(cf_d.h, cf_d.b + 1)
            t.wait_ge(wd.h, wd.b + 16 * 4)
            for kc in range(NCH):
                ins = t.matmul(PS[:, 2048:2049], WQ2[:, kc * 128:(kc + 1) * 128],
                               XM[:, kc:kc + 1], start=(kc == 0), stop=(kc == NCH - 1))
            ins.then_inc(cf_p.h, 1)
            t.wait_ge(cf_a.h, cf_a.b + 1)
            t.matmul(PS[0:1, 2560:2568], QS[:, :], KEYSB[:, :],
                     start=True, stop=True).then_inc(cf_p.h, 1)
            t.wait_ge(cf_a.h, cf_a.b + 3)
            t.wait_ge(nl.h, nl.b + NCH)
            for e in range(8):
                if e >= 1:
                    t.wait_ge(he.h, he.b + e)
                ins = t.matmul(PS[0:64, 3072:3073], ONES[0:1, 0:64],
                               CFB[:, e:e + 1], start=True, stop=True)
                ins.then_inc(cf_p.h, 1)
                t.wait_ge(wd.h, wd.b + wd_dw(e))
                if e >= 2:
                    t.wait_ge(he.h, he.b + e - 1)
                for half_ in range(2):
                    pv = PS[0:64, (e % 2) * 1024 + half_ * 512:(e % 2) * 1024 + (half_ + 1) * 512]
                    for kc in range(NCH):
                        ins = t.matmul(pv, DWN[e % 2][:, kc * 64:(kc + 1) * 64],
                                       X[:, kc * T + half_ * 512: kc * T + half_ * 512 + 512],
                                       start=(kc == 0), stop=(kc == NCH - 1))
                ins.then_inc(ph.h, 1)
            t.wait_ge(he.h, he.b + 8)

            def l5_stats(pc2):
                mg2, half2, mc2 = pc2 // NPC, (pc2 % NPC) // 2, pc2 % 2
                t.wait_ge(ey.h, ey.b + pc2 + 1)
                t.wait_ge(es.h, es.b + pc2 + 1)
                sy_start = (mg2 == 0 and mc2 == 0)
                sy_stop = (mg2 == NMG - 1 and mc2 == 1)
                t.matmul(PS[:, half2 * 512:(half2 + 1) * 512],
                         ONES[:, :],
                         YOB[mg2 % 2][:, mc2 * T + half2 * 512: mc2 * T + half2 * 512 + 512],
                         start=sy_start, stop=sy_stop)
                ins2 = t.matmul(PS[:, 1024 + half2 * 512:1024 + (half2 + 1) * 512],
                                ONES[:, :], SQY[pc2 % 2][:, :],
                                start=sy_start, stop=sy_stop)
                ins2.then_inc(sp.h, 1)

            for mg in range(NMG):
                t.wait_ge(wd.h, wd.b + wd_wo(mg))
                for half_ in range(2):
                    for mc in range(2):
                        pc = mg * NPC + half_ * 2 + mc
                        if pc >= 2:
                            t.wait_ge(ey.h, ey.b + pc - 1)
                        pj = PS[:, 3072 + mc * 512:3072 + (mc + 1) * 512]
                        ch = 2 * mg + mc
                        for kc in range(NCH):
                            t.matmul(pj,
                                     WOB[mg % 2][:, kc * 256 + mc * 128: kc * 256 + (mc + 1) * 128],
                                     X[:, kc * T + half_ * 512: kc * T + half_ * 512 + 512],
                                     start=(kc == 0), stop=False)
                        for pp in range(4):
                            ins = t.matmul(pj,
                                           UPP[:, pp * 4096 + ch * 128: pp * 4096 + (ch + 1) * 128],
                                           HEB[:, pp * T + half_ * 512: pp * T + half_ * 512 + 512],
                                           start=False, stop=(pp == 3))
                        ins.then_inc(wp.h, 1)
                        if pc >= 2:
                            l5_stats(pc - 2)
            l5_stats(NMG * NPC - 2)
            l5_stats(NMG * NPC - 1)

    sC.close()
    zclose(p, Z, C_TOTALS)


def emit_fq(p, half, io, bufs):
    nc = p.nc
    x_h = io["x0i"][half]
    wq_h, wk_h, wv_h = io["wqi"], io["wki"], io["wvi"]
    cos_h, sin_h, ones_h = io["cosi"][half], io["sini"][half], io["onesi"]
    qo_h, ko_h, vo_h = io["qsc"][half], io["ksc"][half], io["vsc"][half]
    qn_h, kn_h = io["qnsc"][half], io["knsc"][half]

    (X, WB, QB, VB, CO, SI, SC, ONES, SQB, RMX, QNB, KNB, FLU, PS) = bufs

    wd = p.sem("q_wd"); pe = p.sem("q_pe"); ev = p.sem("q_ev"); rp = p.sem("q_rp")
    so = p.sem("q_so"); dv = p.sem("q_dv"); sq = p.sem("q_sq"); pn = p.sem("q_pn")
    dn = p.sem("q_dn")

    IN_DMAS = 4
    OWD = half * 16 * (IN_DMAS + 24)
    OPE = half * 192
    OEV = OPE
    ORP = half * 16
    OSQ = half * 128
    OPN = half * 64
    ODN = half * 64
    OSO = half * 16 * 26

    def wd_after(i):
        return OWD + 16 * (IN_DMAS + i + 1)

    def pe_qk(g, hf, mc):
        return OPE + g * 8 + hf * 4 + mc + 1

    def pe_v(mb, kt):
        return OPE + 16 * 8 + mb * 8 + kt + 1

    ev_qk = pe_qk

    def ev_v(mb, kt):
        return OEV + 16 * 8 + mb * 8 + kt + 1

    def so_qk(g):
        return OSO + 16 * (g + 1)

    def so_v(mb):
        return OSO + 16 * (16 + mb + 1) + 32

    with nc.Block() as blk:
        @blk.gpsimd
        def _(g):
            g.dma_start(out=X[:, :], in_=x_h).then_inc(wd.h, 16)
            g.dma_start(out=CO[:, :], in_=cos_h).then_inc(wd.h, 16)
            g.dma_start(out=SI[:, :], in_=sin_h).then_inc(wd.h, 16)
            g.dma_start(out=ONES[:, :], in_=ones_h).then_inc(wd.h, 16)
            srcs = [wq_h, wk_h]
            for i in range(24):
                if i >= 2:
                    if i - 2 < 16:
                        g.wait_ge(pe.h, pe_qk(i - 2, 1, 3))
                    else:
                        g.wait_ge(pe.h, pe_v(i - 2 - 16, 7))
                if i < 16:
                    srch = srcs[i // 8][i % 8]
                else:
                    srch = wv_h[i - 16]
                g.dma_start(out=WB[i % 2][:, :], in_=srch).then_inc(wd.h, 16)

        @blk.tensor
        def _(t):
            t.wait_ge(wd.h, OWD + 16 * IN_DMAS)
            for gq in range(16):
                t.wait_ge(wd.h, wd_after(gq))
                for hf in range(2):
                    for mc in range(4):
                        piece = gq * 8 + hf * 4 + mc
                        bank = piece % 6
                        if piece - 6 >= 0:
                            t.wait_ge(ev.h, OEV + piece - 6 + 1)
                        pv = PS[:, bank * 512:(bank + 1) * 512]
                        for kc in range(NCH):
                            ins = t.matmul(
                                pv,
                                WB[gq % 2][:, kc * 512 + mc * 128: kc * 512 + (mc + 1) * 128],
                                X[:, kc * T + hf * 512: kc * T + hf * 512 + 512],
                                start=(kc == 0), stop=(kc == NCH - 1))
                        ins.then_inc(pe.h, 1)
                for u in range(4):
                    par = u % 2
                    if gq * 4 + u - 2 >= 0:
                        t.wait_ge(dn.h, ODN + gq * 4 + u - 2 + 1)
                    npv = PS[:, 3072 + par * 512: 3072 + par * 512 + 512]
                    for c in range(2):
                        t.wait_ge(sq.h, OSQ + gq * 8 + u * 2 + c + 1)
                        ins = t.matmul(npv, ONES[:, :], SQB[c][:, :],
                                       start=(c == 0), stop=(c == 1))
                    ins.then_inc(pn.h, 1)
            for mb in range(8):
                t.wait_ge(wd.h, wd_after(16 + mb))
                if mb > 0:
                    t.wait_ge(ev.h, ev_v(mb - 1, 7))
                else:
                    t.wait_ge(ev.h, OEV + 128)
                    t.wait_ge(dn.h, ODN + 64)
                for kc in range(NCH):
                    for kt in range(8):
                        pv = PS[:, kt * 512:(kt + 1) * 512]
                        ins = t.matmul(
                            pv,
                            X[:, kc * T + kt * 128: kc * T + (kt + 1) * 128],
                            WB[(16 + mb) % 2][:, kc * 512:(kc + 1) * 512],
                            start=(kc == 0), stop=(kc == NCH - 1))
                        if kc == NCH - 1:
                            ins.then_inc(pe.h, 1)

        @blk.scalar
        def _(s):
            for gq in range(16):
                for hf in range(2):
                    for mc in range(4):
                        piece = gq * 8 + hf * 4 + mc
                        bank = piece % 6
                        s.wait_ge(pe.h, pe_qk(gq, hf, mc))
                        if gq >= 2:
                            s.wait_ge(so.h, so_qk(gq - 2))
                        pv = PS[:, bank * 512:(bank + 1) * 512]
                        s.copy(
                            out=QB[gq % 2][:, mc * T + hf * 512: mc * T + hf * 512 + 512],
                            in_=pv).then_inc(ev.h, 1)
                s.wait_ge(rp.h, ORP + gq + 1)
                for u in range(4):
                    hl, hf = u // 2, u % 2
                    guni = gq * 4 + u
                    if guni >= 1:
                        s.wait_ge(pn.h, OPN + guni)
                    for c in range(2):
                        colc = (2 * hl + c) * T + hf * 512
                        s.activation(SQB[c][:, :], QB[gq % 2][:, colc:colc + 512],
                                     AF.Square).then_inc(sq.h, 1)
            for mb in range(8):
                for kt in range(8):
                    s.wait_ge(pe.h, pe_v(mb, kt))
                    if mb >= 2:
                        s.wait_ge(so.h, so_v(mb - 2))
                    pv = PS[:, kt * 512:(kt + 1) * 512]
                    s.copy(out=VB[mb % 2][:, kt * 512:(kt + 1) * 512],
                           in_=pv).then_inc(ev.h, 1)

        @blk.vector
        def _(v):
            ndv = p.misc.setdefault("q_ndv", [0])

            def ssync(ins):
                ins.then_inc(dv.h, 1)
                ndv[0] += 1
                v.wait_ge(dv.h, ndv[0])

            for gq in range(16):
                v.wait_ge(ev.h, ev_qk(gq, 1, 3))
                qb = QB[gq % 2]
                for h2 in range(2):
                    base = h2 * 2 * T
                    E = qb[0:32, base:base + T]
                    O = qb[32:64, base:base + T]
                    ssync(v.tensor_tensor(SC[0][:, :], E, CO[0:32, :], op=ALU.mult))
                    ssync(v.tensor_tensor(SC[1][:, :], E, SI[0:32, :], op=ALU.mult))
                    ssync(v.tensor_tensor(SC[2][:, :], O, CO[32:64, :], op=ALU.mult))
                    ssync(v.tensor_tensor(SC[3][:, :], O, SI[32:64, :], op=ALU.mult))
                    ssync(v.tensor_tensor(E, SC[0][:, :], SC[3][:, :], op=ALU.subtract))
                    ssync(v.tensor_tensor(O, SC[2][:, :], SC[1][:, :], op=ALU.add))
                ssync(v.tensor_copy(FLU[0:1, 0:2], FLU[0:1, 2:4]))
                v.nop().then_inc(rp.h, 1)
                HMX = QNB if gq < 8 else KNB
                for u in range(4):
                    hl, hf = u // 2, u % 2
                    par = u % 2
                    head_col = (gq % 8) * 2 + hl
                    v.wait_ge(pn.h, OPN + gq * 4 + u + 1)
                    npv = PS[:, 3072 + par * 512: 3072 + par * 512 + 512]
                    ssync(v.reduce_max(RMX[:, par:par + 1], npv,
                                       axis=mybir.AxisListType.X))
                    if hf == 0:
                        ssync(v.tensor_copy(HMX[:, head_col:head_col + 1],
                                            RMX[:, par:par + 1]))
                    else:
                        ssync(v.tensor_tensor(HMX[:, head_col:head_col + 1],
                                              RMX[:, par:par + 1],
                                              HMX[:, head_col:head_col + 1], op=ALU.max))
                    ssync(v.tensor_copy(FLU[0:1, 0:2], FLU[0:1, 2:4]))
                    v.nop().then_inc(dn.h, 1)

        @blk.sync
        def _(sy):
            for gq in range(16):
                sy.wait_ge(rp.h, ORP + gq + 1)
                dst = qo_h if gq < 8 else ko_h
                mg = gq % 8
                sy.dma_start(out=dst[:, mg * 4 * T:(mg + 1) * 4 * T],
                             in_=QB[gq % 2][:, :]).then_inc(so.h, 16)
            sy.wait_ge(dn.h, ODN + 32)
            sy.dma_start(out=qn_h, in_=QNB[:, :]).then_inc(so.h, 16)
            sy.wait_ge(dn.h, ODN + 64)
            sy.dma_start(out=kn_h, in_=KNB[:, :]).then_inc(so.h, 16)
            for mb in range(8):
                sy.wait_ge(ev.h, ev_v(mb, 7))
                srcv = VB[mb % 2][:, :].rearrange("p (kt m) -> p kt m", m=512)
                dstv = vo_h.rearrange("p (kt c) -> p kt c", c=D)[:, :, mb * 512:(mb + 1) * 512]
                sy.dma_start(out=dstv, in_=srcv).then_inc(so.h, 16)
            sy.wait_ge(so.h, OSO + 16 * 26)


def emit_scatter_q(p, io, FSB, FLG, mode=3):
    nc = p.nc
    bar = p.sem("barso")
    bard = p.sem("bardone")
    KABS, VABS, NSH = io["kabs"], io["vabs"], io["nsh"]

    import contextlib
    with nc.Block() as blk, contextlib.ExitStack() as regs:
        f = regs.enter_context(nc.sync.register("b0f"))
        c = regs.enter_context(nc.sync.register("b0c"))
        nr = regs.enter_context(nc.sync.register("b0n"))

        @blk.sync
        def _(sy):
            nd = 0
            for arm in nc.sync.switch_core_id(n=128):
                rank = arm.logical % 8
                r = rank % 2
                n = 0
                for half in range(2):
                    ch = ([0, 3, 4, 7] if r == 0 else [1, 2, 5, 6])[2 * half:2 * half + 2]
                    for j, ac in enumerate(ch):
                        srck = io["ksc"][half].rearrange("p (c t) -> p c t", t=T)[
                            :, :, j * 512:(j + 1) * 512]
                        dstk = KABS.rearrange("p (c t) -> p c t", t=SEQ)[
                            :, :, ac * 512:(ac + 1) * 512]
                        nc.sync.dma_start(out=dstk, in_=srck).then_inc(bar.h, 16)
                        n += 16
                        srcv = io["vsc"][half].rearrange("p (kt c) -> p kt c", c=D)[
                            :, j * 4:(j + 1) * 4, :]
                        dstv = VABS.rearrange("p (kt c) -> p kt c", c=D)[
                            :, ac * 4:(ac + 1) * 4, :]
                        nc.sync.dma_start(out=dstv, in_=srcv).then_inc(bar.h, 16)
                        n += 16
                    nc.sync.dma_start(
                        out=NSH[:, (r * 4 + half * 2) * H0:(r * 4 + half * 2 + 1) * H0],
                        in_=io["qnsc"][half]).then_inc(bar.h, 16)
                    n += 16
                    nc.sync.dma_start(
                        out=NSH[:, (r * 4 + half * 2 + 1) * H0:(r * 4 + half * 2 + 2) * H0],
                        in_=io["knsc"][half]).then_inc(bar.h, 16)
                    n += 16
                nc.sync.wait_ge(bar.h, bar.n + n)
                if mode >= 2:
                    nc.sync.dma_start(out=FLG[0:1, r:r + 1],
                                      in_=FSB[0:1, 0:1]).then_inc(bar.h, 16)
                    nc.sync.wait_ge(bar.h, bar.n + n + 16)
                    n += 16
                if mode >= 3:
                    col = 1 - r
                    nc.sync.load(nr, FSB[0:1, 0:1])
                    nc.sync.load(f, FLG[0:1, col:col + 1])
                    nc.sync.reg_alu(c, f, nr, ALU.not_equal)
                    with nc.sync.While(c):
                        nc.sync.load(f, FLG[0:1, col:col + 1])
                        nc.sync.reg_alu(c, f, nr, ALU.not_equal)
                nc.sync.nop().then_inc(bard.h, 1)
                nd = n
            bar.n += nd
            bard.n += 1




NSQB_F = [(8, 16), (24, 32)]


def emit_fa_prep(p, io, bufs):
    nc = p.nc
    (MK, QMX, KMX, MBALL, NS4, SDB, MSQ, ONES, FLU) = bufs
    wd = p.sem("a_wd"); dv = p.sem("a_dv"); mbd = p.sem("a_mbd"); mba = p.sem("a_mba")

    with nc.Block() as blk:
        @blk.gpsimd
        def _(g):
            g.dma_start(out=MK[:, :], in_=io["mki"]).then_inc(wd.h, 16)
            g.dma_start(out=ONES[:, :], in_=io["onesi"]).then_inc(wd.h, 16)
            g.wait_ge(p.sem("bardone").h, p.sem("bardone").n)
            g.dma_start(out=NS4[:, :], in_=io["nsh"]).then_inc(wd.h, 16)
            wd.n += 48

        @blk.vector
        def _(v):
            ndv = p.misc.setdefault("a_ndv", [0])

            def ssync(ins):
                ins.then_inc(dv.h, 1)
                ndv[0] += 1
                v.wait_ge(dv.h, ndv[0])

            def ms(semh, k=1):
                ssync(v.tensor_copy(FLU[0:1, 0:2], FLU[0:1, 2:4]))
                v.nop().then_inc(semh, k)

            v.wait_ge(wd.h, 48)
            for dst, off in ((QMX, 0), (KMX, 1)):
                ssync(v.tensor_copy(dst[:, :], NS4[:, off * H0:(off + 1) * H0]))
                for j in range(1, 4):
                    sl = j * 2 + off
                    ssync(v.tensor_tensor(dst[:, :], NS4[:, sl * H0:(sl + 1) * H0],
                                          dst[:, :], op=ALU.max))
            ssync(v.tensor_tensor(MSQ[:, 0:H0], QMX[:, :], KMX[:, :], op=ALU.mult))
            ms(mbd.h, 1)
            v.wait_ge(mba.h, 1)
            ssync(v.tensor_scalar(MBALL[:, :], SDB[:, 0:H0], -1.01 / 16.0, None,
                                  op0=ALU.mult))
            ms(mbd.h, 1)

        @blk.scalar
        def _(s):
            s.wait_ge(mbd.h, 1)
            s.activation(SDB[:, 0:H0], MSQ[:, 0:H0], AF.Sqrt).then_inc(mba.h, 1)


def emit_fa(p, half, io, bufs):
    nc = p.nc
    (AB, KHB, VHB, QHB, MK, PT, MBALL, ONES, RL, WOB, X0C, YOB, SQY,
     MU, MSQ, VARB, SDB, RSTD, FLU, PS) = bufs
    KABS, VABS = io["kabs"], io["vabs"]
    q_h = io["qsc"][half]
    x_h = io["x0i"][half]
    yo_h, mu_h, rs_h = io["y1sc"][half], io["mu1sc"][half], io["rs1sc"][half]

    wd = p.sem("a_wd"); qd = p.sem("a_qd"); st = p.sem("a_st"); ae = p.sem("a_ae")
    pm = p.sem("a_pm"); pa = p.sem("a_pa"); ab = p.sem("a_ab"); dv = p.sem("a_dv")
    wp = p.sem("a_wp"); ey = p.sem("a_ey"); es = p.sem("a_es"); sp = p.sem("a_sp")
    fd = p.sem("a_fd"); fa = p.sem("a_fa"); so = p.sem("a_so"); mbd = p.sem("a_mbd")

    NS = NSQB_F[half]
    SBK = (0, 512, 2560, 3072)
    NSL = NS[0] + NS[1]
    NG = NSL // 8
    QBS = (0, NS[0])
    QBE = (NS[0] - 1, NSL - 1)
    MOFF = 0 if half == 0 else (NSQB_F[0][0] + NSQB_F[0][1])

    NS_PREV = 0 if half == 0 else (NSQB_F[0][0] + NSQB_F[0][1])
    OSU = half * 16 * NS_PREV
    OHQ = half * 32
    NG0 = (NSQB_F[0][0] + NSQB_F[0][1]) // 8
    OQD = half * 16 * 16 * (1 + 3 * NG0)
    NMG = 16
    NPC = 4
    OWP = half * 64
    OEY = half * 65
    OSO = half * 16 * 18
    OWD_BASE = 48
    OWD = OWD_BASE + half * 16 * 48

    def qd_group(head, g):
        return OQD + 16 * (head * (1 + 3 * NG) + 1 + 3 * (g + 1))

    def wd_w(mg):
        return OWD + 16 * (3 * mg + 1)

    def wd_x0(mg):
        return OWD + 16 * (3 * mg + 3)

    with nc.Block() as blk:
        @blk.gpsimd
        def _(g):
            for h in range(H0):
                for u in range(NSL):
                    su = OSU + h * NSL + u
                    g.wait_ge(ae.h, su + 1)
                    g.tensor_tensor(PT[u % 4][:, :], PT[u % 4][:, :],
                                    MK[:, (MOFF + u) * 512:(MOFF + u + 1) * 512],
                                    op=ALU.mult).then_inc(pm.h, 1)
            for mg in range(NMG):
                if mg >= 2:
                    g.wait_ge(wp.h, OWP + (mg - 2) * NPC + NPC)
                g.dma_start(out=WOB[mg % 2][:, :], in_=io["woi"][mg]).then_inc(wd.h, 16)
                for mc in range(2):
                    if mg >= 1:
                        g.wait_ge(ey.h, OEY + (mg - 1) * NPC + NPC)
                    g.dma_start(out=X0C[mc][:, :],
                                in_=x_h[:, (2 * mg + mc) * T:(2 * mg + mc + 1) * T]
                                ).then_inc(wd.h, 16)

        @blk.sync
        def _(sy):
            for h in range(H0):
                if h >= 2:
                    sy.wait_ge(pa.h, OSU + NSL * (h - 1))
                sy.dma_start(out=QHB[h % 2][:, :],
                             in_=q_h[:, 2 * h * T:(2 * h + 2) * T]).then_inc(qd.h, 16)
                for g in range(NG):
                    gidx = h * NG + g
                    if gidx >= 2:
                        sy.wait_ge(pa.h, OSU + (gidx - 1) * 8)
                    qb = 0 if 8 * g < NS[0] else 1
                    t0 = 8 * g - (0 if qb == 0 else NS[0])
                    for c in range(2):
                        src = KABS.rearrange("p (ch t) -> p ch t", t=SEQ)[
                            :, 2 * h + c, t0 * 128:(t0 + 8) * 128]
                        sy.dma_start(out=KHB[gidx % 2][:, c * 1024:(c + 1) * 1024],
                                     in_=src).then_inc(qd.h, 16)
                    vs = VABS.rearrange("p (kt c) -> p kt c", c=D)
                    sy.dma_start(
                        out=VHB[gidx % 2][:, :].rearrange("p (s c) -> p s c", c=256),
                        in_=vs[:, t0:t0 + 8, h * 256:(h + 1) * 256]
                    ).then_inc(qd.h, 16)
            for mg in range(NMG):
                sy.wait_ge(ey.h, OEY + mg * NPC + NPC + 1)
                sy.dma_start(out=yo_h[:, 2 * mg * T:(2 * mg + 2) * T],
                             in_=YOB[mg % 2][:, :]).then_inc(so.h, 16)
            sy.wait_ge(fd.h, half * 3 + 3)
            sy.dma_start(out=mu_h, in_=MU[:, :]).then_inc(so.h, 16)
            sy.dma_start(out=rs_h, in_=RSTD[:, :]).then_inc(so.h, 16)
            sy.wait_ge(so.h, OSO + 16 * 18)

        @blk.vector
        def _(v):
            ndv = p.misc["a_ndv"]

            def ssync(ins):
                ins.then_inc(dv.h, 1)
                ndv[0] += 1
                v.wait_ge(dv.h, ndv[0])

            def ms(semh, k=1):
                ssync(v.tensor_copy(FLU[0:1, 0:2], FLU[0:1, 2:4]))
                v.nop().then_inc(semh, k)

            for h in range(H0):
                for qb in range(2):
                    v.wait_ge(pa.h, OSU + h * NSL + QBE[qb] + 1)
                    ssync(v.reciprocal(RL[:, :], PS[:, 2048:2560]))
                    for c in range(2):
                        ssync(v.tensor_tensor(
                            AB[:, (2 * h + c) * T + qb * 512:(2 * h + c) * T + qb * 512 + 512],
                            PS[:, 1024 + c * 512:1024 + (c + 1) * 512],
                            RL[:, :], op=ALU.mult))
                    ms(ab.h, 1)
            for mg in range(NMG):
                if mg >= 2:
                    v.wait_ge(so.h, OSO + 16 * (mg - 1))
                for hf in range(2):
                    for mc in range(2):
                        pc = mg * NPC + hf * 2 + mc
                        v.wait_ge(wp.h, OWP + pc + 1)
                        v.wait_ge(wd.h, wd_x0(mg))
                        ssync(v.tensor_tensor(
                            YOB[mg % 2][:, mc * T + hf * 512: mc * T + hf * 512 + 512],
                            PS[:, mc * 512:(mc + 1) * 512],
                            X0C[mc][:, hf * 512:hf * 512 + 512],
                            op=ALU.add))
                        ms(ey.h, 1)
            ms(ey.h, 1)
            v.wait_ge(sp.h, OWP + NMG * NPC)
            ssync(v.tensor_scalar(MU[:, :], PS[:, 1024:2048], 1.0 / D, None,
                                  op0=ALU.mult))
            ssync(v.tensor_tensor(MSQ[:, :], MU[:, :], MU[:, :], op=ALU.mult))
            ssync(v.scalar_tensor_tensor(VARB[:, :], PS[:, 2048:3072], 1.0 / D,
                                         MSQ[:, :], op0=ALU.mult, op1=ALU.subtract))
            ssync(v.tensor_scalar(VARB[:, :], VARB[:, :], EPS, None, op0=ALU.add))
            ms(fd.h, 1)
            v.wait_ge(fa.h, half + 1)
            ssync(v.reciprocal(RSTD[:, :], SDB[:, :]))
            ms(fd.h, 2)

        @blk.scalar
        def _(s):
            for h in range(H0):
                for u in range(NSL):
                    su = OSU + h * NSL + u
                    s.wait_ge(st.h, su + 1)
                    if su >= 4:
                        s.wait_ge(pa.h, su - 3)
                    if su == 0:
                        s.wait_ge(mbd.h, 2)
                    s.activation(PT[u % 4][:, :], PS[:, SBK[u % 4]:SBK[u % 4] + 512],
                                 AF.Exp, bias=MBALL[:, h:h + 1],
                                 scale=1.0 / 16.0).then_inc(ae.h, 1)
            for mg in range(NMG):
                for hf in range(2):
                    for mc in range(2):
                        pc = mg * NPC + hf * 2 + mc
                        s.wait_ge(ey.h, OEY + pc + 2)
                        if pc >= 2:
                            s.wait_ge(sp.h, OWP + pc - 1)
                        s.activation(SQY[pc % 2][:, :],
                                     YOB[mg % 2][:, mc * T + hf * 512: mc * T + hf * 512 + 512],
                                     AF.Square).then_inc(es.h, 1)
            s.wait_ge(fd.h, half * 3 + 1)
            s.activation(SDB[:, :], VARB[:, :], AF.Sqrt).then_inc(fa.h, 1)

        @blk.tensor
        def _(t):
            def a_s(h, u):
                su = OSU + h * NSL + u
                qb = 0 if u < NS[0] else 1
                qoff = qb * 512
                g = u // 8
                gb = (h * NG + g) % 2
                sl = u % 8
                t.wait_ge(qd.h, qd_group(h, g))
                if su >= 4:
                    t.wait_ge(ae.h, su - 3)
                stp = PS[:, SBK[u % 4]:SBK[u % 4] + 512]
                for c in range(2):
                    ins = t.matmul(stp,
                                   KHB[gb][:, c * 1024 + sl * 128: c * 1024 + (sl + 1) * 128],
                                   QHB[h % 2][:, c * T + qoff: c * T + qoff + 512],
                                   start=(c == 0), stop=(c == 1))
                ins.then_inc(st.h, 1)

            def a_pv(h, u):
                su = OSU + h * NSL + u
                qb = 0 if u < NS[0] else 1
                g = u // 8
                gb = (h * NG + g) % 2
                sl = u % 8
                t.wait_ge(pm.h, su + 1)
                first = (u == QBS[qb])
                last = (u == QBE[qb])
                if first:
                    ghq = OHQ + h * 2 + qb
                    if ghq >= 1:
                        t.wait_ge(ab.h, ghq)
                for c in range(2):
                    t.matmul(PS[:, 1024 + c * 512:1024 + (c + 1) * 512],
                             VHB[gb][:, sl * 256 + c * 128: sl * 256 + (c + 1) * 128],
                             PT[u % 4][:, :], start=first, stop=last)
                ins = t.matmul(PS[:, 2048:2560], ONES[:, :], PT[u % 4][:, :],
                               start=first, stop=last)
                ins.then_inc(pa.h, 1)

            for h in range(H0):
                for u in range(NSL):
                    a_s(h, u)
                    if u >= 2:
                        a_pv(h, u - 2)
                a_pv(h, NSL - 2)
                a_pv(h, NSL - 1)
            t.wait_ge(ae.h, OSU + H0 * NSL)
            t.wait_ge(ab.h, OHQ + H0 * 2)

            def fa_stats(pc2l):
                mg2, hf2, mc2 = pc2l // NPC, (pc2l % NPC) // 2, pc2l % 2
                t.wait_ge(ey.h, OEY + pc2l + 1)
                t.wait_ge(es.h, OWP + pc2l + 1)
                sy_start = (mg2 == 0 and mc2 == 0)
                sy_stop = (mg2 == NMG - 1 and mc2 == 1)
                t.matmul(PS[:, 1024 + hf2 * 512:1024 + (hf2 + 1) * 512],
                         ONES[:, :],
                         YOB[mg2 % 2][:, mc2 * T + hf2 * 512: mc2 * T + hf2 * 512 + 512],
                         start=sy_start, stop=sy_stop)
                ins2 = t.matmul(PS[:, 2048 + hf2 * 512:2048 + (hf2 + 1) * 512],
                                ONES[:, :], SQY[pc2l % 2][:, :],
                                start=sy_start, stop=sy_stop)
                ins2.then_inc(sp.h, 1)

            for mg in range(NMG):
                t.wait_ge(wd.h, wd_w(mg))
                for hf in range(2):
                    for mc in range(2):
                        pcl = mg * NPC + hf * 2 + mc
                        if pcl >= 2:
                            t.wait_ge(ey.h, OEY + pcl - 1)
                        pj = PS[:, mc * 512:(mc + 1) * 512]
                        for kc in range(NCH):
                            ins = t.matmul(
                                pj,
                                WOB[mg % 2][:, kc * 256 + mc * 128: kc * 256 + (mc + 1) * 128],
                                AB[:, kc * T + hf * 512: kc * T + hf * 512 + 512],
                                start=(kc == 0), stop=(kc == NCH - 1))
                        ins.then_inc(wp.h, 1)
                        if pcl >= 2:
                            fa_stats(pcl - 2)
            fa_stats(NMG * NPC - 2)
            fa_stats(NMG * NPC - 1)


def build_fused(upto=99, barmode=3):
    p = Prog("fused")
    p.misc["barmode"] = barmode
    nc = p.nc
    io = {}
    io["x0i"] = [p.din(f"x0i{h}", [128, NCH * T]) for h in range(2)]
    io["wqi"] = p.din("wqi", [8, 128, NCH * 512])
    io["wki"] = p.din("wki", [8, 128, NCH * 512])
    io["wvi"] = p.din("wvi", [8, 128, NCH * 512])
    io["cosi"] = [p.din(f"cosi{h}", [64, T]) for h in range(2)]
    io["sini"] = [p.din(f"sini{h}", [64, T]) for h in range(2)]
    io["onesi"] = p.din("onesi", [128, 128])
    io["bnonce"] = p.din("bnonce", [1, 16], mybir.dt.int32)

    def priv(name, shape, dtype=BF16):
        return nc.dram_tensor(name, list(shape), dtype).ap()

    io["qsc"] = [priv(f"qsc{h}", [128, NCH * T]) for h in range(2)]
    io["ksc"] = [priv(f"ksc{h}", [128, NCH * T]) for h in range(2)]
    io["vsc"] = [priv(f"vsc{h}", [128, 8 * D]) for h in range(2)]
    io["qnsc"] = [priv(f"qnsc{h}", [128, H0], F32) for h in range(2)]
    io["knsc"] = [priv(f"knsc{h}", [128, H0], F32) for h in range(2)]
    if upto <= 1:
        kdbg = p.dout("kdbg", [128, NCH * SEQ], BF16)
        vdbg = p.dout("vdbg", [128, (SEQ // 128) * D], BF16)
        ndbg = p.dout("ndbg", [128, 8 * H0], F32)
    io["kabs"] = nc.dram_tensor("kabs", [128, NCH * SEQ], BF16, addr_space="Shared").ap()
    io["vabs"] = nc.dram_tensor("vabs", [128, (SEQ // 128) * D], BF16, addr_space="Shared").ap()
    io["nsh"] = nc.dram_tensor("nsh", [128, 8 * H0], F32, addr_space="Shared").ap()
    FLG = nc.dram_tensor("flg", [1, 16], mybir.dt.int32, addr_space="Shared").ap()

    FSB = p.sbuf("FSB", [1, 16], mybir.dt.int32)
    FLU = p.sbuf("FLUf", [1, 4], F32)
    PS = p.psum("PSf", [128, 8 * 512], F32)
    p.misc["FLU"] = FLU
    p.misc["PS"] = PS
    fprep = p.sem("fprep")

    with nc.Block() as blk:
        @blk.sync
        def _(sy):
            sy.dma_start(out=FSB[0:1, :], in_=io["bnonce"]).then_inc(fprep.h, 16)
            sy.wait_ge(fprep.h, 16)

    import contextlib
    sQ = contextlib.ExitStack()
    X = sQ.enter_context(nc.sbuf_tensor("qX", [128, NCH * T], BF16))
    WB = [sQ.enter_context(nc.sbuf_tensor(f"qWB{i}", [128, NCH * 512], BF16)) for i in range(2)]
    QB = [sQ.enter_context(nc.sbuf_tensor(f"qQB{i}", [128, 4 * T], BF16)) for i in range(2)]
    VB = [sQ.enter_context(nc.sbuf_tensor(f"qVB{i}", [128, 8 * 512], BF16)) for i in range(2)]
    CO = sQ.enter_context(nc.sbuf_tensor("qCO", [64, T], F32))
    SI = sQ.enter_context(nc.sbuf_tensor("qSI", [64, T], F32))
    SC = [sQ.enter_context(nc.sbuf_tensor(f"qSC{i}", [32, T], F32)) for i in range(4)]
    ONESB = sQ.enter_context(nc.sbuf_tensor("qONES", [128, 128], BF16))
    SQB = [sQ.enter_context(nc.sbuf_tensor(f"qSQB{i}", [128, 512], BF16)) for i in range(2)]
    RMX = sQ.enter_context(nc.sbuf_tensor("qRMX", [128, 2], F32))
    QNB = sQ.enter_context(nc.sbuf_tensor("qQNB", [128, H0], F32))
    KNB = sQ.enter_context(nc.sbuf_tensor("qKNB", [128, H0], F32))
    bufsQ = (X, WB, QB, VB, CO, SI, SC, ONESB, SQB, RMX, QNB, KNB, FLU, PS)
    for half in range(2):
        emit_fq(p, half, io, bufsQ)
    sQ.close()

    if upto >= 1:
        emit_scatter_q(p, io, FSB, FLG, mode=p.misc.get("barmode", 3))

    if upto <= 1:
        fso = p.sem("fso")
        with nc.Block() as blk:
            @blk.sync
            def _(sy):
                sy.dma_start(out=kdbg, in_=io["kabs"]).then_inc(fso.h, 16)
                sy.dma_start(out=vdbg, in_=io["vabs"]).then_inc(fso.h, 16)
                sy.dma_start(out=ndbg, in_=io["nsh"]).then_inc(fso.h, 16)
                sy.wait_ge(fso.h, 48)
        return p

    if upto >= 2:
        io["mki"] = p.din("mki", [128, 80 * 512], mybir.dt.uint8)
        io["woi"] = p.din("woi", [16, 128, NCH * 256])
        if upto <= 2:
            io["y1sc"] = [p.dout(f"y1sc{h}", [128, NCH * T], BF16) for h in range(2)]
            io["mu1sc"] = [p.dout(f"mu1sc{h}", [128, T], F32) for h in range(2)]
            io["rs1sc"] = [p.dout(f"rs1sc{h}", [128, T], F32) for h in range(2)]
        else:
            io["y1sc"] = [priv(f"y1sc{h}", [128, NCH * T]) for h in range(2)]
            io["mu1sc"] = [priv(f"mu1sc{h}", [128, T], F32) for h in range(2)]
            io["rs1sc"] = [priv(f"rs1sc{h}", [128, T], F32) for h in range(2)]
        sA = contextlib.ExitStack()
        AB = sA.enter_context(nc.sbuf_tensor("aAB", [128, NCH * T], BF16))
        MK = sA.enter_context(nc.sbuf_tensor("aMK", [128, 80 * 512], mybir.dt.uint8))
        KHB = [sA.enter_context(nc.sbuf_tensor(f"aKHB{i}", [128, 2048], BF16)) for i in range(2)]
        VHB = [sA.enter_context(nc.sbuf_tensor(f"aVHB{i}", [128, 2048], BF16)) for i in range(2)]
        QHB = [sA.enter_context(nc.sbuf_tensor(f"aQHB{i}", [128, 2 * T], BF16)) for i in range(2)]
        PT = [sA.enter_context(nc.sbuf_tensor(f"aPT{i}", [128, 512], BF16)) for i in range(4)]
        QMX = sA.enter_context(nc.sbuf_tensor("aQMX", [128, H0], F32))
        KMX = sA.enter_context(nc.sbuf_tensor("aKMX", [128, H0], F32))
        MBALL = sA.enter_context(nc.sbuf_tensor("aMBALL", [128, H0], F32))
        NS4 = sA.enter_context(nc.sbuf_tensor("aNS4", [128, 8 * H0], F32))
        ONESA = sA.enter_context(nc.sbuf_tensor("aONES", [128, 128], BF16))
        RL = sA.enter_context(nc.sbuf_tensor("aRL", [128, 512], F32))
        WOB = [sA.enter_context(nc.sbuf_tensor(f"aWOB{i}", [128, NCH * 256], BF16)) for i in range(2)]
        X0C = [sA.enter_context(nc.sbuf_tensor(f"aX0C{i}", [128, T], BF16)) for i in range(2)]
        YOB = [sA.enter_context(nc.sbuf_tensor(f"aYOB{i}", [128, 2 * T], BF16)) for i in range(2)]
        SQY = [sA.enter_context(nc.sbuf_tensor(f"aSQY{i}", [128, 512], BF16)) for i in range(2)]
        MUA = sA.enter_context(nc.sbuf_tensor("aMU", [128, T], F32))
        MSQA = sA.enter_context(nc.sbuf_tensor("aMSQ", [128, T], F32))
        VARA = sA.enter_context(nc.sbuf_tensor("aVAR", [128, T], F32))
        SDA = sA.enter_context(nc.sbuf_tensor("aSD", [128, T], F32))
        RSTDA = sA.enter_context(nc.sbuf_tensor("aRSTD", [128, T], F32))
        emit_fa_prep(p, io, (MK, QMX, KMX, MBALL, NS4, SDA, MSQA, ONESA, FLU))
        bufsA = (AB, KHB, VHB, QHB, MK, PT, MBALL, ONESA, RL, WOB, X0C, YOB, SQY,
                 MUA, MSQA, VARA, SDA, RSTDA, FLU, PS)
        for half in range(2):
            emit_fa(p, half, io, bufsA)
        sA.close()

    if upto <= 2:
        return p

    io["w1i"] = p.din("w1i", [NSTR, 128, 1024])
    io["w3i"] = p.din("w3i", [NSTR, 128, 1024])
    io["w2i"] = p.din("w2i", [16, 128, 22 * 256])
    io["a1i"] = p.din("a1i", [128, 64])
    io["a3i"] = p.din("a3i", [128, 64])
    io["b1i"] = p.din("b1i", [2, NSTR * 128])
    io["b3i"] = p.din("b3i", [2, NSTR * 128])
    io["a2i"] = p.din("a2i", [128, NSTR * 2])
    io["b2i"] = p.din("b2i", [2, D])
    io["wq1i"] = p.din("wq1i", [16, 128, NCH * 256])
    io["wk1i"] = p.din("wk1i", [4, 128, NCH * 256])
    io["wv1i"] = p.din("wv1i", [4, 128, NCH * 256])
    if upto <= 3:
        io["x2sc"] = [p.dout(f"x2sc{h}", [128, NCH * T], BF16) for h in range(2)]
        io["q1sc"] = [p.dout(f"q1sc{h}", [128, NCH * T], BF16) for h in range(2)]
        io["k1sc"] = [p.dout(f"k1sc{h}", [128, 8 * T], BF16) for h in range(2)]
        io["v1sc"] = [p.dout(f"v1sc{h}", [128, 8 * T], BF16) for h in range(2)]
        io["q1nsc"] = [p.dout(f"q1nsc{h}", [128, 32], F32) for h in range(2)]
        io["k1nsc"] = [p.dout(f"k1nsc{h}", [128, 8], F32) for h in range(2)]
    else:
        io["x2sc"] = [priv(f"x2sc{h}", [128, NCH * T]) for h in range(2)]
        io["q1sc"] = [priv(f"q1sc{h}", [128, NCH * T]) for h in range(2)]
        io["k1sc"] = [priv(f"k1sc{h}", [128, 8 * T]) for h in range(2)]
        io["v1sc"] = [priv(f"v1sc{h}", [128, 8 * T]) for h in range(2)]
        io["q1nsc"] = [priv(f"q1nsc{h}", [128, 32], F32) for h in range(2)]
        io["k1nsc"] = [priv(f"k1nsc{h}", [128, 8], F32) for h in range(2)]

    emit_fence(p, sync_waits=[(p.sem("a_so").h, 16 * 18 * 2)])
    emit_ff(p, 0, io)
    emit_fence(p)
    emit_ff(p, 1, io)

    if upto <= 3:
        return p

    io["wo1i"] = p.din("wo1i", [16, 128, NCH * 256])
    io["k1sh"] = nc.dram_tensor("k1sh", [128, 8 * SEQ], BF16, addr_space="Shared").ap()
    io["v1sh"] = nc.dram_tensor("v1sh", [128, 32 * T], BF16, addr_space="Shared").ap()
    io["n1sh"] = nc.dram_tensor("n1sh", [128, 4 * 40], F32, addr_space="Shared").ap()
    if upto <= 4:
        io["y3sc"] = [p.dout(f"y3sc{h}", [128, NCH * T], BF16) for h in range(2)]
        io["mu3sc"] = [p.dout(f"mu3sc{h}", [128, T], F32) for h in range(2)]
        io["rs3sc"] = [p.dout(f"rs3sc{h}", [128, T], F32) for h in range(2)]
        io["cssc"] = [p.dout(f"cssc{h}", [128, NCH], F32) for h in range(2)]
    else:
        io["y3sc"] = [priv(f"y3sc{h}", [128, NCH * T]) for h in range(2)]
        io["mu3sc"] = [priv(f"mu3sc{h}", [128, T], F32) for h in range(2)]
        io["rs3sc"] = [priv(f"rs3sc{h}", [128, T], F32) for h in range(2)]
        io["cssc"] = [priv(f"cssc{h}", [128, NCH], F32) for h in range(2)]

    emit_fence(p)
    emit_scatter2(p, io, FSB, FLG)
    emit_fb(p, 0, io)
    emit_fence(p)
    emit_fb(p, 1, io)

    if upto <= 4:
        return p

    io["worgi"] = p.din("worgi", [16, 128, NCH * 256])
    io["downi"] = p.din("downi", [8, 128, NCH * 64])
    io["uppi"] = p.din("uppi", [4, 128, 4 * 1024])
    io["keysi"] = p.din("keysi", [128, 8])
    io["wq2i"] = p.din("wq2i", [128, NCH * 128])
    io["cssh"] = nc.dram_tensor("cssh", [128, 4 * NCH], F32, addr_space="Shared").ap()
    io["y3out"] = priv("y3out", [128, NCH * T])
    io["mu3out"] = priv("mu3out", [128, T], F32)
    io["rs3out"] = priv("rs3out", [128, T], F32)
    io["y4d"] = priv("y4d", [128, NCH * T])
    io["outo"] = p.dout("outo", [128, NCH * T], F32)

    emit_fence(p)
    emit_scatter3(p, io, FSB, FLG)
    emit_fc(p, io)
    return p


def _fused_masks(core):
    _, ch = _fused_chunks(core)
    masks = np.zeros((80, 128, 512), np.uint8)
    ktl = np.arange(128)[:, None]
    qtl = np.arange(512)[None, :]
    off = 0
    for h in range(2):
        for qb in range(2):
            i = ch[2 * h + qb]
            nsl = NSQB_F[h][qb]
            nreal = 4 * (i + 1)
            for s_ in range(nsl):
                if s_ < nreal - 4:
                    masks[off + s_] = 1
                elif s_ < nreal:
                    d = -128 * (s_ - (nreal - 4))
                    masks[off + s_] = (ktl <= qtl + d).astype(np.uint8)
            off += nsl
    return np.ascontiguousarray(masks.transpose(1, 0, 2).reshape(128, 80 * 512))


def run_fused(inputs, static, upto=99, barmode=3):
    p = build_fused(upto, barmode)
    x0 = np.asarray(inputs["L0_attn_hidden_states"], dtype=np.float32)
    in_maps = []
    import time as _time
    nonce = (int(_time.time() * 37) & 0x3FFFFFFF) + 7
    for c in range(NCORES):
        m = {
            "wqi": static["wq_img"], "wki": static["wk_img"], "wvi": static["wv_img"],
            "onesi": np.ones((128, 128), np.float32),
            "bnonce": np.arange(nonce, nonce + 16, dtype=np.int32).reshape(1, 16),
        }
        if upto >= 2:
            m["mki"] = _fused_masks(c)
            m["woi"] = static["wo_img"]
        if upto >= 3:
            m.update({
                "w1i": static["w1_img"], "w3i": static["w3_img"], "w2i": static["w2_img"],
                "a1i": static["a1_img"], "a3i": static["a3_img"],
                "b1i": static["b1_arr"], "b3i": static["b3_arr"],
                "a2i": static["a2_img"], "b2i": static["b2_arr"],
                "wq1i": static["wq1_img"], "wk1i": static["wk1_img"],
                "wv1i": static["wv1_img"],
            })
        if upto >= 4:
            m["wo1i"] = static["wo1_img"]
        if upto >= 5:
            m.update({
                "worgi": static["worg_img"], "downi": static["down_img"],
                "uppi": static["upp_arr"], "keysi": static["keys_arr"],
                "wq2i": static["wq2_img"],
            })
        for h in range(2):
            b, toks, _ = _fused_toks(c, h)
            xT = np.ascontiguousarray(x0[b][toks].T)
            m[f"x0i{h}"] = to_image(xT.astype(np.float32))
            cos, sin = rope_tables(toks)
            m[f"cosi{h}"] = np.ascontiguousarray(np.vstack([cos, cos]))
            m[f"sini{h}"] = np.ascontiguousarray(np.vstack([sin, sin]))
        in_maps.append(m)
    res = run_bass_kernel_spmd(p.nc, in_maps, core_ids=list(range(NCORES)))
    p.close()
    return res.results


def kernel(**inputs):
    inputs = {k: np.asarray(v) for k, v in inputs.items()}
    static = _prep_static(inputs)
    res = run_fused(inputs, static, upto=5)
    out = np.zeros((2, SEQ, D), np.float32)
    for c in range(NCORES):
        oh = _fused_out_half(c)
        b, toks, _ = _fused_toks(c, oh)
        out[b][toks] = from_image(res[c]["outo"], D).T
    return out
